# version 11
# baseline (speedup 1.0000x reference)
from collections import defaultdict
from contextlib import ExitStack

import numpy as np

import concourse.bass as bass
import concourse.mybir as mybir
from concourse.bass import IndirectOffsetOnAxis
from concourse.bass_utils import run_bass_kernel_spmd

F32 = mybir.dt.float32
BF16 = mybir.dt.bfloat16
I32 = mybir.dt.int32
AF = mybir.ActivationFunctionType
ALU = mybir.AluOpType
AX = mybir.AxisListType

D = 1024
H = 16
NCH = 8
R = 64
E = 8
EPS = 1e-5

ENGINES = ("sync", "gpsimd", "tensor", "vector", "scalar")


class Plan:

    def __init__(self, nc):
        self.nc = nc
        self.ops = {e: [] for e in ENGINES}
        self.cnt = defaultdict(int)
        self.handles = {}

    def sem(self, stack, name):
        self.handles[name] = stack.enter_context(self.nc.semaphore(name))

    ENG_SEMS = ("T", "V", "S", "G", "BAR")

    def op(self, eng, fn, waits=(), sem=None, inc=1):
        ev = None
        ws = [w for w in waits if w is not None]
        for w in ws:
            assert isinstance(w, tuple) and len(w) == 2, f"bad wait {w!r}"
        if sem is not None:
            if sem not in self.ENG_SEMS and self.cnt[sem] > 0:
                ws.append((sem, self.cnt[sem]))
            self.cnt[sem] += inc
            ev = (sem, self.cnt[sem])
        self.ops[eng].append((fn, ws, (sem, inc) if sem else None))
        return ev

    def t(self, fn, waits=()):
        return self.op("tensor", fn, waits, "T")

    def v(self, fn, waits=()):
        return self.op("vector", fn, waits, "V")

    def s(self, fn, waits=()):
        return self.op("scalar", fn, waits, "S")

    def g(self, fn, waits=()):
        return self.op("gpsimd", fn, waits, "G")

    def dma(self, eng, fn, waits=(), sem="WG"):
        return self.op(eng, fn, waits, sem, inc=16)

    def barrier(self):
        self.cnt["BAR"] += len(ENGINES)
        tgt = self.cnt["BAR"]
        for e in ENGINES:
            self.ops[e].append((lambda eng: eng.nop(), [], ("BAR", 1)))
            self.ops[e].append((None, [("BAR", tgt)], None))

    def emit(self, blk):
        for eng in ENGINES:
            ops = self.ops[eng]
            if not ops:
                continue

            def body(e, ops=ops):
                last = {}
                for fn, waits, inc in ops:
                    for (s, v) in waits:
                        if last.get(s, -1) >= v:
                            continue
                        e.wait_ge(self.handles[s], v)
                        last[s] = v
                    if fn is None:
                        continue
                    inst = fn(e)
                    if inc is not None and inc[0] is not None:
                        inst.then_inc(self.handles[inc[0]], inc[1])
            getattr(blk, eng)(body)


def _ln(P, bn6, st, rr, src_fn, state, key, extra_waits=()):
    prev = state.get(key, {})
    e1a = P.v(lambda v: v.bn_stats(bn6[:, 0:6], src_fn()[:, 0:512]),
              waits=list(extra_waits) + [prev.get("aggr")])
    e1 = P.v(lambda v: v.bn_stats(bn6[:, 6:12], src_fn()[:, 512:1024]),
             waits=[e1a])
    e2 = P.v(lambda v: v.bn_aggr(st[:], bn6[:]),
             waits=[e1, prev.get("st_read")])
    e3 = P.v(lambda v: v.tensor_scalar(rr[:, 0:1], st[:, 1:2], EPS, None,
                                       ALU.add),
             waits=[e2, prev.get("rr_read")])
    e4 = P.s(lambda s: s.activation(rr[:, 0:1], rr[:, 0:1], AF.Ln), waits=[e3])
    e5 = P.s(lambda s: s.activation(rr[:, 0:1], rr[:, 0:1], AF.Exp, scale=-0.5),
             waits=[e4])
    e6 = P.v(lambda v: v.tensor_scalar(rr[:, 1:2], st[:, 0:1], rr[:, 0:1],
                                       -1.0, ALU.mult, ALU.mult),
             waits=[e5, e2])
    st_new = {"aggr": e2, "st_read": e6}
    state[key] = st_new

    def set_last_reader(ev):
        st_new["rr_read"] = ev
    return e6, set_last_reader


def build_program(T=4096, t_total=None, debug_dump=False, phases="ABC", bstop=None):
    if t_total is None:
        t_total = 2 * T
    nS = T // 128
    nM = T // 512
    SUB = 4

    nc = bass.Bass("TRN2", target_bir_lowering=False)

    x_in = nc.dram_tensor("x", [T, D], F32, kind="ExternalInput").ap()
    wqkv = nc.dram_tensor("wqkv", [D, 3 * D], F32, kind="ExternalInput").ap()
    wout = nc.dram_tensor("wout", [D, D], F32, kind="ExternalInput").ap()
    bout = nc.dram_tensor("bout", [D], F32, kind="ExternalInput").ap()
    basew = nc.dram_tensor("basew", [D, D], F32, kind="ExternalInput").ap()
    baseb = nc.dram_tensor("baseb", [D], F32, kind="ExternalInput").ap()
    mu_in = nc.dram_tensor("mu", [E, D], F32, kind="ExternalInput").ap()
    sig_in = nc.dram_tensor("sigma", [E, D], F32, kind="ExternalInput").ap()
    wd_in = nc.dram_tensor("wd", [E, D, R], F32, kind="ExternalInput").ap()
    bd_in = nc.dram_tensor("bd", [E, R], F32, kind="ExternalInput").ap()
    wu_in = nc.dram_tensor("wu", [E, R, D], F32, kind="ExternalInput").ap()
    bu_in = nc.dram_tensor("bu", [E, D], F32, kind="ExternalInput").ap()
    wc_in = nc.dram_tensor("wc", [E, D, D], F32, kind="ExternalInput").ap()
    bc_in = nc.dram_tensor("bc", [E, D], F32, kind="ExternalInput").ap()
    y_out = nc.dram_tensor("y", [T, D], F32, kind="ExternalOutput").ap()

    qTd = nc.dram_tensor("qTd", [nM, 128, 4096], BF16).ap()
    x1d = nc.dram_tensor("x1d", [T, D], F32).ap()
    wscb = nc.dram_tensor("wscb", [D, D], BF16).ap()
    wccb = nc.dram_tensor("wccb", [D, D], BF16).ap()
    cc1i = nc.dram_tensor("cc1i", [129, D], F32).ap()
    cc1o = nc.dram_tensor("cc1o", [129, D], F32).ap()
    cc2i = nc.dram_tensor("cc2i", [4, 2048], F32).ap()
    cc2o = nc.dram_tensor("cc2o", [4, 2048], F32).ap()
    idxd = nc.dram_tensor("idxd", [1, 1], I32).ap()
    lossd = nc.dram_tensor("lossd", [1, 8], F32).ap()
    if debug_dump:
        kdbg = nc.dram_tensor("kdbg", [T, D], BF16).ap()
        vdbg = nc.dram_tensor("vdbg", [T, D], BF16).ap()
        adbg = nc.dram_tensor("adbg", [T, D], BF16).ap()

    P = Plan(nc)
    RG = [[0, 1], [2, 3], [4, 5], [6, 7]]

    with ExitStack() as st:
        def sb(name, shape, dt):
            return st.enter_context(nc.sbuf_tensor(name, shape, dt))

        for s in ("T", "V", "S", "G", "BAR", "CC", "WH", "WG", "LB",
                  "XO0", "XO1", "GD0", "GD1", "GU", "GC0", "GC1", "WT0", "WT1",
                  "XA0", "XA1", "QO0", "QO1", "QI0", "QI1",
                  "X2O0", "X2O1", "XC0", "XC1", "DBG"):
            P.sem(st, s)

        ident = sb("ident", [128, 128], BF16)
        identf = sb("identf", [8, 8], F32)
        ones_b = sb("ones_b", [128, 1], BF16)
        ones_f = sb("ones_f", [128, 1], F32)
        iota_p = sb("iota_p", [128, 1], I32)
        iota8x = sb("iota8x", [128, 8], I32)
        iota8r = sb("iota8r", [1, 8], I32)
        bwT = sb("bwT", [128, NCH * D], BF16)
        bcombT = sb("bcombT", [128, NCH], F32)
        bbT = sb("bbT", [128, NCH], F32)
        bcT = sb("bcT", [128, NCH], F32)
        buT = sb("buT", [128, NCH], F32)
        bdT = sb("bdT", [64, 1], F32)
        loss8 = sb("loss8", [8, 1], F32)
        lrow = sb("lrow", [1, 8], F32)
        lmin = sb("lmin", [1, 1], F32)
        lone = sb("lone", [1, 8], F32)
        liot = sb("liot", [1, 8], F32)
        idxf1 = sb("idxf1", [1, 1], F32)
        idxi1 = sb("idxi1", [1, 1], I32)
        idxb = sb("idxb", [128, 1], I32)
        idxbf = sb("idxbf", [128, 1], F32)
        idc = sb("idc", [128, 1], F32)
        idu = sb("idu", [64, 1], F32)
        gcf = sb("gcf", [128, 8], F32)
        gci = sb("gci", [128, 8], I32)
        guf = sb("guf", [64, 1], F32)
        gui = sb("gui", [64, 1], I32)
        kvz_r = sb("kvz_r", [128, 8 * 130], BF16)
        zrow = sb("zrow", [1, 1024], BF16)
        r2c = sb("r2c", [128, nS], F32)
        q2c = sb("q2c", [128, nS], F32)
        qc2 = sb("qc2", [128, nS], F32)

        evI1 = P.g(lambda g: g.memset(ident[:], 0.0))
        evI2 = P.g(lambda g: g.affine_select(
            out=ident[:], in_=ident[:], compare_op=ALU.not_equal, fill=1.0,
            base=0, pattern=[[-1, 128]], channel_multiplier=1), waits=[evI1])
        evI3 = P.g(lambda g: g.memset(identf[:], 0.0))
        evI4 = P.g(lambda g: g.affine_select(
            out=identf[:], in_=identf[:], compare_op=ALU.not_equal, fill=1.0,
            base=0, pattern=[[-1, 8]], channel_multiplier=1), waits=[evI3])
        evOB = P.g(lambda g: g.memset(ones_b[:], 1.0))
        evOF = P.g(lambda g: g.memset(ones_f[:], 1.0))
        evIp = P.g(lambda g: g.iota(iota_p[:], pattern=[[0, 1]], base=0,
                                    channel_multiplier=1))
        evI8 = P.g(lambda g: g.iota(iota8x[:], pattern=[[128, 8]], base=0,
                                    channel_multiplier=1))
        evIr = P.g(lambda g: g.iota(iota8r[:], pattern=[[1, 8]], base=0,
                                    channel_multiplier=0))
        evWb = P.dma("gpsimd", lambda g: g.dma_start(out=wscb, in_=basew))
        evBW = []
        for c in range(NCH):
            evBW.append(P.dma("scalar", lambda s, c=c: s.dma_start(
                out=bwT[:, c * D:(c + 1) * D],
                in_=wscb[:, c * 128:(c + 1) * 128], transpose=True),
                waits=[evWb], sem="WT%d" % (c % 2)))
        evBB = []
        for c in range(NCH):
            evBB.append(P.dma("sync", lambda e, c=c: e.dma_start(
                out=bbT[:, c:c + 1],
                in_=baseb[c * 128:(c + 1) * 128].unsqueeze(1)), sem="WH"))
        WHs = ("WH", P.cnt["WH"])
        WTs = [("WT0", P.cnt["WT0"]), ("WT1", P.cnt["WT1"])]

        with ExitStack() as stA:
            def sba(n, sh, dt):
                return stA.enter_context(nc.sbuf_tensor(n, sh, dt))
            def psa(n, sh, dt):
                return stA.enter_context(nc.psum_tensor(n, sh, dt))

            wq_sb = sba("wq_sb", [128, NCH * 3 * D], BF16)
            kvsb = sba("kvsb", [128, 1024], F32)
            zsb = sba("zsb", [1, 1024], F32)
            xbuf = [sba("xA0", [128, 4096], F32), sba("xA1", [128, 4096], F32)]
            xn1 = [sba("xn1_%d" % i, [128, D], BF16) for i in range(4)]
            st1 = [sba("st1_0", [128, 2], F32), sba("st1_1", [128, 2], F32)]
            bn6 = [sba("bn6_0", [128, 12], F32), sba("bn6_1", [128, 12], F32)]
            rr1 = [sba("rr1_0", [128, 2], F32), sba("rr1_1", [128, 2], F32)]
            xn1T = sba("xn1T", [128, NCH * 512], BF16)
            kbf = [sba("kbf0", [128, D], BF16), sba("kbf1", [128, D], BF16)]
            vbf = [sba("vbf0", [128, D], BF16), sba("vbf1", [128, D], BF16)]
            texp = [sba("texp0", [128, 512], F32), sba("texp1", [128, 512], F32)]
            tqxp = [sba("tqxp0", [128, 512], F32), sba("tqxp1", [128, 512], F32)]
            ktmp = [sba("ktmp0", [128, 512], F32), sba("ktmp1", [128, 512], F32)]
            qtmp = [sba("qtmp0", [128, 512], F32), sba("qtmp1", [128, 512], F32)]
            kt2 = [sba("kt2_0", [128, 512], F32), sba("kt2_1", [128, 512], F32)]
            qt2 = [sba("qt2_0", [128, 512], F32), sba("qt2_1", [128, 512], F32)]
            qTs = [sba("qTs0", [128, 4096], BF16), sba("qTs1", [128, 4096], BF16)]

            pkv = [psa("pkv0", [128, 512], F32), psa("pkv1", [128, 512], F32)]
            pz = [psa("pz0", [1, 512], F32), psa("pz1", [1, 512], F32)]
            pmm = [psa("pmm0", [128, 512], F32), psa("pmm1", [128, 512], F32)]
            ptrA = psa("ptrA", [128, 1024], BF16)
            pq = psa("pq", [128, 512], F32)

            def x1nT(c, lo, hi):
                return xn1T[:, c * 512 + lo:c * 512 + hi]

            P.dma("gpsimd", lambda g: g.dma_start(
                out=wq_sb.ap().rearrange("p (c j) -> p c j", c=NCH), in_=wqkv.rearrange("(c p) j -> p c j", p=128)))
            WGs = ("WG", P.cnt["WG"])

            ln_state = {}
            ev_xfree = [None, None]
            ev_kread = [None, None]
            ev_texp_read = [None, None]
            ev_tq_read = [None, None]
            ev_ptr_free = None
            ev_pq_free = None
            ev_pmm_free = [None, None]
            ev_qT_free = [None, None]
            ev_xn1_free = [None, None, None, None]
            ev_kv_done = []
            ev_z_done = []
            ev_qT_out = []
            ev_xn1T_read = None

            pool_banks = [pmm[0], pmm[1], pq]
            pool_free = [None, None, None]
            pool_i = [0]

            def take_bank():
                i = pool_i[0] % 3
                pool_i[0] += 1
                return i, pool_banks[i], pool_free[i] or []

            def prepA(m):
                pb = m % 2
                evX = P.dma("sync", lambda e, m=m, pb=pb: e.dma_start(
                    out=xbuf[pb].ap().rearrange("p (s d) -> p s d", s=4),
                    in_=x_in[m * 512:(m + 1) * 512, :].rearrange(
                        "(s p) d -> p s d", p=128)),
                    waits=[ev_xfree[pb]], sem="XA%d" % pb)
                applies = []
                for s4 in range(SUB):
                    sp = s4 % 2
                    ev_rr, rr_done = _ln(
                        P, bn6[sp], st1[sp], rr1[sp],
                        (lambda pb=pb, s4=s4: xbuf[pb][:, s4 * D:(s4 + 1) * D]),
                        ln_state, ("A", sp), extra_waits=[evX])
                    e7 = P.s(lambda s, pb=pb, s4=s4, sp=sp: s.activation(
                        xn1[s4][:], xbuf[pb][:, s4 * D:(s4 + 1) * D], AF.Identity,
                        scale=rr1[sp][:, 0:1], bias=rr1[sp][:, 1:2]),
                        waits=[ev_rr, ev_xn1_free[s4]])
                    rr_done(e7)
                    applies.append(e7)
                return evX, applies

            prepdata = [None] * nM
            prepdata[0] = prepA(0)

            for m in range(nM):
                pb = m % 2
                evX, applies = prepdata[m]
                ev_macro_evac = []
                for s4 in range(SUB):
                    evT = []
                    for c in range(NCH):
                        evT.append(P.t(lambda t, c=c, s4=s4: t.transpose(
                            ptrA[:, c * 128:(c + 1) * 128],
                            xn1[s4][:, c * 128:(c + 1) * 128], ident[:]),
                            waits=[applies[s4], evI2, ev_ptr_free]))
                    ev_xn1_free[s4] = evT[-1]
                    evE = P.s(lambda s, s4=s4: s.activation(
                        xn1T.ap().rearrange("p (c n) -> p c n", c=NCH)[:, :, s4 * 128:(s4 + 1) * 128],
                        ptrA.ap().rearrange("p (c n) -> p c n", c=NCH), AF.Copy),
                        waits=[evT[-1], ev_xn1T_read if s4 == 0 and m > 0 else None])
                    ev_ptr_free = evE
                    ev_macro_evac.append(evE)
                ev_xfree[pb] = applies[-1]
                if m + 1 < nM:
                    prepdata[m + 1] = prepA(m + 1)

                for s4 in range(SUB):
                    stile = m * SUB + s4
                    sp = s4 % 2
                    evE = ev_macro_evac[s4]
                    ev_k_ready = None
                    ev_v_ready = None
                    for ci in range(4):
                        bi, bank_t, bank_w = take_bank()
                        evMM = None
                        for c in range(NCH):
                            evMM = P.t(lambda t, c=c, ci=ci, bank_t=bank_t, s4=s4: t.matmul(
                                bank_t[:],
                                lhsT=x1nT(c, s4 * 128, (s4 + 1) * 128),
                                rhs=wq_sb[:, c * 3 * D + D + ci * 512:
                                          c * 3 * D + D + (ci + 1) * 512],
                                start=(c == 0), stop=(c == NCH - 1)),
                                waits=[evE, WGs] + bank_w)
                        if ci < 2:
                            eU = P.s(lambda s, bank_t=bank_t, ci=ci: s.activation(
                                texp[ci][:], bank_t[:], AF.Relu, scale=-1.0),
                                waits=[evMM, ev_texp_read[ci],
                                       ev_kread[sp] if ci == 0 else None])
                            eW = P.s(lambda s, ci=ci: s.activation(
                                ktmp[ci][:], texp[ci][:], AF.Exp, scale=-1.0),
                                waits=[eU])
                            eT = P.v(lambda v, bank_t=bank_t, ci=ci: v.tensor_add(
                                kt2[ci][:], bank_t[:], texp[ci][:]),
                                waits=[eU])
                            eE2 = P.v(lambda v, ci=ci, sp=sp: v.tensor_add(
                                kbf[sp][:, ci * 512:(ci + 1) * 512], kt2[ci][:],
                                ktmp[ci][:]),
                                waits=[eW, eT])
                            ev_texp_read[ci] = eE2
                            pool_free[bi] = [eU, eT]
                            if ci == 1:
                                ev_k_ready = eE2
                        else:
                            eE2 = P.s(lambda s, bank_t=bank_t, ci=ci, sp=sp: s.copy(
                                vbf[sp][:, (ci - 2) * 512:(ci - 1) * 512],
                                bank_t[:]),
                                waits=[evMM, ev_kread[sp] if ci == 2 else None])
                            pool_free[bi] = [eE2]
                            if ci == 3:
                                ev_v_ready = eE2
                    evKV = []
                    for p8 in range(8):
                        half = p8 // 4
                        evKV.append(P.t(lambda t, p8=p8, half=half, sp=sp, stile=stile: t.matmul(
                            pkv[half][:, (p8 % 4) * 128:(p8 % 4 + 1) * 128],
                            lhsT=kbf[sp][:, p8 * 128:(p8 + 1) * 128],
                            rhs=vbf[sp][:, p8 * 128:(p8 + 1) * 128],
                            start=(stile == 0 and p8 % 4 == 0),
                            stop=(stile == nS - 1 and p8 % 4 == 3)),
                            waits=[ev_k_ready, ev_v_ready]))
                    evZ = []
                    for half in range(2):
                        evZ.append(P.t(lambda t, half=half, sp=sp, stile=stile: t.matmul(
                            pz[half][:],
                            lhsT=ones_b[:],
                            rhs=kbf[sp][:, half * 512:(half + 1) * 512],
                            start=(stile == 0), stop=(stile == nS - 1)),
                            waits=[ev_k_ready, evOB]))
                    if debug_dump:
                        P.dma("gpsimd", lambda g, sp=sp, stile=stile: g.dma_start(
                            out=kdbg[stile * 128:(stile + 1) * 128, :],
                            in_=kbf[sp][:]), waits=[ev_k_ready], sem="DBG")
                        P.dma("gpsimd", lambda g, sp=sp, stile=stile: g.dma_start(
                            out=vdbg[stile * 128:(stile + 1) * 128, :],
                            in_=vbf[sp][:]), waits=[ev_v_ready], sem="DBG")
                    ev_kread[sp] = evZ[-1] if not debug_dump else ("DBG", P.cnt["DBG"])
                    if stile == nS - 1:
                        ev_kv_done = evKV
                        ev_z_done = evZ

                ev_q_evac = []
                for j in range(NCH):
                    bi, bank_t, bank_w = take_bank()
                    evQ = None
                    for c in range(NCH):
                        evQ = P.t(lambda t, c=c, j=j, bank_t=bank_t: t.matmul(
                            bank_t[:],
                            lhsT=wq_sb[:, c * 3 * D + j * 128:
                                       c * 3 * D + (j + 1) * 128],
                            rhs=x1nT(c, 0, 512),
                            start=(c == 0), stop=(c == NCH - 1)),
                            waits=[ev_macro_evac[-1], WGs] + bank_w)
                    sp = j % 2
                    eU = P.s(lambda s, sp=sp, bank_t=bank_t: s.activation(
                        tqxp[sp][:], bank_t[:], AF.Relu, scale=-1.0),
                        waits=[evQ, ev_tq_read[sp]])
                    eW = P.s(lambda s, sp=sp: s.activation(
                        qtmp[sp][:], tqxp[sp][:], AF.Exp, scale=-1.0),
                        waits=[eU])
                    eT = P.v(lambda v, sp=sp, bank_t=bank_t: v.tensor_add(
                        qt2[sp][:], bank_t[:], tqxp[sp][:]),
                        waits=[eU])
                    eQ2 = P.v(lambda v, j=j, sp=sp, pb=pb: v.tensor_add(
                        qTs[pb][:, j * 512:(j + 1) * 512], qt2[sp][:],
                        qtmp[sp][:]),
                        waits=[eW, eT, ev_qT_free[pb] if j == 0 else None])
                    ev_tq_read[sp] = eQ2
                    ev_q_evac.append(eQ2)
                    pool_free[bi] = [eU, eT]
                ev_xn1T_read = evQ
                evQO = P.dma("gpsimd", lambda g, m=m, pb=pb: g.dma_start(
                    out=qTd[m], in_=qTs[pb][:]),
                    waits=ev_q_evac, sem="QO%d" % pb)
                ev_qT_free[pb] = evQO
                ev_qT_out.append(evQO)

            eK0 = P.v(lambda v: v.tensor_copy(kvsb[:, 0:512], pkv[0][:]),
                      waits=ev_kv_done)
            eK1 = P.v(lambda v: v.tensor_copy(kvsb[:, 512:1024], pkv[1][:]),
                      waits=[eK0])
            eZ0 = P.v(lambda v: v.tensor_copy(zsb[:, 0:512], pz[0][:]),
                      waits=ev_z_done + [eK1])
            eZ1 = P.v(lambda v: v.tensor_copy(zsb[:, 512:1024], pz[1][:]),
                      waits=[eZ0])
            evD1 = P.dma("gpsimd", lambda g: g.dma_start(
                out=cc1i[0:128, :], in_=kvsb[:]), waits=[eK1])
            evD2 = P.dma("gpsimd", lambda g: g.dma_start(
                out=cc1i[128:129, :], in_=zsb[:]), waits=[eZ1])
            evCC1 = P.op("gpsimd", lambda g: g.collective_compute(
                "AllReduce", ALU.add, replica_groups=RG,
                ins=[cc1i.opt()], outs=[cc1o.opt()]),
                waits=[("WG", P.cnt["WG"])], sem="CC", inc=1)

            evKz = P.g(lambda g: g.memset(kvz_r[:], 0.0))
            kvv = lambda ap: ap.rearrange("p (g c) -> p g c", c=130)
            cco = lambda ap: ap.rearrange("p (g c) -> p g c", c=128)
            P.dma("gpsimd", lambda g: g.dma_start(
                out=kvv(kvz_r[0:64, :])[:, :, 0:64],
                in_=cco(cc1o[0:64, :])[:, :, 0:64]),
                waits=[evCC1, evKz])
            P.dma("gpsimd", lambda g: g.dma_start(
                out=kvv(kvz_r[64:128, :])[:, :, 65:129],
                in_=cco(cc1o[64:128, :])[:, :, 64:128]),
                waits=[evCC1, evKz])
            evZr = P.dma("gpsimd", lambda g: g.dma_start(
                out=zrow[:], in_=cc1o[128:129, :]), waits=[evCC1])
            evZt = []
            for c in range(NCH):
                evZt.append(P.t(lambda t, c=c: t.transpose(
                    ptrA[:, 2 * c:2 * c + 1], zrow[0:1, c * 128:(c + 1) * 128],
                    ident[0:1, 0:1]), waits=[evZr, evI2, ev_ptr_free]))
            zc1 = P.s(lambda s: s.activation(
                kvv(kvz_r[0:64, :])[:, :, 64:65],
                ptrA[0:64, 0:16:2].unsqueeze(2),
                AF.Copy), waits=[evZt[-1], evKz])
            zc2 = P.s(lambda s: s.activation(
                kvv(kvz_r[64:128, :])[:, :, 129:130],
                ptrA[64:128, 0:16:2].unsqueeze(2),
                AF.Copy), waits=[evZt[-1], evKz])
            ev_rhs = ("WG", P.cnt["WG"])
            ev_rhs2 = [zc1, zc2]

            P.barrier()

        if phases == "A":
            P.op("sync", None, waits=[("CC", 1)])
        if "B" in phases and True:
         with ExitStack() as stB:
            def sbb(n, sh, dt):
                return stB.enter_context(nc.sbuf_tensor(n, sh, dt))
            def psb(n, sh, dt):
                return stB.enter_context(nc.psum_tensor(n, sh, dt))

            wo_sb = sbb("wo_sb", [128, NCH * D], BF16)
            boutb = sbb("boutb", [128, D], F32)
            ms_sb = sbb("ms_sb", [3, 2048], F32)
            qs_sb = sbb("qs_sb", [2, 1], F32)
            zr2 = sbb("zr2", [1, 2048], F32)
            qrp = sbb("qrp", [128, 2], F32)
            xbufB = [sbb("xB0", [128, 4096], F32), sbb("xB1", [128, 4096], F32)]
            qTb = [sbb("qTb0", [128, 4096], BF16), sbb("qTb1", [128, 4096], BF16)]
            attn = [sbb("attn0", [128, D], BF16), sbb("attn1", [128, D], BF16)]
            attnT = [sbb("attnT0", [128, D], BF16), sbb("attnT1", [128, D], BF16)]
            rden = [sbb("rden0", [128, 16], F32), sbb("rden1", [128, 16], F32)]
            aout = [sbb("aout%d" % i, [128, D], F32) for i in range(8)]
            aln = [sbb("aln0", [128, D], F32), sbb("aln1", [128, D], F32)]
            x1b_ = [sbb("x1b0", [128, D], F32), sbb("x1b1", [128, D], F32)]
            st2 = [sbb("st2_0", [128, 2], F32), sbb("st2_1", [128, 2], F32)]
            bn6b = [sbb("bn6b_0", [128, 12], F32), sbb("bn6b_1", [128, 12], F32)]
            rr2 = [sbb("rr2_0", [128, 2], F32), sbb("rr2_1", [128, 2], F32)]
            xcat = [sbb("xcat0", [128, 2048], BF16), sbb("xcat1", [128, 2048], BF16)]
            rv3f = [sbb("rv3f_0", [128, 3], F32), sbb("rv3f_1", [128, 3], F32)]
            rv3 = [sbb("rv3_0", [128, 3], BF16), sbb("rv3_1", [128, 3], BF16)]

            png = [psb("png0", [128, 390], F32), psb("png1", [128, 390], F32)]
            ptrB = psb("ptrB", [128, 1024], BF16)
            pwo = psb("pwo", [128, 512], F32)
            pm = [psb("pm0", [3, 512], F32), psb("pm1", [3, 512], F32),
                  psb("pm2", [3, 512], F32), psb("pm3", [3, 512], F32)]
            pqs_view = None

            P.dma("gpsimd", lambda g: g.dma_start(
                out=wo_sb.ap().rearrange("p (c j) -> p c j", c=NCH), in_=wout.rearrange("(c p) j -> p c j", p=128)))
            WGb = ("WG", P.cnt["WG"])
            evBo = P.dma("sync", lambda e: e.dma_start(
                out=boutb[:], in_=bout.unsqueeze(0).partition_broadcast(128)),
                sem="WH")
            WHb = ("WH", P.cnt["WH"])
            GROUPS = [(0, 3, 0), (3, 6, 6), (6, 8, 12)]
            ln_state = {}
            ev_png_free = [None, None]
            ev_x1_free = [None, None]
            ev_attn_free = [None, None]
            ev_rden_free = [None, None]
            ev_aout_free = [None] * 8
            ev_aln_free = [None, None]
            ev_xfreeB = [None, None]
            ev_qTb_free = [None, None]
            ev_ptrB_free = [zc2]
            ev_attnT_free = [None, None]
            ev_pwo_free = [None]
            ev_xc_free = [None, None]
            ev_rv_free = [None, None]
            ev_pm_last = [None] * 4
            ev_x1_stores = []
            ev_cache_w = []
            pending_m = []
            state_m = {"xc": ev_xc_free, "rv": ev_rv_free}

            def flush_m(before_stile):
                nonlocal ev_pm_last
                while pending_m and pending_m[0][1] < before_stile:
                    sp_, st_, ws_ = pending_m.pop(0)
                    evMs = []
                    for ci in range(4):
                        evMs.append(P.t(lambda t, ci=ci, sp_=sp_, st_=st_: t.matmul(
                            pm[ci][:],
                            lhsT=rv3[sp_][:],
                            rhs=xcat[sp_][:, ci * 512:(ci + 1) * 512],
                            start=(st_ == 0), stop=(st_ == nS - 1)),
                            waits=ws_))
                    ev_xc_free[sp_] = evMs[-1]
                    ev_rv_free[sp_] = evMs[-1]
                    if st_ == nS - 1:
                        ev_pm_last = evMs

            evXBm = [None] * nM
            evQBm = [None] * nM

            def load_x(m):
                pb = m % 2
                evXBm[m] = P.dma("sync", lambda e, m=m, pb=pb: e.dma_start(
                    out=xbufB[pb].ap().rearrange("p (s d) -> p s d", s=4),
                    in_=x_in[m * 512:(m + 1) * 512, :].rearrange(
                        "(s p) d -> p s d", p=128)),
                    waits=[ev_xfreeB[pb]], sem="XA%d" % pb)

            def load_q(m):
                pb = m % 2
                evQBm[m] = P.dma("scalar", lambda e, m=m, pb=pb: e.dma_start(
                    out=qTb[pb][:], in_=qTd[m]),
                    waits=[ev_qTb_free[pb],
                           ("QO0", P.cnt["QO0"]), ("QO1", P.cnt["QO1"])],
                    sem="QI%d" % pb)

            def numdiv(stile):
                m = stile // SUB
                s4 = stile % SUB
                pb = m % 2
                sp = stile % 2
                evQ = evQBm[m]
                ev_div = []
                last_q = None
                for (p0, p1, rc) in GROUPS:
                    gi = rc // 6
                    bank = gi % 2
                    npair = p1 - p0
                    evN = []
                    for p8 in range(p0, p1):
                        off = (p8 - p0) * 130
                        evN.append(P.t(lambda t, p8=p8, off=off, bank=bank, pb=pb, s4=s4, p0=p0, p1=p1: t.matmul(
                            png[bank][:, off:off + 130],
                            lhsT=qTb[pb][:, p8 * 512 + s4 * 128:
                                         p8 * 512 + (s4 + 1) * 128],
                            rhs=kvz_r[:, p8 * 130:(p8 + 1) * 130],
                            start=(p8 == p0), stop=(p8 == p1 - 1)),
                            waits=[evQ, ev_rhs] + ev_rhs2 + (ev_png_free[bank] or [])))
                    last_q = evN[-1]
                    eD1 = P.v(lambda v, rc=rc, bank=bank, sp=sp, npair=npair: v.tensor_scalar(
                        rden[sp][:, rc:rc + 2 * npair],
                        png[bank][:, 64:npair * 130:65],
                        1e-6, None, ALU.add),
                        waits=evN + (ev_rden_free[sp] or []))
                    eD2 = P.v(lambda v, rc=rc, sp=sp, npair=npair: v.reciprocal(
                        rden[sp][:, rc:rc + 2 * npair],
                        rden[sp][:, rc:rc + 2 * npair]), waits=[eD1])

                    def dvv(e, bank=bank, p0=p0, rc=rc, npair=npair, sp=sp):
                        return e.tensor_tensor(
                            attn[sp].ap().rearrange("p (j c) -> p j c", c=64)[:, 2 * p0:2 * p0 + 2 * npair, :],
                            png[bank].ap().rearrange("p (j c) -> p j c", c=65)[:, 0:2 * npair, 0:64],
                            rden[sp][:, rc:rc + 2 * npair].unsqueeze(2).broadcast_to([128, 2 * npair, 64]),
                            ALU.mult)
                    evDv = P.v(dvv, waits=[eD2, ev_attn_free[sp]])
                    ev_div.append(evDv)
                    ev_png_free[bank] = [evDv]
                ev_rden_free[sp] = [ev_div[-1]]
                return ev_div, last_q

            load_q(0)
            load_x(0)
            div_evs = [None] * (nS + 1)
            div_evs[0] = numdiv(0)

            def headB(stile):
                m = stile // SUB
                s4 = stile % SUB
                pb = m % 2
                sp = stile % 2
                ev_div, last_q = div_evs[stile]
                if debug_dump:
                    P.dma("gpsimd", lambda g, sp=sp, stile=stile: g.dma_start(
                        out=adbg[stile * 128:(stile + 1) * 128, :],
                        in_=attn[sp][:]), waits=ev_div, sem="DBG")
                    ev_div = ev_div + [("DBG", P.cnt["DBG"])]
                evT = []
                for c in range(NCH):
                    evT.append(P.t(lambda t, c=c, sp=sp: t.transpose(
                        ptrB[:, c * 128:(c + 1) * 128],
                        attn[sp][:, c * 128:(c + 1) * 128], ident[:]),
                        waits=ev_div + [ev_ptrB_free[0], evI2]))
                ev_attn_free[sp] = evT[-1]
                eTe = P.s(lambda s, sp=sp: s.activation(
                    attnT[sp][:], ptrB[:], AF.Copy),
                    waits=[evT[-1], ev_attnT_free[sp]])
                ev_ptrB_free[0] = eTe
                if stile + 1 < nS:
                    div_evs[stile + 1] = numdiv(stile + 1)
                    if (stile + 1) % SUB == SUB - 1:
                        ev_qTb_free[((stile + 1) // SUB) % 2] = div_evs[stile + 1][1]
                ao = stile % 8
                ev_wout = None
                for half in range(2):
                    evW = None
                    for c in range(NCH):
                        evW = P.t(lambda t, c=c, half=half, sp=sp: t.matmul(
                            pwo[:],
                            lhsT=attnT[sp][:, c * 128:(c + 1) * 128],
                            rhs=wo_sb[:, c * D + half * 512:
                                      c * D + (half + 1) * 512],
                            start=(c == 0), stop=(c == NCH - 1)),
                            waits=[eTe, WGb, ev_pwo_free[0]])
                    eB = P.v(lambda v, half=half, ao=ao: v.tensor_add(
                        aout[ao][:, half * 512:(half + 1) * 512], pwo[:],
                        boutb[:, half * 512:(half + 1) * 512]),
                        waits=[evW, WHb,
                               ev_aout_free[ao] if half == 0 else None])
                    ev_pwo_free[0] = eB
                    ev_wout = eB
                    if half == 1:
                        ev_attnT_free[sp] = evW
                wout_evs[stile] = ev_wout

            def tailB(stile):
                m = stile // SUB
                s4 = stile % SUB
                pb = m % 2
                sp = stile % 2
                ao = stile % 8
                ev_wout = wout_evs[stile]
                ev_rr2, rr2_done = _ln(
                    P, bn6b[sp], st2[sp], rr2[sp],
                    (lambda ao=ao: aout[ao][:]),
                    ln_state, ("B", sp), extra_waits=[ev_wout])
                b7 = P.s(lambda s, sp=sp, ao=ao: s.activation(
                    aln[sp][:], aout[ao][:], AF.Identity,
                    scale=rr2[sp][:, 0:1], bias=rr2[sp][:, 1:2]),
                    waits=[ev_rr2, ev_aln_free[sp]])
                rr2_done(b7)
                b8 = P.v(lambda v, sp=sp, s4=s4, pb=pb, m=m: v.tensor_add(
                    x1b_[sp][:], aln[sp][:],
                    xbufB[pb][:, s4 * D:(s4 + 1) * D]),
                    waits=[b7, evXBm[m], ev_x1_free[sp]])
                ev_aout_free[ao] = b7
                ev_aln_free[sp] = b8
                if s4 == SUB - 1:
                    ev_xfreeB[pb] = b8
                ev_rr2b, rr2b_done = _ln(
                    P, bn6b[sp], st2[sp], rr2[sp],
                    (lambda sp=sp: x1b_[sp][:]),
                    ln_state, ("B", sp), extra_waits=[b8, b7])
                c7 = P.s(lambda s, sp=sp, stile=stile: s.copy(
                    r2c[:, stile:stile + 1], rr2[sp][:, 0:1]),
                    waits=[ev_rr2b])
                c8 = P.s(lambda s, sp=sp, stile=stile: s.copy(
                    q2c[:, stile:stile + 1], rr2[sp][:, 1:2]),
                    waits=[ev_rr2b])
                c10 = P.s(lambda s, sp=sp, stile=stile: s.activation(
                    qc2[:, stile:stile + 1], rr2[sp][:, 1:2], AF.Square),
                    waits=[ev_rr2b])
                ev_cache_w.extend([c7, c8, c10])
                d1 = P.v(lambda v, sp=sp: v.tensor_copy(
                    rv3f[sp][:, 0:1], rr2[sp][:, 0:1]),
                    waits=[ev_rr2b, ev_rv_free[sp]])
                d2 = P.v(lambda v, sp=sp: v.tensor_mul(
                    rv3f[sp][:, 1:2], rr2[sp][:, 0:1], rr2[sp][:, 1:2]),
                    waits=[ev_rr2b, ev_rv_free[sp]])
                d3 = P.v(lambda v, sp=sp: v.tensor_mul(
                    rv3f[sp][:, 2:3], rr2[sp][:, 0:1], rr2[sp][:, 0:1]),
                    waits=[ev_rr2b, ev_rv_free[sp]])
                rr2b_done(d3)
                d4 = P.v(lambda v, sp=sp: v.tensor_copy(
                    rv3[sp][:], rv3f[sp][:]),
                    waits=[d1, d2, d3, ev_rv_free[sp]])
                d5 = P.s(lambda s, sp=sp: s.copy(
                    xcat[sp][:, 0:1024], x1b_[sp][:]),
                    waits=[b8, ev_xc_free[sp]])
                d6 = P.s(lambda s, sp=sp: s.activation(
                    xcat[sp][:, 1024:2048], x1b_[sp][:], AF.Square),
                    waits=[b8, ev_xc_free[sp]])
                pending_m.append((sp, stile, [d4, d5, d6]))
                if stile == nS - 1:
                    flush_m(nS)
                else:
                    flush_m(stile)
                evXO = P.dma("gpsimd", lambda g, sp=sp, stile=stile: g.dma_start(
                    out=x1d[stile * 128:(stile + 1) * 128, :],
                    in_=x1b_[sp][:]),
                    waits=[b8], sem="XO%d" % sp)
                ev_x1_free[sp] = evXO
                ev_x1_stores.append(evXO)

            wout_evs = [None] * nS
            for m in range(nM):
                if m + 1 < nM:
                    load_q(m + 1)
                for s4 in range(SUB):
                    headB(m * SUB + s4)
                    if m >= 1:
                        tailB((m - 1) * SUB + s4)
                if m + 1 < nM:
                    load_x(m + 1)
            for s4 in range(SUB):
                tailB((nM - 1) * SUB + s4)

            f1 = P.v(lambda v: v.tensor_reduce(qrp[:, 0:1], q2c[:], AX.X,
                                               ALU.add), waits=ev_cache_w)
            f2 = P.v(lambda v: v.tensor_reduce(qrp[:, 1:2], qc2[:], AX.X,
                                               ALU.add), waits=ev_cache_w)
            f3 = P.t(lambda t: t.matmul(ptrB.ap().bitcast(F32)[0:2, 0:1],
                                        lhsT=qrp[:], rhs=ones_f[:],
                                        start=True, stop=True),
                     waits=[f1, f2, evOF, ev_ptrB_free[0]] + ev_pm_last)
            f4 = P.v(lambda v: v.tensor_copy(
                qs_sb[:], ptrB.ap().bitcast(F32)[0:2, 0:1]), waits=[f3])
            evMe = []
            for ci in range(4):
                evMe.append(P.v(lambda v, ci=ci: v.tensor_copy(
                    ms_sb.ap().rearrange("p (c n) -> p c n", c=4)[:, ci, :],
                    pm[ci][:]),
                    waits=ev_pm_last + [f4]))
            gz0 = P.g(lambda g: g.memset(zr2[:], 0.0))
            P.dma("gpsimd", lambda g: g.dma_start(
                out=cc2i[3:4, :], in_=zr2[:]), waits=[gz0])
            g1 = P.dma("gpsimd", lambda g: g.dma_start(
                out=cc2i[0:3, :], in_=ms_sb[:]), waits=evMe)
            g2 = P.dma("gpsimd", lambda g: g.dma_start(
                out=cc2i[3:4, 0:2].rearrange("a b -> b a"), in_=qs_sb[:],
                allow_slow_non_contiguous=True), waits=[f4])
            if bstop is not None:
                evCC2 = None
            evCC2 = evCC2 if bstop is not None else P.op("gpsimd", lambda g: g.collective_compute(
                "AllReduce", ALU.add, replica_groups=RG,
                ins=[cc2i.opt()], outs=[cc2o.opt()]),
                waits=[("WG", P.cnt["WG"])], sem="CC", inc=1)

            P.barrier()

        if phases == "AB":
            P.op("sync", None, waits=[("CC", 1 if bstop else 2)])
        if "C" in phases and True:
         with ExitStack() as stC:
            def sbc(n, sh, dt):
                return stC.enter_context(nc.sbuf_tensor(n, sh, dt))
            def psc(n, sh, dt):
                return stC.enter_context(nc.psum_tensor(n, sh, dt))

            wcT = sbc("wcT", [128, NCH * D], BF16)
            wd_sb = sbc("wd_sb", [128, NCH * R], BF16)
            wu_sb = sbc("wu_sb", [64, D], BF16)
            mu_sb = sbc("mu_sb", [8, D], F32)
            sg_sb = sbc("sg_sb", [8, D], F32)
            iv_sb = sbc("iv_sb", [8, D], F32)
            muiv = sbc("muiv", [8, D], F32)
            c3_sb = sbc("c3_sb", [8, D], F32)
            lw1 = sbc("lw1", [8, D], F32)
            lw2 = sbc("lw2", [8, D], F32)
            m2ab = sbc("m2ab", [8, D], F32)
            q1b = sbc("q1b", [8, 1], F32)
            q2b = sbc("q2b", [8, 1], F32)
            wstg = sbc("wstg", [128, D], F32)
            wstg2 = sbc("wstg2", [128, D], F32)
            wustg = sbc("wustg", [64, D], F32)
            wdstg = sbc("wdstg", [128, NCH * R], F32)
            bu_b = sbc("bu_b", [8, D], BF16)
            bc_b = sbc("bc_b", [8, D], BF16)
            bd_b = sbc("bd_b", [8, R], BF16)
            ohc = sbc("ohc", [8, 1], BF16)

            P.dma("gpsimd", lambda g: g.dma_start(out=bu_b[:], in_=bu_in))
            P.dma("gpsimd", lambda g: g.dma_start(out=bc_b[:], in_=bc_in))
            P.dma("gpsimd", lambda g: g.dma_start(out=bd_b[:], in_=bd_in))
            WGc = ("WG", P.cnt["WG"])
            P.dma("sync", lambda e: e.dma_start(out=mu_sb[:], in_=mu_in),
                  sem="LB")
            P.dma("sync", lambda e: e.dma_start(out=sg_sb[:], in_=sig_in),
                  sem="LB")
            h1 = P.v(lambda v: v.tensor_mul(iv_sb[:], sg_sb[:], sg_sb[:]),
                         waits=[("LB", P.cnt["LB"])])
            h2 = P.v(lambda v: v.reciprocal(iv_sb[:], iv_sb[:]), waits=[h1])
            h3 = P.v(lambda v: v.tensor_mul(muiv[:], mu_sb[:], iv_sb[:]),
                     waits=[h2, WHs])
            h4 = P.v(lambda v: v.tensor_mul(c3_sb[:], mu_sb[:], muiv[:]),
                     waits=[h3])
            for fn in (lambda e: e.dma_start(out=lw1[:], in_=cc2o[0:1, 0:1024].partition_broadcast(8)),
                       lambda e: e.dma_start(out=lw2[:], in_=cc2o[1:2, 0:1024].partition_broadcast(8)),
                       lambda e: e.dma_start(out=m2ab[:], in_=cc2o[2:3, 1024:2048].partition_broadcast(8)),
                       lambda e: e.dma_start(out=q1b[:], in_=cc2o[3:4, 0:1].partition_broadcast(8)),
                       lambda e: e.dma_start(out=q2b[:], in_=cc2o[3:4, 1:2].partition_broadcast(8))):
                P.dma("sync", fn, waits=[evCC2], sem="LB")
            evLB = ("LB", P.cnt["LB"])
            inv_t = 1.0 / float(t_total)
            k1 = P.v(lambda v: v.tensor_scalar(
                lw1[:], lw1[:], q1b[:], inv_t, ALU.add, ALU.mult), waits=[evLB])
            k2a = P.v(lambda v: v.tensor_scalar(
                lw2[:], lw2[:], 2.0, None, ALU.mult), waits=[evLB])
            k2 = P.v(lambda v: v.tensor_add(lw2[:], lw2[:], m2ab[:]),
                     waits=[k2a])
            k3 = P.v(lambda v: v.tensor_scalar(
                lw2[:], lw2[:], q2b[:], inv_t, ALU.add, ALU.mult), waits=[k2])
            k4 = P.v(lambda v: v.tensor_mul(lw2[:], lw2[:], iv_sb[:]),
                     waits=[k3, h2])
            k5 = P.v(lambda v: v.tensor_mul(lw1[:], lw1[:], muiv[:]),
                     waits=[k1, h3])
            k6a = P.v(lambda v: v.tensor_scalar(
                lw1[:], lw1[:], -2.0, None, ALU.mult), waits=[k5])
            k6 = P.v(lambda v: v.tensor_add(lw1[:], lw1[:], lw2[:]),
                     waits=[k4, k6a])
            k7 = P.v(lambda v: v.tensor_add(lw1[:], lw1[:], c3_sb[:]),
                     waits=[k6, h4])
            k8 = P.v(lambda v: v.tensor_reduce(loss8[:], lw1[:], AX.X, ALU.add),
                     waits=[k7])
            k9 = P.dma("gpsimd", lambda g: g.dma_start(
                out=lossd.rearrange("a b -> b a"), in_=loss8[:],
                allow_slow_non_contiguous=True), waits=[k8])
            k10 = P.dma("gpsimd", lambda g: g.dma_start(out=lrow[:], in_=lossd),
                        waits=[k9])
            k11 = P.v(lambda v: v.tensor_reduce(lmin[:], lrow[:], AX.X, ALU.min),
                      waits=[k10])
            k12 = P.v(lambda v: v.tensor_single_scalar(
                lone[:], lrow[:], lmin[:], ALU.is_equal), waits=[k11])
            k13 = P.v(lambda v: v.tensor_copy(liot[:], iota8r[:]), waits=[evIr])
            k14 = P.v(lambda v: v.tensor_mul(liot[:], liot[:], lone[:]),
                      waits=[k13, k12])
            k15 = P.v(lambda v: v.tensor_reduce(idxf1[:], liot[:], AX.X, ALU.add),
                      waits=[k14])
            k16 = P.v(lambda v: v.tensor_copy(idxi1[:], idxf1[:]), waits=[k15])
            k17 = P.dma("gpsimd", lambda g: g.dma_start(out=idxd, in_=idxi1[:]),
                        waits=[k16])
            k18 = P.dma("gpsimd", lambda g: g.dma_start(
                out=idxb[:], in_=idxd.partition_broadcast(128)), waits=[k17])
            k19 = P.v(lambda v: v.tensor_copy(idxbf[:], idxb[:]), waits=[k18])
            k19b = P.v(lambda v: v.tensor_scalar(
                idc[:], idxbf[:], 1024.0, None, ALU.mult), waits=[k19])
            k19c = P.v(lambda v: v.tensor_scalar(
                idu[:], idxbf[0:64, :], 64.0, None, ALU.mult), waits=[k19])
            k20 = P.v(lambda v: v.tensor_copy(gcf[:], iota8x[:]), waits=[evI8])
            k21 = P.v(lambda v: v.tensor_scalar(
                gcf[:], gcf[:], idc[:], None, ALU.add), waits=[k19b, k20])
            k22 = P.v(lambda v: v.tensor_copy(gci[:], gcf[:]), waits=[k21])
            k23 = P.v(lambda v: v.tensor_copy(guf[:], iota_p[0:64, :]),
                      waits=[evIp])
            k24 = P.v(lambda v: v.tensor_scalar(
                guf[:], guf[:], idu[:], None, ALU.add), waits=[k19c, k23])
            k25 = P.v(lambda v: v.tensor_copy(gui[:], guf[:]), waits=[k24])

            ev_stg_free = [k22, k22]
            for c in range(NCH):
                stg = wstg if c % 2 == 0 else wstg2
                gsem = "GC%d" % (c % 2)
                gA = P.dma("gpsimd", lambda g, c=c, stg=stg: g.indirect_dma_start(
                    out=stg[:], out_offset=None,
                    in_=wc_in.rearrange("e o d -> (e o) d"),
                    in_offset=IndirectOffsetOnAxis(gci[:, c:c + 1], 0)),
                    waits=[k22, ev_stg_free[c % 2]], sem=gsem)
                gB = P.dma("gpsimd", lambda g, c=c, stg=stg: g.dma_start(
                    out=wccb[c * 128:(c + 1) * 128, :], in_=stg[:]), waits=[gA],
                    sem=gsem)
                ev_stg_free[c % 2] = gB
            gD = []
            for c in range(NCH):
                gD.append(P.dma("gpsimd", lambda g, c=c: g.indirect_dma_start(
                    out=wdstg[:, c * R:(c + 1) * R], out_offset=None,
                    in_=wd_in.rearrange("e d r -> (e d) r"),
                    in_offset=IndirectOffsetOnAxis(gci[:, c:c + 1], 0)),
                    waits=[k22], sem="GD%d" % (c % 2)))
            gD2 = P.dma("gpsimd", lambda g: g.dma_start(out=wd_sb[:], in_=wdstg[:]),
                        waits=[("GD0", P.cnt["GD0"]), ("GD1", P.cnt["GD1"])],
                        sem="GD0")
            gU = P.dma("gpsimd", lambda g: g.indirect_dma_start(
                out=wustg[:], out_offset=None,
                in_=wu_in.rearrange("e r d -> (e r) d"),
                in_offset=IndirectOffsetOnAxis(gui[:, 0:1], 0)), waits=[k25],
                sem="GU")
            gU2 = P.dma("gpsimd", lambda g: g.dma_start(out=wu_sb[:], in_=wustg[:]),
                        waits=[gU], sem="GU")
            bias_evs = {}

            def emit_bias_sel():
                e_oh = P.t(lambda t: t.transpose(
                    ptrC2[1].ap().bitcast(F32)[0:8, 32:33], lone[:],
                    identf[0:1, 0:1]),
                    waits=[k12, evI4, ev_ptrC_free[1]])
                e_ohc = P.s(lambda s: s.activation(
                    ohc[:], ptrC2[1].ap().bitcast(F32)[0:8, 32:33], AF.Copy),
                    waits=[e_oh])
                evm = []
                evm.append(P.t(lambda t: t.matmul(
                    pbias[:, 0:1], lhsT=bu_b[:, 0:128], rhs=ohc[:],
                    start=True, stop=False), waits=[e_ohc, WGc]))
                evm.append(P.t(lambda t: t.matmul(
                    pbias[0:64, 16:17], lhsT=bd_b[:], rhs=ohc[:],
                    start=False, stop=False), waits=[e_ohc, WGc]))
                for c in range(1, NCH):
                    evm.append(P.t(lambda t, c=c: t.matmul(
                        pbias[:, c:c + 1],
                        lhsT=bu_b[:, c * 128:(c + 1) * 128], rhs=ohc[:],
                        start=False, stop=False), waits=[e_ohc, WGc]))
                for c in range(NCH):
                    evm.append(P.t(lambda t, c=c: t.matmul(
                        pbias[:, 8 + c:9 + c],
                        lhsT=bc_b[:, c * 128:(c + 1) * 128], rhs=ohc[:],
                        start=False, stop=(c == NCH - 1)), waits=[e_ohc, WGc]))
                eb1 = P.s(lambda s: s.activation(buT[:], pbias[:, 0:8], AF.Copy),
                          waits=[evm[-1]])
                eb2 = P.s(lambda s: s.activation(bcT[:], pbias[:, 8:16], AF.Copy),
                          waits=[evm[-1]])
                eb3 = P.s(lambda s: s.activation(bdT[:], pbias[0:64, 16:17],
                                                 AF.Copy), waits=[evm[-1]])
                eb4 = P.v(lambda v: v.tensor_add(bcombT[:], bcT[:], bbT[:]),
                          waits=[eb2, WHs])
                bias_evs["bu"] = eb1
                bias_evs["bd"] = eb3
                bias_evs["bcomb"] = eb4
                ev_ptrC_free[1] = eb3
            ev_wc_tr = []
            ev_sb2sb = [("GC0", P.cnt["GC0"]), ("GC1", P.cnt["GC1"]),
                        ("GD0", P.cnt["GD0"]), ("GD1", P.cnt["GD1"]),
                        ("GU", P.cnt["GU"])]
            for c in range(NCH):
                ev_wc_tr.append(P.dma("scalar", lambda s, c=c: s.dma_start(
                    out=wcT[:, c * D:(c + 1) * D],
                    in_=wccb[:, c * 128:(c + 1) * 128], transpose=True),
                    waits=ev_sb2sb, sem="WT%d" % (c % 2)))
            WHwc = [("WT0", P.cnt["WT0"]), ("WT1", P.cnt["WT1"])]
            GDt = ("GD0", P.cnt["GD0"])
            GUt = ("GU", P.cnt["GU"])


            x1buf = [sbc("x1C%d" % i, [128, 4096], F32) for i in range(3)]
            xn2 = [sbc("xn2_%d" % i, [128, D], BF16) for i in range(4)]
            xn2T = sbc("xn2T", [128, NCH * 512], BF16)
            hbf = sbc("hbf", [64, 512], BF16)
            aT = sbc("aT", [128, NCH * 512], BF16)
            ybf = sbc("ybf", [128, NCH * 512], BF16)
            ybt = [sbc("ybt0", [128, D], BF16), sbc("ybt1", [128, D], BF16)]
            st3 = [sbc("st3_0", [128, 2], F32), sbc("st3_1", [128, 2], F32)]
            bn6c = [sbc("bn6c_0", [128, 12], F32), sbc("bn6c_1", [128, 12], F32)]
            rr3 = [sbc("rr3_0", [128, 2], F32), sbc("rr3_1", [128, 2], F32)]
            tl = [sbc("tl0", [128, D], F32), sbc("tl1", [128, D], F32)]
            x2b = [sbc("x2b0", [128, D], F32), sbc("x2b1", [128, D], F32)]

            ptrC2 = [psc("ptrC", [128, 1024], BF16), psc("pxtr", [128, 1024], BF16)]
            ptrC = ptrC2[0]
            pbias = ptrC2[1].ap().bitcast(F32)[:, 0:24]
            pa = [psc("pa0", [128, 512], F32), psc("pa1", [128, 512], F32)]
            py = [psc("py0", [128, 512], F32), psc("py1", [128, 512], F32)]
            ptry2 = [psc("ptry", [128, 1024], BF16), psc("ptry2", [128, 1024], BF16)]
            ph = pa[0].ap()[0:64, :]

            def x2nT(c, lo, hi):
                return xn2T[:, c * 512 + lo:c * 512 + hi]

            ln_state = {}
            ev_x1free = [None, None, None]
            ev_ptrC_free = [None, None]
            ev_xn2_free = [None, None, None, None]
            ev_hb_free = None
            ev_ph_free = None
            ev_pa_free = [None, None]
            ev_aT_free = None
            ev_py_free = [None, None]
            ev_ybf_free = None
            ev_ptry_free = [None, None]
            ev_ybt_free = [None, None]
            ev_tl_free = [None, None]
            ev_x2_free = [None, None]
            evXO_all = [("XO0", P.cnt["XO0"]), ("XO1", P.cnt["XO1"])]

            e1_evs = {}

            def load_c(m):
                pb = m % 3
                return P.dma("sync", lambda e, m=m, pb=pb: e.dma_start(
                    out=x1buf[pb].ap().rearrange("p (s d) -> p s d", s=4),
                    in_=x1d[m * 512:(m + 1) * 512, :].rearrange(
                        "(s p) d -> p s d", p=128)),
                    waits=[ev_x1free[pb]] + (evXO_all if m < 3 else []),
                    sem="XC%d" % (m % 2))

            def prep(m, evX1):
                pb = m % 3
                for s4 in range(SUB):
                    stile = m * SUB + s4
                    e1 = P.s(lambda s, pb=pb, s4=s4, stile=stile: s.activation(
                        xn2[s4][:], x1buf[pb][:, s4 * D:(s4 + 1) * D], AF.Identity,
                        scale=r2c[:, stile:stile + 1], bias=q2c[:, stile:stile + 1]),
                        waits=[evX1, ev_xn2_free[s4]])
                    e1_evs[(m, s4)] = e1

            prep_x1 = [None] * nM
            prep_x1[0] = load_c(0)
            if nM > 1:
                prep_x1[1] = load_c(1)
            prep(0, prep_x1[0])

            for m in range(nM):
                pb = m % 2
                xpb = m % 3
                evX1 = prep_x1[m]
                ev_evE = []
                for s4 in range(SUB):
                    trb = ptrC2[s4 % 2]
                    evT = []
                    for c in range(NCH):
                        evT.append(P.t(lambda t, c=c, s4=s4, trb=trb: t.transpose(
                            trb[:, c * 128:(c + 1) * 128],
                            xn2[s4][:, c * 128:(c + 1) * 128], ident[:]),
                            waits=[e1_evs[(m, s4)], evI2, ev_ptrC_free[s4 % 2]]))
                    ev_xn2_free[s4] = evT[-1]
                    eE = P.s(lambda s, s4=s4, trb=trb: s.activation(
                        xn2T.ap().rearrange("p (c n) -> p c n", c=NCH)[:, :, s4 * 128:(s4 + 1) * 128],
                        trb.ap().rearrange("p (c n) -> p c n", c=NCH), AF.Copy),
                        waits=[evT[-1]])
                    ev_ptrC_free[s4 % 2] = eE
                    ev_evE.append(eE)
                if m + 2 < nM:
                    prep_x1[m + 2] = load_c(m + 2)
                if m + 1 < nM:
                    prep(m + 1, prep_x1[m + 1])
                if m == 0:
                    emit_bias_sel()
                evDn = None
                for c in range(NCH):
                    evDn = P.t(lambda t, c=c: t.matmul(
                        ph,
                        lhsT=wd_sb[:, c * R:(c + 1) * R],
                        rhs=x2nT(c, 0, 512),
                        start=(c == 0), stop=(c == NCH - 1)),
                        waits=ev_evE + [GDt, ev_ph_free] + (ev_pa_free[0] or []))
                evRl = P.s(lambda s: s.activation(hbf[:], ph, AF.Relu,
                                                  bias=bdT[:]),
                           waits=[evDn, bias_evs["bd"], ev_hb_free])
                ev_ph_free = evRl
                ev_up = []
                for c in range(NCH):
                    bank = (c + 1) % 2
                    evU = P.t(lambda t, c=c, bank=bank: t.matmul(
                        pa[bank][:],
                        lhsT=wu_sb[:, c * 128:(c + 1) * 128],
                        rhs=hbf[:],
                        start=True, stop=True),
                        waits=[evRl, GUt] + (ev_pa_free[bank] or []))
                    evUe = P.s(lambda s, c=c, bank=bank: s.activation(
                        aT[:, c * 512:(c + 1) * 512],
                        pa[bank][:], AF.Identity, bias=buT[:, c:c + 1]),
                        waits=[evU, bias_evs["bu"]] + ([ev_aT_free] if c == 0 else []))
                    ev_pa_free[bank] = [evUe]
                    ev_up.append(evUe)
                ev_hb_free = ev_up[-1]
                ev_yev = []
                for co in range(NCH):
                    bank = co % 2
                    evY = None
                    for c in range(NCH):
                        evY = P.t(lambda t, c=c, co=co, bank=bank: t.matmul(
                            py[bank][:],
                            lhsT=bwT[:, c * D + co * 128:c * D + (co + 1) * 128],
                            rhs=x2nT(c, 0, 512),
                            start=(c == 0), stop=False),
                            waits=ev_evE + WTs + [ev_py_free[bank], evY])
                    for c in range(NCH):
                        evY = P.t(lambda t, c=c, co=co, bank=bank: t.matmul(
                            py[bank][:],
                            lhsT=wcT[:, c * D + co * 128:c * D + (co + 1) * 128],
                            rhs=aT[:, c * 512:(c + 1) * 512],
                            start=False, stop=(c == NCH - 1)),
                            waits=ev_up + WHwc + [evY])
                    eYe = P.s(lambda s, co=co, bank=bank: s.activation(
                        ybf[:, co * 512:(co + 1) * 512],
                        py[bank][:], AF.Identity, bias=bcombT[:, co:co + 1]),
                        waits=[evY, bias_evs["bcomb"]] + ([ev_ybf_free] if co == 0 else []))
                    ev_py_free[bank] = eYe
                    ev_yev.append(eYe)
                ev_aT_free = ev_yev[-1]
                ev_x2l = None
                for s4 in range(SUB):
                    sp = s4 % 2
                    yb = ptry2[sp]
                    evT = []
                    for c in range(NCH):
                        evT.append(P.t(lambda t, c=c, s4=s4, yb=yb: t.transpose(
                            yb[:, c * 128:(c + 1) * 128],
                            ybf[:, c * 512 + s4 * 128:c * 512 + (s4 + 1) * 128],
                            ident[:]),
                            waits=ev_yev + [ev_ptry_free[sp], evI2]))
                    eE = P.v(lambda v, sp=sp, yb=yb: v.tensor_copy(
                        ybt[sp][:], yb[:]),
                             waits=[evT[-1], ev_ybt_free[sp]])
                    ev_ptry_free[sp] = eE
                    if s4 == SUB - 1:
                        ev_ybf_free = eE
                    ev_rr3, rr3_done = _ln(
                        P, bn6c[sp], st3[sp], rr3[sp],
                        (lambda sp=sp: ybt[sp][:]),
                        ln_state, ("C", sp), extra_waits=[eE])
                    b7 = P.v(lambda v, sp=sp: v.tensor_scalar(
                        tl[sp][:], ybt[sp][:], rr3[sp][:, 0:1],
                        rr3[sp][:, 1:2], ALU.mult, ALU.add),
                        waits=[ev_rr3, ev_tl_free[sp]])
                    rr3_done(b7)
                    ev_ybt_free[sp] = b7
                    b8 = P.v(lambda v, sp=sp, s4=s4, xpb=xpb: v.tensor_add(
                        x2b[sp][:], tl[sp][:],
                        x1buf[xpb][:, s4 * D:(s4 + 1) * D]),
                        waits=[b7, evX1, ev_x2_free[sp]])
                    ev_tl_free[sp] = b8
                    if s4 == SUB - 1:
                        ev_x1free[xpb] = b8
                    stile = m * SUB + s4
                    evX2 = P.dma("sync", lambda e, stile=stile, sp=sp: e.dma_start(
                        out=y_out[stile * 128:(stile + 1) * 128, :],
                        in_=x2b[sp][:]),
                        waits=[b8], sem="X2O%d" % sp)
                    ev_x2_free[sp] = evX2

            P.op("sync", None, waits=[("X2O0", P.cnt["X2O0"]),
                                      ("X2O1", P.cnt["X2O1"])])

        with nc.Block() as blk:
            P.emit(blk)
    return nc


def kernel(L0_attn_x, L0_attn_W_qkv, L0_attn_W_out, L0_attn_b_out,
           L0_ffn_base_W, L0_ffn_base_b, L0_ffn_disc_mu, L0_ffn_disc_sigma,
           L0_ffn_down_W, L0_ffn_down_b, L0_ffn_up_W, L0_ffn_up_b,
           L0_ffn_conv_W, L0_ffn_conv_b):
    x = np.ascontiguousarray(np.asarray(L0_attn_x, dtype=np.float32))
    B, N, _ = x.shape
    T = N // 2
    shared = {
        "wqkv": np.ascontiguousarray(L0_attn_W_qkv, dtype=np.float32),
        "wout": np.ascontiguousarray(L0_attn_W_out, dtype=np.float32),
        "bout": np.ascontiguousarray(L0_attn_b_out, dtype=np.float32),
        "basew": np.ascontiguousarray(L0_ffn_base_W, dtype=np.float32),
        "baseb": np.ascontiguousarray(L0_ffn_base_b, dtype=np.float32),
        "mu": np.ascontiguousarray(L0_ffn_disc_mu, dtype=np.float32),
        "sigma": np.ascontiguousarray(L0_ffn_disc_sigma, dtype=np.float32),
        "wd": np.ascontiguousarray(L0_ffn_down_W, dtype=np.float32),
        "bd": np.ascontiguousarray(L0_ffn_down_b, dtype=np.float32),
        "wu": np.ascontiguousarray(L0_ffn_up_W, dtype=np.float32),
        "bu": np.ascontiguousarray(L0_ffn_up_b, dtype=np.float32),
        "wc": np.ascontiguousarray(L0_ffn_conv_W, dtype=np.float32),
        "bc": np.ascontiguousarray(L0_ffn_conv_b, dtype=np.float32),
    }
    nc = build_program(T=T, t_total=N)
    in_maps = []
    for c in range(8):
        b, h = c // 2, c % 2
        mp = dict(shared)
        mp["x"] = np.ascontiguousarray(x[b, h * T:(h + 1) * T, :])
        in_maps.append(mp)
    res = run_bass_kernel_spmd(nc, in_maps, core_ids=list(range(8)))
    out = np.empty((B, N, D), dtype=np.float32)
    for c in range(8):
        b, h = c // 2, c % 2
        out[b, h * T:(h + 1) * T, :] = res.results[c]["y"]
    return out


# version 13
# speedup vs baseline: 1.2615x; 1.2615x over previous
from collections import defaultdict
from contextlib import ExitStack

import numpy as np

import concourse.bass as bass
import concourse.mybir as mybir
from concourse.bass import IndirectOffsetOnAxis
from concourse.bass_utils import run_bass_kernel_spmd

F32 = mybir.dt.float32
BF16 = mybir.dt.bfloat16
I32 = mybir.dt.int32
AF = mybir.ActivationFunctionType
ALU = mybir.AluOpType
AX = mybir.AxisListType

D = 1024
H = 16
NCH = 8
R = 64
E = 8
EPS = 1e-5

ENGINES = ("sync", "gpsimd", "tensor", "vector", "scalar")


class Plan:

    def __init__(self, nc):
        self.nc = nc
        self.ops = {e: [] for e in ENGINES}
        self.cnt = defaultdict(int)
        self.handles = {}

    def sem(self, stack, name):
        self.handles[name] = stack.enter_context(self.nc.semaphore(name))

    ENG_SEMS = ("T", "V", "S", "G", "BAR")

    def op(self, eng, fn, waits=(), sem=None, inc=1):
        ev = None
        ws = [w for w in waits if w is not None]
        for w in ws:
            assert isinstance(w, tuple) and len(w) == 2, f"bad wait {w!r}"
        if sem is not None:
            if sem not in self.ENG_SEMS and self.cnt[sem] > 0:
                ws.append((sem, self.cnt[sem]))
            self.cnt[sem] += inc
            ev = (sem, self.cnt[sem])
        self.ops[eng].append((fn, ws, (sem, inc) if sem else None))
        return ev

    def t(self, fn, waits=()):
        return self.op("tensor", fn, waits, "T")

    def v(self, fn, waits=()):
        return self.op("vector", fn, waits, "V")

    def s(self, fn, waits=()):
        return self.op("scalar", fn, waits, "S")

    def g(self, fn, waits=()):
        return self.op("gpsimd", fn, waits, "G")

    def dma(self, eng, fn, waits=(), sem="WG"):
        return self.op(eng, fn, waits, sem, inc=16)

    def barrier(self):
        self.cnt["BAR"] += len(ENGINES)
        tgt = self.cnt["BAR"]
        for e in ENGINES:
            self.ops[e].append((lambda eng: eng.nop(), [], ("BAR", 1)))
            self.ops[e].append((None, [("BAR", tgt)], None))

    OWN_SEM = {"tensor": "T", "vector": "V", "scalar": "S", "gpsimd": "G"}

    def emit(self, blk):
        for eng in ENGINES:
            ops = self.ops[eng]
            if not ops:
                continue

            def body(e, ops=ops, own=self.OWN_SEM.get(eng)):
                last = {}
                for fn, waits, inc in ops:
                    for (s, v) in waits:
                        if last.get(s, -1) >= v:
                            continue
                        e.wait_ge(self.handles[s], v)
                        last[s] = v
                    if fn is None:
                        continue
                    inst = fn(e)
                    if inc is not None and inc[0] is not None:
                        inst.then_inc(self.handles[inc[0]], inc[1])
            getattr(blk, eng)(body)


def _ln(P, bn6, st, rr, src_fn, state, key, extra_waits=()):
    prev = state.get(key, {})
    e1a = P.v(lambda v: v.bn_stats(bn6[:, 0:6], src_fn()[:, 0:512]),
              waits=list(extra_waits) + [prev.get("aggr")])
    e1 = P.v(lambda v: v.bn_stats(bn6[:, 6:12], src_fn()[:, 512:1024]),
             waits=[e1a])
    e2 = P.v(lambda v: v.bn_aggr(st[:], bn6[:]),
             waits=[e1, prev.get("st_read")])
    e3 = P.v(lambda v: v.tensor_scalar(rr[:, 0:1], st[:, 1:2], EPS, None,
                                       ALU.add),
             waits=[e2, prev.get("rr_read")])
    e4 = P.s(lambda s: s.activation(rr[:, 0:1], rr[:, 0:1], AF.Ln), waits=[e3])
    e5 = P.s(lambda s: s.activation(rr[:, 0:1], rr[:, 0:1], AF.Exp, scale=-0.5),
             waits=[e4])
    e6 = P.v(lambda v: v.tensor_scalar(rr[:, 1:2], st[:, 0:1], rr[:, 0:1],
                                       -1.0, ALU.mult, ALU.mult),
             waits=[e5, e2])
    st_new = {"aggr": e2, "st_read": e6}
    state[key] = st_new

    def set_last_reader(ev):
        st_new["rr_read"] = ev
    return e6, set_last_reader


def build_program(T=4096, t_total=None, debug_dump=False, phases="ABC", bstop=None):
    if t_total is None:
        t_total = 2 * T
    nS = T // 128
    nM = T // 512
    SUB = 4

    nc = bass.Bass("TRN2", target_bir_lowering=False)

    x_in = nc.dram_tensor("x", [T, D], F32, kind="ExternalInput").ap()
    wqkv = nc.dram_tensor("wqkv", [D, 3 * D], F32, kind="ExternalInput").ap()
    wout = nc.dram_tensor("wout", [D, D], F32, kind="ExternalInput").ap()
    bout = nc.dram_tensor("bout", [D], F32, kind="ExternalInput").ap()
    basew = nc.dram_tensor("basew", [D, D], F32, kind="ExternalInput").ap()
    baseb = nc.dram_tensor("baseb", [D], F32, kind="ExternalInput").ap()
    mu_in = nc.dram_tensor("mu", [E, D], F32, kind="ExternalInput").ap()
    sig_in = nc.dram_tensor("sigma", [E, D], F32, kind="ExternalInput").ap()
    wd_in = nc.dram_tensor("wd", [E, D, R], F32, kind="ExternalInput").ap()
    bd_in = nc.dram_tensor("bd", [E, R], F32, kind="ExternalInput").ap()
    wu_in = nc.dram_tensor("wu", [E, R, D], F32, kind="ExternalInput").ap()
    bu_in = nc.dram_tensor("bu", [E, D], F32, kind="ExternalInput").ap()
    wc_in = nc.dram_tensor("wc", [E, D, D], F32, kind="ExternalInput").ap()
    bc_in = nc.dram_tensor("bc", [E, D], F32, kind="ExternalInput").ap()
    y_out = nc.dram_tensor("y", [T, D], F32, kind="ExternalOutput").ap()

    qTd = nc.dram_tensor("qTd", [nM, 128, 4096], BF16).ap()
    x1d = nc.dram_tensor("x1d", [T, D], F32).ap()
    wscb = nc.dram_tensor("wscb", [D, D], BF16).ap()
    wccb = nc.dram_tensor("wccb", [D, D], BF16).ap()
    cc1i = nc.dram_tensor("cc1i", [129, D], F32).ap()
    cc1o = nc.dram_tensor("cc1o", [129, D], F32).ap()
    cc2i = nc.dram_tensor("cc2i", [4, 2048], F32).ap()
    cc2o = nc.dram_tensor("cc2o", [4, 2048], F32).ap()
    idxd = nc.dram_tensor("idxd", [1, 1], I32).ap()
    lossd = nc.dram_tensor("lossd", [1, 8], F32).ap()
    if debug_dump:
        kdbg = nc.dram_tensor("kdbg", [T, D], BF16).ap()
        vdbg = nc.dram_tensor("vdbg", [T, D], BF16).ap()
        adbg = nc.dram_tensor("adbg", [T, D], BF16).ap()

    P = Plan(nc)
    RG = [[0, 1], [2, 3], [4, 5], [6, 7]]

    with ExitStack() as st:
        def sb(name, shape, dt):
            return st.enter_context(nc.sbuf_tensor(name, shape, dt))

        for s in ("T", "V", "S", "G", "BAR", "CC", "WH", "WG", "LB",
                  "XO0", "XO1", "GD0", "GD1", "GU", "GC0", "GC1", "WT0", "WT1",
                  "XA0", "XA1", "QO0", "QO1", "QI0", "QI1",
                  "X2O0", "X2O1", "XC0", "XC1", "DBG"):
            P.sem(st, s)

        ident = sb("ident", [128, 128], BF16)
        identf = sb("identf", [8, 8], F32)
        ones_b = sb("ones_b", [128, 1], BF16)
        ones_f = sb("ones_f", [128, 1], F32)
        iota_p = sb("iota_p", [128, 1], I32)
        iota8x = sb("iota8x", [128, 8], I32)
        iota8r = sb("iota8r", [1, 8], I32)
        bwT = sb("bwT", [128, NCH * D], BF16)
        bcombT = sb("bcombT", [128, NCH], F32)
        bbT = sb("bbT", [128, NCH], F32)
        bcT = sb("bcT", [128, NCH], F32)
        buT = sb("buT", [128, NCH], F32)
        bdT = sb("bdT", [64, 1], F32)
        loss8 = sb("loss8", [8, 1], F32)
        lrow = sb("lrow", [1, 8], F32)
        lmin = sb("lmin", [1, 1], F32)
        lone = sb("lone", [1, 8], F32)
        liot = sb("liot", [1, 8], F32)
        idxf1 = sb("idxf1", [1, 1], F32)
        idxi1 = sb("idxi1", [1, 1], I32)
        idxb = sb("idxb", [128, 1], I32)
        idxbf = sb("idxbf", [128, 1], F32)
        idc = sb("idc", [128, 1], F32)
        idu = sb("idu", [64, 1], F32)
        gcf = sb("gcf", [128, 8], F32)
        gci = sb("gci", [128, 8], I32)
        guf = sb("guf", [64, 1], F32)
        gui = sb("gui", [64, 1], I32)
        kvz_r = sb("kvz_r", [128, 8 * 130], BF16)
        zrow = sb("zrow", [1, 1024], BF16)
        r2c = sb("r2c", [128, nS], F32)
        q2c = sb("q2c", [128, nS], F32)
        qc2 = sb("qc2", [128, nS], F32)

        evI1 = P.g(lambda g: g.memset(ident[:], 0.0))
        evI2 = P.g(lambda g: g.affine_select(
            out=ident[:], in_=ident[:], compare_op=ALU.not_equal, fill=1.0,
            base=0, pattern=[[-1, 128]], channel_multiplier=1), waits=[evI1])
        evI3 = P.g(lambda g: g.memset(identf[:], 0.0))
        evI4 = P.g(lambda g: g.affine_select(
            out=identf[:], in_=identf[:], compare_op=ALU.not_equal, fill=1.0,
            base=0, pattern=[[-1, 8]], channel_multiplier=1), waits=[evI3])
        evOB = P.g(lambda g: g.memset(ones_b[:], 1.0))
        evOF = P.g(lambda g: g.memset(ones_f[:], 1.0))
        evIp = P.g(lambda g: g.iota(iota_p[:], pattern=[[0, 1]], base=0,
                                    channel_multiplier=1))
        evI8 = P.g(lambda g: g.iota(iota8x[:], pattern=[[128, 8]], base=0,
                                    channel_multiplier=1))
        evIr = P.g(lambda g: g.iota(iota8r[:], pattern=[[1, 8]], base=0,
                                    channel_multiplier=0))
        evWb = P.dma("gpsimd", lambda g: g.dma_start(out=wscb, in_=basew))
        evBW = []
        for c in range(NCH):
            evBW.append(P.dma("scalar", lambda s, c=c: s.dma_start(
                out=bwT[:, c * D:(c + 1) * D],
                in_=wscb[:, c * 128:(c + 1) * 128], transpose=True),
                waits=[evWb], sem="WT%d" % (c % 2)))
        evBB = []
        for c in range(NCH):
            evBB.append(P.dma("sync", lambda e, c=c: e.dma_start(
                out=bbT[:, c:c + 1],
                in_=baseb[c * 128:(c + 1) * 128].unsqueeze(1)), sem="WH"))
        WHs = ("WH", P.cnt["WH"])
        WTs = [("WT0", P.cnt["WT0"]), ("WT1", P.cnt["WT1"])]

        with ExitStack() as stA:
            def sba(n, sh, dt):
                return stA.enter_context(nc.sbuf_tensor(n, sh, dt))
            def psa(n, sh, dt):
                return stA.enter_context(nc.psum_tensor(n, sh, dt))

            wq_sb = sba("wq_sb", [128, NCH * 3 * D], BF16)
            kvsb = sba("kvsb", [128, 1024], F32)
            zsb = sba("zsb", [1, 1024], F32)
            xbuf = [sba("xA0", [128, 4096], F32), sba("xA1", [128, 4096], F32)]
            xn1 = [sba("xn1_%d" % i, [128, D], BF16) for i in range(4)]
            st1 = [sba("st1_0", [128, 2], F32), sba("st1_1", [128, 2], F32)]
            bn6 = [sba("bn6_0", [128, 12], F32), sba("bn6_1", [128, 12], F32)]
            rr1 = [sba("rr1_0", [128, 2], F32), sba("rr1_1", [128, 2], F32)]
            xn1T = sba("xn1T", [128, NCH * 512], BF16)
            kbf = [sba("kbf0", [128, D], BF16), sba("kbf1", [128, D], BF16)]
            vbf = [sba("vbf0", [128, D], BF16), sba("vbf1", [128, D], BF16)]
            texp = [sba("texp0", [128, 512], F32), sba("texp1", [128, 512], F32)]
            tqxp = [sba("tqxp0", [128, 512], F32), sba("tqxp1", [128, 512], F32)]
            ktmp = [sba("ktmp0", [128, 512], F32), sba("ktmp1", [128, 512], F32)]
            qtmp = [sba("qtmp0", [128, 512], F32), sba("qtmp1", [128, 512], F32)]
            kt2 = [sba("kt2_0", [128, 512], F32), sba("kt2_1", [128, 512], F32)]
            qt2 = [sba("qt2_0", [128, 512], F32), sba("qt2_1", [128, 512], F32)]
            qTs = [sba("qTs0", [128, 4096], BF16), sba("qTs1", [128, 4096], BF16)]

            pkv = [psa("pkv0", [128, 512], F32), psa("pkv1", [128, 512], F32)]
            pz = [psa("pz0", [1, 512], F32), psa("pz1", [1, 512], F32)]
            pmm = [psa("pmm0", [128, 512], F32), psa("pmm1", [128, 512], F32)]
            ptrA = psa("ptrA", [128, 1024], BF16)
            pq = psa("pq", [128, 512], F32)

            def x1nT(c, lo, hi):
                return xn1T[:, c * 512 + lo:c * 512 + hi]

            P.dma("gpsimd", lambda g: g.dma_start(
                out=wq_sb.ap().rearrange("p (c j) -> p c j", c=NCH), in_=wqkv.rearrange("(c p) j -> p c j", p=128)))
            WGs = ("WG", P.cnt["WG"])

            ln_state = {}
            ev_xfree = [None, None]
            ev_kread = [None, None]
            ev_texp_read = [None, None]
            ev_tq_read = [None, None]
            ev_ptr_free = None
            ev_pq_free = None
            ev_pmm_free = [None, None]
            ev_qT_free = [None, None]
            ev_xn1_free = [None, None, None, None]
            ev_kv_done = []
            ev_z_done = []
            ev_qT_out = []
            ev_xn1T_read = None

            pool_banks = [pmm[0], pmm[1], pq]
            pool_free = [None, None, None]
            pool_i = [0]

            def take_bank():
                i = pool_i[0] % 3
                pool_i[0] += 1
                return i, pool_banks[i], pool_free[i] or []

            def prepA(m):
                pb = m % 2
                evX = P.dma("sync", lambda e, m=m, pb=pb: e.dma_start(
                    out=xbuf[pb].ap().rearrange("p (s d) -> p s d", s=4),
                    in_=x_in[m * 512:(m + 1) * 512, :].rearrange(
                        "(s p) d -> p s d", p=128)),
                    waits=[ev_xfree[pb]], sem="XA%d" % pb)
                applies = []
                for s4 in range(SUB):
                    sp = s4 % 2
                    ev_rr, rr_done = _ln(
                        P, bn6[sp], st1[sp], rr1[sp],
                        (lambda pb=pb, s4=s4: xbuf[pb][:, s4 * D:(s4 + 1) * D]),
                        ln_state, ("A", sp), extra_waits=[evX])
                    e7 = P.s(lambda s, pb=pb, s4=s4, sp=sp: s.activation(
                        xn1[s4][:], xbuf[pb][:, s4 * D:(s4 + 1) * D], AF.Identity,
                        scale=rr1[sp][:, 0:1], bias=rr1[sp][:, 1:2]),
                        waits=[ev_rr, ev_xn1_free[s4]])
                    rr_done(e7)
                    applies.append(e7)
                return evX, applies

            prepdata = [None] * nM
            prepdata[0] = prepA(0)

            for m in range(nM):
                pb = m % 2
                evX, applies = prepdata[m]
                ev_macro_evac = []
                for s4 in range(SUB):
                    evT = []
                    for c in range(NCH):
                        evT.append(P.t(lambda t, c=c, s4=s4: t.transpose(
                            ptrA[:, c * 128:(c + 1) * 128],
                            xn1[s4][:, c * 128:(c + 1) * 128], ident[:]),
                            waits=[applies[s4], evI2, ev_ptr_free]))
                    ev_xn1_free[s4] = evT[-1]
                    evE = P.s(lambda s, s4=s4: s.activation(
                        xn1T.ap().rearrange("p (c n) -> p c n", c=NCH)[:, :, s4 * 128:(s4 + 1) * 128],
                        ptrA.ap().rearrange("p (c n) -> p c n", c=NCH), AF.Copy),
                        waits=[evT[-1], ev_xn1T_read if s4 == 0 and m > 0 else None])
                    ev_ptr_free = evE
                    ev_macro_evac.append(evE)
                ev_xfree[pb] = applies[-1]
                if m + 1 < nM:
                    prepdata[m + 1] = prepA(m + 1)

                for s4 in range(SUB):
                    stile = m * SUB + s4
                    sp = s4 % 2
                    evE = ev_macro_evac[s4]
                    ev_k_ready = None
                    ev_v_ready = None
                    for ci in range(4):
                        bi, bank_t, bank_w = take_bank()
                        evMM = None
                        for c in range(NCH):
                            evMM = P.t(lambda t, c=c, ci=ci, bank_t=bank_t, s4=s4: t.matmul(
                                bank_t[:],
                                lhsT=x1nT(c, s4 * 128, (s4 + 1) * 128),
                                rhs=wq_sb[:, c * 3 * D + D + ci * 512:
                                          c * 3 * D + D + (ci + 1) * 512],
                                start=(c == 0), stop=(c == NCH - 1)),
                                waits=[evE, WGs] + bank_w)
                        if ci < 2:
                            eU = P.s(lambda s, bank_t=bank_t, ci=ci: s.activation(
                                texp[ci][:], bank_t[:], AF.Relu, scale=-1.0),
                                waits=[evMM, ev_texp_read[ci],
                                       ev_kread[sp] if ci == 0 else None])
                            eW = P.s(lambda s, ci=ci: s.activation(
                                ktmp[ci][:], texp[ci][:], AF.Exp, scale=-1.0),
                                waits=[eU])
                            eT = P.v(lambda v, bank_t=bank_t, ci=ci: v.tensor_add(
                                kt2[ci][:], bank_t[:], texp[ci][:]),
                                waits=[eU])
                            eE2 = P.v(lambda v, ci=ci, sp=sp: v.tensor_add(
                                kbf[sp][:, ci * 512:(ci + 1) * 512], kt2[ci][:],
                                ktmp[ci][:]),
                                waits=[eW, eT])
                            ev_texp_read[ci] = eE2
                            pool_free[bi] = [eU, eT]
                            if ci == 1:
                                ev_k_ready = eE2
                        else:
                            eE2 = P.s(lambda s, bank_t=bank_t, ci=ci, sp=sp: s.copy(
                                vbf[sp][:, (ci - 2) * 512:(ci - 1) * 512],
                                bank_t[:]),
                                waits=[evMM, ev_kread[sp] if ci == 2 else None])
                            pool_free[bi] = [eE2]
                            if ci == 3:
                                ev_v_ready = eE2
                    evKV = []
                    for p8 in range(8):
                        half = p8 // 4
                        evKV.append(P.t(lambda t, p8=p8, half=half, sp=sp, stile=stile: t.matmul(
                            pkv[half][:, (p8 % 4) * 128:(p8 % 4 + 1) * 128],
                            lhsT=kbf[sp][:, p8 * 128:(p8 + 1) * 128],
                            rhs=vbf[sp][:, p8 * 128:(p8 + 1) * 128],
                            start=(stile == 0 and p8 % 4 == 0),
                            stop=(stile == nS - 1 and p8 % 4 == 3)),
                            waits=[ev_k_ready, ev_v_ready]))
                    evZ = []
                    for half in range(2):
                        evZ.append(P.t(lambda t, half=half, sp=sp, stile=stile: t.matmul(
                            pz[half][:],
                            lhsT=ones_b[:],
                            rhs=kbf[sp][:, half * 512:(half + 1) * 512],
                            start=(stile == 0), stop=(stile == nS - 1)),
                            waits=[ev_k_ready, evOB]))
                    if debug_dump:
                        P.dma("gpsimd", lambda g, sp=sp, stile=stile: g.dma_start(
                            out=kdbg[stile * 128:(stile + 1) * 128, :],
                            in_=kbf[sp][:]), waits=[ev_k_ready], sem="DBG")
                        P.dma("gpsimd", lambda g, sp=sp, stile=stile: g.dma_start(
                            out=vdbg[stile * 128:(stile + 1) * 128, :],
                            in_=vbf[sp][:]), waits=[ev_v_ready], sem="DBG")
                    ev_kread[sp] = evZ[-1] if not debug_dump else ("DBG", P.cnt["DBG"])
                    if stile == nS - 1:
                        ev_kv_done = evKV
                        ev_z_done = evZ

                ev_q_evac = []
                for j in range(NCH):
                    bi, bank_t, bank_w = take_bank()
                    evQ = None
                    for c in range(NCH):
                        evQ = P.t(lambda t, c=c, j=j, bank_t=bank_t: t.matmul(
                            bank_t[:],
                            lhsT=wq_sb[:, c * 3 * D + j * 128:
                                       c * 3 * D + (j + 1) * 128],
                            rhs=x1nT(c, 0, 512),
                            start=(c == 0), stop=(c == NCH - 1)),
                            waits=[ev_macro_evac[-1], WGs] + bank_w)
                    sp = j % 2
                    eU = P.s(lambda s, sp=sp, bank_t=bank_t: s.activation(
                        tqxp[sp][:], bank_t[:], AF.Relu, scale=-1.0),
                        waits=[evQ, ev_tq_read[sp]])
                    eW = P.s(lambda s, sp=sp: s.activation(
                        qtmp[sp][:], tqxp[sp][:], AF.Exp, scale=-1.0),
                        waits=[eU])
                    eT = P.v(lambda v, sp=sp, bank_t=bank_t: v.tensor_add(
                        qt2[sp][:], bank_t[:], tqxp[sp][:]),
                        waits=[eU])
                    eQ2 = P.v(lambda v, j=j, sp=sp, pb=pb: v.tensor_add(
                        qTs[pb][:, j * 512:(j + 1) * 512], qt2[sp][:],
                        qtmp[sp][:]),
                        waits=[eW, eT, ev_qT_free[pb] if j == 0 else None])
                    ev_tq_read[sp] = eQ2
                    ev_q_evac.append(eQ2)
                    pool_free[bi] = [eU, eT]
                ev_xn1T_read = evQ
                evQO = P.dma("gpsimd", lambda g, m=m, pb=pb: g.dma_start(
                    out=qTd[m], in_=qTs[pb][:]),
                    waits=ev_q_evac, sem="QO%d" % pb)
                ev_qT_free[pb] = evQO
                ev_qT_out.append(evQO)

            eK0 = P.v(lambda v: v.tensor_copy(kvsb[:, 0:512], pkv[0][:]),
                      waits=ev_kv_done)
            eK1 = P.v(lambda v: v.tensor_copy(kvsb[:, 512:1024], pkv[1][:]),
                      waits=[eK0])
            eZ0 = P.v(lambda v: v.tensor_copy(zsb[:, 0:512], pz[0][:]),
                      waits=ev_z_done + [eK1])
            eZ1 = P.v(lambda v: v.tensor_copy(zsb[:, 512:1024], pz[1][:]),
                      waits=[eZ0])
            evD1 = P.dma("gpsimd", lambda g: g.dma_start(
                out=cc1i[0:128, :], in_=kvsb[:]), waits=[eK1])
            evD2 = P.dma("gpsimd", lambda g: g.dma_start(
                out=cc1i[128:129, :], in_=zsb[:]), waits=[eZ1])
            evCC1 = P.op("gpsimd", lambda g: g.collective_compute(
                "AllReduce", ALU.add, replica_groups=RG,
                ins=[cc1i.opt()], outs=[cc1o.opt()]),
                waits=[("WG", P.cnt["WG"])], sem="CC", inc=1)

            evKz = P.g(lambda g: g.memset(kvz_r[:], 0.0))
            kvv = lambda ap: ap.rearrange("p (g c) -> p g c", c=130)
            cco = lambda ap: ap.rearrange("p (g c) -> p g c", c=128)
            P.dma("gpsimd", lambda g: g.dma_start(
                out=kvv(kvz_r[0:64, :])[:, :, 0:64],
                in_=cco(cc1o[0:64, :])[:, :, 0:64]),
                waits=[evCC1, evKz])
            P.dma("gpsimd", lambda g: g.dma_start(
                out=kvv(kvz_r[64:128, :])[:, :, 65:129],
                in_=cco(cc1o[64:128, :])[:, :, 64:128]),
                waits=[evCC1, evKz])
            evZr = P.dma("gpsimd", lambda g: g.dma_start(
                out=zrow[:], in_=cc1o[128:129, :]), waits=[evCC1])
            evZt = []
            for c in range(NCH):
                evZt.append(P.t(lambda t, c=c: t.transpose(
                    ptrA[:, 2 * c:2 * c + 1], zrow[0:1, c * 128:(c + 1) * 128],
                    ident[0:1, 0:1]), waits=[evZr, evI2, ev_ptr_free]))
            zc1 = P.s(lambda s: s.activation(
                kvv(kvz_r[0:64, :])[:, :, 64:65],
                ptrA[0:64, 0:16:2].unsqueeze(2),
                AF.Copy), waits=[evZt[-1], evKz])
            zc2 = P.s(lambda s: s.activation(
                kvv(kvz_r[64:128, :])[:, :, 129:130],
                ptrA[64:128, 0:16:2].unsqueeze(2),
                AF.Copy), waits=[evZt[-1], evKz])
            ev_rhs = ("WG", P.cnt["WG"])
            ev_rhs2 = [zc1, zc2]

            P.barrier()

        if phases == "A":
            P.op("sync", None, waits=[("CC", 1)])
        if "B" in phases and True:
         with ExitStack() as stB:
            def sbb(n, sh, dt):
                return stB.enter_context(nc.sbuf_tensor(n, sh, dt))
            def psb(n, sh, dt):
                return stB.enter_context(nc.psum_tensor(n, sh, dt))

            wo_sb = sbb("wo_sb", [128, NCH * D], BF16)
            boutb = sbb("boutb", [128, D], F32)
            ms_sb = sbb("ms_sb", [3, 2048], F32)
            qs_sb = sbb("qs_sb", [2, 1], F32)
            zr2 = sbb("zr2", [1, 2048], F32)
            qrp = sbb("qrp", [128, 2], F32)
            xbufB = [sbb("xB0", [128, 4096], F32), sbb("xB1", [128, 4096], F32)]
            qTb = [sbb("qTb0", [128, 4096], BF16), sbb("qTb1", [128, 4096], BF16)]
            attn = [sbb("attn0", [128, D], BF16), sbb("attn1", [128, D], BF16)]
            attnT = [sbb("attnT0", [128, D], BF16), sbb("attnT1", [128, D], BF16)]
            rden = [sbb("rden0", [128, 16], F32), sbb("rden1", [128, 16], F32)]
            aout = [sbb("aout%d" % i, [128, D], F32) for i in range(8)]
            aln = [sbb("aln0", [128, D], F32), sbb("aln1", [128, D], F32)]
            x1b_ = [sbb("x1b0", [128, D], F32), sbb("x1b1", [128, D], F32)]
            st2 = [sbb("st2_0", [128, 2], F32), sbb("st2_1", [128, 2], F32)]
            bn6b = [sbb("bn6b_0", [128, 12], F32), sbb("bn6b_1", [128, 12], F32)]
            rr2 = [sbb("rr2_0", [128, 2], F32), sbb("rr2_1", [128, 2], F32)]
            xcat = [sbb("xcat0", [128, 2048], BF16), sbb("xcat1", [128, 2048], BF16)]
            rv3f = [sbb("rv3f_0", [128, 3], F32), sbb("rv3f_1", [128, 3], F32)]
            rv3 = [sbb("rv3_0", [128, 3], BF16), sbb("rv3_1", [128, 3], BF16)]

            png = [psb("png0", [128, 390], F32), psb("png1", [128, 390], F32)]
            ptrB = psb("ptrB", [128, 1024], BF16)
            pwo = psb("pwo", [128, 512], F32)
            pm = [psb("pm0", [3, 512], F32), psb("pm1", [3, 512], F32),
                  psb("pm2", [3, 512], F32), psb("pm3", [3, 512], F32)]
            pqs_view = None

            P.dma("gpsimd", lambda g: g.dma_start(
                out=wo_sb.ap().rearrange("p (c j) -> p c j", c=NCH), in_=wout.rearrange("(c p) j -> p c j", p=128)))
            WGb = ("WG", P.cnt["WG"])
            evBo = P.dma("sync", lambda e: e.dma_start(
                out=boutb[:], in_=bout.unsqueeze(0).partition_broadcast(128)),
                sem="WH")
            WHb = ("WH", P.cnt["WH"])
            GROUPS = [(0, 3, 0), (3, 6, 6), (6, 8, 12)]
            ln_state = {}
            ev_png_free = [None, None]
            ev_x1_free = [None, None]
            ev_attn_free = [None, None]
            ev_rden_free = [None, None]
            ev_aout_free = [None] * 8
            ev_aln_free = [None, None]
            ev_xfreeB = [None, None]
            ev_qTb_free = [None, None]
            ev_ptrB_free = [zc2]
            ev_attnT_free = [None, None]
            ev_pwo_free = [None]
            ev_xc_free = [None, None]
            ev_rv_free = [None, None]
            ev_pm_last = [None] * 4
            ev_x1_stores = []
            ev_cache_w = []
            pending_m = []
            state_m = {"xc": ev_xc_free, "rv": ev_rv_free}

            def flush_m(before_stile):
                nonlocal ev_pm_last
                while pending_m and pending_m[0][1] < before_stile:
                    sp_, st_, ws_ = pending_m.pop(0)
                    evMs = []
                    for ci in range(4):
                        evMs.append(P.t(lambda t, ci=ci, sp_=sp_, st_=st_: t.matmul(
                            pm[ci][:],
                            lhsT=rv3[sp_][:],
                            rhs=xcat[sp_][:, ci * 512:(ci + 1) * 512],
                            start=(st_ == 0), stop=(st_ == nS - 1)),
                            waits=ws_))
                    ev_xc_free[sp_] = evMs[-1]
                    ev_rv_free[sp_] = evMs[-1]
                    if st_ == nS - 1:
                        ev_pm_last = evMs

            evXBm = [None] * nM
            evQBm = [None] * nM

            def load_x(m):
                pb = m % 2
                evXBm[m] = P.dma("sync", lambda e, m=m, pb=pb: e.dma_start(
                    out=xbufB[pb].ap().rearrange("p (s d) -> p s d", s=4),
                    in_=x_in[m * 512:(m + 1) * 512, :].rearrange(
                        "(s p) d -> p s d", p=128)),
                    waits=[ev_xfreeB[pb]], sem="XA%d" % pb)

            def load_q(m):
                pb = m % 2
                evQBm[m] = P.dma("scalar", lambda e, m=m, pb=pb: e.dma_start(
                    out=qTb[pb][:], in_=qTd[m]),
                    waits=[ev_qTb_free[pb],
                           ("QO0", P.cnt["QO0"]), ("QO1", P.cnt["QO1"])],
                    sem="QI%d" % pb)

            def numdiv(stile):
                m = stile // SUB
                s4 = stile % SUB
                pb = m % 2
                sp = stile % 2
                evQ = evQBm[m]
                ev_div = []
                last_q = None
                for (p0, p1, rc) in GROUPS:
                    gi = rc // 6
                    bank = gi % 2
                    npair = p1 - p0
                    evN = []
                    for p8 in range(p0, p1):
                        off = (p8 - p0) * 130
                        evN.append(P.t(lambda t, p8=p8, off=off, bank=bank, pb=pb, s4=s4, p0=p0, p1=p1: t.matmul(
                            png[bank][:, off:off + 130],
                            lhsT=qTb[pb][:, p8 * 512 + s4 * 128:
                                         p8 * 512 + (s4 + 1) * 128],
                            rhs=kvz_r[:, p8 * 130:(p8 + 1) * 130],
                            start=(p8 == p0), stop=(p8 == p1 - 1)),
                            waits=[evQ, ev_rhs] + ev_rhs2 + (ev_png_free[bank] or [])))
                    last_q = evN[-1]
                    eD1 = P.v(lambda v, rc=rc, bank=bank, sp=sp, npair=npair: v.tensor_scalar(
                        rden[sp][:, rc:rc + 2 * npair],
                        png[bank][:, 64:npair * 130:65],
                        1e-6, None, ALU.add),
                        waits=evN + (ev_rden_free[sp] or []))
                    eD2 = P.v(lambda v, rc=rc, sp=sp, npair=npair: v.reciprocal(
                        rden[sp][:, rc:rc + 2 * npair],
                        rden[sp][:, rc:rc + 2 * npair]), waits=[eD1])

                    def dvv(e, bank=bank, p0=p0, rc=rc, npair=npair, sp=sp):
                        return e.tensor_tensor(
                            attn[sp].ap().rearrange("p (j c) -> p j c", c=64)[:, 2 * p0:2 * p0 + 2 * npair, :],
                            png[bank].ap().rearrange("p (j c) -> p j c", c=65)[:, 0:2 * npair, 0:64],
                            rden[sp][:, rc:rc + 2 * npair].unsqueeze(2).broadcast_to([128, 2 * npair, 64]),
                            ALU.mult)
                    evDv = P.v(dvv, waits=[eD2, ev_attn_free[sp]])
                    ev_div.append(evDv)
                    ev_png_free[bank] = [evDv]
                ev_rden_free[sp] = [ev_div[-1]]
                return ev_div, last_q

            load_q(0)
            load_x(0)
            div_evs = [None] * (nS + 1)
            div_evs[0] = numdiv(0)

            def headB(stile):
                m = stile // SUB
                s4 = stile % SUB
                pb = m % 2
                sp = stile % 2
                ev_div, last_q = div_evs[stile]
                if debug_dump:
                    P.dma("gpsimd", lambda g, sp=sp, stile=stile: g.dma_start(
                        out=adbg[stile * 128:(stile + 1) * 128, :],
                        in_=attn[sp][:]), waits=ev_div, sem="DBG")
                    ev_div = ev_div + [("DBG", P.cnt["DBG"])]
                evT = []
                for c in range(NCH):
                    evT.append(P.t(lambda t, c=c, sp=sp: t.transpose(
                        ptrB[:, c * 128:(c + 1) * 128],
                        attn[sp][:, c * 128:(c + 1) * 128], ident[:]),
                        waits=ev_div + [ev_ptrB_free[0], evI2]))
                ev_attn_free[sp] = evT[-1]
                eTe = P.s(lambda s, sp=sp: s.activation(
                    attnT[sp][:], ptrB[:], AF.Copy),
                    waits=[evT[-1], ev_attnT_free[sp]])
                ev_ptrB_free[0] = eTe
                if stile + 1 < nS:
                    div_evs[stile + 1] = numdiv(stile + 1)
                    if (stile + 1) % SUB == SUB - 1:
                        ev_qTb_free[((stile + 1) // SUB) % 2] = div_evs[stile + 1][1]
                ao = stile % 8
                ev_wout = None
                for half in range(2):
                    evW = None
                    for c in range(NCH):
                        evW = P.t(lambda t, c=c, half=half, sp=sp: t.matmul(
                            pwo[:],
                            lhsT=attnT[sp][:, c * 128:(c + 1) * 128],
                            rhs=wo_sb[:, c * D + half * 512:
                                      c * D + (half + 1) * 512],
                            start=(c == 0), stop=(c == NCH - 1)),
                            waits=[eTe, WGb, ev_pwo_free[0]])
                    eB = P.v(lambda v, half=half, ao=ao: v.tensor_add(
                        aout[ao][:, half * 512:(half + 1) * 512], pwo[:],
                        boutb[:, half * 512:(half + 1) * 512]),
                        waits=[evW, WHb,
                               ev_aout_free[ao] if half == 0 else None])
                    ev_pwo_free[0] = eB
                    ev_wout = eB
                    if half == 1:
                        ev_attnT_free[sp] = evW
                wout_evs[stile] = ev_wout

            def tailB(stile):
                m = stile // SUB
                s4 = stile % SUB
                pb = m % 2
                sp = stile % 2
                ao = stile % 8
                ev_wout = wout_evs[stile]
                ev_rr2, rr2_done = _ln(
                    P, bn6b[sp], st2[sp], rr2[sp],
                    (lambda ao=ao: aout[ao][:]),
                    ln_state, ("B", sp), extra_waits=[ev_wout])
                b7 = P.s(lambda s, sp=sp, ao=ao: s.activation(
                    aln[sp][:], aout[ao][:], AF.Identity,
                    scale=rr2[sp][:, 0:1], bias=rr2[sp][:, 1:2]),
                    waits=[ev_rr2, ev_aln_free[sp]])
                rr2_done(b7)
                b8 = P.v(lambda v, sp=sp, s4=s4, pb=pb, m=m: v.tensor_add(
                    x1b_[sp][:], aln[sp][:],
                    xbufB[pb][:, s4 * D:(s4 + 1) * D]),
                    waits=[b7, evXBm[m], ev_x1_free[sp]])
                ev_aout_free[ao] = b7
                ev_aln_free[sp] = b8
                if s4 == SUB - 1:
                    ev_xfreeB[pb] = b8
                ev_rr2b, rr2b_done = _ln(
                    P, bn6b[sp], st2[sp], rr2[sp],
                    (lambda sp=sp: x1b_[sp][:]),
                    ln_state, ("B", sp), extra_waits=[b8, b7])
                c7 = P.s(lambda s, sp=sp, stile=stile: s.copy(
                    r2c[:, stile:stile + 1], rr2[sp][:, 0:1]),
                    waits=[ev_rr2b])
                c8 = P.s(lambda s, sp=sp, stile=stile: s.copy(
                    q2c[:, stile:stile + 1], rr2[sp][:, 1:2]),
                    waits=[ev_rr2b])
                c10 = P.s(lambda s, sp=sp, stile=stile: s.activation(
                    qc2[:, stile:stile + 1], rr2[sp][:, 1:2], AF.Square),
                    waits=[ev_rr2b])
                ev_cache_w.extend([c7, c8, c10])
                d1 = P.v(lambda v, sp=sp: v.tensor_copy(
                    rv3f[sp][:, 0:1], rr2[sp][:, 0:1]),
                    waits=[ev_rr2b, ev_rv_free[sp]])
                d2 = P.v(lambda v, sp=sp: v.tensor_mul(
                    rv3f[sp][:, 1:2], rr2[sp][:, 0:1], rr2[sp][:, 1:2]),
                    waits=[ev_rr2b, ev_rv_free[sp]])
                d3 = P.v(lambda v, sp=sp: v.tensor_mul(
                    rv3f[sp][:, 2:3], rr2[sp][:, 0:1], rr2[sp][:, 0:1]),
                    waits=[ev_rr2b, ev_rv_free[sp]])
                rr2b_done(d3)
                d4 = P.v(lambda v, sp=sp: v.tensor_copy(
                    rv3[sp][:], rv3f[sp][:]),
                    waits=[d1, d2, d3, ev_rv_free[sp]])
                d5 = P.s(lambda s, sp=sp: s.copy(
                    xcat[sp][:, 0:1024], x1b_[sp][:]),
                    waits=[b8, ev_xc_free[sp]])
                d6 = P.s(lambda s, sp=sp: s.activation(
                    xcat[sp][:, 1024:2048], x1b_[sp][:], AF.Square),
                    waits=[b8, ev_xc_free[sp]])
                pending_m.append((sp, stile, [d4, d5, d6]))
                if stile == nS - 1:
                    flush_m(nS)
                else:
                    flush_m(stile)
                evXO = P.dma("gpsimd", lambda g, sp=sp, stile=stile: g.dma_start(
                    out=x1d[stile * 128:(stile + 1) * 128, :],
                    in_=x1b_[sp][:]),
                    waits=[b8], sem="XO%d" % sp)
                ev_x1_free[sp] = evXO
                ev_x1_stores.append(evXO)

            wout_evs = [None] * nS
            for m in range(nM):
                if m + 1 < nM:
                    load_q(m + 1)
                for s4 in range(SUB):
                    headB(m * SUB + s4)
                    if m >= 1:
                        tailB((m - 1) * SUB + s4)
                if m + 1 < nM:
                    load_x(m + 1)
            for s4 in range(SUB):
                tailB((nM - 1) * SUB + s4)

            f1 = P.v(lambda v: v.tensor_reduce(qrp[:, 0:1], q2c[:], AX.X,
                                               ALU.add), waits=ev_cache_w)
            f2 = P.v(lambda v: v.tensor_reduce(qrp[:, 1:2], qc2[:], AX.X,
                                               ALU.add), waits=ev_cache_w)
            f3 = P.t(lambda t: t.matmul(ptrB.ap().bitcast(F32)[0:2, 0:1],
                                        lhsT=qrp[:], rhs=ones_f[:],
                                        start=True, stop=True),
                     waits=[f1, f2, evOF, ev_ptrB_free[0]] + ev_pm_last)
            f4 = P.v(lambda v: v.tensor_copy(
                qs_sb[:], ptrB.ap().bitcast(F32)[0:2, 0:1]), waits=[f3])
            evMe = []
            for ci in range(4):
                evMe.append(P.v(lambda v, ci=ci: v.tensor_copy(
                    ms_sb.ap().rearrange("p (c n) -> p c n", c=4)[:, ci, :],
                    pm[ci][:]),
                    waits=ev_pm_last + [f4]))
            gz0 = P.g(lambda g: g.memset(zr2[:], 0.0))
            P.dma("gpsimd", lambda g: g.dma_start(
                out=cc2i[3:4, :], in_=zr2[:]), waits=[gz0])
            g1 = P.dma("gpsimd", lambda g: g.dma_start(
                out=cc2i[0:3, :], in_=ms_sb[:]), waits=evMe)
            g2 = P.dma("gpsimd", lambda g: g.dma_start(
                out=cc2i[3:4, 0:2].rearrange("a b -> b a"), in_=qs_sb[:],
                allow_slow_non_contiguous=True), waits=[f4])
            if bstop is not None:
                evCC2 = None
            evCC2 = evCC2 if bstop is not None else P.op("gpsimd", lambda g: g.collective_compute(
                "AllReduce", ALU.add, replica_groups=RG,
                ins=[cc2i.opt()], outs=[cc2o.opt()]),
                waits=[("WG", P.cnt["WG"])], sem="CC", inc=1)

            P.barrier()

        if phases == "AB":
            P.op("sync", None, waits=[("CC", 1 if bstop else 2)])
        if "C" in phases and True:
         with ExitStack() as stC:
            def sbc(n, sh, dt):
                return stC.enter_context(nc.sbuf_tensor(n, sh, dt))
            def psc(n, sh, dt):
                return stC.enter_context(nc.psum_tensor(n, sh, dt))

            wcT = sbc("wcT", [128, NCH * D], BF16)
            wd_sb = sbc("wd_sb", [128, NCH * R], BF16)
            wu_sb = sbc("wu_sb", [64, D], BF16)
            mu_sb = sbc("mu_sb", [8, D], F32)
            sg_sb = sbc("sg_sb", [8, D], F32)
            iv_sb = sbc("iv_sb", [8, D], F32)
            muiv = sbc("muiv", [8, D], F32)
            c3_sb = sbc("c3_sb", [8, D], F32)
            lw1 = sbc("lw1", [8, D], F32)
            lw2 = sbc("lw2", [8, D], F32)
            m2ab = sbc("m2ab", [8, D], F32)
            q1b = sbc("q1b", [8, 1], F32)
            q2b = sbc("q2b", [8, 1], F32)
            wstg = sbc("wstg", [128, D], F32)
            wstg2 = sbc("wstg2", [128, D], F32)
            wustg = sbc("wustg", [64, D], F32)
            wdstg = sbc("wdstg", [128, NCH * R], F32)
            bu_b = sbc("bu_b", [8, D], BF16)
            bc_b = sbc("bc_b", [8, D], BF16)
            bd_b = sbc("bd_b", [8, R], BF16)
            ohc = sbc("ohc", [8, 1], BF16)

            P.dma("gpsimd", lambda g: g.dma_start(out=bu_b[:], in_=bu_in))
            P.dma("gpsimd", lambda g: g.dma_start(out=bc_b[:], in_=bc_in))
            P.dma("gpsimd", lambda g: g.dma_start(out=bd_b[:], in_=bd_in))
            WGc = ("WG", P.cnt["WG"])
            P.dma("sync", lambda e: e.dma_start(out=mu_sb[:], in_=mu_in),
                  sem="LB")
            P.dma("sync", lambda e: e.dma_start(out=sg_sb[:], in_=sig_in),
                  sem="LB")
            h1 = P.v(lambda v: v.tensor_mul(iv_sb[:], sg_sb[:], sg_sb[:]),
                         waits=[("LB", P.cnt["LB"])])
            h2 = P.v(lambda v: v.reciprocal(iv_sb[:], iv_sb[:]), waits=[h1])
            h3 = P.v(lambda v: v.tensor_mul(muiv[:], mu_sb[:], iv_sb[:]),
                     waits=[h2, WHs])
            h4 = P.v(lambda v: v.tensor_mul(c3_sb[:], mu_sb[:], muiv[:]),
                     waits=[h3])
            for fn in (lambda e: e.dma_start(out=lw1[:], in_=cc2o[0:1, 0:1024].partition_broadcast(8)),
                       lambda e: e.dma_start(out=lw2[:], in_=cc2o[1:2, 0:1024].partition_broadcast(8)),
                       lambda e: e.dma_start(out=m2ab[:], in_=cc2o[2:3, 1024:2048].partition_broadcast(8)),
                       lambda e: e.dma_start(out=q1b[:], in_=cc2o[3:4, 0:1].partition_broadcast(8)),
                       lambda e: e.dma_start(out=q2b[:], in_=cc2o[3:4, 1:2].partition_broadcast(8))):
                P.dma("sync", fn, waits=[evCC2], sem="LB")
            evLB = ("LB", P.cnt["LB"])
            inv_t = 1.0 / float(t_total)
            k1 = P.v(lambda v: v.tensor_scalar(
                lw1[:], lw1[:], q1b[:], inv_t, ALU.add, ALU.mult), waits=[evLB])
            k2a = P.v(lambda v: v.tensor_scalar(
                lw2[:], lw2[:], 2.0, None, ALU.mult), waits=[evLB])
            k2 = P.v(lambda v: v.tensor_add(lw2[:], lw2[:], m2ab[:]),
                     waits=[k2a])
            k3 = P.v(lambda v: v.tensor_scalar(
                lw2[:], lw2[:], q2b[:], inv_t, ALU.add, ALU.mult), waits=[k2])
            k4 = P.v(lambda v: v.tensor_mul(lw2[:], lw2[:], iv_sb[:]),
                     waits=[k3, h2])
            k5 = P.v(lambda v: v.tensor_mul(lw1[:], lw1[:], muiv[:]),
                     waits=[k1, h3])
            k6a = P.v(lambda v: v.tensor_scalar(
                lw1[:], lw1[:], -2.0, None, ALU.mult), waits=[k5])
            k6 = P.v(lambda v: v.tensor_add(lw1[:], lw1[:], lw2[:]),
                     waits=[k4, k6a])
            k7 = P.v(lambda v: v.tensor_add(lw1[:], lw1[:], c3_sb[:]),
                     waits=[k6, h4])
            k8 = P.v(lambda v: v.tensor_reduce(loss8[:], lw1[:], AX.X, ALU.add),
                     waits=[k7])
            k9 = P.dma("gpsimd", lambda g: g.dma_start(
                out=lossd.rearrange("a b -> b a"), in_=loss8[:],
                allow_slow_non_contiguous=True), waits=[k8])
            k10 = P.dma("gpsimd", lambda g: g.dma_start(out=lrow[:], in_=lossd),
                        waits=[k9])
            k11 = P.v(lambda v: v.tensor_reduce(lmin[:], lrow[:], AX.X, ALU.min),
                      waits=[k10])
            k12 = P.v(lambda v: v.tensor_single_scalar(
                lone[:], lrow[:], lmin[:], ALU.is_equal), waits=[k11])
            k13 = P.v(lambda v: v.tensor_copy(liot[:], iota8r[:]), waits=[evIr])
            k14 = P.v(lambda v: v.tensor_mul(liot[:], liot[:], lone[:]),
                      waits=[k13, k12])
            k15 = P.v(lambda v: v.tensor_reduce(idxf1[:], liot[:], AX.X, ALU.add),
                      waits=[k14])
            k16 = P.v(lambda v: v.tensor_copy(idxi1[:], idxf1[:]), waits=[k15])
            k17 = P.dma("gpsimd", lambda g: g.dma_start(out=idxd, in_=idxi1[:]),
                        waits=[k16])
            k18 = P.dma("gpsimd", lambda g: g.dma_start(
                out=idxb[:], in_=idxd.partition_broadcast(128)), waits=[k17])
            k19 = P.v(lambda v: v.tensor_copy(idxbf[:], idxb[:]), waits=[k18])
            k19b = P.v(lambda v: v.tensor_scalar(
                idc[:], idxbf[:], 1024.0, None, ALU.mult), waits=[k19])
            k19c = P.v(lambda v: v.tensor_scalar(
                idu[:], idxbf[0:64, :], 64.0, None, ALU.mult), waits=[k19])
            k20 = P.v(lambda v: v.tensor_copy(gcf[:], iota8x[:]), waits=[evI8])
            k21 = P.v(lambda v: v.tensor_scalar(
                gcf[:], gcf[:], idc[:], None, ALU.add), waits=[k19b, k20])
            k22 = P.v(lambda v: v.tensor_copy(gci[:], gcf[:]), waits=[k21])
            k23 = P.v(lambda v: v.tensor_copy(guf[:], iota_p[0:64, :]),
                      waits=[evIp])
            k24 = P.v(lambda v: v.tensor_scalar(
                guf[:], guf[:], idu[:], None, ALU.add), waits=[k19c, k23])
            k25 = P.v(lambda v: v.tensor_copy(gui[:], guf[:]), waits=[k24])

            ev_stg_free = [k22, k22]
            for c in range(NCH):
                stg = wstg if c % 2 == 0 else wstg2
                gsem = "GC%d" % (c % 2)
                gA = P.dma("gpsimd", lambda g, c=c, stg=stg: g.indirect_dma_start(
                    out=stg[:], out_offset=None,
                    in_=wc_in.rearrange("e o d -> (e o) d"),
                    in_offset=IndirectOffsetOnAxis(gci[:, c:c + 1], 0)),
                    waits=[k22, ev_stg_free[c % 2]], sem=gsem)
                gB = P.dma("gpsimd", lambda g, c=c, stg=stg: g.dma_start(
                    out=wccb[c * 128:(c + 1) * 128, :], in_=stg[:]), waits=[gA],
                    sem=gsem)
                ev_stg_free[c % 2] = gB
            gD = []
            for c in range(NCH):
                gD.append(P.dma("gpsimd", lambda g, c=c: g.indirect_dma_start(
                    out=wdstg[:, c * R:(c + 1) * R], out_offset=None,
                    in_=wd_in.rearrange("e d r -> (e d) r"),
                    in_offset=IndirectOffsetOnAxis(gci[:, c:c + 1], 0)),
                    waits=[k22], sem="GD%d" % (c % 2)))
            gD2 = P.dma("gpsimd", lambda g: g.dma_start(out=wd_sb[:], in_=wdstg[:]),
                        waits=[("GD0", P.cnt["GD0"]), ("GD1", P.cnt["GD1"])],
                        sem="GD0")
            gU = P.dma("gpsimd", lambda g: g.indirect_dma_start(
                out=wustg[:], out_offset=None,
                in_=wu_in.rearrange("e r d -> (e r) d"),
                in_offset=IndirectOffsetOnAxis(gui[:, 0:1], 0)), waits=[k25],
                sem="GU")
            gU2 = P.dma("gpsimd", lambda g: g.dma_start(out=wu_sb[:], in_=wustg[:]),
                        waits=[gU], sem="GU")
            bias_evs = {}

            def emit_bias_sel():
                e_oh = P.t(lambda t: t.transpose(
                    ptrC2[1].ap().bitcast(F32)[0:8, 32:33], lone[:],
                    identf[0:1, 0:1]),
                    waits=[k12, evI4, ev_ptrC_free[1]])
                e_ohc = P.s(lambda s: s.activation(
                    ohc[:], ptrC2[1].ap().bitcast(F32)[0:8, 32:33], AF.Copy),
                    waits=[e_oh])
                evm = []
                evm.append(P.t(lambda t: t.matmul(
                    pbias[:, 0:1], lhsT=bu_b[:, 0:128], rhs=ohc[:],
                    start=True, stop=False), waits=[e_ohc, WGc]))
                evm.append(P.t(lambda t: t.matmul(
                    pbias[0:64, 16:17], lhsT=bd_b[:], rhs=ohc[:],
                    start=False, stop=False), waits=[e_ohc, WGc]))
                for c in range(1, NCH):
                    evm.append(P.t(lambda t, c=c: t.matmul(
                        pbias[:, c:c + 1],
                        lhsT=bu_b[:, c * 128:(c + 1) * 128], rhs=ohc[:],
                        start=False, stop=False), waits=[e_ohc, WGc]))
                for c in range(NCH):
                    evm.append(P.t(lambda t, c=c: t.matmul(
                        pbias[:, 8 + c:9 + c],
                        lhsT=bc_b[:, c * 128:(c + 1) * 128], rhs=ohc[:],
                        start=False, stop=(c == NCH - 1)), waits=[e_ohc, WGc]))
                eb1 = P.s(lambda s: s.activation(buT[:], pbias[:, 0:8], AF.Copy),
                          waits=[evm[-1]])
                eb2 = P.s(lambda s: s.activation(bcT[:], pbias[:, 8:16], AF.Copy),
                          waits=[evm[-1]])
                eb3 = P.s(lambda s: s.activation(bdT[:], pbias[0:64, 16:17],
                                                 AF.Copy), waits=[evm[-1]])
                eb4 = P.v(lambda v: v.tensor_add(bcombT[:], bcT[:], bbT[:]),
                          waits=[eb2, WHs])
                bias_evs["bu"] = eb1
                bias_evs["bd"] = eb3
                bias_evs["bcomb"] = eb4
                ev_ptrC_free[1] = eb3
            ev_wc_tr = []
            ev_sb2sb = [("GC0", P.cnt["GC0"]), ("GC1", P.cnt["GC1"]),
                        ("GD0", P.cnt["GD0"]), ("GD1", P.cnt["GD1"]),
                        ("GU", P.cnt["GU"])]
            for c in range(NCH):
                ev_wc_tr.append(P.dma("scalar", lambda s, c=c: s.dma_start(
                    out=wcT[:, c * D:(c + 1) * D],
                    in_=wccb[:, c * 128:(c + 1) * 128], transpose=True),
                    waits=ev_sb2sb, sem="WT%d" % (c % 2)))
            WHwc = [("WT0", P.cnt["WT0"]), ("WT1", P.cnt["WT1"])]
            GDt = ("GD0", P.cnt["GD0"])
            GUt = ("GU", P.cnt["GU"])


            x1buf = [sbc("x1C%d" % i, [128, 4096], F32) for i in range(3)]
            xn2 = [sbc("xn2_%d" % i, [128, D], BF16) for i in range(4)]
            xn2T = sbc("xn2T", [128, NCH * 512], BF16)
            hbf = sbc("hbf", [64, 512], BF16)
            aT = sbc("aT", [128, NCH * 512], BF16)
            ybf = [sbc("ybf0", [128, NCH * 512], BF16),
                   sbc("ybf1", [128, NCH * 512], BF16)]
            ybt = [sbc("ybt0", [128, D], BF16), sbc("ybt1", [128, D], BF16)]
            st3 = [sbc("st3_0", [128, 2], F32), sbc("st3_1", [128, 2], F32)]
            bn6c = [sbc("bn6c_0", [128, 12], F32), sbc("bn6c_1", [128, 12], F32)]
            rr3 = [sbc("rr3_0", [128, 2], F32), sbc("rr3_1", [128, 2], F32)]
            tl = [sbc("tl0", [128, D], F32), sbc("tl1", [128, D], F32)]
            x2b = [sbc("x2b0", [128, D], F32), sbc("x2b1", [128, D], F32)]

            ptrC2 = [psc("ptrC", [128, 1024], BF16), psc("pxtr", [128, 1024], BF16)]
            ptrC = ptrC2[0]
            pbias = ptrC2[1].ap().bitcast(F32)[:, 0:24]
            pa = [psc("pa0", [128, 512], F32), psc("pa1", [128, 512], F32)]
            py = [psc("py0", [128, 512], F32), psc("py1", [128, 512], F32)]
            ptry2 = [psc("ptry", [128, 1024], BF16), psc("ptry2", [128, 1024], BF16)]
            ph = pa[0].ap()[0:64, :]

            def x2nT(c, lo, hi):
                return xn2T[:, c * 512 + lo:c * 512 + hi]

            ln_state = {}
            ev_x1free = [None, None, None]
            ev_ptrC_free = [None, None]
            ev_xn2_free = [None, None, None, None]
            ev_hb_free = None
            ev_ph_free = None
            ev_pa_free = [None, None]
            ev_aT_free = None
            ev_py_free = [None, None]
            ev_ybf_free = [None, None]
            ev_ptry_free = [None, None]
            ev_ybt_free = [None, None]
            ev_tl_free = [None, None]
            ev_x2_free = [None, None]
            evXO_all = [("XO0", P.cnt["XO0"]), ("XO1", P.cnt["XO1"])]

            e1_evs = {}

            def load_c(m):
                pb = m % 3
                return P.dma("sync", lambda e, m=m, pb=pb: e.dma_start(
                    out=x1buf[pb].ap().rearrange("p (s d) -> p s d", s=4),
                    in_=x1d[m * 512:(m + 1) * 512, :].rearrange(
                        "(s p) d -> p s d", p=128)),
                    waits=[ev_x1free[pb]] + (evXO_all if m < 3 else []),
                    sem="XC%d" % (m % 2))

            def prep(m, evX1):
                pb = m % 3
                for s4 in range(SUB):
                    stile = m * SUB + s4
                    e1 = P.s(lambda s, pb=pb, s4=s4, stile=stile: s.activation(
                        xn2[s4][:], x1buf[pb][:, s4 * D:(s4 + 1) * D], AF.Identity,
                        scale=r2c[:, stile:stile + 1], bias=q2c[:, stile:stile + 1]),
                        waits=[evX1, ev_xn2_free[s4]])
                    e1_evs[(m, s4)] = e1

            yev_evs = {}

            def Ctail(mm):
                xpb2 = mm % 3
                pb2 = mm % 2
                ev_yev = yev_evs.pop(mm)
                evX1t = prep_x1[mm]
                nonlocal ev_ybf_free, ev_ptry_free, ev_ybt_free
                nonlocal ev_tl_free, ev_x2_free, ev_x1free
                for s4 in range(SUB):
                    sp = s4 % 2
                    yb = ptry2[sp]
                    evT = []
                    for c in range(NCH):
                        evT.append(P.t(lambda t, c=c, s4=s4, yb=yb, pb2=pb2:
                                       t.transpose(
                            yb[:, c * 128:(c + 1) * 128],
                            ybf[pb2][:, c * 512 + s4 * 128:c * 512 + (s4 + 1) * 128],
                            ident[:]),
                            waits=ev_yev + [ev_ptry_free[sp], evI2]))
                    if s4 == SUB - 1:
                        ev_ybf_free[pb2] = evT[-1]
                    eE = P.v(lambda v, sp=sp, yb=yb: v.tensor_copy(
                        ybt[sp][:], yb[:]),
                             waits=[evT[-1], ev_ybt_free[sp]])
                    ev_ptry_free[sp] = eE
                    ev_rr3, rr3_done = _ln(
                        P, bn6c[sp], st3[sp], rr3[sp],
                        (lambda sp=sp: ybt[sp][:]),
                        ln_state, ("C", sp), extra_waits=[eE])
                    b7 = P.v(lambda v, sp=sp: v.tensor_scalar(
                        tl[sp][:], ybt[sp][:], rr3[sp][:, 0:1],
                        rr3[sp][:, 1:2], ALU.mult, ALU.add),
                        waits=[ev_rr3, ev_tl_free[sp]])
                    rr3_done(b7)
                    ev_ybt_free[sp] = b7
                    b8 = P.v(lambda v, sp=sp, s4=s4, xpb2=xpb2: v.tensor_add(
                        x2b[sp][:], tl[sp][:],
                        x1buf[xpb2][:, s4 * D:(s4 + 1) * D]),
                        waits=[b7, evX1t, ev_x2_free[sp]])
                    ev_tl_free[sp] = b8
                    if s4 == SUB - 1:
                        ev_x1free[xpb2] = b8
                    stile = mm * SUB + s4
                    evX2 = P.dma("sync", lambda e, stile=stile, sp=sp: e.dma_start(
                        out=y_out[stile * 128:(stile + 1) * 128, :],
                        in_=x2b[sp][:]),
                        waits=[b8], sem="X2O%d" % sp)
                    ev_x2_free[sp] = evX2

            prep_x1 = [None] * nM
            prep_x1[0] = load_c(0)
            if nM > 1:
                prep_x1[1] = load_c(1)
            prep(0, prep_x1[0])

            for m in range(nM):
                pb = m % 2
                xpb = m % 3
                evX1 = prep_x1[m]
                ev_evE = []
                for s4 in range(SUB):
                    trb = ptrC2[s4 % 2]
                    evT = []
                    for c in range(NCH):
                        evT.append(P.t(lambda t, c=c, s4=s4, trb=trb: t.transpose(
                            trb[:, c * 128:(c + 1) * 128],
                            xn2[s4][:, c * 128:(c + 1) * 128], ident[:]),
                            waits=[e1_evs[(m, s4)], evI2, ev_ptrC_free[s4 % 2]]))
                    ev_xn2_free[s4] = evT[-1]
                    eE = P.s(lambda s, s4=s4, trb=trb: s.activation(
                        xn2T.ap().rearrange("p (c n) -> p c n", c=NCH)[:, :, s4 * 128:(s4 + 1) * 128],
                        trb.ap().rearrange("p (c n) -> p c n", c=NCH), AF.Copy),
                        waits=[evT[-1]])
                    ev_ptrC_free[s4 % 2] = eE
                    ev_evE.append(eE)
                if m + 1 < nM:
                    prep(m + 1, prep_x1[m + 1])
                if m == 0:
                    emit_bias_sel()
                evDn = None
                for c in range(NCH):
                    evDn = P.t(lambda t, c=c: t.matmul(
                        ph,
                        lhsT=wd_sb[:, c * R:(c + 1) * R],
                        rhs=x2nT(c, 0, 512),
                        start=(c == 0), stop=(c == NCH - 1)),
                        waits=ev_evE + [GDt, ev_ph_free] + (ev_pa_free[0] or []))
                evRl = P.s(lambda s: s.activation(hbf[:], ph, AF.Relu,
                                                  bias=bdT[:]),
                           waits=[evDn, bias_evs["bd"], ev_hb_free])
                ev_ph_free = evRl
                ev_up = []
                for c in range(NCH):
                    bank = (c + 1) % 2
                    evU = P.t(lambda t, c=c, bank=bank: t.matmul(
                        pa[bank][:],
                        lhsT=wu_sb[:, c * 128:(c + 1) * 128],
                        rhs=hbf[:],
                        start=True, stop=True),
                        waits=[evRl, GUt] + (ev_pa_free[bank] or []))
                    evUe = P.s(lambda s, c=c, bank=bank: s.activation(
                        aT[:, c * 512:(c + 1) * 512],
                        pa[bank][:], AF.Identity, bias=buT[:, c:c + 1]),
                        waits=[evU, bias_evs["bu"]] + ([ev_aT_free] if c == 0 else []))
                    ev_pa_free[bank] = [evUe]
                    ev_up.append(evUe)
                ev_hb_free = ev_up[-1]
                if m >= 1:
                    Ctail(m - 1)
                if m + 2 < nM:
                    prep_x1[m + 2] = load_c(m + 2)
                ev_yev = []
                for co in range(NCH):
                    bank = co % 2
                    evY = None
                    for c in range(NCH):
                        evY = P.t(lambda t, c=c, co=co, bank=bank: t.matmul(
                            py[bank][:],
                            lhsT=bwT[:, c * D + co * 128:c * D + (co + 1) * 128],
                            rhs=x2nT(c, 0, 512),
                            start=(c == 0), stop=False),
                            waits=ev_evE + WTs + [ev_py_free[bank]])
                    for c in range(NCH):
                        evY = P.t(lambda t, c=c, co=co, bank=bank: t.matmul(
                            py[bank][:],
                            lhsT=wcT[:, c * D + co * 128:c * D + (co + 1) * 128],
                            rhs=aT[:, c * 512:(c + 1) * 512],
                            start=False, stop=(c == NCH - 1)),
                            waits=ev_up + WHwc)
                    eYe = P.s(lambda s, co=co, bank=bank, pb=pb: s.activation(
                        ybf[pb][:, co * 512:(co + 1) * 512],
                        py[bank][:], AF.Identity, bias=bcombT[:, co:co + 1]),
                        waits=[evY, bias_evs["bcomb"]] + ([ev_ybf_free[pb]] if co == 0 else []))
                    ev_py_free[bank] = eYe
                    ev_yev.append(eYe)
                ev_aT_free = ev_yev[-1]
                yev_evs[m] = ev_yev

            Ctail(nM - 1)

            P.op("sync", None, waits=[("X2O0", P.cnt["X2O0"]),
                                      ("X2O1", P.cnt["X2O1"])])

        with nc.Block() as blk:
            P.emit(blk)
    return nc


def kernel(L0_attn_x, L0_attn_W_qkv, L0_attn_W_out, L0_attn_b_out,
           L0_ffn_base_W, L0_ffn_base_b, L0_ffn_disc_mu, L0_ffn_disc_sigma,
           L0_ffn_down_W, L0_ffn_down_b, L0_ffn_up_W, L0_ffn_up_b,
           L0_ffn_conv_W, L0_ffn_conv_b):
    x = np.ascontiguousarray(np.asarray(L0_attn_x, dtype=np.float32))
    B, N, _ = x.shape
    T = N // 2
    shared = {
        "wqkv": np.ascontiguousarray(L0_attn_W_qkv, dtype=np.float32),
        "wout": np.ascontiguousarray(L0_attn_W_out, dtype=np.float32),
        "bout": np.ascontiguousarray(L0_attn_b_out, dtype=np.float32),
        "basew": np.ascontiguousarray(L0_ffn_base_W, dtype=np.float32),
        "baseb": np.ascontiguousarray(L0_ffn_base_b, dtype=np.float32),
        "mu": np.ascontiguousarray(L0_ffn_disc_mu, dtype=np.float32),
        "sigma": np.ascontiguousarray(L0_ffn_disc_sigma, dtype=np.float32),
        "wd": np.ascontiguousarray(L0_ffn_down_W, dtype=np.float32),
        "bd": np.ascontiguousarray(L0_ffn_down_b, dtype=np.float32),
        "wu": np.ascontiguousarray(L0_ffn_up_W, dtype=np.float32),
        "bu": np.ascontiguousarray(L0_ffn_up_b, dtype=np.float32),
        "wc": np.ascontiguousarray(L0_ffn_conv_W, dtype=np.float32),
        "bc": np.ascontiguousarray(L0_ffn_conv_b, dtype=np.float32),
    }
    nc = build_program(T=T, t_total=N)
    in_maps = []
    for c in range(8):
        b, h = c // 2, c % 2
        mp = dict(shared)
        mp["x"] = np.ascontiguousarray(x[b, h * T:(h + 1) * T, :])
        in_maps.append(mp)
    res = run_bass_kernel_spmd(nc, in_maps, core_ids=list(range(8)))
    out = np.empty((B, N, D), dtype=np.float32)
    for c in range(8):
        b, h = c // 2, c % 2
        out[b, h * T:(h + 1) * T, :] = res.results[c]["y"]
    return out


# version 14
# speedup vs baseline: 1.3472x; 1.0679x over previous
from collections import defaultdict
from contextlib import ExitStack

import numpy as np

import concourse.bass as bass
import concourse.mybir as mybir
from concourse.bass import IndirectOffsetOnAxis
from concourse.bass_utils import run_bass_kernel_spmd

F32 = mybir.dt.float32
BF16 = mybir.dt.bfloat16
I32 = mybir.dt.int32
AF = mybir.ActivationFunctionType
ALU = mybir.AluOpType
AX = mybir.AxisListType

D = 1024
H = 16
NCH = 8
R = 64
E = 8
EPS = 1e-5

ENGINES = ("sync", "gpsimd", "tensor", "vector", "scalar")


class Plan:

    def __init__(self, nc):
        self.nc = nc
        self.ops = {e: [] for e in ENGINES}
        self.cnt = defaultdict(int)
        self.handles = {}

    def sem(self, stack, name):
        self.handles[name] = stack.enter_context(self.nc.semaphore(name))

    ENG_SEMS = ("T", "V", "S", "G", "BAR")

    def op(self, eng, fn, waits=(), sem=None, inc=1):
        ev = None
        ws = [w for w in waits if w is not None]
        for w in ws:
            assert isinstance(w, tuple) and len(w) == 2, f"bad wait {w!r}"
        if sem is not None:
            if sem not in self.ENG_SEMS and self.cnt[sem] > 0:
                ws.append((sem, self.cnt[sem]))
            self.cnt[sem] += inc
            ev = (sem, self.cnt[sem])
        self.ops[eng].append((fn, ws, (sem, inc) if sem else None))
        return ev

    def t(self, fn, waits=()):
        return self.op("tensor", fn, waits, "T")

    def v(self, fn, waits=()):
        return self.op("vector", fn, waits, "V")

    def s(self, fn, waits=()):
        return self.op("scalar", fn, waits, "S")

    def g(self, fn, waits=()):
        return self.op("gpsimd", fn, waits, "G")

    def dma(self, eng, fn, waits=(), sem="WG"):
        return self.op(eng, fn, waits, sem, inc=16)

    def barrier(self):
        self.cnt["BAR"] += len(ENGINES)
        tgt = self.cnt["BAR"]
        for e in ENGINES:
            self.ops[e].append((lambda eng: eng.nop(), [], ("BAR", 1)))
            self.ops[e].append((None, [("BAR", tgt)], None))

    OWN_SEM = {"tensor": "T", "vector": "V", "scalar": "S", "gpsimd": "G"}

    def emit(self, blk):
        for eng in ENGINES:
            ops = self.ops[eng]
            if not ops:
                continue

            def body(e, ops=ops, own=self.OWN_SEM.get(eng)):
                last = {}
                for fn, waits, inc in ops:
                    for (s, v) in waits:
                        if last.get(s, -1) >= v:
                            continue
                        e.wait_ge(self.handles[s], v)
                        last[s] = v
                    if fn is None:
                        continue
                    inst = fn(e)
                    if inc is not None and inc[0] is not None:
                        inst.then_inc(self.handles[inc[0]], inc[1])
            getattr(blk, eng)(body)


def _ln(P, bn6, st, rr, src_fn, state, key, extra_waits=()):
    prev = state.get(key, {})
    e1a = P.v(lambda v: v.bn_stats(bn6[:, 0:6], src_fn()[:, 0:512]),
              waits=list(extra_waits) + [prev.get("aggr")])
    e1 = P.v(lambda v: v.bn_stats(bn6[:, 6:12], src_fn()[:, 512:1024]),
             waits=[e1a])
    e2 = P.v(lambda v: v.bn_aggr(st[:], bn6[:]),
             waits=[e1, prev.get("st_read")])
    e3 = P.v(lambda v: v.tensor_scalar(rr[:, 0:1], st[:, 1:2], EPS, None,
                                       ALU.add),
             waits=[e2, prev.get("rr_read")])
    e4 = P.s(lambda s: s.activation(rr[:, 0:1], rr[:, 0:1], AF.Ln), waits=[e3])
    e5 = P.s(lambda s: s.activation(rr[:, 0:1], rr[:, 0:1], AF.Exp, scale=-0.5),
             waits=[e4])
    e6 = P.v(lambda v: v.tensor_scalar(rr[:, 1:2], st[:, 0:1], rr[:, 0:1],
                                       -1.0, ALU.mult, ALU.mult),
             waits=[e5, e2])
    st_new = {"aggr": e2, "st_read": e6}
    state[key] = st_new

    def set_last_reader(ev):
        st_new["rr_read"] = ev
    return e6, set_last_reader


def build_program(T=4096, t_total=None, debug_dump=False, phases="ABC", bstop=None):
    if t_total is None:
        t_total = 2 * T
    nS = T // 128
    nM = T // 512
    SUB = 4

    nc = bass.Bass("TRN2", target_bir_lowering=False)

    x_in = nc.dram_tensor("x", [T, D], F32, kind="ExternalInput").ap()
    wqkv = nc.dram_tensor("wqkv", [D, 3 * D], F32, kind="ExternalInput").ap()
    wout = nc.dram_tensor("wout", [D, D], F32, kind="ExternalInput").ap()
    bout = nc.dram_tensor("bout", [D], F32, kind="ExternalInput").ap()
    basew = nc.dram_tensor("basew", [D, D], F32, kind="ExternalInput").ap()
    baseb = nc.dram_tensor("baseb", [D], F32, kind="ExternalInput").ap()
    mu_in = nc.dram_tensor("mu", [E, D], F32, kind="ExternalInput").ap()
    sig_in = nc.dram_tensor("sigma", [E, D], F32, kind="ExternalInput").ap()
    wd_in = nc.dram_tensor("wd", [E, D, R], F32, kind="ExternalInput").ap()
    bd_in = nc.dram_tensor("bd", [E, R], F32, kind="ExternalInput").ap()
    wu_in = nc.dram_tensor("wu", [E, R, D], F32, kind="ExternalInput").ap()
    bu_in = nc.dram_tensor("bu", [E, D], F32, kind="ExternalInput").ap()
    wc_in = nc.dram_tensor("wc", [E, D, D], F32, kind="ExternalInput").ap()
    bc_in = nc.dram_tensor("bc", [E, D], F32, kind="ExternalInput").ap()
    y_out = nc.dram_tensor("y", [T, D], F32, kind="ExternalOutput").ap()

    qTd = nc.dram_tensor("qTd", [nM, 128, 4096], BF16).ap()
    x1d = nc.dram_tensor("x1d", [T, D], F32).ap()
    wscb = nc.dram_tensor("wscb", [D, D], BF16).ap()
    wccb = nc.dram_tensor("wccb", [D, D], BF16).ap()
    cc1i = nc.dram_tensor("cc1i", [129, D], F32).ap()
    cc1o = nc.dram_tensor("cc1o", [129, D], F32).ap()
    cc2i = nc.dram_tensor("cc2i", [4, 2048], F32).ap()
    cc2o = nc.dram_tensor("cc2o", [4, 2048], F32).ap()
    idxd = nc.dram_tensor("idxd", [1, 1], I32).ap()
    lossd = nc.dram_tensor("lossd", [1, 8], F32).ap()
    if debug_dump:
        kdbg = nc.dram_tensor("kdbg", [T, D], BF16).ap()
        vdbg = nc.dram_tensor("vdbg", [T, D], BF16).ap()
        adbg = nc.dram_tensor("adbg", [T, D], BF16).ap()

    P = Plan(nc)
    RG = [[0, 1], [2, 3], [4, 5], [6, 7]]

    with ExitStack() as st:
        def sb(name, shape, dt):
            return st.enter_context(nc.sbuf_tensor(name, shape, dt))

        for s in ("T", "V", "S", "G", "BAR", "CC", "WH", "WG", "LB",
                  "XO0", "XO1", "GD0", "GD1", "GU", "GC0", "GC1", "WT0", "WT1",
                  "XA0", "XA1", "QO0", "QO1", "QI0", "QI1", "WB",
                  "X2O0", "X2O1", "XC0", "XC1", "DBG"):
            P.sem(st, s)

        ident = sb("ident", [128, 128], BF16)
        identf = sb("identf", [8, 8], F32)
        ones_b = sb("ones_b", [128, 1], BF16)
        ones_f = sb("ones_f", [128, 1], F32)
        iota_p = sb("iota_p", [128, 1], I32)
        iota8x = sb("iota8x", [128, 8], I32)
        iota8r = sb("iota8r", [1, 8], I32)
        bwT = sb("bwT", [128, NCH * D], BF16)
        bcombT = sb("bcombT", [128, NCH], F32)
        bbT = sb("bbT", [128, NCH], F32)
        bcT = sb("bcT", [128, NCH], F32)
        buT = sb("buT", [128, NCH], F32)
        bdT = sb("bdT", [64, 1], F32)
        loss8 = sb("loss8", [8, 1], F32)
        lrow = sb("lrow", [1, 8], F32)
        lmin = sb("lmin", [1, 1], F32)
        lone = sb("lone", [1, 8], F32)
        liot = sb("liot", [1, 8], F32)
        idxf1 = sb("idxf1", [1, 1], F32)
        idxi1 = sb("idxi1", [1, 1], I32)
        idxb = sb("idxb", [128, 1], I32)
        idxbf = sb("idxbf", [128, 1], F32)
        idc = sb("idc", [128, 1], F32)
        idu = sb("idu", [64, 1], F32)
        gcf = sb("gcf", [128, 8], F32)
        gci = sb("gci", [128, 8], I32)
        guf = sb("guf", [64, 1], F32)
        gui = sb("gui", [64, 1], I32)
        kvz_r = sb("kvz_r", [128, 8 * 130], BF16)
        zrow = sb("zrow", [1, 1024], BF16)
        r2c = sb("r2c", [128, nS], F32)
        q2c = sb("q2c", [128, nS], F32)
        qc2 = sb("qc2", [128, nS], F32)

        evI1 = P.g(lambda g: g.memset(ident[:], 0.0))
        evI2 = P.g(lambda g: g.affine_select(
            out=ident[:], in_=ident[:], compare_op=ALU.not_equal, fill=1.0,
            base=0, pattern=[[-1, 128]], channel_multiplier=1), waits=[evI1])
        evI3 = P.g(lambda g: g.memset(identf[:], 0.0))
        evI4 = P.g(lambda g: g.affine_select(
            out=identf[:], in_=identf[:], compare_op=ALU.not_equal, fill=1.0,
            base=0, pattern=[[-1, 8]], channel_multiplier=1), waits=[evI3])
        evOB = P.g(lambda g: g.memset(ones_b[:], 1.0))
        evOF = P.g(lambda g: g.memset(ones_f[:], 1.0))
        evIp = P.g(lambda g: g.iota(iota_p[:], pattern=[[0, 1]], base=0,
                                    channel_multiplier=1))
        evI8 = P.g(lambda g: g.iota(iota8x[:], pattern=[[128, 8]], base=0,
                                    channel_multiplier=1))
        evIr = P.g(lambda g: g.iota(iota8r[:], pattern=[[1, 8]], base=0,
                                    channel_multiplier=0))
        evWb = P.dma("gpsimd", lambda g: g.dma_start(out=wscb, in_=basew))
        evBW = []
        for c in range(NCH):
            evBW.append(P.dma("scalar", lambda s, c=c: s.dma_start(
                out=bwT[:, c * D:(c + 1) * D],
                in_=wscb[:, c * 128:(c + 1) * 128], transpose=True),
                waits=[evWb], sem="WT%d" % (c % 2)))
        evBB = []
        for c in range(NCH):
            evBB.append(P.dma("sync", lambda e, c=c: e.dma_start(
                out=bbT[:, c:c + 1],
                in_=baseb[c * 128:(c + 1) * 128].unsqueeze(1)), sem="WH"))
        WHs = ("WH", P.cnt["WH"])
        WTs = [("WT0", P.cnt["WT0"]), ("WT1", P.cnt["WT1"])]

        with ExitStack() as stA:
            def sba(n, sh, dt):
                return stA.enter_context(nc.sbuf_tensor(n, sh, dt))
            def psa(n, sh, dt):
                return stA.enter_context(nc.psum_tensor(n, sh, dt))

            wq_sb = sba("wq_sb", [128, NCH * 3 * D], BF16)
            kvsb = sba("kvsb", [128, 1024], F32)
            zsb = sba("zsb", [1, 1024], F32)
            xbuf = [sba("xA0", [128, 4096], F32), sba("xA1", [128, 4096], F32)]
            xn1 = [sba("xn1_%d" % i, [128, D], BF16) for i in range(4)]
            st1 = [sba("st1_0", [128, 2], F32), sba("st1_1", [128, 2], F32)]
            bn6 = [sba("bn6_0", [128, 12], F32), sba("bn6_1", [128, 12], F32)]
            rr1 = [sba("rr1_0", [128, 2], F32), sba("rr1_1", [128, 2], F32)]
            xn1T = sba("xn1T", [128, NCH * 512], BF16)
            kbf = [sba("kbf0", [128, D], BF16), sba("kbf1", [128, D], BF16)]
            vbf = [sba("vbf0", [128, D], BF16), sba("vbf1", [128, D], BF16)]
            texp = [sba("texp0", [128, 512], F32), sba("texp1", [128, 512], F32)]
            tqxp = [sba("tqxp0", [128, 512], F32), sba("tqxp1", [128, 512], F32)]
            ktmp = [sba("ktmp0", [128, 512], F32), sba("ktmp1", [128, 512], F32)]
            qtmp = [sba("qtmp0", [128, 512], F32), sba("qtmp1", [128, 512], F32)]
            kt2 = [sba("kt2_0", [128, 512], F32), sba("kt2_1", [128, 512], F32)]
            qt2 = [sba("qt2_0", [128, 512], F32), sba("qt2_1", [128, 512], F32)]
            qTs = [sba("qTs0", [128, 4096], BF16), sba("qTs1", [128, 4096], BF16)]

            pkv = [psa("pkv0", [128, 512], F32), psa("pkv1", [128, 512], F32)]
            pz = [psa("pz0", [1, 512], F32), psa("pz1", [1, 512], F32)]
            pmm = [psa("pmm0", [128, 512], F32), psa("pmm1", [128, 512], F32)]
            ptrA = psa("ptrA", [128, 1024], BF16)
            pq = psa("pq", [128, 512], F32)

            def x1nT(c, lo, hi):
                return xn1T[:, c * 512 + lo:c * 512 + hi]

            P.dma("gpsimd", lambda g: g.dma_start(
                out=wq_sb.ap().rearrange("p (c j) -> p c j", c=NCH), in_=wqkv.rearrange("(c p) j -> p c j", p=128)))
            WGs = ("WG", P.cnt["WG"])

            ln_state = {}
            ev_xfree = [None, None]
            ev_kread = [None, None]
            ev_texp_read = [None, None]
            ev_tq_read = [None, None]
            ev_ptr_free = None
            ev_pq_free = None
            ev_pmm_free = [None, None]
            ev_qT_free = [None, None]
            ev_xn1_free = [None, None, None, None]
            ev_kv_done = []
            ev_z_done = []
            ev_qT_out = []
            ev_xn1T_read = None

            pool_banks = [pmm[0], pmm[1], pq]
            pool_free = [None, None, None]
            pool_i = [0]

            def take_bank():
                i = pool_i[0] % 3
                pool_i[0] += 1
                return i, pool_banks[i], pool_free[i] or []

            def prepA(m):
                pb = m % 2
                evX = P.dma("sync", lambda e, m=m, pb=pb: e.dma_start(
                    out=xbuf[pb].ap().rearrange("p (s d) -> p s d", s=4),
                    in_=x_in[m * 512:(m + 1) * 512, :].rearrange(
                        "(s p) d -> p s d", p=128)),
                    waits=[ev_xfree[pb]], sem="XA%d" % pb)
                applies = []
                for s4 in range(SUB):
                    sp = s4 % 2
                    ev_rr, rr_done = _ln(
                        P, bn6[sp], st1[sp], rr1[sp],
                        (lambda pb=pb, s4=s4: xbuf[pb][:, s4 * D:(s4 + 1) * D]),
                        ln_state, ("A", sp), extra_waits=[evX])
                    e7 = P.s(lambda s, pb=pb, s4=s4, sp=sp: s.activation(
                        xn1[s4][:], xbuf[pb][:, s4 * D:(s4 + 1) * D], AF.Identity,
                        scale=rr1[sp][:, 0:1], bias=rr1[sp][:, 1:2]),
                        waits=[ev_rr, ev_xn1_free[s4]])
                    rr_done(e7)
                    applies.append(e7)
                return evX, applies

            prepdata = [None] * nM
            prepdata[0] = prepA(0)

            for m in range(nM):
                pb = m % 2
                evX, applies = prepdata[m]
                ev_macro_evac = []
                for s4 in range(SUB):
                    evT = []
                    for c in range(NCH):
                        evT.append(P.t(lambda t, c=c, s4=s4: t.transpose(
                            ptrA[:, c * 128:(c + 1) * 128],
                            xn1[s4][:, c * 128:(c + 1) * 128], ident[:]),
                            waits=[applies[s4], evI2, ev_ptr_free]))
                    ev_xn1_free[s4] = evT[-1]
                    evE = P.s(lambda s, s4=s4: s.activation(
                        xn1T.ap().rearrange("p (c n) -> p c n", c=NCH)[:, :, s4 * 128:(s4 + 1) * 128],
                        ptrA.ap().rearrange("p (c n) -> p c n", c=NCH), AF.Copy),
                        waits=[evT[-1], ev_xn1T_read if s4 == 0 and m > 0 else None])
                    ev_ptr_free = evE
                    ev_macro_evac.append(evE)
                ev_xfree[pb] = applies[-1]
                if m + 1 < nM:
                    prepdata[m + 1] = prepA(m + 1)

                for s4 in range(SUB):
                    stile = m * SUB + s4
                    sp = s4 % 2
                    evE = ev_macro_evac[s4]
                    ev_k_ready = None
                    ev_v_ready = None
                    for ci in range(4):
                        bi, bank_t, bank_w = take_bank()
                        evMM = None
                        for c in range(NCH):
                            evMM = P.t(lambda t, c=c, ci=ci, bank_t=bank_t, s4=s4: t.matmul(
                                bank_t[:],
                                lhsT=x1nT(c, s4 * 128, (s4 + 1) * 128),
                                rhs=wq_sb[:, c * 3 * D + D + ci * 512:
                                          c * 3 * D + D + (ci + 1) * 512],
                                start=(c == 0), stop=(c == NCH - 1)),
                                waits=[evE, WGs] + bank_w)
                        if ci < 2:
                            eU = P.s(lambda s, bank_t=bank_t, ci=ci: s.activation(
                                texp[ci][:], bank_t[:], AF.Relu, scale=-1.0),
                                waits=[evMM, ev_texp_read[ci],
                                       ev_kread[sp] if ci == 0 else None])
                            eW = P.s(lambda s, ci=ci: s.activation(
                                ktmp[ci][:], texp[ci][:], AF.Exp, scale=-1.0),
                                waits=[eU])
                            eT = P.v(lambda v, bank_t=bank_t, ci=ci: v.tensor_add(
                                kt2[ci][:], bank_t[:], texp[ci][:]),
                                waits=[eU])
                            eE2 = P.v(lambda v, ci=ci, sp=sp: v.tensor_add(
                                kbf[sp][:, ci * 512:(ci + 1) * 512], kt2[ci][:],
                                ktmp[ci][:]),
                                waits=[eW, eT])
                            ev_texp_read[ci] = eE2
                            pool_free[bi] = [eU, eT]
                            if ci == 1:
                                ev_k_ready = eE2
                        else:
                            eE2 = P.s(lambda s, bank_t=bank_t, ci=ci, sp=sp: s.copy(
                                vbf[sp][:, (ci - 2) * 512:(ci - 1) * 512],
                                bank_t[:]),
                                waits=[evMM, ev_kread[sp] if ci == 2 else None])
                            pool_free[bi] = [eE2]
                            if ci == 3:
                                ev_v_ready = eE2
                    evKV = []
                    for p8 in range(8):
                        half = p8 // 4
                        evKV.append(P.t(lambda t, p8=p8, half=half, sp=sp, stile=stile: t.matmul(
                            pkv[half][:, (p8 % 4) * 128:(p8 % 4 + 1) * 128],
                            lhsT=kbf[sp][:, p8 * 128:(p8 + 1) * 128],
                            rhs=vbf[sp][:, p8 * 128:(p8 + 1) * 128],
                            start=(stile == 0 and p8 % 4 == 0),
                            stop=(stile == nS - 1 and p8 % 4 == 3)),
                            waits=[ev_k_ready, ev_v_ready]))
                    evZ = []
                    for half in range(2):
                        evZ.append(P.t(lambda t, half=half, sp=sp, stile=stile: t.matmul(
                            pz[half][:],
                            lhsT=ones_b[:],
                            rhs=kbf[sp][:, half * 512:(half + 1) * 512],
                            start=(stile == 0), stop=(stile == nS - 1)),
                            waits=[ev_k_ready, evOB]))
                    if debug_dump:
                        P.dma("gpsimd", lambda g, sp=sp, stile=stile: g.dma_start(
                            out=kdbg[stile * 128:(stile + 1) * 128, :],
                            in_=kbf[sp][:]), waits=[ev_k_ready], sem="DBG")
                        P.dma("gpsimd", lambda g, sp=sp, stile=stile: g.dma_start(
                            out=vdbg[stile * 128:(stile + 1) * 128, :],
                            in_=vbf[sp][:]), waits=[ev_v_ready], sem="DBG")
                    ev_kread[sp] = evZ[-1] if not debug_dump else ("DBG", P.cnt["DBG"])
                    if stile == nS - 1:
                        ev_kv_done = evKV
                        ev_z_done = evZ

                ev_q_evac = []
                for j in range(NCH):
                    bi, bank_t, bank_w = take_bank()
                    evQ = None
                    for c in range(NCH):
                        evQ = P.t(lambda t, c=c, j=j, bank_t=bank_t: t.matmul(
                            bank_t[:],
                            lhsT=wq_sb[:, c * 3 * D + j * 128:
                                       c * 3 * D + (j + 1) * 128],
                            rhs=x1nT(c, 0, 512),
                            start=(c == 0), stop=(c == NCH - 1)),
                            waits=[ev_macro_evac[-1], WGs] + bank_w)
                    sp = j % 2
                    eU = P.s(lambda s, sp=sp, bank_t=bank_t: s.activation(
                        tqxp[sp][:], bank_t[:], AF.Relu, scale=-1.0),
                        waits=[evQ, ev_tq_read[sp]])
                    eW = P.s(lambda s, sp=sp: s.activation(
                        qtmp[sp][:], tqxp[sp][:], AF.Exp, scale=-1.0),
                        waits=[eU])
                    eT = P.v(lambda v, sp=sp, bank_t=bank_t: v.tensor_add(
                        qt2[sp][:], bank_t[:], tqxp[sp][:]),
                        waits=[eU])
                    eQ2 = P.v(lambda v, j=j, sp=sp, pb=pb: v.tensor_add(
                        qTs[pb][:, j * 512:(j + 1) * 512], qt2[sp][:],
                        qtmp[sp][:]),
                        waits=[eW, eT, ev_qT_free[pb] if j == 0 else None])
                    ev_tq_read[sp] = eQ2
                    ev_q_evac.append(eQ2)
                    pool_free[bi] = [eU, eT]
                ev_xn1T_read = evQ
                evQO = P.dma("gpsimd", lambda g, m=m, pb=pb: g.dma_start(
                    out=qTd[m], in_=qTs[pb][:]),
                    waits=ev_q_evac, sem="QO%d" % pb)
                ev_qT_free[pb] = evQO
                ev_qT_out.append(evQO)

            eK0 = P.v(lambda v: v.tensor_copy(kvsb[:, 0:512], pkv[0][:]),
                      waits=ev_kv_done)
            eK1 = P.v(lambda v: v.tensor_copy(kvsb[:, 512:1024], pkv[1][:]),
                      waits=[eK0])
            eZ0 = P.v(lambda v: v.tensor_copy(zsb[:, 0:512], pz[0][:]),
                      waits=ev_z_done + [eK1])
            eZ1 = P.v(lambda v: v.tensor_copy(zsb[:, 512:1024], pz[1][:]),
                      waits=[eZ0])
            evD1 = P.dma("gpsimd", lambda g: g.dma_start(
                out=cc1i[0:128, :], in_=kvsb[:]), waits=[eK1])
            evD2 = P.dma("gpsimd", lambda g: g.dma_start(
                out=cc1i[128:129, :], in_=zsb[:]), waits=[eZ1])
            evCC1 = P.op("gpsimd", lambda g: g.collective_compute(
                "AllReduce", ALU.add, replica_groups=RG,
                ins=[cc1i.opt()], outs=[cc1o.opt()]),
                waits=[("WG", P.cnt["WG"])], sem="CC", inc=1)

            evKz = P.g(lambda g: g.memset(kvz_r[:], 0.0))
            kvv = lambda ap: ap.rearrange("p (g c) -> p g c", c=130)
            cco = lambda ap: ap.rearrange("p (g c) -> p g c", c=128)
            P.dma("gpsimd", lambda g: g.dma_start(
                out=kvv(kvz_r[0:64, :])[:, :, 0:64],
                in_=cco(cc1o[0:64, :])[:, :, 0:64]),
                waits=[evCC1, evKz])
            P.dma("gpsimd", lambda g: g.dma_start(
                out=kvv(kvz_r[64:128, :])[:, :, 65:129],
                in_=cco(cc1o[64:128, :])[:, :, 64:128]),
                waits=[evCC1, evKz])
            evZr = P.dma("gpsimd", lambda g: g.dma_start(
                out=zrow[:], in_=cc1o[128:129, :]), waits=[evCC1])
            evZt = []
            for c in range(NCH):
                evZt.append(P.t(lambda t, c=c: t.transpose(
                    ptrA[:, 2 * c:2 * c + 1], zrow[0:1, c * 128:(c + 1) * 128],
                    ident[0:1, 0:1]), waits=[evZr, evI2, ev_ptr_free]))
            zc1 = P.s(lambda s: s.activation(
                kvv(kvz_r[0:64, :])[:, :, 64:65],
                ptrA[0:64, 0:16:2].unsqueeze(2),
                AF.Copy), waits=[evZt[-1], evKz])
            zc2 = P.s(lambda s: s.activation(
                kvv(kvz_r[64:128, :])[:, :, 129:130],
                ptrA[64:128, 0:16:2].unsqueeze(2),
                AF.Copy), waits=[evZt[-1], evKz])
            ev_rhs = ("WG", P.cnt["WG"])
            ev_rhs2 = [zc1, zc2]

            P.barrier()

        if phases == "A":
            P.op("sync", None, waits=[("CC", 1)])
        if "B" in phases and True:
         with ExitStack() as stB:
            def sbb(n, sh, dt):
                return stB.enter_context(nc.sbuf_tensor(n, sh, dt))
            def psb(n, sh, dt):
                return stB.enter_context(nc.psum_tensor(n, sh, dt))

            wo_sb = sbb("wo_sb", [128, NCH * D], BF16)
            ones_r = sbb("ones_r", [1, 128], BF16)
            bo_row = sbb("bo_row", [1, D], BF16)
            ms_sb = sbb("ms_sb", [3, 2048], F32)
            qs_sb = sbb("qs_sb", [2, 1], F32)
            zr2 = sbb("zr2", [1, 2048], F32)
            qrp = sbb("qrp", [128, 2], F32)
            xbufB = [sbb("xB0", [128, 4096], F32), sbb("xB1", [128, 4096], F32)]
            qTb = [sbb("qTb0", [128, 4096], BF16), sbb("qTb1", [128, 4096], BF16)]
            attn = [sbb("attn0", [128, D], BF16), sbb("attn1", [128, D], BF16)]
            attnT = [sbb("attnT0", [128, D], BF16), sbb("attnT1", [128, D], BF16)]
            rden = [sbb("rden0", [128, 16], F32), sbb("rden1", [128, 16], F32)]
            aout = [sbb("aout%d" % i, [128, D], F32) for i in range(8)]
            aln = [sbb("aln0", [128, D], F32), sbb("aln1", [128, D], F32)]
            x1b_ = [sbb("x1b0", [128, D], F32), sbb("x1b1", [128, D], F32)]
            st2 = [sbb("st2_0", [128, 2], F32), sbb("st2_1", [128, 2], F32)]
            bn6b = [sbb("bn6b_0", [128, 12], F32), sbb("bn6b_1", [128, 12], F32)]
            rr2 = [sbb("rr2_0", [128, 2], F32), sbb("rr2_1", [128, 2], F32)]
            xcat = [sbb("xcat0", [128, 2048], BF16), sbb("xcat1", [128, 2048], BF16)]
            rv3f = [sbb("rv3f_0", [128, 3], F32), sbb("rv3f_1", [128, 3], F32)]
            rv3 = [sbb("rv3_0", [128, 3], BF16), sbb("rv3_1", [128, 3], BF16)]

            png = [psb("png0", [128, 390], F32), psb("png1", [128, 390], F32)]
            ptrB = psb("ptrB", [128, 1024], BF16)
            pwo = psb("pwo", [128, 512], F32)
            pm = [psb("pm0", [3, 512], F32), psb("pm1", [3, 512], F32),
                  psb("pm2", [3, 512], F32), psb("pm3", [3, 512], F32)]
            pqs_view = None

            P.dma("gpsimd", lambda g: g.dma_start(
                out=wo_sb.ap().rearrange("p (c j) -> p c j", c=NCH), in_=wout.rearrange("(c p) j -> p c j", p=128)))
            WGb = ("WG", P.cnt["WG"])
            evOR = P.g(lambda g: g.memset(ones_r[:], 1.0))
            evBo = P.dma("gpsimd", lambda g: g.dma_start(
                out=bo_row[:], in_=bout.unsqueeze(0)), sem="WB")
            WHb = ("WB", P.cnt["WB"])
            ptrBf32 = ptrB.ap().bitcast(F32)
            GROUPS = [(0, 3, 0), (3, 6, 6), (6, 8, 12)]
            ln_state = {}
            ev_png_free = [None, None]
            ev_x1_free = [None, None]
            ev_attn_free = [None, None]
            ev_rden_free = [None, None]
            ev_aout_free = [None] * 8
            ev_aln_free = [None, None]
            ev_xfreeB = [None, None]
            ev_qTb_free = [None, None]
            ev_ptrB_free = [zc2]
            ev_attnT_free = [None, None]
            ev_pwo_free = [None]
            ev_xc_free = [None, None]
            ev_rv_free = [None, None]
            ev_pm_last = [None] * 4
            ev_x1_stores = []
            ev_cache_w = []
            xo_marks = {}

            evXBm = [None] * nM
            evQBm = [None] * nM

            def load_x(m):
                pb = m % 2
                evXBm[m] = P.dma("sync", lambda e, m=m, pb=pb: e.dma_start(
                    out=xbufB[pb].ap().rearrange("p (s d) -> p s d", s=4),
                    in_=x_in[m * 512:(m + 1) * 512, :].rearrange(
                        "(s p) d -> p s d", p=128)),
                    waits=[ev_xfreeB[pb]], sem="XA%d" % pb)

            def load_q(m):
                pb = m % 2
                evQBm[m] = P.dma("scalar", lambda e, m=m, pb=pb: e.dma_start(
                    out=qTb[pb][:], in_=qTd[m]),
                    waits=[ev_qTb_free[pb],
                           ("QO0", P.cnt["QO0"]), ("QO1", P.cnt["QO1"])],
                    sem="QI%d" % pb)

            def numdiv(stile):
                m = stile // SUB
                s4 = stile % SUB
                pb = m % 2
                sp = stile % 2
                evQ = evQBm[m]
                ev_div = []
                last_q = None
                for (p0, p1, rc) in GROUPS:
                    gi = rc // 6
                    bank = gi % 2
                    npair = p1 - p0
                    evN = []
                    for p8 in range(p0, p1):
                        off = (p8 - p0) * 130
                        evN.append(P.t(lambda t, p8=p8, off=off, bank=bank, pb=pb, s4=s4, p0=p0, p1=p1: t.matmul(
                            png[bank][:, off:off + 130],
                            lhsT=qTb[pb][:, p8 * 512 + s4 * 128:
                                         p8 * 512 + (s4 + 1) * 128],
                            rhs=kvz_r[:, p8 * 130:(p8 + 1) * 130],
                            start=(p8 == p0), stop=(p8 == p1 - 1)),
                            waits=[evQ, ev_rhs] + ev_rhs2 + (ev_png_free[bank] or [])))
                    last_q = evN[-1]
                    eD1 = P.v(lambda v, rc=rc, bank=bank, sp=sp, npair=npair: v.tensor_scalar(
                        rden[sp][:, rc:rc + 2 * npair],
                        png[bank][:, 64:npair * 130:65],
                        1e-6, None, ALU.add),
                        waits=evN + (ev_rden_free[sp] or []))
                    eD2 = P.v(lambda v, rc=rc, sp=sp, npair=npair: v.reciprocal(
                        rden[sp][:, rc:rc + 2 * npair],
                        rden[sp][:, rc:rc + 2 * npair]), waits=[eD1])

                    def dvv(e, bank=bank, p0=p0, rc=rc, npair=npair, sp=sp):
                        return e.tensor_tensor(
                            attn[sp].ap().rearrange("p (j c) -> p j c", c=64)[:, 2 * p0:2 * p0 + 2 * npair, :],
                            png[bank].ap().rearrange("p (j c) -> p j c", c=65)[:, 0:2 * npair, 0:64],
                            rden[sp][:, rc:rc + 2 * npair].unsqueeze(2).broadcast_to([128, 2 * npair, 64]),
                            ALU.mult)
                    evDv = P.v(dvv, waits=[eD2, ev_attn_free[sp]])
                    ev_div.append(evDv)
                    ev_png_free[bank] = [evDv]
                ev_rden_free[sp] = [ev_div[-1]]
                return ev_div, last_q

            load_q(0)
            load_x(0)
            div_evs = [None] * (nS + 1)
            div_evs[0] = numdiv(0)

            def headB(stile):
                m = stile // SUB
                s4 = stile % SUB
                pb = m % 2
                sp = stile % 2
                ev_div, last_q = div_evs[stile]
                if debug_dump:
                    P.dma("gpsimd", lambda g, sp=sp, stile=stile: g.dma_start(
                        out=adbg[stile * 128:(stile + 1) * 128, :],
                        in_=attn[sp][:]), waits=ev_div, sem="DBG")
                    ev_div = ev_div + [("DBG", P.cnt["DBG"])]
                evT = []
                for c in range(NCH):
                    evT.append(P.t(lambda t, c=c, sp=sp: t.transpose(
                        ptrB[:, c * 128:(c + 1) * 128],
                        attn[sp][:, c * 128:(c + 1) * 128], ident[:]),
                        waits=ev_div + [ev_ptrB_free[0], evI2]))
                ev_attn_free[sp] = evT[-1]
                eTe = P.s(lambda s, sp=sp: s.activation(
                    attnT[sp][:], ptrB[:], AF.Copy),
                    waits=[evT[-1], ev_attnT_free[sp]])
                ev_ptrB_free[0] = eTe
                if stile + 1 < nS:
                    div_evs[stile + 1] = numdiv(stile + 1)
                    if (stile + 1) % SUB == SUB - 1:
                        ev_qTb_free[((stile + 1) // SUB) % 2] = div_evs[stile + 1][1]
                ao = stile % 8
                ev_wout = None
                for half in range(2):
                    free_w = ev_pwo_free if half == 0 else ev_ptrB_free
                    dst = pwo[:] if half == 0 else ptrBf32[:, 0:512]
                    evWc = None
                    for c in range(NCH):
                        evWc = P.t(lambda t, c=c, half=half, sp=sp, dst=dst:
                                   t.matmul(
                            dst,
                            lhsT=attnT[sp][:, c * 128:(c + 1) * 128],
                            rhs=wo_sb[:, c * D + half * 512:
                                      c * D + (half + 1) * 512],
                            start=(c == 0), stop=False),
                            waits=[eTe, WGb, free_w[0]])
                    evW = P.t(lambda t, half=half, dst=dst: t.matmul(
                        dst,
                        lhsT=ones_r[0:1, :],
                        rhs=bo_row[0:1, half * 512:(half + 1) * 512],
                        start=False, stop=True),
                        waits=[WHb, evOR])
                    eB = P.s(lambda s, half=half, ao=ao, dst=dst: s.copy(
                        aout[ao][:, half * 512:(half + 1) * 512], dst),
                        waits=[evW,
                               ev_aout_free[ao] if half == 0 else None])
                    free_w[0] = eB
                    ev_wout = eB
                    if half == 1:
                        ev_attnT_free[sp] = evWc
                wout_evs[stile] = ev_wout

            def tailB(stile):
                m = stile // SUB
                s4 = stile % SUB
                pb = m % 2
                sp = stile % 2
                ao = stile % 8
                ev_wout = wout_evs[stile]
                ev_rr2, rr2_done = _ln(
                    P, bn6b[sp], st2[sp], rr2[sp],
                    (lambda ao=ao: aout[ao][:]),
                    ln_state, ("B", sp), extra_waits=[ev_wout])
                b7 = P.s(lambda s, sp=sp, ao=ao: s.activation(
                    aln[sp][:], aout[ao][:], AF.Identity,
                    scale=rr2[sp][:, 0:1], bias=rr2[sp][:, 1:2]),
                    waits=[ev_rr2, ev_aln_free[sp]])
                rr2_done(b7)
                b8 = P.g(lambda g, sp=sp, s4=s4, pb=pb, m=m: g.tensor_tensor(
                    x1b_[sp][:], aln[sp][:],
                    xbufB[pb][:, s4 * D:(s4 + 1) * D], ALU.add),
                    waits=[b7, evXBm[m], ev_x1_free[sp]])
                ev_aout_free[ao] = b7
                ev_aln_free[sp] = b8
                if s4 == SUB - 1:
                    ev_xfreeB[pb] = b8
                evXO = P.dma("gpsimd", lambda g, sp=sp, stile=stile: g.dma_start(
                    out=x1d[stile * 128:(stile + 1) * 128, :],
                    in_=x1b_[sp][:]),
                    waits=[b8], sem="XO%d" % sp)
                ev_x1_free[sp] = evXO
                ev_x1_stores.append(evXO)
                xo_marks[stile] = [("XO0", P.cnt["XO0"]),
                                   ("XO1", P.cnt["XO1"])]

            wout_evs = [None] * nS
            for m in range(nM):
                if m + 1 < nM:
                    load_q(m + 1)
                for s4 in range(SUB):
                    headB(m * SUB + s4)
                    if m >= 1:
                        tailB((m - 1) * SUB + s4)
                if m + 1 < nM:
                    load_x(m + 1)
            for s4 in range(SUB):
                tailB((nM - 1) * SUB + s4)

            pm_gate = ([ev_pwo_free[0]] + (ev_png_free[0] or [])
                       + (ev_png_free[1] or []))
            for sp in range(2):
                ln_state[("B2", sp)] = dict(ln_state.get(("B", sp), {}))
            evX2m = [None] * nM
            xfree2 = [list(ev_xfreeB[0:1]), list(ev_xfreeB[1:2])]

            def load_x2(m):
                pb = m % 2
                evX2m[m] = P.dma("sync", lambda e, m=m, pb=pb: e.dma_start(
                    out=xbufB[pb].ap().rearrange("p (s d) -> p s d", s=4),
                    in_=x1d[m * 512:(m + 1) * 512, :].rearrange(
                        "(s p) d -> p s d", p=128)),
                    waits=xfree2[pb] + xo_marks[m * SUB + SUB - 1],
                    sem="XA%d" % pb)

            load_x2(0)
            if nM > 1:
                load_x2(1)
            for m in range(nM):
                pb = m % 2
                newfree = []
                for s4 in range(SUB):
                    stile = m * SUB + s4
                    sp = stile % 2
                    src = (lambda pb=pb, s4=s4:
                           xbufB[pb][:, s4 * D:(s4 + 1) * D])
                    ev_rr2b, rr2b_done = _ln(
                        P, bn6b[sp], st2[sp], rr2[sp], src,
                        ln_state, ("B2", sp), extra_waits=[evX2m[m]])
                    c7 = P.s(lambda s, sp=sp, stile=stile: s.copy(
                        r2c[:, stile:stile + 1], rr2[sp][:, 0:1]),
                        waits=[ev_rr2b])
                    c8 = P.s(lambda s, sp=sp, stile=stile: s.copy(
                        q2c[:, stile:stile + 1], rr2[sp][:, 1:2]),
                        waits=[ev_rr2b])
                    c10 = P.s(lambda s, sp=sp, stile=stile: s.activation(
                        qc2[:, stile:stile + 1], rr2[sp][:, 1:2], AF.Square),
                        waits=[ev_rr2b])
                    ev_cache_w.extend([c7, c8, c10])
                    d1 = P.v(lambda v, sp=sp: v.tensor_copy(
                        rv3f[sp][:, 0:1], rr2[sp][:, 0:1]),
                        waits=[ev_rr2b, ev_rv_free[sp]])
                    d2 = P.v(lambda v, sp=sp: v.tensor_mul(
                        rv3f[sp][:, 1:2], rr2[sp][:, 0:1], rr2[sp][:, 1:2]),
                        waits=[ev_rr2b, ev_rv_free[sp]])
                    d3 = P.v(lambda v, sp=sp: v.tensor_mul(
                        rv3f[sp][:, 2:3], rr2[sp][:, 0:1], rr2[sp][:, 0:1]),
                        waits=[ev_rr2b, ev_rv_free[sp]])
                    rr2b_done(d3)
                    d4 = P.v(lambda v, sp=sp: v.tensor_copy(
                        rv3[sp][:], rv3f[sp][:]),
                        waits=[d1, d2, d3, ev_rv_free[sp]])
                    d5 = P.v(lambda v, sp=sp, src=src: v.tensor_copy(
                        xcat[sp][:, 0:1024], src()),
                        waits=[evX2m[m], ev_xc_free[sp]])
                    d6 = P.s(lambda s, sp=sp, src=src: s.activation(
                        xcat[sp][:, 1024:2048], src(), AF.Square),
                        waits=[evX2m[m], ev_xc_free[sp]])
                    evMs = []
                    for ci in range(4):
                        evMs.append(P.t(lambda t, ci=ci, sp=sp, stile=stile:
                                        t.matmul(
                            pm[ci][:],
                            lhsT=rv3[sp][:],
                            rhs=xcat[sp][:, ci * 512:(ci + 1) * 512],
                            start=(stile == 0), stop=(stile == nS - 1)),
                            waits=[d4, d5, d6]
                            + (pm_gate if stile == 0 else [])))
                    ev_xc_free[sp] = evMs[-1]
                    ev_rv_free[sp] = evMs[-1]
                    if stile == nS - 1:
                        ev_pm_last = evMs
                    newfree += [ev_rr2b, d5, d6]
                xfree2[pb] = newfree
                if m + 2 < nM:
                    load_x2(m + 2)

            f1 = P.v(lambda v: v.tensor_reduce(qrp[:, 0:1], q2c[:], AX.X,
                                               ALU.add), waits=ev_cache_w)
            f2 = P.v(lambda v: v.tensor_reduce(qrp[:, 1:2], qc2[:], AX.X,
                                               ALU.add), waits=ev_cache_w)
            f3 = P.t(lambda t: t.matmul(ptrB.ap().bitcast(F32)[0:2, 0:1],
                                        lhsT=qrp[:], rhs=ones_f[:],
                                        start=True, stop=True),
                     waits=[f1, f2, evOF, ev_ptrB_free[0]] + ev_pm_last)
            f4 = P.v(lambda v: v.tensor_copy(
                qs_sb[:], ptrB.ap().bitcast(F32)[0:2, 0:1]), waits=[f3])
            evMe = []
            for ci in range(4):
                evMe.append(P.v(lambda v, ci=ci: v.tensor_copy(
                    ms_sb.ap().rearrange("p (c n) -> p c n", c=4)[:, ci, :],
                    pm[ci][:]),
                    waits=ev_pm_last + [f4]))
            gz0 = P.g(lambda g: g.memset(zr2[:], 0.0))
            P.dma("gpsimd", lambda g: g.dma_start(
                out=cc2i[3:4, :], in_=zr2[:]), waits=[gz0])
            g1 = P.dma("gpsimd", lambda g: g.dma_start(
                out=cc2i[0:3, :], in_=ms_sb[:]), waits=evMe)
            g2 = P.dma("gpsimd", lambda g: g.dma_start(
                out=cc2i[3:4, 0:2].rearrange("a b -> b a"), in_=qs_sb[:],
                allow_slow_non_contiguous=True), waits=[f4])
            if bstop is not None:
                evCC2 = None
            evCC2 = evCC2 if bstop is not None else P.op("gpsimd", lambda g: g.collective_compute(
                "AllReduce", ALU.add, replica_groups=RG,
                ins=[cc2i.opt()], outs=[cc2o.opt()]),
                waits=[("WG", P.cnt["WG"])], sem="CC", inc=1)

            P.barrier()

        if phases == "AB":
            P.op("sync", None, waits=[("CC", 1 if bstop else 2)])
        if "C" in phases and True:
         with ExitStack() as stC:
            def sbc(n, sh, dt):
                return stC.enter_context(nc.sbuf_tensor(n, sh, dt))
            def psc(n, sh, dt):
                return stC.enter_context(nc.psum_tensor(n, sh, dt))

            wcT = sbc("wcT", [128, NCH * D], BF16)
            wd_sb = sbc("wd_sb", [128, NCH * R], BF16)
            wu_sb = sbc("wu_sb", [64, D], BF16)
            mu_sb = sbc("mu_sb", [8, D], F32)
            sg_sb = sbc("sg_sb", [8, D], F32)
            iv_sb = sbc("iv_sb", [8, D], F32)
            muiv = sbc("muiv", [8, D], F32)
            c3_sb = sbc("c3_sb", [8, D], F32)
            lw1 = sbc("lw1", [8, D], F32)
            lw2 = sbc("lw2", [8, D], F32)
            m2ab = sbc("m2ab", [8, D], F32)
            q1b = sbc("q1b", [8, 1], F32)
            q2b = sbc("q2b", [8, 1], F32)
            wstg = sbc("wstg", [128, D], F32)
            wstg2 = sbc("wstg2", [128, D], F32)
            wustg = sbc("wustg", [64, D], F32)
            wdstg = sbc("wdstg", [128, NCH * R], F32)
            bu_b = sbc("bu_b", [8, D], BF16)
            bc_b = sbc("bc_b", [8, D], BF16)
            bd_b = sbc("bd_b", [8, R], BF16)
            ohc = sbc("ohc", [8, 1], BF16)

            P.dma("gpsimd", lambda g: g.dma_start(out=bu_b[:], in_=bu_in))
            P.dma("gpsimd", lambda g: g.dma_start(out=bc_b[:], in_=bc_in))
            P.dma("gpsimd", lambda g: g.dma_start(out=bd_b[:], in_=bd_in))
            WGc = ("WG", P.cnt["WG"])
            P.dma("sync", lambda e: e.dma_start(out=mu_sb[:], in_=mu_in),
                  sem="LB")
            P.dma("sync", lambda e: e.dma_start(out=sg_sb[:], in_=sig_in),
                  sem="LB")
            h1 = P.v(lambda v: v.tensor_mul(iv_sb[:], sg_sb[:], sg_sb[:]),
                         waits=[("LB", P.cnt["LB"])])
            h2 = P.v(lambda v: v.reciprocal(iv_sb[:], iv_sb[:]), waits=[h1])
            h3 = P.v(lambda v: v.tensor_mul(muiv[:], mu_sb[:], iv_sb[:]),
                     waits=[h2, WHs])
            h4 = P.v(lambda v: v.tensor_mul(c3_sb[:], mu_sb[:], muiv[:]),
                     waits=[h3])
            for fn in (lambda e: e.dma_start(out=lw1[:], in_=cc2o[0:1, 0:1024].partition_broadcast(8)),
                       lambda e: e.dma_start(out=lw2[:], in_=cc2o[1:2, 0:1024].partition_broadcast(8)),
                       lambda e: e.dma_start(out=m2ab[:], in_=cc2o[2:3, 1024:2048].partition_broadcast(8)),
                       lambda e: e.dma_start(out=q1b[:], in_=cc2o[3:4, 0:1].partition_broadcast(8)),
                       lambda e: e.dma_start(out=q2b[:], in_=cc2o[3:4, 1:2].partition_broadcast(8))):
                P.dma("sync", fn, waits=[evCC2], sem="LB")
            evLB = ("LB", P.cnt["LB"])
            inv_t = 1.0 / float(t_total)
            k1 = P.v(lambda v: v.tensor_scalar(
                lw1[:], lw1[:], q1b[:], inv_t, ALU.add, ALU.mult), waits=[evLB])
            k2a = P.v(lambda v: v.tensor_scalar(
                lw2[:], lw2[:], 2.0, None, ALU.mult), waits=[evLB])
            k2 = P.v(lambda v: v.tensor_add(lw2[:], lw2[:], m2ab[:]),
                     waits=[k2a])
            k3 = P.v(lambda v: v.tensor_scalar(
                lw2[:], lw2[:], q2b[:], inv_t, ALU.add, ALU.mult), waits=[k2])
            k4 = P.v(lambda v: v.tensor_mul(lw2[:], lw2[:], iv_sb[:]),
                     waits=[k3, h2])
            k5 = P.v(lambda v: v.tensor_mul(lw1[:], lw1[:], muiv[:]),
                     waits=[k1, h3])
            k6a = P.v(lambda v: v.tensor_scalar(
                lw1[:], lw1[:], -2.0, None, ALU.mult), waits=[k5])
            k6 = P.v(lambda v: v.tensor_add(lw1[:], lw1[:], lw2[:]),
                     waits=[k4, k6a])
            k7 = P.v(lambda v: v.tensor_add(lw1[:], lw1[:], c3_sb[:]),
                     waits=[k6, h4])
            k8 = P.v(lambda v: v.tensor_reduce(loss8[:], lw1[:], AX.X, ALU.add),
                     waits=[k7])
            k9 = P.dma("gpsimd", lambda g: g.dma_start(
                out=lossd.rearrange("a b -> b a"), in_=loss8[:],
                allow_slow_non_contiguous=True), waits=[k8])
            k10 = P.dma("gpsimd", lambda g: g.dma_start(out=lrow[:], in_=lossd),
                        waits=[k9])
            k11 = P.v(lambda v: v.tensor_reduce(lmin[:], lrow[:], AX.X, ALU.min),
                      waits=[k10])
            k12 = P.v(lambda v: v.tensor_single_scalar(
                lone[:], lrow[:], lmin[:], ALU.is_equal), waits=[k11])
            k13 = P.v(lambda v: v.tensor_copy(liot[:], iota8r[:]), waits=[evIr])
            k14 = P.v(lambda v: v.tensor_mul(liot[:], liot[:], lone[:]),
                      waits=[k13, k12])
            k15 = P.v(lambda v: v.tensor_reduce(idxf1[:], liot[:], AX.X, ALU.add),
                      waits=[k14])
            k16 = P.v(lambda v: v.tensor_copy(idxi1[:], idxf1[:]), waits=[k15])
            k17 = P.dma("gpsimd", lambda g: g.dma_start(out=idxd, in_=idxi1[:]),
                        waits=[k16])
            k18 = P.dma("gpsimd", lambda g: g.dma_start(
                out=idxb[:], in_=idxd.partition_broadcast(128)), waits=[k17])
            k19 = P.v(lambda v: v.tensor_copy(idxbf[:], idxb[:]), waits=[k18])
            k19b = P.v(lambda v: v.tensor_scalar(
                idc[:], idxbf[:], 1024.0, None, ALU.mult), waits=[k19])
            k19c = P.v(lambda v: v.tensor_scalar(
                idu[:], idxbf[0:64, :], 64.0, None, ALU.mult), waits=[k19])
            k20 = P.v(lambda v: v.tensor_copy(gcf[:], iota8x[:]), waits=[evI8])
            k21 = P.v(lambda v: v.tensor_scalar(
                gcf[:], gcf[:], idc[:], None, ALU.add), waits=[k19b, k20])
            k22 = P.v(lambda v: v.tensor_copy(gci[:], gcf[:]), waits=[k21])
            k23 = P.v(lambda v: v.tensor_copy(guf[:], iota_p[0:64, :]),
                      waits=[evIp])
            k24 = P.v(lambda v: v.tensor_scalar(
                guf[:], guf[:], idu[:], None, ALU.add), waits=[k19c, k23])
            k25 = P.v(lambda v: v.tensor_copy(gui[:], guf[:]), waits=[k24])

            ev_stg_free = [k22, k22]
            for c in range(NCH):
                stg = wstg if c % 2 == 0 else wstg2
                gsem = "GC%d" % (c % 2)
                gA = P.dma("gpsimd", lambda g, c=c, stg=stg: g.indirect_dma_start(
                    out=stg[:], out_offset=None,
                    in_=wc_in.rearrange("e o d -> (e o) d"),
                    in_offset=IndirectOffsetOnAxis(gci[:, c:c + 1], 0)),
                    waits=[k22, ev_stg_free[c % 2]], sem=gsem)
                gB = P.dma("gpsimd", lambda g, c=c, stg=stg: g.dma_start(
                    out=wccb[c * 128:(c + 1) * 128, :], in_=stg[:]), waits=[gA],
                    sem=gsem)
                ev_stg_free[c % 2] = gB
            gD = []
            for c in range(NCH):
                gD.append(P.dma("gpsimd", lambda g, c=c: g.indirect_dma_start(
                    out=wdstg[:, c * R:(c + 1) * R], out_offset=None,
                    in_=wd_in.rearrange("e d r -> (e d) r"),
                    in_offset=IndirectOffsetOnAxis(gci[:, c:c + 1], 0)),
                    waits=[k22], sem="GD%d" % (c % 2)))
            gD2 = P.dma("gpsimd", lambda g: g.dma_start(out=wd_sb[:], in_=wdstg[:]),
                        waits=[("GD0", P.cnt["GD0"]), ("GD1", P.cnt["GD1"])],
                        sem="GD0")
            gU = P.dma("gpsimd", lambda g: g.indirect_dma_start(
                out=wustg[:], out_offset=None,
                in_=wu_in.rearrange("e r d -> (e r) d"),
                in_offset=IndirectOffsetOnAxis(gui[:, 0:1], 0)), waits=[k25],
                sem="GU")
            gU2 = P.dma("gpsimd", lambda g: g.dma_start(out=wu_sb[:], in_=wustg[:]),
                        waits=[gU], sem="GU")
            bias_evs = {}

            def emit_bias_sel():
                e_oh = P.t(lambda t: t.transpose(
                    ptrC2[1].ap().bitcast(F32)[0:8, 32:33], lone[:],
                    identf[0:1, 0:1]),
                    waits=[k12, evI4, ev_ptrC_free[1]])
                e_ohc = P.s(lambda s: s.activation(
                    ohc[:], ptrC2[1].ap().bitcast(F32)[0:8, 32:33], AF.Copy),
                    waits=[e_oh])
                evm = []
                evm.append(P.t(lambda t: t.matmul(
                    pbias[:, 0:1], lhsT=bu_b[:, 0:128], rhs=ohc[:],
                    start=True, stop=False), waits=[e_ohc, WGc]))
                evm.append(P.t(lambda t: t.matmul(
                    pbias[0:64, 16:17], lhsT=bd_b[:], rhs=ohc[:],
                    start=False, stop=False), waits=[e_ohc, WGc]))
                for c in range(1, NCH):
                    evm.append(P.t(lambda t, c=c: t.matmul(
                        pbias[:, c:c + 1],
                        lhsT=bu_b[:, c * 128:(c + 1) * 128], rhs=ohc[:],
                        start=False, stop=False), waits=[e_ohc, WGc]))
                for c in range(NCH):
                    evm.append(P.t(lambda t, c=c: t.matmul(
                        pbias[:, 8 + c:9 + c],
                        lhsT=bc_b[:, c * 128:(c + 1) * 128], rhs=ohc[:],
                        start=False, stop=(c == NCH - 1)), waits=[e_ohc, WGc]))
                eb1 = P.s(lambda s: s.activation(buT[:], pbias[:, 0:8], AF.Copy),
                          waits=[evm[-1]])
                eb2 = P.s(lambda s: s.activation(bcT[:], pbias[:, 8:16], AF.Copy),
                          waits=[evm[-1]])
                eb3 = P.s(lambda s: s.activation(bdT[:], pbias[0:64, 16:17],
                                                 AF.Copy), waits=[evm[-1]])
                eb4 = P.v(lambda v: v.tensor_add(bcombT[:], bcT[:], bbT[:]),
                          waits=[eb2, WHs])
                bias_evs["bu"] = eb1
                bias_evs["bd"] = eb3
                bias_evs["bcomb"] = eb4
                ev_ptrC_free[1] = eb3
            ev_wc_tr = []
            ev_sb2sb = [("GC0", P.cnt["GC0"]), ("GC1", P.cnt["GC1"]),
                        ("GD0", P.cnt["GD0"]), ("GD1", P.cnt["GD1"]),
                        ("GU", P.cnt["GU"])]
            for c in range(NCH):
                ev_wc_tr.append(P.dma("scalar", lambda s, c=c: s.dma_start(
                    out=wcT[:, c * D:(c + 1) * D],
                    in_=wccb[:, c * 128:(c + 1) * 128], transpose=True),
                    waits=ev_sb2sb, sem="WT%d" % (c % 2)))
            WHwc = [("WT0", P.cnt["WT0"]), ("WT1", P.cnt["WT1"])]
            GDt = ("GD0", P.cnt["GD0"])
            GUt = ("GU", P.cnt["GU"])


            x1buf = [sbc("x1C%d" % i, [128, 4096], F32) for i in range(3)]
            xn2 = [sbc("xn2_%d" % i, [128, D], BF16) for i in range(4)]
            xn2T = sbc("xn2T", [128, NCH * 512], BF16)
            hbf = sbc("hbf", [64, 512], BF16)
            aT = sbc("aT", [128, NCH * 512], BF16)
            ybf = [sbc("ybf0", [128, NCH * 512], BF16),
                   sbc("ybf1", [128, NCH * 512], BF16)]
            ybt = [sbc("ybt0", [128, D], BF16), sbc("ybt1", [128, D], BF16)]
            st3 = [sbc("st3_0", [128, 2], F32), sbc("st3_1", [128, 2], F32)]
            bn6c = [sbc("bn6c_0", [128, 12], F32), sbc("bn6c_1", [128, 12], F32)]
            rr3 = [sbc("rr3_0", [128, 2], F32), sbc("rr3_1", [128, 2], F32)]
            tl = [sbc("tl0", [128, D], F32), sbc("tl1", [128, D], F32)]
            x2b = [sbc("x2b0", [128, D], F32), sbc("x2b1", [128, D], F32)]

            ptrC2 = [psc("ptrC", [128, 1024], BF16), psc("pxtr", [128, 1024], BF16)]
            ptrC = ptrC2[0]
            pbias = ptrC2[1].ap().bitcast(F32)[:, 0:24]
            pa = [psc("pa0", [128, 512], F32), psc("pa1", [128, 512], F32)]
            py = [psc("py0", [128, 512], F32), psc("py1", [128, 512], F32)]
            ptry2 = [psc("ptry", [128, 1024], BF16), psc("ptry2", [128, 1024], BF16)]
            ph = pa[0].ap()[0:64, :]

            def x2nT(c, lo, hi):
                return xn2T[:, c * 512 + lo:c * 512 + hi]

            ln_state = {}
            ev_x1free = [None, None, None]
            ev_ptrC_free = [None, None]
            ev_xn2_free = [None, None, None, None]
            ev_hb_free = None
            ev_ph_free = None
            ev_pa_free = [None, None]
            ev_aT_free = None
            ev_py_free = [None, None]
            ev_ybf_free = [None, None]
            ev_ptry_free = [None, None]
            ev_ybt_free = [None, None]
            ev_tl_free = [None, None]
            ev_x2_free = [None, None]
            evXO_all = [("XO0", P.cnt["XO0"]), ("XO1", P.cnt["XO1"])]

            e1_evs = {}

            def load_c(m):
                pb = m % 3
                return P.dma("sync", lambda e, m=m, pb=pb: e.dma_start(
                    out=x1buf[pb].ap().rearrange("p (s d) -> p s d", s=4),
                    in_=x1d[m * 512:(m + 1) * 512, :].rearrange(
                        "(s p) d -> p s d", p=128)),
                    waits=[ev_x1free[pb]] + (evXO_all if m < 3 else []),
                    sem="XC%d" % (m % 2))

            def prep(m, evX1):
                pb = m % 3
                for s4 in range(SUB):
                    stile = m * SUB + s4
                    e1 = P.s(lambda s, pb=pb, s4=s4, stile=stile: s.activation(
                        xn2[s4][:], x1buf[pb][:, s4 * D:(s4 + 1) * D], AF.Identity,
                        scale=r2c[:, stile:stile + 1], bias=q2c[:, stile:stile + 1]),
                        waits=[evX1, ev_xn2_free[s4]])
                    e1_evs[(m, s4)] = e1

            yev_evs = {}

            def Ctail(mm):
                xpb2 = mm % 3
                pb2 = mm % 2
                ev_yev = yev_evs.pop(mm)
                evX1t = prep_x1[mm]
                nonlocal ev_ybf_free, ev_ptry_free, ev_ybt_free
                nonlocal ev_tl_free, ev_x2_free, ev_x1free
                for s4 in range(SUB):
                    sp = s4 % 2
                    yb = ptry2[sp]
                    evT = []
                    for c in range(NCH):
                        evT.append(P.t(lambda t, c=c, s4=s4, yb=yb, pb2=pb2:
                                       t.transpose(
                            yb[:, c * 128:(c + 1) * 128],
                            ybf[pb2][:, c * 512 + s4 * 128:c * 512 + (s4 + 1) * 128],
                            ident[:]),
                            waits=ev_yev + [ev_ptry_free[sp], evI2]))
                    if s4 == SUB - 1:
                        ev_ybf_free[pb2] = evT[-1]
                    eE = P.v(lambda v, sp=sp, yb=yb: v.tensor_copy(
                        ybt[sp][:], yb[:]),
                             waits=[evT[-1], ev_ybt_free[sp]])
                    ev_ptry_free[sp] = eE
                    ev_rr3, rr3_done = _ln(
                        P, bn6c[sp], st3[sp], rr3[sp],
                        (lambda sp=sp: ybt[sp][:]),
                        ln_state, ("C", sp), extra_waits=[eE])
                    b7 = P.v(lambda v, sp=sp: v.tensor_scalar(
                        tl[sp][:], ybt[sp][:], rr3[sp][:, 0:1],
                        rr3[sp][:, 1:2], ALU.mult, ALU.add),
                        waits=[ev_rr3, ev_tl_free[sp]])
                    rr3_done(b7)
                    ev_ybt_free[sp] = b7
                    b8 = P.v(lambda v, sp=sp, s4=s4, xpb2=xpb2: v.tensor_add(
                        x2b[sp][:], tl[sp][:],
                        x1buf[xpb2][:, s4 * D:(s4 + 1) * D]),
                        waits=[b7, evX1t, ev_x2_free[sp]])
                    ev_tl_free[sp] = b8
                    if s4 == SUB - 1:
                        ev_x1free[xpb2] = b8
                    stile = mm * SUB + s4
                    evX2 = P.dma("sync", lambda e, stile=stile, sp=sp: e.dma_start(
                        out=y_out[stile * 128:(stile + 1) * 128, :],
                        in_=x2b[sp][:]),
                        waits=[b8], sem="X2O%d" % sp)
                    ev_x2_free[sp] = evX2

            prep_x1 = [None] * nM
            prep_x1[0] = load_c(0)
            if nM > 1:
                prep_x1[1] = load_c(1)
            prep(0, prep_x1[0])

            for m in range(nM):
                pb = m % 2
                xpb = m % 3
                evX1 = prep_x1[m]
                ev_evE = []
                for s4 in range(SUB):
                    trb = ptrC2[s4 % 2]
                    evT = []
                    for c in range(NCH):
                        evT.append(P.t(lambda t, c=c, s4=s4, trb=trb: t.transpose(
                            trb[:, c * 128:(c + 1) * 128],
                            xn2[s4][:, c * 128:(c + 1) * 128], ident[:]),
                            waits=[e1_evs[(m, s4)], evI2, ev_ptrC_free[s4 % 2]]))
                    ev_xn2_free[s4] = evT[-1]
                    eE = P.s(lambda s, s4=s4, trb=trb: s.activation(
                        xn2T.ap().rearrange("p (c n) -> p c n", c=NCH)[:, :, s4 * 128:(s4 + 1) * 128],
                        trb.ap().rearrange("p (c n) -> p c n", c=NCH), AF.Copy),
                        waits=[evT[-1]])
                    ev_ptrC_free[s4 % 2] = eE
                    ev_evE.append(eE)
                if m + 1 < nM:
                    prep(m + 1, prep_x1[m + 1])
                if m == 0:
                    emit_bias_sel()
                evDn = None
                for c in range(NCH):
                    evDn = P.t(lambda t, c=c: t.matmul(
                        ph,
                        lhsT=wd_sb[:, c * R:(c + 1) * R],
                        rhs=x2nT(c, 0, 512),
                        start=(c == 0), stop=(c == NCH - 1)),
                        waits=ev_evE + [GDt, ev_ph_free] + (ev_pa_free[0] or []))
                evRl = P.s(lambda s: s.activation(hbf[:], ph, AF.Relu,
                                                  bias=bdT[:]),
                           waits=[evDn, bias_evs["bd"], ev_hb_free])
                ev_ph_free = evRl
                ev_up = []
                for c in range(NCH):
                    bank = (c + 1) % 2
                    evU = P.t(lambda t, c=c, bank=bank: t.matmul(
                        pa[bank][:],
                        lhsT=wu_sb[:, c * 128:(c + 1) * 128],
                        rhs=hbf[:],
                        start=True, stop=True),
                        waits=[evRl, GUt] + (ev_pa_free[bank] or []))
                    evUe = P.s(lambda s, c=c, bank=bank: s.activation(
                        aT[:, c * 512:(c + 1) * 512],
                        pa[bank][:], AF.Identity, bias=buT[:, c:c + 1]),
                        waits=[evU, bias_evs["bu"]] + ([ev_aT_free] if c == 0 else []))
                    ev_pa_free[bank] = [evUe]
                    ev_up.append(evUe)
                ev_hb_free = ev_up[-1]
                if m >= 1:
                    Ctail(m - 1)
                if m + 2 < nM:
                    prep_x1[m + 2] = load_c(m + 2)
                ev_yev = []
                for co in range(NCH):
                    bank = co % 2
                    evY = None
                    for c in range(NCH):
                        evY = P.t(lambda t, c=c, co=co, bank=bank: t.matmul(
                            py[bank][:],
                            lhsT=bwT[:, c * D + co * 128:c * D + (co + 1) * 128],
                            rhs=x2nT(c, 0, 512),
                            start=(c == 0), stop=False),
                            waits=ev_evE + WTs + [ev_py_free[bank]])
                    for c in range(NCH):
                        evY = P.t(lambda t, c=c, co=co, bank=bank: t.matmul(
                            py[bank][:],
                            lhsT=wcT[:, c * D + co * 128:c * D + (co + 1) * 128],
                            rhs=aT[:, c * 512:(c + 1) * 512],
                            start=False, stop=(c == NCH - 1)),
                            waits=ev_up + WHwc)
                    eYe = P.s(lambda s, co=co, bank=bank, pb=pb: s.activation(
                        ybf[pb][:, co * 512:(co + 1) * 512],
                        py[bank][:], AF.Identity, bias=bcombT[:, co:co + 1]),
                        waits=[evY, bias_evs["bcomb"]] + ([ev_ybf_free[pb]] if co == 0 else []))
                    ev_py_free[bank] = eYe
                    ev_yev.append(eYe)
                ev_aT_free = ev_yev[-1]
                yev_evs[m] = ev_yev

            Ctail(nM - 1)

            P.op("sync", None, waits=[("X2O0", P.cnt["X2O0"]),
                                      ("X2O1", P.cnt["X2O1"])])

        with nc.Block() as blk:
            P.emit(blk)
    return nc


def kernel(L0_attn_x, L0_attn_W_qkv, L0_attn_W_out, L0_attn_b_out,
           L0_ffn_base_W, L0_ffn_base_b, L0_ffn_disc_mu, L0_ffn_disc_sigma,
           L0_ffn_down_W, L0_ffn_down_b, L0_ffn_up_W, L0_ffn_up_b,
           L0_ffn_conv_W, L0_ffn_conv_b):
    x = np.ascontiguousarray(np.asarray(L0_attn_x, dtype=np.float32))
    B, N, _ = x.shape
    T = N // 2
    shared = {
        "wqkv": np.ascontiguousarray(L0_attn_W_qkv, dtype=np.float32),
        "wout": np.ascontiguousarray(L0_attn_W_out, dtype=np.float32),
        "bout": np.ascontiguousarray(L0_attn_b_out, dtype=np.float32),
        "basew": np.ascontiguousarray(L0_ffn_base_W, dtype=np.float32),
        "baseb": np.ascontiguousarray(L0_ffn_base_b, dtype=np.float32),
        "mu": np.ascontiguousarray(L0_ffn_disc_mu, dtype=np.float32),
        "sigma": np.ascontiguousarray(L0_ffn_disc_sigma, dtype=np.float32),
        "wd": np.ascontiguousarray(L0_ffn_down_W, dtype=np.float32),
        "bd": np.ascontiguousarray(L0_ffn_down_b, dtype=np.float32),
        "wu": np.ascontiguousarray(L0_ffn_up_W, dtype=np.float32),
        "bu": np.ascontiguousarray(L0_ffn_up_b, dtype=np.float32),
        "wc": np.ascontiguousarray(L0_ffn_conv_W, dtype=np.float32),
        "bc": np.ascontiguousarray(L0_ffn_conv_b, dtype=np.float32),
    }
    nc = build_program(T=T, t_total=N)
    in_maps = []
    for c in range(8):
        b, h = c // 2, c % 2
        mp = dict(shared)
        mp["x"] = np.ascontiguousarray(x[b, h * T:(h + 1) * T, :])
        in_maps.append(mp)
    res = run_bass_kernel_spmd(nc, in_maps, core_ids=list(range(8)))
    out = np.empty((B, N, D), dtype=np.float32)
    for c in range(8):
        b, h = c // 2, c % 2
        out[b, h * T:(h + 1) * T, :] = res.results[c]["y"]
    return out


# version 16
# speedup vs baseline: 1.3795x; 1.0240x over previous
from collections import defaultdict
from contextlib import ExitStack

import numpy as np

import concourse.bass as bass
import concourse.mybir as mybir
from concourse.bass import IndirectOffsetOnAxis
from concourse.bass_utils import run_bass_kernel_spmd

F32 = mybir.dt.float32
BF16 = mybir.dt.bfloat16
I32 = mybir.dt.int32
AF = mybir.ActivationFunctionType
ALU = mybir.AluOpType
AX = mybir.AxisListType

D = 1024
H = 16
NCH = 8
R = 64
E = 8
EPS = 1e-5

ENGINES = ("sync", "gpsimd", "tensor", "vector", "scalar")


class Plan:

    def __init__(self, nc):
        self.nc = nc
        self.ops = {e: [] for e in ENGINES}
        self.cnt = defaultdict(int)
        self.handles = {}

    def sem(self, stack, name):
        self.handles[name] = stack.enter_context(self.nc.semaphore(name))

    ENG_SEMS = ("T", "V", "S", "G", "BAR")

    def op(self, eng, fn, waits=(), sem=None, inc=1):
        ev = None
        ws = [w for w in waits if w is not None]
        for w in ws:
            assert isinstance(w, tuple) and len(w) == 2, f"bad wait {w!r}"
        if sem is not None:
            if sem not in self.ENG_SEMS and self.cnt[sem] > 0:
                ws.append((sem, self.cnt[sem]))
            self.cnt[sem] += inc
            ev = (sem, self.cnt[sem])
        self.ops[eng].append((fn, ws, (sem, inc) if sem else None))
        return ev

    def t(self, fn, waits=()):
        return self.op("tensor", fn, waits, "T")

    def v(self, fn, waits=()):
        return self.op("vector", fn, waits, "V")

    def s(self, fn, waits=()):
        return self.op("scalar", fn, waits, "S")

    def g(self, fn, waits=()):
        return self.op("gpsimd", fn, waits, "G")

    def dma(self, eng, fn, waits=(), sem="WG"):
        return self.op(eng, fn, waits, sem, inc=16)

    def barrier(self):
        self.cnt["BAR"] += len(ENGINES)
        tgt = self.cnt["BAR"]
        for e in ENGINES:
            self.ops[e].append((lambda eng: eng.nop(), [], ("BAR", 1)))
            self.ops[e].append((None, [("BAR", tgt)], None))

    OWN_SEM = {"tensor": "T", "vector": "V", "scalar": "S", "gpsimd": "G"}

    def emit(self, blk):
        for eng in ENGINES:
            ops = self.ops[eng]
            if not ops:
                continue

            def body(e, ops=ops, own=self.OWN_SEM.get(eng)):
                last = {}
                for fn, waits, inc in ops:
                    for (s, v) in waits:
                        if last.get(s, -1) >= v:
                            continue
                        e.wait_ge(self.handles[s], v)
                        last[s] = v
                    if fn is None:
                        continue
                    inst = fn(e)
                    if inc is not None and inc[0] is not None:
                        inst.then_inc(self.handles[inc[0]], inc[1])
            getattr(blk, eng)(body)


def _ln(P, bn6, st, rr, src_fn, state, key, extra_waits=()):
    prev = state.get(key, {})
    e1a = P.v(lambda v: v.bn_stats(bn6[:, 0:6], src_fn()[:, 0:512]),
              waits=list(extra_waits) + [prev.get("aggr")])
    e1 = P.v(lambda v: v.bn_stats(bn6[:, 6:12], src_fn()[:, 512:1024]),
             waits=[e1a])
    e2 = P.v(lambda v: v.bn_aggr(st[:], bn6[:]),
             waits=[e1, prev.get("st_read")])
    e3 = P.v(lambda v: v.tensor_scalar(rr[:, 0:1], st[:, 1:2], EPS, None,
                                       ALU.add),
             waits=[e2, prev.get("rr_read")])
    e4 = P.s(lambda s: s.activation(rr[:, 0:1], rr[:, 0:1], AF.Ln), waits=[e3])
    e5 = P.s(lambda s: s.activation(rr[:, 0:1], rr[:, 0:1], AF.Exp, scale=-0.5),
             waits=[e4])
    e6 = P.v(lambda v: v.tensor_scalar(rr[:, 1:2], st[:, 0:1], rr[:, 0:1],
                                       -1.0, ALU.mult, ALU.mult),
             waits=[e5, e2])
    st_new = {"aggr": e2, "st_read": e6}
    state[key] = st_new

    def set_last_reader(ev):
        st_new["rr_read"] = ev
    return e6, set_last_reader


def build_program(T=4096, t_total=None, debug_dump=False, phases="ABC", bstop=None):
    if t_total is None:
        t_total = 2 * T
    nS = T // 128
    nM = T // 512
    SUB = 4

    nc = bass.Bass("TRN2", target_bir_lowering=False)

    x_in = nc.dram_tensor("x", [T, D], F32, kind="ExternalInput").ap()
    wqkv = nc.dram_tensor("wqkv", [D, 3 * D], F32, kind="ExternalInput").ap()
    wout = nc.dram_tensor("wout", [D, D], F32, kind="ExternalInput").ap()
    bout = nc.dram_tensor("bout", [D], F32, kind="ExternalInput").ap()
    basew = nc.dram_tensor("basew", [D, D], F32, kind="ExternalInput").ap()
    baseb = nc.dram_tensor("baseb", [D], F32, kind="ExternalInput").ap()
    mu_in = nc.dram_tensor("mu", [E, D], F32, kind="ExternalInput").ap()
    sig_in = nc.dram_tensor("sigma", [E, D], F32, kind="ExternalInput").ap()
    wd_in = nc.dram_tensor("wd", [E, D, R], F32, kind="ExternalInput").ap()
    bd_in = nc.dram_tensor("bd", [E, R], F32, kind="ExternalInput").ap()
    wu_in = nc.dram_tensor("wu", [E, R, D], F32, kind="ExternalInput").ap()
    bu_in = nc.dram_tensor("bu", [E, D], F32, kind="ExternalInput").ap()
    wc_in = nc.dram_tensor("wc", [E, D, D], F32, kind="ExternalInput").ap()
    bc_in = nc.dram_tensor("bc", [E, D], F32, kind="ExternalInput").ap()
    y_out = nc.dram_tensor("y", [T, D], F32, kind="ExternalOutput").ap()

    qTd = nc.dram_tensor("qTd", [nM, 128, 4096], BF16).ap()
    x1d = nc.dram_tensor("x1d", [T, D], F32).ap()
    wscb = nc.dram_tensor("wscb", [D, D], BF16).ap()
    wccb = nc.dram_tensor("wccb", [D, D], BF16).ap()
    cc1i = nc.dram_tensor("cc1i", [129, D], F32).ap()
    cc1o = nc.dram_tensor("cc1o", [129, D], F32).ap()
    cc2i = nc.dram_tensor("cc2i", [4, 2048], F32).ap()
    cc2o = nc.dram_tensor("cc2o", [4, 2048], F32).ap()
    idxd = nc.dram_tensor("idxd", [1, 1], I32).ap()
    lossd = nc.dram_tensor("lossd", [1, 8], F32).ap()
    if debug_dump:
        kdbg = nc.dram_tensor("kdbg", [T, D], BF16).ap()
        vdbg = nc.dram_tensor("vdbg", [T, D], BF16).ap()
        adbg = nc.dram_tensor("adbg", [T, D], BF16).ap()

    P = Plan(nc)
    RG = [[0, 1], [2, 3], [4, 5], [6, 7]]

    with ExitStack() as st:
        def sb(name, shape, dt):
            return st.enter_context(nc.sbuf_tensor(name, shape, dt))

        for s in ("T", "V", "S", "G", "BAR", "CC", "WH", "WG", "LB",
                  "XO0", "XO1", "GD0", "GD1", "GU", "GC0", "GC1", "WT0", "WT1",
                  "XA0", "XA1", "QO0", "QO1", "QI0", "QI1", "WB", "GC2", "GC3",
                  "X2O0", "X2O1", "XC0", "XC1", "DBG"):
            P.sem(st, s)

        ident = sb("ident", [128, 128], BF16)
        identf = sb("identf", [8, 8], F32)
        ones_b = sb("ones_b", [128, 1], BF16)
        ones_f = sb("ones_f", [128, 1], F32)
        iota_p = sb("iota_p", [128, 1], I32)
        iota8x = sb("iota8x", [128, 8], I32)
        iota8r = sb("iota8r", [1, 8], I32)
        bwT = sb("bwT", [128, NCH * D], BF16)
        bcombT = sb("bcombT", [128, NCH], F32)
        bbT = sb("bbT", [128, NCH], F32)
        bcT = sb("bcT", [128, NCH], F32)
        buT = sb("buT", [128, NCH], F32)
        bdT = sb("bdT", [64, 1], F32)
        loss8 = sb("loss8", [8, 1], F32)
        lrow = sb("lrow", [1, 8], F32)
        lmin = sb("lmin", [1, 1], F32)
        lone = sb("lone", [1, 8], F32)
        liot = sb("liot", [1, 8], F32)
        idxf1 = sb("idxf1", [1, 1], F32)
        idxi1 = sb("idxi1", [1, 1], I32)
        idxb = sb("idxb", [128, 1], I32)
        idxbf = sb("idxbf", [128, 1], F32)
        idc = sb("idc", [128, 1], F32)
        idu = sb("idu", [64, 1], F32)
        gcf = sb("gcf", [128, 8], F32)
        gci = sb("gci", [128, 8], I32)
        guf = sb("guf", [64, 1], F32)
        gui = sb("gui", [64, 1], I32)
        kvz_r = sb("kvz_r", [128, 8 * 130], BF16)
        zrow = sb("zrow", [1, 1024], BF16)
        r2c = sb("r2c", [128, nS], F32)
        q2c = sb("q2c", [128, nS], F32)
        qc2 = sb("qc2", [128, nS], F32)

        evI1 = P.g(lambda g: g.memset(ident[:], 0.0))
        evI2 = P.g(lambda g: g.affine_select(
            out=ident[:], in_=ident[:], compare_op=ALU.not_equal, fill=1.0,
            base=0, pattern=[[-1, 128]], channel_multiplier=1), waits=[evI1])
        evI3 = P.g(lambda g: g.memset(identf[:], 0.0))
        evI4 = P.g(lambda g: g.affine_select(
            out=identf[:], in_=identf[:], compare_op=ALU.not_equal, fill=1.0,
            base=0, pattern=[[-1, 8]], channel_multiplier=1), waits=[evI3])
        evOB = P.g(lambda g: g.memset(ones_b[:], 1.0))
        evOF = P.g(lambda g: g.memset(ones_f[:], 1.0))
        evIp = P.g(lambda g: g.iota(iota_p[:], pattern=[[0, 1]], base=0,
                                    channel_multiplier=1))
        evI8 = P.g(lambda g: g.iota(iota8x[:], pattern=[[128, 8]], base=0,
                                    channel_multiplier=1))
        evIr = P.g(lambda g: g.iota(iota8r[:], pattern=[[1, 8]], base=0,
                                    channel_multiplier=0))
        evWb = P.dma("gpsimd", lambda g: g.dma_start(out=wscb, in_=basew))
        evBW = []
        for c in range(NCH):
            evBW.append(P.dma("scalar", lambda s, c=c: s.dma_start(
                out=bwT[:, c * D:(c + 1) * D],
                in_=wscb[:, c * 128:(c + 1) * 128], transpose=True),
                waits=[evWb], sem="WT%d" % (c % 2)))
        evBB = []
        for c in range(NCH):
            evBB.append(P.dma("sync", lambda e, c=c: e.dma_start(
                out=bbT[:, c:c + 1],
                in_=baseb[c * 128:(c + 1) * 128].unsqueeze(1)), sem="WH"))
        WHs = ("WH", P.cnt["WH"])
        WTs = [("WT0", P.cnt["WT0"]), ("WT1", P.cnt["WT1"])]

        with ExitStack() as stA:
            def sba(n, sh, dt):
                return stA.enter_context(nc.sbuf_tensor(n, sh, dt))
            def psa(n, sh, dt):
                return stA.enter_context(nc.psum_tensor(n, sh, dt))

            wq_sb = sba("wq_sb", [128, NCH * 3 * D], BF16)
            kvsb = sba("kvsb", [128, 1024], F32)
            zsb = sba("zsb", [1, 1024], F32)
            xbuf = [sba("xA0", [128, 4096], F32), sba("xA1", [128, 4096], F32)]
            xn1 = [sba("xn1_%d" % i, [128, D], BF16) for i in range(4)]
            st1 = [sba("st1_0", [128, 2], F32), sba("st1_1", [128, 2], F32)]
            bn6 = [sba("bn6_0", [128, 12], F32), sba("bn6_1", [128, 12], F32)]
            rr1 = [sba("rr1_0", [128, 2], F32), sba("rr1_1", [128, 2], F32)]
            xn1T = sba("xn1T", [128, NCH * 512], BF16)
            kbf = [sba("kbf0", [128, D], BF16), sba("kbf1", [128, D], BF16)]
            vbf = [sba("vbf0", [128, D], BF16), sba("vbf1", [128, D], BF16)]
            texp = [sba("texp0", [128, 512], F32), sba("texp1", [128, 512], F32)]
            tqxp = [sba("tqxp0", [128, 512], F32), sba("tqxp1", [128, 512], F32)]
            ktmp = [sba("ktmp0", [128, 512], F32), sba("ktmp1", [128, 512], F32)]
            qtmp = [sba("qtmp0", [128, 512], F32), sba("qtmp1", [128, 512], F32)]
            kt2 = [sba("kt2_0", [128, 512], F32), sba("kt2_1", [128, 512], F32)]
            qt2 = [sba("qt2_0", [128, 512], F32), sba("qt2_1", [128, 512], F32)]
            qTs = [sba("qTs0", [128, 4096], BF16), sba("qTs1", [128, 4096], BF16)]

            pkv = [psa("pkv0", [128, 512], F32), psa("pkv1", [128, 512], F32)]
            pz = [psa("pz0", [1, 512], F32), psa("pz1", [1, 512], F32)]
            pmm = [psa("pmm0", [128, 512], F32), psa("pmm1", [128, 512], F32)]
            ptrA = psa("ptrA", [128, 1024], BF16)
            pq = psa("pq", [128, 512], F32)

            def x1nT(c, lo, hi):
                return xn1T[:, c * 512 + lo:c * 512 + hi]

            P.dma("gpsimd", lambda g: g.dma_start(
                out=wq_sb.ap().rearrange("p (c j) -> p c j", c=NCH), in_=wqkv.rearrange("(c p) j -> p c j", p=128)))
            WGs = ("WG", P.cnt["WG"])

            ln_state = {}
            ev_xfree = [None, None]
            ev_kread = [None, None]
            ev_texp_read = [None, None]
            ev_tq_read = [None, None]
            ev_ptr_free = None
            ev_pq_free = None
            ev_pmm_free = [None, None]
            ev_qT_free = [None, None]
            ev_xn1_free = [None, None, None, None]
            ev_kv_done = []
            ev_z_done = []
            ev_qT_out = []
            ev_xn1T_read = None

            pool_banks = [pmm[0], pmm[1], pq]
            pool_free = [None, None, None]
            pool_i = [0]

            def take_bank():
                i = pool_i[0] % 3
                pool_i[0] += 1
                return i, pool_banks[i], pool_free[i] or []

            def prepA(m):
                pb = m % 2
                evX = P.dma("sync", lambda e, m=m, pb=pb: e.dma_start(
                    out=xbuf[pb].ap().rearrange("p (s d) -> p s d", s=4),
                    in_=x_in[m * 512:(m + 1) * 512, :].rearrange(
                        "(s p) d -> p s d", p=128)),
                    waits=[ev_xfree[pb]], sem="XA%d" % pb)
                applies = []
                for s4 in range(SUB):
                    sp = s4 % 2
                    ev_rr, rr_done = _ln(
                        P, bn6[sp], st1[sp], rr1[sp],
                        (lambda pb=pb, s4=s4: xbuf[pb][:, s4 * D:(s4 + 1) * D]),
                        ln_state, ("A", sp), extra_waits=[evX])
                    e7 = P.s(lambda s, pb=pb, s4=s4, sp=sp: s.activation(
                        xn1[s4][:], xbuf[pb][:, s4 * D:(s4 + 1) * D], AF.Identity,
                        scale=rr1[sp][:, 0:1], bias=rr1[sp][:, 1:2]),
                        waits=[ev_rr, ev_xn1_free[s4]])
                    rr_done(e7)
                    applies.append(e7)
                return evX, applies

            prepdata = [None] * nM
            prepdata[0] = prepA(0)

            for m in range(nM):
                pb = m % 2
                evX, applies = prepdata[m]
                ev_macro_evac = []
                for s4 in range(SUB):
                    evT = []
                    for c in range(NCH):
                        evT.append(P.t(lambda t, c=c, s4=s4: t.transpose(
                            ptrA[:, c * 128:(c + 1) * 128],
                            xn1[s4][:, c * 128:(c + 1) * 128], ident[:]),
                            waits=[applies[s4], evI2, ev_ptr_free]))
                    ev_xn1_free[s4] = evT[-1]
                    evE = P.s(lambda s, s4=s4: s.activation(
                        xn1T.ap().rearrange("p (c n) -> p c n", c=NCH)[:, :, s4 * 128:(s4 + 1) * 128],
                        ptrA.ap().rearrange("p (c n) -> p c n", c=NCH), AF.Copy),
                        waits=[evT[-1], ev_xn1T_read if s4 == 0 and m > 0 else None])
                    ev_ptr_free = evE
                    ev_macro_evac.append(evE)
                ev_xfree[pb] = applies[-1]
                if m + 1 < nM:
                    prepdata[m + 1] = prepA(m + 1)

                for s4 in range(SUB):
                    stile = m * SUB + s4
                    sp = s4 % 2
                    evE = ev_macro_evac[s4]
                    ev_k_ready = None
                    ev_v_ready = None
                    for ci in range(4):
                        bi, bank_t, bank_w = take_bank()
                        evMM = None
                        for c in range(NCH):
                            evMM = P.t(lambda t, c=c, ci=ci, bank_t=bank_t, s4=s4: t.matmul(
                                bank_t[:],
                                lhsT=x1nT(c, s4 * 128, (s4 + 1) * 128),
                                rhs=wq_sb[:, c * 3 * D + D + ci * 512:
                                          c * 3 * D + D + (ci + 1) * 512],
                                start=(c == 0), stop=(c == NCH - 1)),
                                waits=[evE, WGs] + bank_w)
                        if ci < 2:
                            eU = P.s(lambda s, bank_t=bank_t, ci=ci: s.activation(
                                texp[ci][:], bank_t[:], AF.Relu, scale=-1.0),
                                waits=[evMM, ev_texp_read[ci],
                                       ev_kread[sp] if ci == 0 else None])
                            eW = P.s(lambda s, ci=ci: s.activation(
                                ktmp[ci][:], texp[ci][:], AF.Exp, scale=-1.0),
                                waits=[eU])
                            eT = P.v(lambda v, bank_t=bank_t, ci=ci: v.tensor_add(
                                kt2[ci][:], bank_t[:], texp[ci][:]),
                                waits=[eU])
                            eE2 = P.v(lambda v, ci=ci, sp=sp: v.tensor_add(
                                kbf[sp][:, ci * 512:(ci + 1) * 512], kt2[ci][:],
                                ktmp[ci][:]),
                                waits=[eW, eT])
                            ev_texp_read[ci] = eE2
                            pool_free[bi] = [eU, eT]
                            if ci == 1:
                                ev_k_ready = eE2
                        else:
                            eE2 = P.s(lambda s, bank_t=bank_t, ci=ci, sp=sp: s.copy(
                                vbf[sp][:, (ci - 2) * 512:(ci - 1) * 512],
                                bank_t[:]),
                                waits=[evMM, ev_kread[sp] if ci == 2 else None])
                            pool_free[bi] = [eE2]
                            if ci == 3:
                                ev_v_ready = eE2
                    evKV = []
                    for p8 in range(8):
                        half = p8 // 4
                        evKV.append(P.t(lambda t, p8=p8, half=half, sp=sp, stile=stile: t.matmul(
                            pkv[half][:, (p8 % 4) * 128:(p8 % 4 + 1) * 128],
                            lhsT=kbf[sp][:, p8 * 128:(p8 + 1) * 128],
                            rhs=vbf[sp][:, p8 * 128:(p8 + 1) * 128],
                            start=(stile == 0 and p8 % 4 == 0),
                            stop=(stile == nS - 1 and p8 % 4 == 3)),
                            waits=[ev_k_ready, ev_v_ready]))
                    evZ = []
                    for half in range(2):
                        evZ.append(P.t(lambda t, half=half, sp=sp, stile=stile: t.matmul(
                            pz[half][:],
                            lhsT=ones_b[:],
                            rhs=kbf[sp][:, half * 512:(half + 1) * 512],
                            start=(stile == 0), stop=(stile == nS - 1)),
                            waits=[ev_k_ready, evOB]))
                    if debug_dump:
                        P.dma("gpsimd", lambda g, sp=sp, stile=stile: g.dma_start(
                            out=kdbg[stile * 128:(stile + 1) * 128, :],
                            in_=kbf[sp][:]), waits=[ev_k_ready], sem="DBG")
                        P.dma("gpsimd", lambda g, sp=sp, stile=stile: g.dma_start(
                            out=vdbg[stile * 128:(stile + 1) * 128, :],
                            in_=vbf[sp][:]), waits=[ev_v_ready], sem="DBG")
                    ev_kread[sp] = evZ[-1] if not debug_dump else ("DBG", P.cnt["DBG"])
                    if stile == nS - 1:
                        ev_kv_done = evKV
                        ev_z_done = evZ

                ev_q_evac = []
                for j in range(NCH):
                    bi, bank_t, bank_w = take_bank()
                    evQ = None
                    for c in range(NCH):
                        evQ = P.t(lambda t, c=c, j=j, bank_t=bank_t: t.matmul(
                            bank_t[:],
                            lhsT=wq_sb[:, c * 3 * D + j * 128:
                                       c * 3 * D + (j + 1) * 128],
                            rhs=x1nT(c, 0, 512),
                            start=(c == 0), stop=(c == NCH - 1)),
                            waits=[ev_macro_evac[-1], WGs] + bank_w)
                    sp = j % 2
                    eU = P.s(lambda s, sp=sp, bank_t=bank_t: s.activation(
                        tqxp[sp][:], bank_t[:], AF.Relu, scale=-1.0),
                        waits=[evQ, ev_tq_read[sp]])
                    eW = P.s(lambda s, sp=sp: s.activation(
                        qtmp[sp][:], tqxp[sp][:], AF.Exp, scale=-1.0),
                        waits=[eU])
                    eT = P.v(lambda v, sp=sp, bank_t=bank_t: v.tensor_add(
                        qt2[sp][:], bank_t[:], tqxp[sp][:]),
                        waits=[eU])
                    eQ2 = P.v(lambda v, j=j, sp=sp, pb=pb: v.tensor_add(
                        qTs[pb][:, j * 512:(j + 1) * 512], qt2[sp][:],
                        qtmp[sp][:]),
                        waits=[eW, eT, ev_qT_free[pb] if j == 0 else None])
                    ev_tq_read[sp] = eQ2
                    ev_q_evac.append(eQ2)
                    pool_free[bi] = [eU, eT]
                ev_xn1T_read = evQ
                evQO = P.dma("gpsimd", lambda g, m=m, pb=pb: g.dma_start(
                    out=qTd[m], in_=qTs[pb][:]),
                    waits=ev_q_evac, sem="QO%d" % pb)
                ev_qT_free[pb] = evQO
                ev_qT_out.append(evQO)

            eK0 = P.v(lambda v: v.tensor_copy(kvsb[:, 0:512], pkv[0][:]),
                      waits=ev_kv_done)
            eK1 = P.v(lambda v: v.tensor_copy(kvsb[:, 512:1024], pkv[1][:]),
                      waits=[eK0])
            eZ0 = P.v(lambda v: v.tensor_copy(zsb[:, 0:512], pz[0][:]),
                      waits=ev_z_done + [eK1])
            eZ1 = P.v(lambda v: v.tensor_copy(zsb[:, 512:1024], pz[1][:]),
                      waits=[eZ0])
            evD1 = P.dma("gpsimd", lambda g: g.dma_start(
                out=cc1i[0:128, :], in_=kvsb[:]), waits=[eK1])
            evD2 = P.dma("gpsimd", lambda g: g.dma_start(
                out=cc1i[128:129, :], in_=zsb[:]), waits=[eZ1])
            evCC1 = P.op("gpsimd", lambda g: g.collective_compute(
                "AllReduce", ALU.add, replica_groups=RG,
                ins=[cc1i.opt()], outs=[cc1o.opt()]),
                waits=[("WG", P.cnt["WG"])], sem="CC", inc=1)

            evKz = P.g(lambda g: g.memset(kvz_r[:], 0.0))
            kvv = lambda ap: ap.rearrange("p (g c) -> p g c", c=130)
            cco = lambda ap: ap.rearrange("p (g c) -> p g c", c=128)
            P.dma("gpsimd", lambda g: g.dma_start(
                out=kvv(kvz_r[0:64, :])[:, :, 0:64],
                in_=cco(cc1o[0:64, :])[:, :, 0:64]),
                waits=[evCC1, evKz])
            P.dma("gpsimd", lambda g: g.dma_start(
                out=kvv(kvz_r[64:128, :])[:, :, 65:129],
                in_=cco(cc1o[64:128, :])[:, :, 64:128]),
                waits=[evCC1, evKz])
            evZr = P.dma("gpsimd", lambda g: g.dma_start(
                out=zrow[:], in_=cc1o[128:129, :]), waits=[evCC1])
            evZt = []
            for c in range(NCH):
                evZt.append(P.t(lambda t, c=c: t.transpose(
                    ptrA[:, 2 * c:2 * c + 1], zrow[0:1, c * 128:(c + 1) * 128],
                    ident[0:1, 0:1]), waits=[evZr, evI2, ev_ptr_free]))
            zc1 = P.s(lambda s: s.activation(
                kvv(kvz_r[0:64, :])[:, :, 64:65],
                ptrA[0:64, 0:16:2].unsqueeze(2),
                AF.Copy), waits=[evZt[-1], evKz])
            zc2 = P.s(lambda s: s.activation(
                kvv(kvz_r[64:128, :])[:, :, 129:130],
                ptrA[64:128, 0:16:2].unsqueeze(2),
                AF.Copy), waits=[evZt[-1], evKz])
            ev_rhs = ("WG", P.cnt["WG"])
            ev_rhs2 = [zc1, zc2]

            P.barrier()

        if phases == "A":
            P.op("sync", None, waits=[("CC", 1)])
        if "B" in phases and True:
         with ExitStack() as stB:
            def sbb(n, sh, dt):
                return stB.enter_context(nc.sbuf_tensor(n, sh, dt))
            def psb(n, sh, dt):
                return stB.enter_context(nc.psum_tensor(n, sh, dt))

            wo_sb = sbb("wo_sb", [128, NCH * D], BF16)
            ones_r = sbb("ones_r", [1, 128], BF16)
            bo_row = sbb("bo_row", [1, D], BF16)
            ms_sb = sbb("ms_sb", [3, 2048], F32)
            qs_sb = sbb("qs_sb", [2, 1], F32)
            zr2 = sbb("zr2", [1, 2048], F32)
            qrp = sbb("qrp", [128, 2], F32)
            xbufB = [sbb("xB0", [128, 4096], F32), sbb("xB1", [128, 4096], F32)]
            qTb = [sbb("qTb0", [128, 4096], BF16), sbb("qTb1", [128, 4096], BF16)]
            attn = [sbb("attn0", [128, D], BF16), sbb("attn1", [128, D], BF16)]
            attnT = [sbb("attnT0", [128, D], BF16), sbb("attnT1", [128, D], BF16)]
            rden = [sbb("rden0", [128, 16], F32), sbb("rden1", [128, 16], F32)]
            aout = [sbb("aout%d" % i, [128, D], F32) for i in range(8)]
            aln = [sbb("aln0", [128, D], F32), sbb("aln1", [128, D], F32)]
            x1b_ = [sbb("x1b0", [128, D], F32), sbb("x1b1", [128, D], F32)]
            st2 = [sbb("st2_0", [128, 2], F32), sbb("st2_1", [128, 2], F32)]
            bn6b = [sbb("bn6b_0", [128, 12], F32), sbb("bn6b_1", [128, 12], F32)]
            rr2 = [sbb("rr2_0", [128, 2], F32), sbb("rr2_1", [128, 2], F32)]
            xcat = [sbb("xcat0", [128, 2048], BF16), sbb("xcat1", [128, 2048], BF16)]
            rv3f = [sbb("rv3f_0", [128, 3], F32), sbb("rv3f_1", [128, 3], F32)]
            rv3 = [sbb("rv3_0", [128, 3], BF16), sbb("rv3_1", [128, 3], BF16)]

            png = [psb("png0", [128, 390], F32), psb("png1", [128, 390], F32)]
            ptrB = psb("ptrB", [128, 1024], BF16)
            pwo = psb("pwo", [128, 512], F32)
            pm = [psb("pm0", [3, 512], F32), psb("pm1", [3, 512], F32),
                  psb("pm2", [3, 512], F32), psb("pm3", [3, 512], F32)]
            pqs_view = None

            P.dma("gpsimd", lambda g: g.dma_start(
                out=wo_sb.ap().rearrange("p (c j) -> p c j", c=NCH), in_=wout.rearrange("(c p) j -> p c j", p=128)))
            WGb = ("WG", P.cnt["WG"])
            evOR = P.g(lambda g: g.memset(ones_r[:], 1.0))
            evBo = P.dma("gpsimd", lambda g: g.dma_start(
                out=bo_row[:], in_=bout.unsqueeze(0)), sem="WB")
            WHb = ("WB", P.cnt["WB"])
            ptrBf32 = ptrB.ap().bitcast(F32)
            GROUPS = [(0, 3, 0), (3, 6, 6), (6, 8, 12)]
            ln_state = {}
            ev_png_free = [None, None]
            ev_x1_free = [None, None]
            ev_attn_free = [None, None]
            ev_rden_free = [None, None]
            ev_aout_free = [None] * 8
            ev_aln_free = [None, None]
            ev_xfreeB = [None, None]
            ev_qTb_free = [None, None]
            ev_ptrB_free = [zc2]
            ev_attnT_free = [None, None]
            ev_pwo_free = [None]
            ev_xc_free = [None, None]
            ev_rv_free = [None, None]
            ev_pm_last = [None] * 4
            ev_x1_stores = []
            ev_cache_w = []
            xo_marks = {}

            evXBm = [None] * nM
            evQBm = [None] * nM

            def load_x(m):
                pb = m % 2
                evXBm[m] = P.dma("sync", lambda e, m=m, pb=pb: e.dma_start(
                    out=xbufB[pb].ap().rearrange("p (s d) -> p s d", s=4),
                    in_=x_in[m * 512:(m + 1) * 512, :].rearrange(
                        "(s p) d -> p s d", p=128)),
                    waits=[ev_xfreeB[pb]], sem="XA%d" % pb)

            def load_q(m):
                pb = m % 2
                evQBm[m] = P.dma("scalar", lambda e, m=m, pb=pb: e.dma_start(
                    out=qTb[pb][:], in_=qTd[m]),
                    waits=[ev_qTb_free[pb],
                           ("QO0", P.cnt["QO0"]), ("QO1", P.cnt["QO1"])],
                    sem="QI%d" % pb)

            def numdiv(stile):
                m = stile // SUB
                s4 = stile % SUB
                pb = m % 2
                sp = stile % 2
                evQ = evQBm[m]
                ev_div = []
                last_q = None
                for (p0, p1, rc) in GROUPS:
                    gi = rc // 6
                    bank = gi % 2
                    npair = p1 - p0
                    evN = []
                    for p8 in range(p0, p1):
                        off = (p8 - p0) * 130
                        evN.append(P.t(lambda t, p8=p8, off=off, bank=bank, pb=pb, s4=s4, p0=p0, p1=p1: t.matmul(
                            png[bank][:, off:off + 130],
                            lhsT=qTb[pb][:, p8 * 512 + s4 * 128:
                                         p8 * 512 + (s4 + 1) * 128],
                            rhs=kvz_r[:, p8 * 130:(p8 + 1) * 130],
                            start=(p8 == p0), stop=(p8 == p1 - 1)),
                            waits=[evQ, ev_rhs] + ev_rhs2 + (ev_png_free[bank] or [])))
                    last_q = evN[-1]
                    eD1 = P.v(lambda v, rc=rc, bank=bank, sp=sp, npair=npair: v.tensor_scalar(
                        rden[sp][:, rc:rc + 2 * npair],
                        png[bank][:, 64:npair * 130:65],
                        1e-6, None, ALU.add),
                        waits=evN + (ev_rden_free[sp] or []))
                    eD2 = P.v(lambda v, rc=rc, sp=sp, npair=npair: v.reciprocal(
                        rden[sp][:, rc:rc + 2 * npair],
                        rden[sp][:, rc:rc + 2 * npair]), waits=[eD1])

                    def dvv(e, bank=bank, p0=p0, rc=rc, npair=npair, sp=sp):
                        return e.tensor_tensor(
                            attn[sp].ap().rearrange("p (j c) -> p j c", c=64)[:, 2 * p0:2 * p0 + 2 * npair, :],
                            png[bank].ap().rearrange("p (j c) -> p j c", c=65)[:, 0:2 * npair, 0:64],
                            rden[sp][:, rc:rc + 2 * npair].unsqueeze(2).broadcast_to([128, 2 * npair, 64]),
                            ALU.mult)
                    evDv = P.v(dvv, waits=[eD2, ev_attn_free[sp]])
                    ev_div.append(evDv)
                    ev_png_free[bank] = [evDv]
                ev_rden_free[sp] = [ev_div[-1]]
                return ev_div, last_q

            load_q(0)
            load_x(0)
            div_evs = [None] * (nS + 1)
            div_evs[0] = numdiv(0)

            def headB(stile):
                m = stile // SUB
                s4 = stile % SUB
                pb = m % 2
                sp = stile % 2
                ev_div, last_q = div_evs[stile]
                if debug_dump:
                    P.dma("gpsimd", lambda g, sp=sp, stile=stile: g.dma_start(
                        out=adbg[stile * 128:(stile + 1) * 128, :],
                        in_=attn[sp][:]), waits=ev_div, sem="DBG")
                    ev_div = ev_div + [("DBG", P.cnt["DBG"])]
                evT = []
                for c in range(NCH):
                    evT.append(P.t(lambda t, c=c, sp=sp: t.transpose(
                        ptrB[:, c * 128:(c + 1) * 128],
                        attn[sp][:, c * 128:(c + 1) * 128], ident[:]),
                        waits=ev_div + [ev_ptrB_free[0], evI2]))
                ev_attn_free[sp] = evT[-1]
                eTe = P.s(lambda s, sp=sp: s.activation(
                    attnT[sp][:], ptrB[:], AF.Copy),
                    waits=[evT[-1], ev_attnT_free[sp]])
                ev_ptrB_free[0] = eTe
                if stile + 1 < nS:
                    div_evs[stile + 1] = numdiv(stile + 1)
                    if (stile + 1) % SUB == SUB - 1:
                        ev_qTb_free[((stile + 1) // SUB) % 2] = div_evs[stile + 1][1]
                ao = stile % 8
                ev_wout = None
                for half in range(2):
                    free_w = ev_pwo_free if half == 0 else ev_ptrB_free
                    dst = pwo[:] if half == 0 else ptrBf32[:, 0:512]
                    evWc = None
                    for c in range(NCH):
                        evWc = P.t(lambda t, c=c, half=half, sp=sp, dst=dst:
                                   t.matmul(
                            dst,
                            lhsT=attnT[sp][:, c * 128:(c + 1) * 128],
                            rhs=wo_sb[:, c * D + half * 512:
                                      c * D + (half + 1) * 512],
                            start=(c == 0), stop=False),
                            waits=[eTe, WGb, free_w[0]])
                    evW = P.t(lambda t, half=half, dst=dst: t.matmul(
                        dst,
                        lhsT=ones_r[0:1, :],
                        rhs=bo_row[0:1, half * 512:(half + 1) * 512],
                        start=False, stop=True),
                        waits=[WHb, evOR])
                    eB = P.s(lambda s, half=half, ao=ao, dst=dst: s.copy(
                        aout[ao][:, half * 512:(half + 1) * 512], dst),
                        waits=[evW,
                               ev_aout_free[ao] if half == 0 else None])
                    free_w[0] = eB
                    ev_wout = eB
                    if half == 1:
                        ev_attnT_free[sp] = evWc
                wout_evs[stile] = ev_wout

            def tailB(stile):
                m = stile // SUB
                s4 = stile % SUB
                pb = m % 2
                sp = stile % 2
                ao = stile % 8
                ev_wout = wout_evs[stile]
                ev_rr2, rr2_done = _ln(
                    P, bn6b[sp], st2[sp], rr2[sp],
                    (lambda ao=ao: aout[ao][:]),
                    ln_state, ("B", sp), extra_waits=[ev_wout])
                b7 = P.s(lambda s, sp=sp, ao=ao: s.activation(
                    aln[sp][:], aout[ao][:], AF.Identity,
                    scale=rr2[sp][:, 0:1], bias=rr2[sp][:, 1:2]),
                    waits=[ev_rr2, ev_aln_free[sp]])
                rr2_done(b7)
                b8 = P.g(lambda g, sp=sp, s4=s4, pb=pb, m=m: g.tensor_tensor(
                    x1b_[sp][:], aln[sp][:],
                    xbufB[pb][:, s4 * D:(s4 + 1) * D], ALU.add),
                    waits=[b7, evXBm[m], ev_x1_free[sp]])
                ev_aout_free[ao] = b7
                ev_aln_free[sp] = b8
                if s4 == SUB - 1:
                    ev_xfreeB[pb] = b8
                evXO = P.dma("gpsimd", lambda g, sp=sp, stile=stile: g.dma_start(
                    out=x1d[stile * 128:(stile + 1) * 128, :],
                    in_=x1b_[sp][:]),
                    waits=[b8], sem="XO%d" % sp)
                ev_x1_free[sp] = evXO
                ev_x1_stores.append(evXO)
                xo_marks[stile] = [("XO0", P.cnt["XO0"]),
                                   ("XO1", P.cnt["XO1"])]

            wout_evs = [None] * nS
            for m in range(nM):
                if m + 1 < nM:
                    load_q(m + 1)
                for s4 in range(SUB):
                    headB(m * SUB + s4)
                    if m >= 1:
                        tailB((m - 1) * SUB + s4)
                if m + 1 < nM:
                    load_x(m + 1)
            for s4 in range(SUB):
                tailB((nM - 1) * SUB + s4)

            pm_gate = ([ev_pwo_free[0]] + (ev_png_free[0] or [])
                       + (ev_png_free[1] or []))
            for sp in range(2):
                ln_state[("B2", sp)] = dict(ln_state.get(("B", sp), {}))
            evX2m = [None] * nM
            xfree2 = [list(ev_xfreeB[0:1]), list(ev_xfreeB[1:2])]

            def load_x2(m):
                pb = m % 2
                evX2m[m] = P.dma("sync", lambda e, m=m, pb=pb: e.dma_start(
                    out=xbufB[pb].ap().rearrange("p (s d) -> p s d", s=4),
                    in_=x1d[m * 512:(m + 1) * 512, :].rearrange(
                        "(s p) d -> p s d", p=128)),
                    waits=xfree2[pb] + xo_marks[m * SUB + SUB - 1],
                    sem="XA%d" % pb)

            load_x2(0)
            if nM > 1:
                load_x2(1)
            for m in range(nM):
                pb = m % 2
                newfree = []
                for s4 in range(SUB):
                    stile = m * SUB + s4
                    sp = stile % 2
                    src = (lambda pb=pb, s4=s4:
                           xbufB[pb][:, s4 * D:(s4 + 1) * D])
                    ev_rr2b, rr2b_done = _ln(
                        P, bn6b[sp], st2[sp], rr2[sp], src,
                        ln_state, ("B2", sp), extra_waits=[evX2m[m]])
                    c7 = P.s(lambda s, sp=sp, stile=stile: s.copy(
                        r2c[:, stile:stile + 1], rr2[sp][:, 0:1]),
                        waits=[ev_rr2b])
                    c8 = P.s(lambda s, sp=sp, stile=stile: s.copy(
                        q2c[:, stile:stile + 1], rr2[sp][:, 1:2]),
                        waits=[ev_rr2b])
                    c10 = P.s(lambda s, sp=sp, stile=stile: s.activation(
                        qc2[:, stile:stile + 1], rr2[sp][:, 1:2], AF.Square),
                        waits=[ev_rr2b])
                    ev_cache_w.extend([c7, c8, c10])
                    d1 = P.v(lambda v, sp=sp: v.tensor_copy(
                        rv3f[sp][:, 0:1], rr2[sp][:, 0:1]),
                        waits=[ev_rr2b, ev_rv_free[sp]])
                    d2 = P.v(lambda v, sp=sp: v.tensor_mul(
                        rv3f[sp][:, 1:2], rr2[sp][:, 0:1], rr2[sp][:, 1:2]),
                        waits=[ev_rr2b, ev_rv_free[sp]])
                    d3 = P.v(lambda v, sp=sp: v.tensor_mul(
                        rv3f[sp][:, 2:3], rr2[sp][:, 0:1], rr2[sp][:, 0:1]),
                        waits=[ev_rr2b, ev_rv_free[sp]])
                    rr2b_done(d3)
                    d4 = P.v(lambda v, sp=sp: v.tensor_copy(
                        rv3[sp][:], rv3f[sp][:]),
                        waits=[d1, d2, d3, ev_rv_free[sp]])
                    d5 = P.v(lambda v, sp=sp, src=src: v.tensor_copy(
                        xcat[sp][:, 0:1024], src()),
                        waits=[evX2m[m], ev_xc_free[sp]])
                    d6 = P.s(lambda s, sp=sp, src=src: s.activation(
                        xcat[sp][:, 1024:2048], src(), AF.Square),
                        waits=[evX2m[m], ev_xc_free[sp]])
                    evMs = []
                    for ci in range(4):
                        evMs.append(P.t(lambda t, ci=ci, sp=sp, stile=stile:
                                        t.matmul(
                            pm[ci][:],
                            lhsT=rv3[sp][:],
                            rhs=xcat[sp][:, ci * 512:(ci + 1) * 512],
                            start=(stile == 0), stop=(stile == nS - 1)),
                            waits=[d4, d5, d6]
                            + (pm_gate if stile == 0 else [])))
                    ev_xc_free[sp] = evMs[-1]
                    ev_rv_free[sp] = evMs[-1]
                    if stile == nS - 1:
                        ev_pm_last = evMs
                    newfree += [ev_rr2b, d5, d6]
                xfree2[pb] = newfree
                if m + 2 < nM:
                    load_x2(m + 2)

            f1 = P.v(lambda v: v.tensor_reduce(qrp[:, 0:1], q2c[:], AX.X,
                                               ALU.add), waits=ev_cache_w)
            f2 = P.v(lambda v: v.tensor_reduce(qrp[:, 1:2], qc2[:], AX.X,
                                               ALU.add), waits=ev_cache_w)
            f3 = P.t(lambda t: t.matmul(ptrB.ap().bitcast(F32)[0:2, 0:1],
                                        lhsT=qrp[:], rhs=ones_f[:],
                                        start=True, stop=True),
                     waits=[f1, f2, evOF, ev_ptrB_free[0]] + ev_pm_last)
            f4 = P.v(lambda v: v.tensor_copy(
                qs_sb[:], ptrB.ap().bitcast(F32)[0:2, 0:1]), waits=[f3])
            evMe = []
            for ci in range(4):
                evMe.append(P.v(lambda v, ci=ci: v.tensor_copy(
                    ms_sb.ap().rearrange("p (c n) -> p c n", c=4)[:, ci, :],
                    pm[ci][:]),
                    waits=ev_pm_last + [f4]))
            gz0 = P.g(lambda g: g.memset(zr2[:], 0.0))
            P.dma("gpsimd", lambda g: g.dma_start(
                out=cc2i[3:4, :], in_=zr2[:]), waits=[gz0])
            g1 = P.dma("gpsimd", lambda g: g.dma_start(
                out=cc2i[0:3, :], in_=ms_sb[:]), waits=evMe)
            g2 = P.dma("gpsimd", lambda g: g.dma_start(
                out=cc2i[3:4, 0:2].rearrange("a b -> b a"), in_=qs_sb[:],
                allow_slow_non_contiguous=True), waits=[f4])
            if bstop is not None:
                evCC2 = None
            evCC2 = evCC2 if bstop is not None else P.op("gpsimd", lambda g: g.collective_compute(
                "AllReduce", ALU.add, replica_groups=RG,
                ins=[cc2i.opt()], outs=[cc2o.opt()]),
                waits=[("WG", P.cnt["WG"])], sem="CC", inc=1)

            P.barrier()

        if phases == "AB":
            P.op("sync", None, waits=[("CC", 1 if bstop else 2)])
        if "C" in phases and True:
         with ExitStack() as stC:
            def sbc(n, sh, dt):
                return stC.enter_context(nc.sbuf_tensor(n, sh, dt))
            def psc(n, sh, dt):
                return stC.enter_context(nc.psum_tensor(n, sh, dt))

            wcT = sbc("wcT", [128, NCH * D], BF16)
            wd_sb = sbc("wd_sb", [128, NCH * R], BF16)
            wu_sb = sbc("wu_sb", [64, D], BF16)
            mu_sb = sbc("mu_sb", [8, D], F32)
            sg_sb = sbc("sg_sb", [8, D], F32)
            iv_sb = sbc("iv_sb", [8, D], F32)
            muiv = sbc("muiv", [8, D], F32)
            c3_sb = sbc("c3_sb", [8, D], F32)
            lw1 = sbc("lw1", [8, D], F32)
            lw2 = sbc("lw2", [8, D], F32)
            m2ab = sbc("m2ab", [8, D], F32)
            q1b = sbc("q1b", [8, 1], F32)
            q2b = sbc("q2b", [8, 1], F32)
            wstg = sbc("wstg", [128, D], F32)
            wstg2 = sbc("wstg2", [128, D], F32)
            wstg3 = sbc("wstg3", [128, D], F32)
            wustg = sbc("wustg", [64, D], F32)
            wdstg = sbc("wdstg", [128, NCH * R], F32)
            bu_b = sbc("bu_b", [8, D], BF16)
            bc_b = sbc("bc_b", [8, D], BF16)
            bd_b = sbc("bd_b", [8, R], BF16)
            ohc = sbc("ohc", [8, 1], BF16)

            P.dma("gpsimd", lambda g: g.dma_start(out=bu_b[:], in_=bu_in))
            P.dma("gpsimd", lambda g: g.dma_start(out=bc_b[:], in_=bc_in))
            P.dma("gpsimd", lambda g: g.dma_start(out=bd_b[:], in_=bd_in))
            WGc = ("WG", P.cnt["WG"])
            P.dma("sync", lambda e: e.dma_start(out=mu_sb[:], in_=mu_in),
                  sem="LB")
            P.dma("sync", lambda e: e.dma_start(out=sg_sb[:], in_=sig_in),
                  sem="LB")
            h1 = P.v(lambda v: v.tensor_mul(iv_sb[:], sg_sb[:], sg_sb[:]),
                         waits=[("LB", P.cnt["LB"])])
            h2 = P.v(lambda v: v.reciprocal(iv_sb[:], iv_sb[:]), waits=[h1])
            h3 = P.v(lambda v: v.tensor_mul(muiv[:], mu_sb[:], iv_sb[:]),
                     waits=[h2, WHs])
            h4 = P.v(lambda v: v.tensor_mul(c3_sb[:], mu_sb[:], muiv[:]),
                     waits=[h3])
            for fn in (lambda e: e.dma_start(out=lw1[:], in_=cc2o[0:1, 0:1024].partition_broadcast(8)),
                       lambda e: e.dma_start(out=lw2[:], in_=cc2o[1:2, 0:1024].partition_broadcast(8)),
                       lambda e: e.dma_start(out=m2ab[:], in_=cc2o[2:3, 1024:2048].partition_broadcast(8)),
                       lambda e: e.dma_start(out=q1b[:], in_=cc2o[3:4, 0:1].partition_broadcast(8)),
                       lambda e: e.dma_start(out=q2b[:], in_=cc2o[3:4, 1:2].partition_broadcast(8))):
                P.dma("sync", fn, waits=[evCC2], sem="LB")
            evLB = ("LB", P.cnt["LB"])
            inv_t = 1.0 / float(t_total)
            k1 = P.v(lambda v: v.tensor_scalar(
                lw1[:], lw1[:], q1b[:], inv_t, ALU.add, ALU.mult), waits=[evLB])
            k2a = P.v(lambda v: v.tensor_scalar(
                lw2[:], lw2[:], 2.0, None, ALU.mult), waits=[evLB])
            k2 = P.v(lambda v: v.tensor_add(lw2[:], lw2[:], m2ab[:]),
                     waits=[k2a])
            k3 = P.v(lambda v: v.tensor_scalar(
                lw2[:], lw2[:], q2b[:], inv_t, ALU.add, ALU.mult), waits=[k2])
            k4 = P.v(lambda v: v.tensor_mul(lw2[:], lw2[:], iv_sb[:]),
                     waits=[k3, h2])
            k5 = P.v(lambda v: v.tensor_mul(lw1[:], lw1[:], muiv[:]),
                     waits=[k1, h3])
            k6a = P.v(lambda v: v.tensor_scalar(
                lw1[:], lw1[:], -2.0, None, ALU.mult), waits=[k5])
            k6 = P.v(lambda v: v.tensor_add(lw1[:], lw1[:], lw2[:]),
                     waits=[k4, k6a])
            k7 = P.v(lambda v: v.tensor_add(lw1[:], lw1[:], c3_sb[:]),
                     waits=[k6, h4])
            k8 = P.v(lambda v: v.tensor_reduce(loss8[:], lw1[:], AX.X, ALU.add),
                     waits=[k7])
            k9 = P.dma("gpsimd", lambda g: g.dma_start(
                out=lossd.rearrange("a b -> b a"), in_=loss8[:],
                allow_slow_non_contiguous=True), waits=[k8])
            k10 = P.dma("gpsimd", lambda g: g.dma_start(out=lrow[:], in_=lossd),
                        waits=[k9])
            k11 = P.v(lambda v: v.tensor_reduce(lmin[:], lrow[:], AX.X, ALU.min),
                      waits=[k10])
            k12 = P.v(lambda v: v.tensor_single_scalar(
                lone[:], lrow[:], lmin[:], ALU.is_equal), waits=[k11])
            k13 = P.v(lambda v: v.tensor_copy(liot[:], iota8r[:]), waits=[evIr])
            k14 = P.v(lambda v: v.tensor_mul(liot[:], liot[:], lone[:]),
                      waits=[k13, k12])
            k15 = P.v(lambda v: v.tensor_reduce(idxf1[:], liot[:], AX.X, ALU.add),
                      waits=[k14])
            k16 = P.v(lambda v: v.tensor_copy(idxi1[:], idxf1[:]), waits=[k15])
            k17 = P.dma("gpsimd", lambda g: g.dma_start(out=idxd, in_=idxi1[:]),
                        waits=[k16])
            k18 = P.dma("gpsimd", lambda g: g.dma_start(
                out=idxb[:], in_=idxd.partition_broadcast(128)), waits=[k17])
            k19 = P.v(lambda v: v.tensor_copy(idxbf[:], idxb[:]), waits=[k18])
            k19b = P.v(lambda v: v.tensor_scalar(
                idc[:], idxbf[:], 1024.0, None, ALU.mult), waits=[k19])
            k19c = P.v(lambda v: v.tensor_scalar(
                idu[:], idxbf[0:64, :], 64.0, None, ALU.mult), waits=[k19])
            k20 = P.v(lambda v: v.tensor_copy(gcf[:], iota8x[:]), waits=[evI8])
            k21 = P.v(lambda v: v.tensor_scalar(
                gcf[:], gcf[:], idc[:], None, ALU.add), waits=[k19b, k20])
            k22 = P.v(lambda v: v.tensor_copy(gci[:], gcf[:]), waits=[k21])
            k23 = P.v(lambda v: v.tensor_copy(guf[:], iota_p[0:64, :]),
                      waits=[evIp])
            k24 = P.v(lambda v: v.tensor_scalar(
                guf[:], guf[:], idu[:], None, ALU.add), waits=[k19c, k23])
            k25 = P.v(lambda v: v.tensor_copy(gui[:], guf[:]), waits=[k24])

            x1buf = [sbc("x1C%d" % i, [128, 4096], F32) for i in range(3)]
            xn2 = [sbc("xn2_%d" % i, [128, D], BF16) for i in range(4)]
            xn2T = sbc("xn2T", [128, NCH * 512], BF16)
            hbf = sbc("hbf", [64, 512], BF16)
            aT = sbc("aT", [128, NCH * 512], BF16)
            ybf = [sbc("ybf0", [128, NCH * 512], BF16),
                   sbc("ybf1", [128, NCH * 512], BF16)]
            ybt = [sbc("ybt0", [128, D], BF16), sbc("ybt1", [128, D], BF16)]
            st3 = [sbc("st3_0", [128, 2], F32), sbc("st3_1", [128, 2], F32)]
            bn6c = [sbc("bn6c_0", [128, 12], F32), sbc("bn6c_1", [128, 12], F32)]
            rr3 = [sbc("rr3_0", [128, 2], F32), sbc("rr3_1", [128, 2], F32)]
            tl = [sbc("tl0", [128, D], F32), sbc("tl1", [128, D], F32)]
            x2b = [sbc("x2b0", [128, D], F32), sbc("x2b1", [128, D], F32)]

            ptrC2 = [psc("ptrC", [128, 1024], BF16), psc("pxtr", [128, 1024], BF16)]
            ptrC = ptrC2[0]
            pbias = ptrC2[1].ap().bitcast(F32)[:, 0:24]
            pa = [psc("pa0", [128, 512], F32), psc("pa1", [128, 512], F32)]
            py = [psc("py0", [128, 512], F32), psc("py1", [128, 512], F32)]
            ptry2 = [psc("ptry", [128, 1024], BF16), psc("ptry2", [128, 1024], BF16)]
            ph = pa[0].ap()[0:64, :]

            def x2nT(c, lo, hi):
                return xn2T[:, c * 512 + lo:c * 512 + hi]

            ln_state = {}
            ev_x1free = [None, None, None]
            ev_ptrC_free = [None, None]
            ev_xn2_free = [None, None, None, None]
            ev_hb_free = None
            ev_ph_free = None
            ev_pa_free = [None, None]
            ev_aT_free = None
            ev_py_free = [None, None]
            ev_ybf_free = [None, None]
            ev_ptry_free = [None, None]
            ev_ybt_free = [None, None]
            ev_tl_free = [None, None]
            ev_x2_free = [None, None]
            evXO_all = [("XO0", P.cnt["XO0"]), ("XO1", P.cnt["XO1"])]

            e1_evs = {}

            def load_c(m):
                pb = m % 3
                return P.dma("sync", lambda e, m=m, pb=pb: e.dma_start(
                    out=x1buf[pb].ap().rearrange("p (s d) -> p s d", s=4),
                    in_=x1d[m * 512:(m + 1) * 512, :].rearrange(
                        "(s p) d -> p s d", p=128)),
                    waits=[ev_x1free[pb]] + (evXO_all if m < 3 else []),
                    sem="XC%d" % (m % 2))

            def prep(m, evX1):
                pb = m % 3
                for s4 in range(SUB):
                    stile = m * SUB + s4
                    e1 = P.s(lambda s, pb=pb, s4=s4, stile=stile: s.activation(
                        xn2[s4][:], x1buf[pb][:, s4 * D:(s4 + 1) * D], AF.Identity,
                        scale=r2c[:, stile:stile + 1], bias=q2c[:, stile:stile + 1]),
                        waits=[evX1, ev_xn2_free[s4]])
                    e1_evs[(m, s4)] = e1

            prep_x1 = [None] * nM
            prep_x1[0] = load_c(0)
            if nM > 1:
                prep_x1[1] = load_c(1)
            prep(0, prep_x1[0])

            gD = []
            for c in range(NCH):
                gD.append(P.dma("gpsimd", lambda g, c=c: g.indirect_dma_start(
                    out=wdstg[:, c * R:(c + 1) * R], out_offset=None,
                    in_=wd_in.rearrange("e d r -> (e d) r"),
                    in_offset=IndirectOffsetOnAxis(gci[:, c:c + 1], 0)),
                    waits=[k22], sem="GD%d" % (c % 2)))
            gD2 = P.dma("gpsimd", lambda g: g.dma_start(out=wd_sb[:], in_=wdstg[:]),
                        waits=[("GD0", P.cnt["GD0"]), ("GD1", P.cnt["GD1"])],
                        sem="GD0")
            gU = P.dma("gpsimd", lambda g: g.indirect_dma_start(
                out=wustg[:], out_offset=None,
                in_=wu_in.rearrange("e r d -> (e r) d"),
                in_offset=IndirectOffsetOnAxis(gui[:, 0:1], 0)), waits=[k25],
                sem="GU")
            gU2 = P.dma("gpsimd", lambda g: g.dma_start(out=wu_sb[:], in_=wustg[:]),
                        waits=[gU], sem="GU")
            ev_stg_free = [None, None, None]
            stgs = [wstg, wstg2, wstg3]
            for c in range(NCH):
                stg = stgs[c % 3]
                gsem = "GC%d" % (c % 3)
                gA = P.dma("gpsimd", lambda g, c=c, stg=stg: g.indirect_dma_start(
                    out=stg[:], out_offset=None,
                    in_=wc_in.rearrange("e o d -> (e o) d"),
                    in_offset=IndirectOffsetOnAxis(gci[:, c:c + 1], 0)),
                    waits=[k22, ev_stg_free[c % 3]], sem=gsem)
                gB = P.dma("gpsimd", lambda g, c=c, stg=stg: g.dma_start(
                    out=wccb[c * 128:(c + 1) * 128, :], in_=stg[:]), waits=[gA],
                    sem=gsem)
                ev_stg_free[c % 3] = gB
            bias_evs = {}

            def emit_bias_sel():
                e_oh = P.t(lambda t: t.transpose(
                    ptrC2[1].ap().bitcast(F32)[0:8, 32:33], lone[:],
                    identf[0:1, 0:1]),
                    waits=[k12, evI4, ev_ptrC_free[1]])
                e_ohc = P.s(lambda s: s.activation(
                    ohc[:], ptrC2[1].ap().bitcast(F32)[0:8, 32:33], AF.Copy),
                    waits=[e_oh])
                evm = []
                evm.append(P.t(lambda t: t.matmul(
                    pbias[:, 0:1], lhsT=bu_b[:, 0:128], rhs=ohc[:],
                    start=True, stop=False), waits=[e_ohc, WGc]))
                evm.append(P.t(lambda t: t.matmul(
                    pbias[0:64, 16:17], lhsT=bd_b[:], rhs=ohc[:],
                    start=False, stop=False), waits=[e_ohc, WGc]))
                for c in range(1, NCH):
                    evm.append(P.t(lambda t, c=c: t.matmul(
                        pbias[:, c:c + 1],
                        lhsT=bu_b[:, c * 128:(c + 1) * 128], rhs=ohc[:],
                        start=False, stop=False), waits=[e_ohc, WGc]))
                for c in range(NCH):
                    evm.append(P.t(lambda t, c=c: t.matmul(
                        pbias[:, 8 + c:9 + c],
                        lhsT=bc_b[:, c * 128:(c + 1) * 128], rhs=ohc[:],
                        start=False, stop=(c == NCH - 1)), waits=[e_ohc, WGc]))
                eb1 = P.s(lambda s: s.activation(buT[:], pbias[:, 0:8], AF.Copy),
                          waits=[evm[-1]])
                eb2 = P.s(lambda s: s.activation(bcT[:], pbias[:, 8:16], AF.Copy),
                          waits=[evm[-1]])
                eb3 = P.s(lambda s: s.activation(bdT[:], pbias[0:64, 16:17],
                                                 AF.Copy), waits=[evm[-1]])
                eb4 = P.v(lambda v: v.tensor_add(bcombT[:], bcT[:], bbT[:]),
                          waits=[eb2, WHs])
                bias_evs["bu"] = eb1
                bias_evs["bd"] = eb3
                bias_evs["bcomb"] = eb4
                ev_ptrC_free[1] = eb3
            ev_wc_tr = []
            ev_sb2sb = [("GC0", P.cnt["GC0"]), ("GC1", P.cnt["GC1"]),
                        ("GC2", P.cnt["GC2"]), ("GC3", P.cnt["GC3"]),
                        ("GD0", P.cnt["GD0"]), ("GD1", P.cnt["GD1"]),
                        ("GU", P.cnt["GU"])]
            for c in range(NCH):
                ev_wc_tr.append(P.dma("sync", lambda s, c=c: s.dma_start(
                    out=wcT[:, c * D:(c + 1) * D],
                    in_=wccb[:, c * 128:(c + 1) * 128], transpose=True),
                    waits=ev_sb2sb, sem="WT%d" % (c % 2)))
            WHwc = [("WT0", P.cnt["WT0"]), ("WT1", P.cnt["WT1"])]
            GDt = ("GD0", P.cnt["GD0"])
            GUt = ("GU", P.cnt["GU"])




            yev_evs = {}

            def Ctail(mm):
                xpb2 = mm % 3
                pb2 = mm % 2
                ev_yev = yev_evs.pop(mm)
                evX1t = prep_x1[mm]
                nonlocal ev_ybf_free, ev_ptry_free, ev_ybt_free
                nonlocal ev_tl_free, ev_x2_free, ev_x1free
                for s4 in range(SUB):
                    sp = s4 % 2
                    yb = ptry2[sp]
                    evT = []
                    for c in range(NCH):
                        evT.append(P.t(lambda t, c=c, s4=s4, yb=yb, pb2=pb2:
                                       t.transpose(
                            yb[:, c * 128:(c + 1) * 128],
                            ybf[pb2][:, c * 512 + s4 * 128:c * 512 + (s4 + 1) * 128],
                            ident[:]),
                            waits=ev_yev + [ev_ptry_free[sp], evI2]))
                    if s4 == SUB - 1:
                        ev_ybf_free[pb2] = evT[-1]
                    eE = P.v(lambda v, sp=sp, yb=yb: v.tensor_copy(
                        ybt[sp][:], yb[:]),
                             waits=[evT[-1], ev_ybt_free[sp]])
                    ev_ptry_free[sp] = eE
                    ev_rr3, rr3_done = _ln(
                        P, bn6c[sp], st3[sp], rr3[sp],
                        (lambda sp=sp: ybt[sp][:]),
                        ln_state, ("C", sp), extra_waits=[eE])
                    b7 = P.v(lambda v, sp=sp: v.tensor_scalar(
                        tl[sp][:], ybt[sp][:], rr3[sp][:, 0:1],
                        rr3[sp][:, 1:2], ALU.mult, ALU.add),
                        waits=[ev_rr3, ev_tl_free[sp]])
                    rr3_done(b7)
                    ev_ybt_free[sp] = b7
                    b8 = P.v(lambda v, sp=sp, s4=s4, xpb2=xpb2: v.tensor_add(
                        x2b[sp][:], tl[sp][:],
                        x1buf[xpb2][:, s4 * D:(s4 + 1) * D]),
                        waits=[b7, evX1t, ev_x2_free[sp]])
                    ev_tl_free[sp] = b8
                    if s4 == SUB - 1:
                        ev_x1free[xpb2] = b8
                    stile = mm * SUB + s4
                    evX2 = P.dma("sync", lambda e, stile=stile, sp=sp: e.dma_start(
                        out=y_out[stile * 128:(stile + 1) * 128, :],
                        in_=x2b[sp][:]),
                        waits=[b8], sem="X2O%d" % sp)
                    ev_x2_free[sp] = evX2

            for m in range(nM):
                pb = m % 2
                xpb = m % 3
                evX1 = prep_x1[m]
                ev_evE = []
                for s4 in range(SUB):
                    trb = ptrC2[s4 % 2]
                    evT = []
                    for c in range(NCH):
                        evT.append(P.t(lambda t, c=c, s4=s4, trb=trb: t.transpose(
                            trb[:, c * 128:(c + 1) * 128],
                            xn2[s4][:, c * 128:(c + 1) * 128], ident[:]),
                            waits=[e1_evs[(m, s4)], evI2, ev_ptrC_free[s4 % 2]]))
                    ev_xn2_free[s4] = evT[-1]
                    eE = P.s(lambda s, s4=s4, trb=trb: s.activation(
                        xn2T.ap().rearrange("p (c n) -> p c n", c=NCH)[:, :, s4 * 128:(s4 + 1) * 128],
                        trb.ap().rearrange("p (c n) -> p c n", c=NCH), AF.Copy),
                        waits=[evT[-1]])
                    ev_ptrC_free[s4 % 2] = eE
                    ev_evE.append(eE)
                if m + 1 < nM:
                    prep(m + 1, prep_x1[m + 1])
                if m == 0:
                    emit_bias_sel()
                evDn = None
                for c in range(NCH):
                    evDn = P.t(lambda t, c=c: t.matmul(
                        ph,
                        lhsT=wd_sb[:, c * R:(c + 1) * R],
                        rhs=x2nT(c, 0, 512),
                        start=(c == 0), stop=(c == NCH - 1)),
                        waits=ev_evE + [GDt, ev_ph_free] + (ev_pa_free[0] or []))
                evRl = P.s(lambda s: s.activation(hbf[:], ph, AF.Relu,
                                                  bias=bdT[:]),
                           waits=[evDn, bias_evs["bd"], ev_hb_free])
                ev_ph_free = evRl
                ev_up = []
                for c in range(NCH):
                    bank = (c + 1) % 2
                    evU = P.t(lambda t, c=c, bank=bank: t.matmul(
                        pa[bank][:],
                        lhsT=wu_sb[:, c * 128:(c + 1) * 128],
                        rhs=hbf[:],
                        start=True, stop=True),
                        waits=[evRl, GUt] + (ev_pa_free[bank] or []))
                    evUe = P.s(lambda s, c=c, bank=bank: s.activation(
                        aT[:, c * 512:(c + 1) * 512],
                        pa[bank][:], AF.Identity, bias=buT[:, c:c + 1]),
                        waits=[evU, bias_evs["bu"]] + ([ev_aT_free] if c == 0 else []))
                    ev_pa_free[bank] = [evUe]
                    ev_up.append(evUe)
                ev_hb_free = ev_up[-1]
                if m >= 1:
                    Ctail(m - 1)
                if m + 2 < nM:
                    prep_x1[m + 2] = load_c(m + 2)
                ev_yev = []
                for co in range(NCH):
                    bank = co % 2
                    evY = None
                    for c in range(NCH):
                        evY = P.t(lambda t, c=c, co=co, bank=bank: t.matmul(
                            py[bank][:],
                            lhsT=bwT[:, c * D + co * 128:c * D + (co + 1) * 128],
                            rhs=x2nT(c, 0, 512),
                            start=(c == 0), stop=False),
                            waits=ev_evE + WTs + [ev_py_free[bank]])
                    for c in range(NCH):
                        evY = P.t(lambda t, c=c, co=co, bank=bank: t.matmul(
                            py[bank][:],
                            lhsT=wcT[:, c * D + co * 128:c * D + (co + 1) * 128],
                            rhs=aT[:, c * 512:(c + 1) * 512],
                            start=False, stop=(c == NCH - 1)),
                            waits=ev_up + WHwc)
                    eYe = P.s(lambda s, co=co, bank=bank, pb=pb: s.activation(
                        ybf[pb][:, co * 512:(co + 1) * 512],
                        py[bank][:], AF.Identity, bias=bcombT[:, co:co + 1]),
                        waits=[evY, bias_evs["bcomb"]] + ([ev_ybf_free[pb]] if co == 0 else []))
                    ev_py_free[bank] = eYe
                    ev_yev.append(eYe)
                ev_aT_free = ev_yev[-1]
                yev_evs[m] = ev_yev

            Ctail(nM - 1)

            P.op("sync", None, waits=[("X2O0", P.cnt["X2O0"]),
                                      ("X2O1", P.cnt["X2O1"])])

        with nc.Block() as blk:
            P.emit(blk)
    return nc


def kernel(L0_attn_x, L0_attn_W_qkv, L0_attn_W_out, L0_attn_b_out,
           L0_ffn_base_W, L0_ffn_base_b, L0_ffn_disc_mu, L0_ffn_disc_sigma,
           L0_ffn_down_W, L0_ffn_down_b, L0_ffn_up_W, L0_ffn_up_b,
           L0_ffn_conv_W, L0_ffn_conv_b):
    x = np.ascontiguousarray(np.asarray(L0_attn_x, dtype=np.float32))
    B, N, _ = x.shape
    T = N // 2
    shared = {
        "wqkv": np.ascontiguousarray(L0_attn_W_qkv, dtype=np.float32),
        "wout": np.ascontiguousarray(L0_attn_W_out, dtype=np.float32),
        "bout": np.ascontiguousarray(L0_attn_b_out, dtype=np.float32),
        "basew": np.ascontiguousarray(L0_ffn_base_W, dtype=np.float32),
        "baseb": np.ascontiguousarray(L0_ffn_base_b, dtype=np.float32),
        "mu": np.ascontiguousarray(L0_ffn_disc_mu, dtype=np.float32),
        "sigma": np.ascontiguousarray(L0_ffn_disc_sigma, dtype=np.float32),
        "wd": np.ascontiguousarray(L0_ffn_down_W, dtype=np.float32),
        "bd": np.ascontiguousarray(L0_ffn_down_b, dtype=np.float32),
        "wu": np.ascontiguousarray(L0_ffn_up_W, dtype=np.float32),
        "bu": np.ascontiguousarray(L0_ffn_up_b, dtype=np.float32),
        "wc": np.ascontiguousarray(L0_ffn_conv_W, dtype=np.float32),
        "bc": np.ascontiguousarray(L0_ffn_conv_b, dtype=np.float32),
    }
    nc = build_program(T=T, t_total=N)
    in_maps = []
    for c in range(8):
        b, h = c // 2, c % 2
        mp = dict(shared)
        mp["x"] = np.ascontiguousarray(x[b, h * T:(h + 1) * T, :])
        in_maps.append(mp)
    res = run_bass_kernel_spmd(nc, in_maps, core_ids=list(range(8)))
    out = np.empty((B, N, D), dtype=np.float32)
    for c in range(8):
        b, h = c // 2, c % 2
        out[b, h * T:(h + 1) * T, :] = res.results[c]["y"]
    return out


# version 17
# speedup vs baseline: 1.4291x; 1.0359x over previous
from collections import defaultdict
from contextlib import ExitStack

import numpy as np

import concourse.bass as bass
import concourse.mybir as mybir
from concourse.bass import IndirectOffsetOnAxis
from concourse.bass_utils import run_bass_kernel_spmd

F32 = mybir.dt.float32
BF16 = mybir.dt.bfloat16
I32 = mybir.dt.int32
AF = mybir.ActivationFunctionType
ALU = mybir.AluOpType
AX = mybir.AxisListType

D = 1024
H = 16
NCH = 8
R = 64
E = 8
EPS = 1e-5

ENGINES = ("sync", "gpsimd", "tensor", "vector", "scalar")


class Plan:

    def __init__(self, nc):
        self.nc = nc
        self.ops = {e: [] for e in ENGINES}
        self.cnt = defaultdict(int)
        self.handles = {}

    def sem(self, stack, name):
        self.handles[name] = stack.enter_context(self.nc.semaphore(name))

    ENG_SEMS = ("T", "V", "S", "G", "BAR")

    def op(self, eng, fn, waits=(), sem=None, inc=1):
        ev = None
        ws = [w for w in waits if w is not None]
        for w in ws:
            assert isinstance(w, tuple) and len(w) == 2, f"bad wait {w!r}"
        if sem is not None:
            if sem not in self.ENG_SEMS and self.cnt[sem] > 0:
                ws.append((sem, self.cnt[sem]))
            self.cnt[sem] += inc
            ev = (sem, self.cnt[sem])
        self.ops[eng].append((fn, ws, (sem, inc) if sem else None))
        return ev

    def t(self, fn, waits=()):
        return self.op("tensor", fn, waits, "T")

    def v(self, fn, waits=()):
        return self.op("vector", fn, waits, "V")

    def s(self, fn, waits=()):
        return self.op("scalar", fn, waits, "S")

    def g(self, fn, waits=()):
        return self.op("gpsimd", fn, waits, "G")

    def dma(self, eng, fn, waits=(), sem="WG"):
        return self.op(eng, fn, waits, sem, inc=16)

    def barrier(self):
        self.cnt["BAR"] += len(ENGINES)
        tgt = self.cnt["BAR"]
        for e in ENGINES:
            self.ops[e].append((lambda eng: eng.nop(), [], ("BAR", 1)))
            self.ops[e].append((None, [("BAR", tgt)], None))

    OWN_SEM = {"tensor": "T", "vector": "V", "scalar": "S", "gpsimd": "G"}

    def emit(self, blk):
        for eng in ENGINES:
            ops = self.ops[eng]
            if not ops:
                continue

            def body(e, ops=ops, own=self.OWN_SEM.get(eng)):
                last = {}
                for fn, waits, inc in ops:
                    for (s, v) in waits:
                        if last.get(s, -1) >= v:
                            continue
                        e.wait_ge(self.handles[s], v)
                        last[s] = v
                    if fn is None:
                        continue
                    inst = fn(e)
                    if inc is not None and inc[0] is not None:
                        inst.then_inc(self.handles[inc[0]], inc[1])
            getattr(blk, eng)(body)


def _ln(P, bn6, st, rr, src_fn, state, key, extra_waits=()):
    prev = state.get(key, {})
    e1a = P.v(lambda v: v.bn_stats(bn6[:, 0:6], src_fn()[:, 0:512]),
              waits=list(extra_waits) + [prev.get("aggr")])
    e1 = P.v(lambda v: v.bn_stats(bn6[:, 6:12], src_fn()[:, 512:1024]),
             waits=[e1a])
    e2 = P.v(lambda v: v.bn_aggr(st[:], bn6[:]),
             waits=[e1, prev.get("st_read")])
    e3 = P.v(lambda v: v.tensor_scalar(rr[:, 0:1], st[:, 1:2], EPS, None,
                                       ALU.add),
             waits=[e2, prev.get("rr_read")])
    e4 = P.s(lambda s: s.activation(rr[:, 0:1], rr[:, 0:1], AF.Ln), waits=[e3])
    e5 = P.s(lambda s: s.activation(rr[:, 0:1], rr[:, 0:1], AF.Exp, scale=-0.5),
             waits=[e4])
    e6 = P.v(lambda v: v.tensor_scalar(rr[:, 1:2], st[:, 0:1], rr[:, 0:1],
                                       -1.0, ALU.mult, ALU.mult),
             waits=[e5, e2])
    st_new = {"aggr": e2, "st_read": e6}
    state[key] = st_new

    def set_last_reader(ev):
        st_new["rr_read"] = ev
    return e6, set_last_reader


def build_program(T=4096, t_total=None, debug_dump=False, phases="ABC", bstop=None):
    if t_total is None:
        t_total = 2 * T
    nS = T // 128
    nM = T // 512
    SUB = 4

    nc = bass.Bass("TRN2", target_bir_lowering=False)

    x_in = nc.dram_tensor("x", [T, D], F32, kind="ExternalInput").ap()
    wqkv = nc.dram_tensor("wqkv", [D, 3 * D], F32, kind="ExternalInput").ap()
    wout = nc.dram_tensor("wout", [D, D], F32, kind="ExternalInput").ap()
    bout = nc.dram_tensor("bout", [D], F32, kind="ExternalInput").ap()
    basew = nc.dram_tensor("basew", [D, D], F32, kind="ExternalInput").ap()
    baseb = nc.dram_tensor("baseb", [D], F32, kind="ExternalInput").ap()
    mu_in = nc.dram_tensor("mu", [E, D], F32, kind="ExternalInput").ap()
    sig_in = nc.dram_tensor("sigma", [E, D], F32, kind="ExternalInput").ap()
    wd_in = nc.dram_tensor("wd", [E, D, R], F32, kind="ExternalInput").ap()
    bd_in = nc.dram_tensor("bd", [E, R], F32, kind="ExternalInput").ap()
    wu_in = nc.dram_tensor("wu", [E, R, D], F32, kind="ExternalInput").ap()
    bu_in = nc.dram_tensor("bu", [E, D], F32, kind="ExternalInput").ap()
    wc_in = nc.dram_tensor("wc", [E, D, D], F32, kind="ExternalInput").ap()
    bc_in = nc.dram_tensor("bc", [E, D], F32, kind="ExternalInput").ap()
    y_out = nc.dram_tensor("y", [T, D], F32, kind="ExternalOutput").ap()

    qTd = nc.dram_tensor("qTd", [nM, 128, 4096], BF16).ap()
    x1d = nc.dram_tensor("x1d", [T, D], F32).ap()
    wscb = nc.dram_tensor("wscb", [D, D], BF16).ap()
    wccb = nc.dram_tensor("wccb", [D, D], BF16).ap()
    cc1i = nc.dram_tensor("cc1i", [129, D], F32).ap()
    cc1o = nc.dram_tensor("cc1o", [129, D], F32).ap()
    cc2i = nc.dram_tensor("cc2i", [4, 2048], F32).ap()
    cc2o = nc.dram_tensor("cc2o", [4, 2048], F32).ap()
    idxd = nc.dram_tensor("idxd", [1, 1], I32).ap()
    lossd = nc.dram_tensor("lossd", [1, 8], F32).ap()
    if debug_dump:
        kdbg = nc.dram_tensor("kdbg", [T, D], BF16).ap()
        vdbg = nc.dram_tensor("vdbg", [T, D], BF16).ap()
        adbg = nc.dram_tensor("adbg", [T, D], BF16).ap()

    P = Plan(nc)
    RG = [[0, 1], [2, 3], [4, 5], [6, 7]]

    with ExitStack() as st:
        def sb(name, shape, dt):
            return st.enter_context(nc.sbuf_tensor(name, shape, dt))

        for s in ("T", "V", "S", "G", "BAR", "CC", "WH", "WG", "LB",
                  "XO0", "XO1", "GD0", "GD1", "GU", "GC0", "GC1", "WT0", "WT1",
                  "XA0", "XA1", "QO0", "QO1", "QI0", "QI1", "WB", "GC2", "GC3",
                  "X2O0", "X2O1", "XC0", "XC1", "DBG"):
            P.sem(st, s)

        ident = sb("ident", [128, 128], BF16)
        identf = sb("identf", [8, 8], F32)
        ones_b = sb("ones_b", [128, 1], BF16)
        ones_f = sb("ones_f", [128, 1], F32)
        iota_p = sb("iota_p", [128, 1], I32)
        iota8x = sb("iota8x", [128, 8], I32)
        iota8r = sb("iota8r", [1, 8], I32)
        bwT = sb("bwT", [128, NCH * D], BF16)
        bcombT = sb("bcombT", [128, NCH], F32)
        bbT = sb("bbT", [128, NCH], F32)
        bcT = sb("bcT", [128, NCH], F32)
        buT = sb("buT", [128, NCH], F32)
        bdT = sb("bdT", [64, 1], F32)
        loss8 = sb("loss8", [8, 1], F32)
        lrow = sb("lrow", [1, 8], F32)
        lmin = sb("lmin", [1, 1], F32)
        lone = sb("lone", [1, 8], F32)
        liot = sb("liot", [1, 8], F32)
        idxf1 = sb("idxf1", [1, 1], F32)
        idxi1 = sb("idxi1", [1, 1], I32)
        idxb = sb("idxb", [128, 1], I32)
        idxbf = sb("idxbf", [128, 1], F32)
        idc = sb("idc", [128, 1], F32)
        idu = sb("idu", [64, 1], F32)
        gcf = sb("gcf", [128, 8], F32)
        gci = sb("gci", [128, 8], I32)
        guf = sb("guf", [64, 1], F32)
        gui = sb("gui", [64, 1], I32)
        kvz_r = sb("kvz_r", [128, 8 * 130], BF16)
        zrow = sb("zrow", [1, 1024], BF16)
        r2c = sb("r2c", [128, nS], F32)
        q2c = sb("q2c", [128, nS], F32)
        qc2 = sb("qc2", [128, nS], F32)

        evI1 = P.g(lambda g: g.memset(ident[:], 0.0))
        evI2 = P.g(lambda g: g.affine_select(
            out=ident[:], in_=ident[:], compare_op=ALU.not_equal, fill=1.0,
            base=0, pattern=[[-1, 128]], channel_multiplier=1), waits=[evI1])
        evI3 = P.g(lambda g: g.memset(identf[:], 0.0))
        evI4 = P.g(lambda g: g.affine_select(
            out=identf[:], in_=identf[:], compare_op=ALU.not_equal, fill=1.0,
            base=0, pattern=[[-1, 8]], channel_multiplier=1), waits=[evI3])
        evOB = P.g(lambda g: g.memset(ones_b[:], 1.0))
        evOF = P.g(lambda g: g.memset(ones_f[:], 1.0))
        evIp = P.g(lambda g: g.iota(iota_p[:], pattern=[[0, 1]], base=0,
                                    channel_multiplier=1))
        evI8 = P.g(lambda g: g.iota(iota8x[:], pattern=[[128, 8]], base=0,
                                    channel_multiplier=1))
        evIr = P.g(lambda g: g.iota(iota8r[:], pattern=[[1, 8]], base=0,
                                    channel_multiplier=0))
        evWb = P.dma("gpsimd", lambda g: g.dma_start(out=wscb, in_=basew))
        evBW = []
        for c in range(NCH):
            evBW.append(P.dma("scalar", lambda s, c=c: s.dma_start(
                out=bwT[:, c * D:(c + 1) * D],
                in_=wscb[:, c * 128:(c + 1) * 128], transpose=True),
                waits=[evWb], sem="WT%d" % (c % 2)))
        evBB = []
        for c in range(NCH):
            evBB.append(P.dma("sync", lambda e, c=c: e.dma_start(
                out=bbT[:, c:c + 1],
                in_=baseb[c * 128:(c + 1) * 128].unsqueeze(1)), sem="WH"))
        WHs = ("WH", P.cnt["WH"])
        WTs = [("WT0", P.cnt["WT0"]), ("WT1", P.cnt["WT1"])]

        with ExitStack() as stA:
            def sba(n, sh, dt):
                return stA.enter_context(nc.sbuf_tensor(n, sh, dt))
            def psa(n, sh, dt):
                return stA.enter_context(nc.psum_tensor(n, sh, dt))

            wq_sb = sba("wq_sb", [128, NCH * 3 * D], BF16)
            kvsb = sba("kvsb", [128, 1024], F32)
            zsb = sba("zsb", [1, 1024], F32)
            xbuf = [sba("xA0", [128, 4096], F32), sba("xA1", [128, 4096], F32)]
            xn1 = [sba("xn1_%d" % i, [128, D], BF16) for i in range(4)]
            st1 = [sba("st1_0", [128, 2], F32), sba("st1_1", [128, 2], F32)]
            bn6 = [sba("bn6_0", [128, 12], F32), sba("bn6_1", [128, 12], F32)]
            rr1 = [sba("rr1_0", [128, 2], F32), sba("rr1_1", [128, 2], F32)]
            xn1T = sba("xn1T", [128, NCH * 512], BF16)
            kbf = [sba("kbf0", [128, D], BF16), sba("kbf1", [128, D], BF16)]
            vbf = [sba("vbf0", [128, D], BF16), sba("vbf1", [128, D], BF16)]
            texp = [sba("texp0", [128, 512], F32), sba("texp1", [128, 512], F32)]
            tqxp = [sba("tqxp0", [128, 512], F32), sba("tqxp1", [128, 512], F32)]
            ktmp = [sba("ktmp0", [128, 512], F32), sba("ktmp1", [128, 512], F32)]
            qtmp = [sba("qtmp0", [128, 512], F32), sba("qtmp1", [128, 512], F32)]
            kt2 = [sba("kt2_0", [128, 512], F32), sba("kt2_1", [128, 512], F32)]
            qt2 = [sba("qt2_0", [128, 512], F32), sba("qt2_1", [128, 512], F32)]
            qTs = [sba("qTs0", [128, 4096], BF16), sba("qTs1", [128, 4096], BF16)]

            pkv = [psa("pkv0", [128, 512], F32), psa("pkv1", [128, 512], F32)]
            pz = [psa("pz0", [1, 512], F32), psa("pz1", [1, 512], F32)]
            pmm = [psa("pmm0", [128, 512], F32), psa("pmm1", [128, 512], F32)]
            ptrA = psa("ptrA", [128, 1024], BF16)
            pq = psa("pq", [128, 512], F32)

            def x1nT(c, lo, hi):
                return xn1T[:, c * 512 + lo:c * 512 + hi]

            P.dma("gpsimd", lambda g: g.dma_start(
                out=wq_sb.ap().rearrange("p (c j) -> p c j", c=NCH), in_=wqkv.rearrange("(c p) j -> p c j", p=128)))
            WGs = ("WG", P.cnt["WG"])

            ln_state = {}
            ev_xfree = [None, None]
            ev_kread = [None, None]
            ev_texp_read = [None, None]
            ev_tq_read = [None, None]
            ev_ptr_free = None
            ev_pq_free = None
            ev_pmm_free = [None, None]
            ev_qT_free = [None, None]
            ev_xn1_free = [None, None, None, None]
            ev_kv_done = []
            ev_z_done = []
            ev_qT_out = []
            ev_xn1T_read = None

            pool_banks = [pmm[0], pmm[1], pq]
            pool_free = [None, None, None]
            pool_i = [0]

            def take_bank():
                i = pool_i[0] % 3
                pool_i[0] += 1
                return i, pool_banks[i], pool_free[i] or []

            def prepA(m):
                pb = m % 2
                evX = P.dma("sync", lambda e, m=m, pb=pb: e.dma_start(
                    out=xbuf[pb].ap().rearrange("p (s d) -> p s d", s=4),
                    in_=x_in[m * 512:(m + 1) * 512, :].rearrange(
                        "(s p) d -> p s d", p=128)),
                    waits=[ev_xfree[pb]], sem="XA%d" % pb)
                applies = []
                for s4 in range(SUB):
                    sp = s4 % 2
                    ev_rr, rr_done = _ln(
                        P, bn6[sp], st1[sp], rr1[sp],
                        (lambda pb=pb, s4=s4: xbuf[pb][:, s4 * D:(s4 + 1) * D]),
                        ln_state, ("A", sp), extra_waits=[evX])
                    e7 = P.s(lambda s, pb=pb, s4=s4, sp=sp: s.activation(
                        xn1[s4][:], xbuf[pb][:, s4 * D:(s4 + 1) * D], AF.Identity,
                        scale=rr1[sp][:, 0:1], bias=rr1[sp][:, 1:2]),
                        waits=[ev_rr, ev_xn1_free[s4]])
                    rr_done(e7)
                    applies.append(e7)
                return evX, applies

            prepdata = [None] * nM
            prepdata[0] = prepA(0)

            for m in range(nM):
                pb = m % 2
                evX, applies = prepdata[m]
                ev_macro_evac = []
                for s4 in range(SUB):
                    evT = []
                    for c in range(NCH):
                        evT.append(P.t(lambda t, c=c, s4=s4: t.transpose(
                            ptrA[:, c * 128:(c + 1) * 128],
                            xn1[s4][:, c * 128:(c + 1) * 128], ident[:]),
                            waits=[applies[s4], evI2, ev_ptr_free]))
                    ev_xn1_free[s4] = evT[-1]
                    evE = P.s(lambda s, s4=s4: s.activation(
                        xn1T.ap().rearrange("p (c n) -> p c n", c=NCH)[:, :, s4 * 128:(s4 + 1) * 128],
                        ptrA.ap().rearrange("p (c n) -> p c n", c=NCH), AF.Copy),
                        waits=[evT[-1], ev_xn1T_read if s4 == 0 and m > 0 else None])
                    ev_ptr_free = evE
                    ev_macro_evac.append(evE)
                ev_xfree[pb] = applies[-1]
                if m + 1 < nM:
                    prepdata[m + 1] = prepA(m + 1)

                for s4 in range(SUB):
                    stile = m * SUB + s4
                    sp = s4 % 2
                    evE = ev_macro_evac[s4]
                    ev_k_ready = None
                    ev_v_ready = None
                    for ci in range(4):
                        bi, bank_t, bank_w = take_bank()
                        evMM = None
                        for c in range(NCH):
                            evMM = P.t(lambda t, c=c, ci=ci, bank_t=bank_t, s4=s4: t.matmul(
                                bank_t[:],
                                lhsT=x1nT(c, s4 * 128, (s4 + 1) * 128),
                                rhs=wq_sb[:, c * 3 * D + D + ci * 512:
                                          c * 3 * D + D + (ci + 1) * 512],
                                start=(c == 0), stop=(c == NCH - 1)),
                                waits=[evE, WGs] + bank_w)
                        if ci < 2:
                            eU = P.s(lambda s, bank_t=bank_t, ci=ci: s.activation(
                                texp[ci][:], bank_t[:], AF.Relu, scale=-1.0),
                                waits=[evMM, ev_texp_read[ci],
                                       ev_kread[sp] if ci == 0 else None])
                            eW = P.s(lambda s, ci=ci: s.activation(
                                ktmp[ci][:], texp[ci][:], AF.Exp, scale=-1.0),
                                waits=[eU])
                            eT = P.v(lambda v, bank_t=bank_t, ci=ci: v.tensor_add(
                                kt2[ci][:], bank_t[:], texp[ci][:]),
                                waits=[eU])
                            eE2 = P.v(lambda v, ci=ci, sp=sp: v.tensor_add(
                                kbf[sp][:, ci * 512:(ci + 1) * 512], kt2[ci][:],
                                ktmp[ci][:]),
                                waits=[eW, eT])
                            ev_texp_read[ci] = eE2
                            pool_free[bi] = [eU, eT]
                            if ci == 1:
                                ev_k_ready = eE2
                        else:
                            eE2 = P.s(lambda s, bank_t=bank_t, ci=ci, sp=sp: s.copy(
                                vbf[sp][:, (ci - 2) * 512:(ci - 1) * 512],
                                bank_t[:]),
                                waits=[evMM, ev_kread[sp] if ci == 2 else None])
                            pool_free[bi] = [eE2]
                            if ci == 3:
                                ev_v_ready = eE2
                    evKV = []
                    for p8 in range(8):
                        half = p8 // 4
                        evKV.append(P.t(lambda t, p8=p8, half=half, sp=sp, stile=stile: t.matmul(
                            pkv[half][:, (p8 % 4) * 128:(p8 % 4 + 1) * 128],
                            lhsT=kbf[sp][:, p8 * 128:(p8 + 1) * 128],
                            rhs=vbf[sp][:, p8 * 128:(p8 + 1) * 128],
                            start=(stile == 0 and p8 % 4 == 0),
                            stop=(stile == nS - 1 and p8 % 4 == 3)),
                            waits=[ev_k_ready, ev_v_ready]))
                    evZ = []
                    for half in range(2):
                        evZ.append(P.t(lambda t, half=half, sp=sp, stile=stile: t.matmul(
                            pz[half][:],
                            lhsT=ones_b[:],
                            rhs=kbf[sp][:, half * 512:(half + 1) * 512],
                            start=(stile == 0), stop=(stile == nS - 1)),
                            waits=[ev_k_ready, evOB]))
                    if debug_dump:
                        P.dma("gpsimd", lambda g, sp=sp, stile=stile: g.dma_start(
                            out=kdbg[stile * 128:(stile + 1) * 128, :],
                            in_=kbf[sp][:]), waits=[ev_k_ready], sem="DBG")
                        P.dma("gpsimd", lambda g, sp=sp, stile=stile: g.dma_start(
                            out=vdbg[stile * 128:(stile + 1) * 128, :],
                            in_=vbf[sp][:]), waits=[ev_v_ready], sem="DBG")
                    ev_kread[sp] = evZ[-1] if not debug_dump else ("DBG", P.cnt["DBG"])
                    if stile == nS - 1:
                        ev_kv_done = evKV
                        ev_z_done = evZ

                ev_q_evac = []
                for j in range(NCH):
                    bi, bank_t, bank_w = take_bank()
                    evQ = None
                    for c in range(NCH):
                        evQ = P.t(lambda t, c=c, j=j, bank_t=bank_t: t.matmul(
                            bank_t[:],
                            lhsT=wq_sb[:, c * 3 * D + j * 128:
                                       c * 3 * D + (j + 1) * 128],
                            rhs=x1nT(c, 0, 512),
                            start=(c == 0), stop=(c == NCH - 1)),
                            waits=[ev_macro_evac[-1], WGs] + bank_w)
                    sp = j % 2
                    eU = P.s(lambda s, sp=sp, bank_t=bank_t: s.activation(
                        tqxp[sp][:], bank_t[:], AF.Relu, scale=-1.0),
                        waits=[evQ, ev_tq_read[sp]])
                    eW = P.s(lambda s, sp=sp: s.activation(
                        qtmp[sp][:], tqxp[sp][:], AF.Exp, scale=-1.0),
                        waits=[eU])
                    eT = P.v(lambda v, sp=sp, bank_t=bank_t: v.tensor_add(
                        qt2[sp][:], bank_t[:], tqxp[sp][:]),
                        waits=[eU])
                    eQ2 = P.v(lambda v, j=j, sp=sp, pb=pb: v.tensor_add(
                        qTs[pb][:, j * 512:(j + 1) * 512], qt2[sp][:],
                        qtmp[sp][:]),
                        waits=[eW, eT, ev_qT_free[pb] if j == 0 else None])
                    ev_tq_read[sp] = eQ2
                    ev_q_evac.append(eQ2)
                    pool_free[bi] = [eU, eT]
                ev_xn1T_read = evQ
                evQO = P.dma("gpsimd", lambda g, m=m, pb=pb: g.dma_start(
                    out=qTd[m], in_=qTs[pb][:]),
                    waits=ev_q_evac, sem="QO%d" % pb)
                ev_qT_free[pb] = evQO
                ev_qT_out.append(evQO)

            eK0 = P.v(lambda v: v.tensor_copy(kvsb[:, 0:512], pkv[0][:]),
                      waits=ev_kv_done)
            eK1 = P.v(lambda v: v.tensor_copy(kvsb[:, 512:1024], pkv[1][:]),
                      waits=[eK0])
            eZ0 = P.v(lambda v: v.tensor_copy(zsb[:, 0:512], pz[0][:]),
                      waits=ev_z_done + [eK1])
            eZ1 = P.v(lambda v: v.tensor_copy(zsb[:, 512:1024], pz[1][:]),
                      waits=[eZ0])
            evD1 = P.dma("gpsimd", lambda g: g.dma_start(
                out=cc1i[0:128, :], in_=kvsb[:]), waits=[eK1])
            evD2 = P.dma("gpsimd", lambda g: g.dma_start(
                out=cc1i[128:129, :], in_=zsb[:]), waits=[eZ1])
            evCC1 = P.op("gpsimd", lambda g: g.collective_compute(
                "AllReduce", ALU.add, replica_groups=RG,
                ins=[cc1i.opt()], outs=[cc1o.opt()]),
                waits=[("WG", P.cnt["WG"])], sem="CC", inc=1)

            evKz = P.g(lambda g: g.memset(kvz_r[:], 0.0))
            kvv = lambda ap: ap.rearrange("p (g c) -> p g c", c=130)
            cco = lambda ap: ap.rearrange("p (g c) -> p g c", c=128)
            P.dma("gpsimd", lambda g: g.dma_start(
                out=kvv(kvz_r[0:64, :])[:, :, 0:64],
                in_=cco(cc1o[0:64, :])[:, :, 0:64]),
                waits=[evCC1, evKz])
            P.dma("gpsimd", lambda g: g.dma_start(
                out=kvv(kvz_r[64:128, :])[:, :, 65:129],
                in_=cco(cc1o[64:128, :])[:, :, 64:128]),
                waits=[evCC1, evKz])
            evZr = P.dma("gpsimd", lambda g: g.dma_start(
                out=zrow[:], in_=cc1o[128:129, :]), waits=[evCC1])
            evZt = []
            for c in range(NCH):
                evZt.append(P.t(lambda t, c=c: t.transpose(
                    ptrA[:, 2 * c:2 * c + 1], zrow[0:1, c * 128:(c + 1) * 128],
                    ident[0:1, 0:1]), waits=[evZr, evI2, ev_ptr_free]))
            zc1 = P.s(lambda s: s.activation(
                kvv(kvz_r[0:64, :])[:, :, 64:65],
                ptrA[0:64, 0:16:2].unsqueeze(2),
                AF.Copy), waits=[evZt[-1], evKz])
            zc2 = P.s(lambda s: s.activation(
                kvv(kvz_r[64:128, :])[:, :, 129:130],
                ptrA[64:128, 0:16:2].unsqueeze(2),
                AF.Copy), waits=[evZt[-1], evKz])
            ev_rhs = ("WG", P.cnt["WG"])
            ev_rhs2 = [zc1, zc2]

            P.barrier()

        if phases == "A":
            P.op("sync", None, waits=[("CC", 1)])
        if "B" in phases and True:
         with ExitStack() as stB:
            def sbb(n, sh, dt):
                return stB.enter_context(nc.sbuf_tensor(n, sh, dt))
            def psb(n, sh, dt):
                return stB.enter_context(nc.psum_tensor(n, sh, dt))

            wo_sb = sbb("wo_sb", [128, NCH * D], BF16)
            ones_r = sbb("ones_r", [1, 128], BF16)
            bo_row = sbb("bo_row", [1, D], BF16)
            ms_sb = sbb("ms_sb", [3, 2048], F32)
            qs_sb = sbb("qs_sb", [2, 1], F32)
            zr2 = sbb("zr2", [1, 2048], F32)
            qrp = sbb("qrp", [128, 2], F32)
            xbufB = [sbb("xB0", [128, 4096], F32), sbb("xB1", [128, 4096], F32)]
            qTb = [sbb("qTb0", [128, 4096], BF16), sbb("qTb1", [128, 4096], BF16)]
            attn = [sbb("attn0", [128, D], BF16), sbb("attn1", [128, D], BF16)]
            attnT = [sbb("attnT0", [128, D], BF16), sbb("attnT1", [128, D], BF16)]
            rden = [sbb("rden0", [128, 16], F32), sbb("rden1", [128, 16], F32)]
            aout = [sbb("aout%d" % i, [128, D], F32) for i in range(8)]
            aln = [sbb("aln0", [128, D], F32), sbb("aln1", [128, D], F32)]
            x1b_ = [sbb("x1b0", [128, D], F32), sbb("x1b1", [128, D], F32)]
            st2 = [sbb("st2_0", [128, 2], F32), sbb("st2_1", [128, 2], F32)]
            bn6b = [sbb("bn6b_0", [128, 12], F32), sbb("bn6b_1", [128, 12], F32)]
            rr2 = [sbb("rr2_0", [128, 2], F32), sbb("rr2_1", [128, 2], F32)]
            xcat = [sbb("xcat0", [128, 2048], BF16), sbb("xcat1", [128, 2048], BF16)]
            rv3f = [sbb("rv3f_0", [128, 3], F32), sbb("rv3f_1", [128, 3], F32)]
            rv3 = [sbb("rv3_0", [128, 3], BF16), sbb("rv3_1", [128, 3], BF16)]

            png = [psb("png0", [128, 390], F32), psb("png1", [128, 390], F32)]
            ptrB = psb("ptrB", [128, 1024], BF16)
            pwo = psb("pwo", [128, 512], F32)
            pm = [psb("pm0", [3, 512], F32), psb("pm1", [3, 512], F32),
                  psb("pm2", [3, 512], F32), psb("pm3", [3, 512], F32)]
            pqs_view = None

            P.dma("gpsimd", lambda g: g.dma_start(
                out=wo_sb.ap().rearrange("p (c j) -> p c j", c=NCH), in_=wout.rearrange("(c p) j -> p c j", p=128)))
            WGb = ("WG", P.cnt["WG"])
            evOR = P.g(lambda g: g.memset(ones_r[:], 1.0))
            evBo = P.dma("gpsimd", lambda g: g.dma_start(
                out=bo_row[:], in_=bout.unsqueeze(0)), sem="WB")
            WHb = ("WB", P.cnt["WB"])
            ptrBf32 = ptrB.ap().bitcast(F32)
            GROUPS = [(0, 3, 0), (3, 6, 6), (6, 8, 12)]
            ln_state = {}
            ev_png_free = [None, None]
            ev_x1_free = [None, None]
            ev_attn_free = [None, None]
            ev_rden_free = [None, None]
            ev_aout_free = [None] * 8
            ev_aln_free = [None, None]
            ev_xfreeB = [None, None]
            ev_qTb_free = [None, None]
            ev_ptrB_free = [zc2]
            ev_attnT_free = [None, None]
            ev_pwo_free = [None]
            ev_xc_free = [None, None]
            ev_rv_free = [None, None]
            ev_pm_last = [None] * 4
            ev_x1_stores = []
            ev_cache_w = []
            xo_marks = {}

            evXBm = [None] * nM
            evQBm = [None] * nM

            def load_x(m):
                pb = m % 2
                evXBm[m] = P.dma("sync", lambda e, m=m, pb=pb: e.dma_start(
                    out=xbufB[pb].ap().rearrange("p (s d) -> p s d", s=4),
                    in_=x_in[m * 512:(m + 1) * 512, :].rearrange(
                        "(s p) d -> p s d", p=128)),
                    waits=[ev_xfreeB[pb]], sem="XA%d" % pb)

            def load_q(m):
                pb = m % 2
                evQBm[m] = P.dma("scalar", lambda e, m=m, pb=pb: e.dma_start(
                    out=qTb[pb][:], in_=qTd[m]),
                    waits=[ev_qTb_free[pb],
                           ("QO0", P.cnt["QO0"]), ("QO1", P.cnt["QO1"])],
                    sem="QI%d" % pb)

            def numdiv(stile):
                m = stile // SUB
                s4 = stile % SUB
                pb = m % 2
                sp = stile % 2
                evQ = evQBm[m]
                ev_div = []
                last_q = None
                for (p0, p1, rc) in GROUPS:
                    gi = rc // 6
                    bank = gi % 2
                    npair = p1 - p0
                    evN = []
                    for p8 in range(p0, p1):
                        off = (p8 - p0) * 130
                        evN.append(P.t(lambda t, p8=p8, off=off, bank=bank, pb=pb, s4=s4, p0=p0, p1=p1: t.matmul(
                            png[bank][:, off:off + 130],
                            lhsT=qTb[pb][:, p8 * 512 + s4 * 128:
                                         p8 * 512 + (s4 + 1) * 128],
                            rhs=kvz_r[:, p8 * 130:(p8 + 1) * 130],
                            start=(p8 == p0), stop=(p8 == p1 - 1)),
                            waits=[evQ, ev_rhs] + ev_rhs2 + (ev_png_free[bank] or [])))
                    last_q = evN[-1]
                    eD1 = P.v(lambda v, rc=rc, bank=bank, sp=sp, npair=npair: v.tensor_scalar(
                        rden[sp][:, rc:rc + 2 * npair],
                        png[bank][:, 64:npair * 130:65],
                        1e-6, None, ALU.add),
                        waits=evN + (ev_rden_free[sp] or []))
                    eD2 = P.v(lambda v, rc=rc, sp=sp, npair=npair: v.reciprocal(
                        rden[sp][:, rc:rc + 2 * npair],
                        rden[sp][:, rc:rc + 2 * npair]), waits=[eD1])

                    def dvv(e, bank=bank, p0=p0, rc=rc, npair=npair, sp=sp):
                        return e.tensor_tensor(
                            attn[sp].ap().rearrange("p (j c) -> p j c", c=64)[:, 2 * p0:2 * p0 + 2 * npair, :],
                            png[bank].ap().rearrange("p (j c) -> p j c", c=65)[:, 0:2 * npair, 0:64],
                            rden[sp][:, rc:rc + 2 * npair].unsqueeze(2).broadcast_to([128, 2 * npair, 64]),
                            ALU.mult)
                    evDv = P.v(dvv, waits=[eD2, ev_attn_free[sp]])
                    ev_div.append(evDv)
                    ev_png_free[bank] = [evDv]
                ev_rden_free[sp] = [ev_div[-1]]
                return ev_div, last_q

            load_q(0)
            load_x(0)
            div_evs = [None] * (nS + 1)
            div_evs[0] = numdiv(0)

            def headB(stile):
                m = stile // SUB
                s4 = stile % SUB
                pb = m % 2
                sp = stile % 2
                ev_div, last_q = div_evs[stile]
                if debug_dump:
                    P.dma("gpsimd", lambda g, sp=sp, stile=stile: g.dma_start(
                        out=adbg[stile * 128:(stile + 1) * 128, :],
                        in_=attn[sp][:]), waits=ev_div, sem="DBG")
                    ev_div = ev_div + [("DBG", P.cnt["DBG"])]
                evT = []
                for c in range(NCH):
                    evT.append(P.t(lambda t, c=c, sp=sp: t.transpose(
                        ptrB[:, c * 128:(c + 1) * 128],
                        attn[sp][:, c * 128:(c + 1) * 128], ident[:]),
                        waits=ev_div + [ev_ptrB_free[0], evI2]))
                ev_attn_free[sp] = evT[-1]
                eTe = P.s(lambda s, sp=sp: s.activation(
                    attnT[sp][:], ptrB[:], AF.Copy),
                    waits=[evT[-1], ev_attnT_free[sp]])
                ev_ptrB_free[0] = eTe
                if stile + 1 < nS:
                    div_evs[stile + 1] = numdiv(stile + 1)
                    if (stile + 1) % SUB == SUB - 1:
                        ev_qTb_free[((stile + 1) // SUB) % 2] = div_evs[stile + 1][1]
                ao = stile % 8
                ev_wout = None
                for half in range(2):
                    free_w = ev_pwo_free if half == 0 else ev_ptrB_free
                    dst = pwo[:] if half == 0 else ptrBf32[:, 0:512]
                    evWc = None
                    for c in range(NCH):
                        evWc = P.t(lambda t, c=c, half=half, sp=sp, dst=dst:
                                   t.matmul(
                            dst,
                            lhsT=attnT[sp][:, c * 128:(c + 1) * 128],
                            rhs=wo_sb[:, c * D + half * 512:
                                      c * D + (half + 1) * 512],
                            start=(c == 0), stop=False),
                            waits=[eTe, WGb, free_w[0]])
                    evW = P.t(lambda t, half=half, dst=dst: t.matmul(
                        dst,
                        lhsT=ones_r[0:1, :],
                        rhs=bo_row[0:1, half * 512:(half + 1) * 512],
                        start=False, stop=True),
                        waits=[WHb, evOR])
                    eB = P.s(lambda s, half=half, ao=ao, dst=dst: s.copy(
                        aout[ao][:, half * 512:(half + 1) * 512], dst),
                        waits=[evW,
                               ev_aout_free[ao] if half == 0 else None])
                    free_w[0] = eB
                    ev_wout = eB
                    if half == 1:
                        ev_attnT_free[sp] = evWc
                wout_evs[stile] = ev_wout

            def tailB(stile):
                m = stile // SUB
                s4 = stile % SUB
                pb = m % 2
                sp = stile % 2
                ao = stile % 8
                ev_wout = wout_evs[stile]
                ev_rr2, rr2_done = _ln(
                    P, bn6b[sp], st2[sp], rr2[sp],
                    (lambda ao=ao: aout[ao][:]),
                    ln_state, ("B", sp), extra_waits=[ev_wout])
                b7 = P.s(lambda s, sp=sp, ao=ao: s.activation(
                    aln[sp][:], aout[ao][:], AF.Identity,
                    scale=rr2[sp][:, 0:1], bias=rr2[sp][:, 1:2]),
                    waits=[ev_rr2, ev_aln_free[sp]])
                rr2_done(b7)
                b8 = P.g(lambda g, sp=sp, s4=s4, pb=pb, m=m: g.tensor_tensor(
                    x1b_[sp][:], aln[sp][:],
                    xbufB[pb][:, s4 * D:(s4 + 1) * D], ALU.add),
                    waits=[b7, evXBm[m], ev_x1_free[sp]])
                ev_aout_free[ao] = b7
                ev_aln_free[sp] = b8
                if s4 == SUB - 1:
                    ev_xfreeB[pb] = b8
                evXO = P.dma("gpsimd", lambda g, sp=sp, stile=stile: g.dma_start(
                    out=x1d[stile * 128:(stile + 1) * 128, :],
                    in_=x1b_[sp][:]),
                    waits=[b8], sem="XO%d" % sp)
                ev_x1_free[sp] = evXO
                ev_x1_stores.append(evXO)
                xo_marks[stile] = [("XO0", P.cnt["XO0"]),
                                   ("XO1", P.cnt["XO1"])]

            wout_evs = [None] * nS
            for m in range(nM):
                if m + 1 < nM:
                    load_q(m + 1)
                for s4 in range(SUB):
                    headB(m * SUB + s4)
                    if m >= 1:
                        tailB((m - 1) * SUB + s4)
                if m + 1 < nM:
                    load_x(m + 1)
            for s4 in range(SUB):
                tailB((nM - 1) * SUB + s4)

            pm_gate = ([ev_pwo_free[0]] + (ev_png_free[0] or [])
                       + (ev_png_free[1] or []))
            for sp in range(2):
                ln_state[("B2", sp)] = dict(ln_state.get(("B", sp), {}))
            evX2m = [None] * nM
            xfree2 = [list(ev_xfreeB[0:1]), list(ev_xfreeB[1:2])]

            def load_x2(m):
                pb = m % 2
                evX2m[m] = P.dma("sync", lambda e, m=m, pb=pb: e.dma_start(
                    out=xbufB[pb].ap().rearrange("p (s d) -> p s d", s=4),
                    in_=x1d[m * 512:(m + 1) * 512, :].rearrange(
                        "(s p) d -> p s d", p=128)),
                    waits=xfree2[pb] + xo_marks[m * SUB + SUB - 1],
                    sem="XA%d" % pb)

            load_x2(0)
            if nM > 1:
                load_x2(1)
            for m in range(nM):
                pb = m % 2
                newfree = []
                for s4 in range(SUB):
                    stile = m * SUB + s4
                    sp = stile % 2
                    src = (lambda pb=pb, s4=s4:
                           xbufB[pb][:, s4 * D:(s4 + 1) * D])
                    ev_rr2b, rr2b_done = _ln(
                        P, bn6b[sp], st2[sp], rr2[sp], src,
                        ln_state, ("B2", sp), extra_waits=[evX2m[m]])
                    c7 = P.s(lambda s, sp=sp, stile=stile: s.copy(
                        r2c[:, stile:stile + 1], rr2[sp][:, 0:1]),
                        waits=[ev_rr2b])
                    c8 = P.s(lambda s, sp=sp, stile=stile: s.copy(
                        q2c[:, stile:stile + 1], rr2[sp][:, 1:2]),
                        waits=[ev_rr2b])
                    c10 = P.s(lambda s, sp=sp, stile=stile: s.activation(
                        qc2[:, stile:stile + 1], rr2[sp][:, 1:2], AF.Square),
                        waits=[ev_rr2b])
                    ev_cache_w.extend([c7, c8, c10])
                    d1 = P.v(lambda v, sp=sp: v.tensor_copy(
                        rv3f[sp][:, 0:1], rr2[sp][:, 0:1]),
                        waits=[ev_rr2b, ev_rv_free[sp]])
                    d2 = P.v(lambda v, sp=sp: v.tensor_mul(
                        rv3f[sp][:, 1:2], rr2[sp][:, 0:1], rr2[sp][:, 1:2]),
                        waits=[ev_rr2b, ev_rv_free[sp]])
                    d3 = P.v(lambda v, sp=sp: v.tensor_mul(
                        rv3f[sp][:, 2:3], rr2[sp][:, 0:1], rr2[sp][:, 0:1]),
                        waits=[ev_rr2b, ev_rv_free[sp]])
                    rr2b_done(d3)
                    d4 = P.v(lambda v, sp=sp: v.tensor_copy(
                        rv3[sp][:], rv3f[sp][:]),
                        waits=[d1, d2, d3, ev_rv_free[sp]])
                    d5 = P.v(lambda v, sp=sp, src=src: v.tensor_copy(
                        xcat[sp][:, 0:1024], src()),
                        waits=[evX2m[m], ev_xc_free[sp]])
                    d6 = P.s(lambda s, sp=sp, src=src: s.activation(
                        xcat[sp][:, 1024:2048], src(), AF.Square),
                        waits=[evX2m[m], ev_xc_free[sp]])
                    evMs = []
                    for ci in range(4):
                        evMs.append(P.t(lambda t, ci=ci, sp=sp, stile=stile:
                                        t.matmul(
                            pm[ci][:],
                            lhsT=rv3[sp][:],
                            rhs=xcat[sp][:, ci * 512:(ci + 1) * 512],
                            start=(stile == 0), stop=(stile == nS - 1)),
                            waits=[d4, d5, d6]
                            + (pm_gate if stile == 0 else [])))
                    ev_xc_free[sp] = evMs[-1]
                    ev_rv_free[sp] = evMs[-1]
                    if stile == nS - 1:
                        ev_pm_last = evMs
                    newfree += [ev_rr2b, d5, d6]
                xfree2[pb] = newfree
                if m + 2 < nM:
                    load_x2(m + 2)

            f1 = P.v(lambda v: v.tensor_reduce(qrp[:, 0:1], q2c[:], AX.X,
                                               ALU.add), waits=ev_cache_w)
            f2 = P.v(lambda v: v.tensor_reduce(qrp[:, 1:2], qc2[:], AX.X,
                                               ALU.add), waits=ev_cache_w)
            f3 = P.t(lambda t: t.matmul(ptrB.ap().bitcast(F32)[0:2, 0:1],
                                        lhsT=qrp[:], rhs=ones_f[:],
                                        start=True, stop=True),
                     waits=[f1, f2, evOF, ev_ptrB_free[0]] + ev_pm_last)
            f4 = P.v(lambda v: v.tensor_copy(
                qs_sb[:], ptrB.ap().bitcast(F32)[0:2, 0:1]), waits=[f3])
            evMe = []
            for ci in range(4):
                evMe.append(P.v(lambda v, ci=ci: v.tensor_copy(
                    ms_sb.ap().rearrange("p (c n) -> p c n", c=4)[:, ci, :],
                    pm[ci][:]),
                    waits=ev_pm_last + [f4]))
            gz0 = P.g(lambda g: g.memset(zr2[:], 0.0))
            P.dma("gpsimd", lambda g: g.dma_start(
                out=cc2i[3:4, :], in_=zr2[:]), waits=[gz0])
            g1 = P.dma("gpsimd", lambda g: g.dma_start(
                out=cc2i[0:3, :], in_=ms_sb[:]), waits=evMe)
            g2 = P.dma("gpsimd", lambda g: g.dma_start(
                out=cc2i[3:4, 0:2].rearrange("a b -> b a"), in_=qs_sb[:],
                allow_slow_non_contiguous=True), waits=[f4])
            if bstop is not None:
                evCC2 = None
            evCC2 = evCC2 if bstop is not None else P.op("gpsimd", lambda g: g.collective_compute(
                "AllReduce", ALU.add, replica_groups=RG,
                ins=[cc2i.opt()], outs=[cc2o.opt()]),
                waits=[("WG", P.cnt["WG"])], sem="CC", inc=1)

            P.barrier()

        if phases == "AB":
            P.op("sync", None, waits=[("CC", 1 if bstop else 2)])
        if "C" in phases and True:
         with ExitStack() as stC:
            def sbc(n, sh, dt):
                return stC.enter_context(nc.sbuf_tensor(n, sh, dt))
            def psc(n, sh, dt):
                return stC.enter_context(nc.psum_tensor(n, sh, dt))

            wcT = sbc("wcT", [128, NCH * D], BF16)
            wd_sb = sbc("wd_sb", [128, NCH * R], BF16)
            wu_sb = sbc("wu_sb", [64, D], BF16)
            mu_sb = sbc("mu_sb", [8, D], F32)
            sg_sb = sbc("sg_sb", [8, D], F32)
            iv_sb = sbc("iv_sb", [8, D], F32)
            muiv = sbc("muiv", [8, D], F32)
            c3_sb = sbc("c3_sb", [8, D], F32)
            lw1 = sbc("lw1", [8, D], F32)
            lw2 = sbc("lw2", [8, D], F32)
            m2ab = sbc("m2ab", [8, D], F32)
            q1b = sbc("q1b", [8, 1], F32)
            q2b = sbc("q2b", [8, 1], F32)
            wstg = sbc("wstg", [128, D], F32)
            wstg2 = sbc("wstg2", [128, D], F32)
            wstg3 = sbc("wstg3", [128, D], F32)
            wustg = sbc("wustg", [64, D], F32)
            wdstg = sbc("wdstg", [128, NCH * R], F32)
            bu_b = sbc("bu_b", [8, D], BF16)
            bc_b = sbc("bc_b", [8, D], BF16)
            bd_b = sbc("bd_b", [8, R], BF16)
            ohc = sbc("ohc", [8, 1], BF16)

            P.dma("gpsimd", lambda g: g.dma_start(out=bu_b[:], in_=bu_in))
            P.dma("gpsimd", lambda g: g.dma_start(out=bc_b[:], in_=bc_in))
            P.dma("gpsimd", lambda g: g.dma_start(out=bd_b[:], in_=bd_in))
            WGc = ("WG", P.cnt["WG"])
            P.dma("sync", lambda e: e.dma_start(out=mu_sb[:], in_=mu_in),
                  sem="LB")
            P.dma("sync", lambda e: e.dma_start(out=sg_sb[:], in_=sig_in),
                  sem="LB")
            h1 = P.v(lambda v: v.tensor_mul(iv_sb[:], sg_sb[:], sg_sb[:]),
                         waits=[("LB", P.cnt["LB"])])
            h2 = P.v(lambda v: v.reciprocal(iv_sb[:], iv_sb[:]), waits=[h1])
            h3 = P.v(lambda v: v.tensor_mul(muiv[:], mu_sb[:], iv_sb[:]),
                     waits=[h2, WHs])
            h4 = P.v(lambda v: v.tensor_mul(c3_sb[:], mu_sb[:], muiv[:]),
                     waits=[h3])
            for fn in (lambda e: e.dma_start(out=lw1[:], in_=cc2o[0:1, 0:1024].partition_broadcast(8)),
                       lambda e: e.dma_start(out=lw2[:], in_=cc2o[1:2, 0:1024].partition_broadcast(8)),
                       lambda e: e.dma_start(out=m2ab[:], in_=cc2o[2:3, 1024:2048].partition_broadcast(8)),
                       lambda e: e.dma_start(out=q1b[:], in_=cc2o[3:4, 0:1].partition_broadcast(8)),
                       lambda e: e.dma_start(out=q2b[:], in_=cc2o[3:4, 1:2].partition_broadcast(8))):
                P.dma("sync", fn, waits=[evCC2], sem="LB")
            evLB = ("LB", P.cnt["LB"])
            inv_t = 1.0 / float(t_total)
            k1 = P.v(lambda v: v.tensor_scalar(
                lw1[:], lw1[:], q1b[:], inv_t, ALU.add, ALU.mult), waits=[evLB])
            k2a = P.v(lambda v: v.tensor_scalar(
                lw2[:], lw2[:], 2.0, None, ALU.mult), waits=[evLB])
            k2 = P.v(lambda v: v.tensor_add(lw2[:], lw2[:], m2ab[:]),
                     waits=[k2a])
            k3 = P.v(lambda v: v.tensor_scalar(
                lw2[:], lw2[:], q2b[:], inv_t, ALU.add, ALU.mult), waits=[k2])
            k4 = P.v(lambda v: v.tensor_mul(lw2[:], lw2[:], iv_sb[:]),
                     waits=[k3, h2])
            k5 = P.v(lambda v: v.tensor_mul(lw1[:], lw1[:], muiv[:]),
                     waits=[k1, h3])
            k6a = P.v(lambda v: v.tensor_scalar(
                lw1[:], lw1[:], -2.0, None, ALU.mult), waits=[k5])
            k6 = P.v(lambda v: v.tensor_add(lw1[:], lw1[:], lw2[:]),
                     waits=[k4, k6a])
            k7 = P.v(lambda v: v.tensor_add(lw1[:], lw1[:], c3_sb[:]),
                     waits=[k6, h4])
            k8 = P.v(lambda v: v.tensor_reduce(loss8[:], lw1[:], AX.X, ALU.add),
                     waits=[k7])
            k9 = P.dma("gpsimd", lambda g: g.dma_start(
                out=lossd.rearrange("a b -> b a"), in_=loss8[:],
                allow_slow_non_contiguous=True), waits=[k8])
            k10 = P.dma("gpsimd", lambda g: g.dma_start(out=lrow[:], in_=lossd),
                        waits=[k9])
            k11 = P.v(lambda v: v.tensor_reduce(lmin[:], lrow[:], AX.X, ALU.min),
                      waits=[k10])
            k12 = P.v(lambda v: v.tensor_single_scalar(
                lone[:], lrow[:], lmin[:], ALU.is_equal), waits=[k11])
            k13 = P.v(lambda v: v.tensor_copy(liot[:], iota8r[:]), waits=[evIr])
            k14 = P.v(lambda v: v.tensor_mul(liot[:], liot[:], lone[:]),
                      waits=[k13, k12])
            k15 = P.v(lambda v: v.tensor_reduce(idxf1[:], liot[:], AX.X, ALU.add),
                      waits=[k14])
            k16 = P.v(lambda v: v.tensor_copy(idxi1[:], idxf1[:]), waits=[k15])
            k17 = P.dma("gpsimd", lambda g: g.dma_start(out=idxd, in_=idxi1[:]),
                        waits=[k16])
            k18 = P.dma("gpsimd", lambda g: g.dma_start(
                out=idxb[:], in_=idxd.partition_broadcast(128)), waits=[k17])
            k19 = P.v(lambda v: v.tensor_copy(idxbf[:], idxb[:]), waits=[k18])
            k19b = P.v(lambda v: v.tensor_scalar(
                idc[:], idxbf[:], 1024.0, None, ALU.mult), waits=[k19])
            k19c = P.v(lambda v: v.tensor_scalar(
                idu[:], idxbf[0:64, :], 64.0, None, ALU.mult), waits=[k19])
            k20 = P.v(lambda v: v.tensor_copy(gcf[:], iota8x[:]), waits=[evI8])
            k21 = P.v(lambda v: v.tensor_scalar(
                gcf[:], gcf[:], idc[:], None, ALU.add), waits=[k19b, k20])
            k22 = P.v(lambda v: v.tensor_copy(gci[:], gcf[:]), waits=[k21])
            k23 = P.v(lambda v: v.tensor_copy(guf[:], iota_p[0:64, :]),
                      waits=[evIp])
            k24 = P.v(lambda v: v.tensor_scalar(
                guf[:], guf[:], idu[:], None, ALU.add), waits=[k19c, k23])
            k25 = P.v(lambda v: v.tensor_copy(gui[:], guf[:]), waits=[k24])

            x1buf = [sbc("x1C%d" % i, [128, 4096], F32) for i in range(3)]
            xn2 = [sbc("xn2_%d" % i, [128, D], BF16) for i in range(4)]
            xn2T = sbc("xn2T", [128, NCH * 512], BF16)
            hbf = sbc("hbf", [64, 512], BF16)
            aT = sbc("aT", [128, NCH * 512], BF16)
            ybf = [sbc("ybf0", [128, NCH * 512], BF16),
                   sbc("ybf1", [128, NCH * 512], BF16)]
            ybt = [sbc("ybt0", [128, D], BF16), sbc("ybt1", [128, D], BF16)]
            st3 = [sbc("st3_0", [128, 2], F32), sbc("st3_1", [128, 2], F32)]
            bn6c = [sbc("bn6c_0", [128, 12], F32), sbc("bn6c_1", [128, 12], F32)]
            rr3 = [sbc("rr3_0", [128, 2], F32), sbc("rr3_1", [128, 2], F32)]
            tl = [sbc("tl0", [128, D], F32), sbc("tl1", [128, D], F32)]
            x2b = [sbc("x2b0", [128, D], F32), sbc("x2b1", [128, D], F32)]

            ptrC2 = [psc("ptrC", [128, 1024], BF16), psc("pxtr", [128, 1024], BF16)]
            ptrC = ptrC2[0]
            pbias = ptrC2[1].ap().bitcast(F32)[:, 0:24]
            pa = [psc("pa0", [128, 512], F32), psc("pa1", [128, 512], F32)]
            py = [psc("py0", [128, 512], F32), psc("py1", [128, 512], F32)]
            ptry2 = [psc("ptry", [128, 1024], BF16), psc("ptry2", [128, 1024], BF16)]
            ph = pa[0].ap()[0:64, :]

            def x2nT(c, lo, hi):
                return xn2T[:, c * 512 + lo:c * 512 + hi]

            ln_state = {}
            ev_x1free = [None, None, None]
            ev_ptrC_free = [None, None]
            ev_xn2_free = [None, None, None, None]
            ev_hb_free = None
            ev_ph_free = None
            ev_pa_free = [None, None]
            ev_aT_free = None
            ev_py_free = [None, None]
            ev_ybf_free = [None, None]
            ev_ptry_free = [None, None]
            ev_ybt_free = [None, None]
            ev_tl_free = [None, None]
            ev_x2_free = [None, None]
            evXO_all = [("XO0", P.cnt["XO0"]), ("XO1", P.cnt["XO1"])]

            e1_evs = {}

            def load_c(m):
                pb = m % 3
                return P.dma("sync", lambda e, m=m, pb=pb: e.dma_start(
                    out=x1buf[pb].ap().rearrange("p (s d) -> p s d", s=4),
                    in_=x1d[m * 512:(m + 1) * 512, :].rearrange(
                        "(s p) d -> p s d", p=128)),
                    waits=[ev_x1free[pb]] + (evXO_all if m < 3 else []),
                    sem="XC%d" % (m % 2))

            def prep(m, evX1):
                pb = m % 3
                for s4 in range(SUB):
                    stile = m * SUB + s4
                    e1 = P.s(lambda s, pb=pb, s4=s4, stile=stile: s.activation(
                        xn2[s4][:], x1buf[pb][:, s4 * D:(s4 + 1) * D], AF.Identity,
                        scale=r2c[:, stile:stile + 1], bias=q2c[:, stile:stile + 1]),
                        waits=[evX1, ev_xn2_free[s4]])
                    e1_evs[(m, s4)] = e1

            prep_x1 = [None] * nM
            prep_x1[0] = load_c(0)
            if nM > 1:
                prep_x1[1] = load_c(1)
            prep(0, prep_x1[0])

            gD = []
            for c in range(NCH):
                gD.append(P.dma("gpsimd", lambda g, c=c: g.indirect_dma_start(
                    out=wdstg[:, c * R:(c + 1) * R], out_offset=None,
                    in_=wd_in.rearrange("e d r -> (e d) r"),
                    in_offset=IndirectOffsetOnAxis(gci[:, c:c + 1], 0)),
                    waits=[k22], sem="GD%d" % (c % 2)))
            gD2 = P.dma("gpsimd", lambda g: g.dma_start(out=wd_sb[:], in_=wdstg[:]),
                        waits=[("GD0", P.cnt["GD0"]), ("GD1", P.cnt["GD1"])],
                        sem="GD0")
            gU = P.dma("gpsimd", lambda g: g.indirect_dma_start(
                out=wustg[:], out_offset=None,
                in_=wu_in.rearrange("e r d -> (e r) d"),
                in_offset=IndirectOffsetOnAxis(gui[:, 0:1], 0)), waits=[k25],
                sem="GU")
            gU2 = P.dma("gpsimd", lambda g: g.dma_start(out=wu_sb[:], in_=wustg[:]),
                        waits=[gU], sem="GU")
            ev_stg_free = [None, None, None]
            stgs = [wstg, wstg2, wstg3]
            for c in range(NCH):
                stg = stgs[c % 3]
                gsem = "GC%d" % (c % 3)
                gA = P.dma("gpsimd", lambda g, c=c, stg=stg: g.indirect_dma_start(
                    out=stg[:], out_offset=None,
                    in_=wc_in.rearrange("e o d -> (e o) d"),
                    in_offset=IndirectOffsetOnAxis(gci[:, c:c + 1], 0)),
                    waits=[k22, ev_stg_free[c % 3]], sem=gsem)
                gB = P.dma("gpsimd", lambda g, c=c, stg=stg: g.dma_start(
                    out=wccb[c * 128:(c + 1) * 128, :], in_=stg[:]), waits=[gA],
                    sem=gsem)
                ev_stg_free[c % 3] = gB
            bias_evs = {}

            def emit_bias_sel():
                e_oh = P.t(lambda t: t.transpose(
                    ptrC2[1].ap().bitcast(F32)[0:8, 32:33], lone[:],
                    identf[0:1, 0:1]),
                    waits=[k12, evI4, ev_ptrC_free[1]])
                e_ohc = P.s(lambda s: s.activation(
                    ohc[:], ptrC2[1].ap().bitcast(F32)[0:8, 32:33], AF.Copy),
                    waits=[e_oh])
                evm = []
                evm.append(P.t(lambda t: t.matmul(
                    pbias[:, 0:1], lhsT=bu_b[:, 0:128], rhs=ohc[:],
                    start=True, stop=False), waits=[e_ohc, WGc]))
                evm.append(P.t(lambda t: t.matmul(
                    pbias[0:64, 16:17], lhsT=bd_b[:], rhs=ohc[:],
                    start=False, stop=False), waits=[e_ohc, WGc]))
                for c in range(1, NCH):
                    evm.append(P.t(lambda t, c=c: t.matmul(
                        pbias[:, c:c + 1],
                        lhsT=bu_b[:, c * 128:(c + 1) * 128], rhs=ohc[:],
                        start=False, stop=False), waits=[e_ohc, WGc]))
                for c in range(NCH):
                    evm.append(P.t(lambda t, c=c: t.matmul(
                        pbias[:, 8 + c:9 + c],
                        lhsT=bc_b[:, c * 128:(c + 1) * 128], rhs=ohc[:],
                        start=False, stop=(c == NCH - 1)), waits=[e_ohc, WGc]))
                eb1 = P.s(lambda s: s.activation(buT[:], pbias[:, 0:8], AF.Copy),
                          waits=[evm[-1]])
                eb2 = P.s(lambda s: s.activation(bcT[:], pbias[:, 8:16], AF.Copy),
                          waits=[evm[-1]])
                eb3 = P.s(lambda s: s.activation(bdT[:], pbias[0:64, 16:17],
                                                 AF.Copy), waits=[evm[-1]])
                eb4 = P.v(lambda v: v.tensor_add(bcombT[:], bcT[:], bbT[:]),
                          waits=[eb2, WHs])
                bias_evs["bu"] = eb1
                bias_evs["bd"] = eb3
                bias_evs["bcomb"] = eb4
                ev_ptrC_free[1] = eb3
            ev_wc_tr = []
            ev_sb2sb = [("GC0", P.cnt["GC0"]), ("GC1", P.cnt["GC1"]),
                        ("GC2", P.cnt["GC2"]), ("GC3", P.cnt["GC3"]),
                        ("GD0", P.cnt["GD0"]), ("GD1", P.cnt["GD1"]),
                        ("GU", P.cnt["GU"])]
            for c in range(NCH):
                ev_wc_tr.append(P.dma("sync", lambda s, c=c: s.dma_start(
                    out=wcT[:, c * D:(c + 1) * D],
                    in_=wccb[:, c * 128:(c + 1) * 128], transpose=True),
                    waits=ev_sb2sb, sem="WT%d" % (c % 2)))
            WHwc = [("WT0", P.cnt["WT0"]), ("WT1", P.cnt["WT1"])]
            GDt = ("GD0", P.cnt["GD0"])
            GUt = ("GU", P.cnt["GU"])




            yev_evs = {}

            def Ctail(mm):
                xpb2 = mm % 3
                pb2 = mm % 2
                ev_yev = yev_evs.pop(mm)
                evX1t = prep_x1[mm]
                nonlocal ev_ybf_free, ev_ptry_free, ev_ybt_free
                nonlocal ev_tl_free, ev_x2_free, ev_x1free
                for s4 in range(SUB):
                    sp = s4 % 2
                    yb = ptry2[sp]
                    evT = []
                    for c in range(NCH):
                        evT.append(P.t(lambda t, c=c, s4=s4, yb=yb, pb2=pb2:
                                       t.transpose(
                            yb[:, c * 128:(c + 1) * 128],
                            ybf[pb2][:, c * 512 + s4 * 128:c * 512 + (s4 + 1) * 128],
                            ident[:]),
                            waits=ev_yev + [ev_ptry_free[sp], evI2]))
                    if s4 == SUB - 1:
                        ev_ybf_free[pb2] = evT[-1]
                    eE = P.v(lambda v, sp=sp, yb=yb: v.tensor_copy(
                        ybt[sp][:], yb[:]),
                             waits=[evT[-1], ev_ybt_free[sp]])
                    ev_ptry_free[sp] = eE
                    ev_rr3, rr3_done = _ln(
                        P, bn6c[sp], st3[sp], rr3[sp],
                        (lambda sp=sp: ybt[sp][:]),
                        ln_state, ("C", sp), extra_waits=[eE])
                    b7 = P.v(lambda v, sp=sp: v.tensor_scalar(
                        tl[sp][:], ybt[sp][:], rr3[sp][:, 0:1],
                        rr3[sp][:, 1:2], ALU.mult, ALU.add),
                        waits=[ev_rr3, ev_tl_free[sp]])
                    rr3_done(b7)
                    ev_ybt_free[sp] = b7
                    b8 = P.g(lambda g, sp=sp, s4=s4, xpb2=xpb2: g.tensor_tensor(
                        x2b[sp][:], tl[sp][:],
                        x1buf[xpb2][:, s4 * D:(s4 + 1) * D], ALU.add),
                        waits=[b7, evX1t, ev_x2_free[sp]])
                    ev_tl_free[sp] = b8
                    if s4 == SUB - 1:
                        ev_x1free[xpb2] = b8
                    stile = mm * SUB + s4
                    evX2 = P.dma("sync", lambda e, stile=stile, sp=sp: e.dma_start(
                        out=y_out[stile * 128:(stile + 1) * 128, :],
                        in_=x2b[sp][:]),
                        waits=[b8], sem="X2O%d" % sp)
                    ev_x2_free[sp] = evX2

            for m in range(nM):
                pb = m % 2
                xpb = m % 3
                evX1 = prep_x1[m]
                ev_evE = []
                for s4 in range(SUB):
                    trb = ptrC2[s4 % 2]
                    evT = []
                    for c in range(NCH):
                        evT.append(P.t(lambda t, c=c, s4=s4, trb=trb: t.transpose(
                            trb[:, c * 128:(c + 1) * 128],
                            xn2[s4][:, c * 128:(c + 1) * 128], ident[:]),
                            waits=[e1_evs[(m, s4)], evI2, ev_ptrC_free[s4 % 2]]))
                    ev_xn2_free[s4] = evT[-1]
                    eE = P.v(lambda v, s4=s4, trb=trb: v.tensor_copy(
                        xn2T.ap().rearrange("p (c n) -> p c n", c=NCH)[:, :, s4 * 128:(s4 + 1) * 128],
                        trb.ap().rearrange("p (c n) -> p c n", c=NCH)),
                        waits=[evT[-1]])
                    ev_ptrC_free[s4 % 2] = eE
                    ev_evE.append(eE)
                if m + 1 < nM:
                    prep(m + 1, prep_x1[m + 1])
                if m == 0:
                    emit_bias_sel()
                evDn = None
                for c in range(NCH):
                    evDn = P.t(lambda t, c=c: t.matmul(
                        ph,
                        lhsT=wd_sb[:, c * R:(c + 1) * R],
                        rhs=x2nT(c, 0, 512),
                        start=(c == 0), stop=(c == NCH - 1)),
                        waits=ev_evE + [GDt, ev_ph_free] + (ev_pa_free[0] or []))
                evRl = P.s(lambda s: s.activation(hbf[:], ph, AF.Relu,
                                                  bias=bdT[:]),
                           waits=[evDn, bias_evs["bd"], ev_hb_free])
                ev_ph_free = evRl
                ev_up = []
                for c in range(NCH):
                    bank = (c + 1) % 2
                    evU = P.t(lambda t, c=c, bank=bank: t.matmul(
                        pa[bank][:],
                        lhsT=wu_sb[:, c * 128:(c + 1) * 128],
                        rhs=hbf[:],
                        start=True, stop=True),
                        waits=[evRl, GUt] + (ev_pa_free[bank] or []))
                    evUe = P.s(lambda s, c=c, bank=bank: s.activation(
                        aT[:, c * 512:(c + 1) * 512],
                        pa[bank][:], AF.Identity, bias=buT[:, c:c + 1]),
                        waits=[evU, bias_evs["bu"]] + ([ev_aT_free] if c == 0 else []))
                    ev_pa_free[bank] = [evUe]
                    ev_up.append(evUe)
                ev_hb_free = ev_up[-1]
                if m >= 1:
                    Ctail(m - 1)
                if m + 2 < nM:
                    prep_x1[m + 2] = load_c(m + 2)
                ev_yev = []
                for co in range(NCH):
                    bank = co % 2
                    evY = None
                    for c in range(NCH):
                        evY = P.t(lambda t, c=c, co=co, bank=bank: t.matmul(
                            py[bank][:],
                            lhsT=bwT[:, c * D + co * 128:c * D + (co + 1) * 128],
                            rhs=x2nT(c, 0, 512),
                            start=(c == 0), stop=False),
                            waits=ev_evE + WTs + [ev_py_free[bank]])
                    for c in range(NCH):
                        evY = P.t(lambda t, c=c, co=co, bank=bank: t.matmul(
                            py[bank][:],
                            lhsT=wcT[:, c * D + co * 128:c * D + (co + 1) * 128],
                            rhs=aT[:, c * 512:(c + 1) * 512],
                            start=False, stop=(c == NCH - 1)),
                            waits=ev_up + WHwc)
                    eYe = P.s(lambda s, co=co, bank=bank, pb=pb: s.activation(
                        ybf[pb][:, co * 512:(co + 1) * 512],
                        py[bank][:], AF.Identity, bias=bcombT[:, co:co + 1]),
                        waits=[evY, bias_evs["bcomb"]] + ([ev_ybf_free[pb]] if co == 0 else []))
                    ev_py_free[bank] = eYe
                    ev_yev.append(eYe)
                ev_aT_free = ev_yev[-1]
                yev_evs[m] = ev_yev

            Ctail(nM - 1)

            P.op("sync", None, waits=[("X2O0", P.cnt["X2O0"]),
                                      ("X2O1", P.cnt["X2O1"])])

        with nc.Block() as blk:
            P.emit(blk)
    return nc


def kernel(L0_attn_x, L0_attn_W_qkv, L0_attn_W_out, L0_attn_b_out,
           L0_ffn_base_W, L0_ffn_base_b, L0_ffn_disc_mu, L0_ffn_disc_sigma,
           L0_ffn_down_W, L0_ffn_down_b, L0_ffn_up_W, L0_ffn_up_b,
           L0_ffn_conv_W, L0_ffn_conv_b):
    x = np.ascontiguousarray(np.asarray(L0_attn_x, dtype=np.float32))
    B, N, _ = x.shape
    T = N // 2
    shared = {
        "wqkv": np.ascontiguousarray(L0_attn_W_qkv, dtype=np.float32),
        "wout": np.ascontiguousarray(L0_attn_W_out, dtype=np.float32),
        "bout": np.ascontiguousarray(L0_attn_b_out, dtype=np.float32),
        "basew": np.ascontiguousarray(L0_ffn_base_W, dtype=np.float32),
        "baseb": np.ascontiguousarray(L0_ffn_base_b, dtype=np.float32),
        "mu": np.ascontiguousarray(L0_ffn_disc_mu, dtype=np.float32),
        "sigma": np.ascontiguousarray(L0_ffn_disc_sigma, dtype=np.float32),
        "wd": np.ascontiguousarray(L0_ffn_down_W, dtype=np.float32),
        "bd": np.ascontiguousarray(L0_ffn_down_b, dtype=np.float32),
        "wu": np.ascontiguousarray(L0_ffn_up_W, dtype=np.float32),
        "bu": np.ascontiguousarray(L0_ffn_up_b, dtype=np.float32),
        "wc": np.ascontiguousarray(L0_ffn_conv_W, dtype=np.float32),
        "bc": np.ascontiguousarray(L0_ffn_conv_b, dtype=np.float32),
    }
    nc = build_program(T=T, t_total=N)
    in_maps = []
    for c in range(8):
        b, h = c // 2, c % 2
        mp = dict(shared)
        mp["x"] = np.ascontiguousarray(x[b, h * T:(h + 1) * T, :])
        in_maps.append(mp)
    res = run_bass_kernel_spmd(nc, in_maps, core_ids=list(range(8)))
    out = np.empty((B, N, D), dtype=np.float32)
    for c in range(8):
        b, h = c // 2, c % 2
        out[b, h * T:(h + 1) * T, :] = res.results[c]["y"]
    return out


# version 19 (final)
# speedup vs baseline: 1.4512x; 1.0154x over previous
from collections import defaultdict
from contextlib import ExitStack

import numpy as np

import concourse.bass as bass
import concourse.mybir as mybir
from concourse.bass import IndirectOffsetOnAxis
from concourse.bass_utils import run_bass_kernel_spmd

F32 = mybir.dt.float32
BF16 = mybir.dt.bfloat16
I32 = mybir.dt.int32
AF = mybir.ActivationFunctionType
ALU = mybir.AluOpType
AX = mybir.AxisListType

D = 1024
H = 16
NCH = 8
R = 64
E = 8
EPS = 1e-5

ENGINES = ("sync", "gpsimd", "tensor", "vector", "scalar")


class Plan:

    def __init__(self, nc):
        self.nc = nc
        self.ops = {e: [] for e in ENGINES}
        self.cnt = defaultdict(int)
        self.handles = {}

    def sem(self, stack, name):
        self.handles[name] = stack.enter_context(self.nc.semaphore(name))

    ENG_SEMS = ("T", "V", "S", "G", "BAR")

    def op(self, eng, fn, waits=(), sem=None, inc=1):
        ev = None
        ws = [w for w in waits if w is not None]
        for w in ws:
            assert isinstance(w, tuple) and len(w) == 2, f"bad wait {w!r}"
        if sem is not None:
            if sem not in self.ENG_SEMS and self.cnt[sem] > 0:
                ws.append((sem, self.cnt[sem]))
            self.cnt[sem] += inc
            ev = (sem, self.cnt[sem])
        self.ops[eng].append((fn, ws, (sem, inc) if sem else None))
        return ev

    def t(self, fn, waits=()):
        return self.op("tensor", fn, waits, "T")

    def v(self, fn, waits=()):
        return self.op("vector", fn, waits, "V")

    def s(self, fn, waits=()):
        return self.op("scalar", fn, waits, "S")

    def g(self, fn, waits=()):
        return self.op("gpsimd", fn, waits, "G")

    def dma(self, eng, fn, waits=(), sem="WG"):
        return self.op(eng, fn, waits, sem, inc=16)

    def barrier(self):
        self.cnt["BAR"] += len(ENGINES)
        tgt = self.cnt["BAR"]
        for e in ENGINES:
            self.ops[e].append((lambda eng: eng.nop(), [], ("BAR", 1)))
            self.ops[e].append((None, [("BAR", tgt)], None))

    OWN_SEM = {"tensor": "T", "vector": "V", "scalar": "S", "gpsimd": "G"}

    def emit(self, blk):
        for eng in ENGINES:
            ops = self.ops[eng]
            if not ops:
                continue

            def body(e, ops=ops, own=self.OWN_SEM.get(eng)):
                last = {}
                for fn, waits, inc in ops:
                    for (s, v) in waits:
                        if last.get(s, -1) >= v:
                            continue
                        e.wait_ge(self.handles[s], v)
                        last[s] = v
                    if fn is None:
                        continue
                    inst = fn(e)
                    if inc is not None and inc[0] is not None:
                        inst.then_inc(self.handles[inc[0]], inc[1])
            getattr(blk, eng)(body)


def _ln(P, bn6, st, rr, src_fn, state, key, extra_waits=()):
    prev = state.get(key, {})
    e1a = P.v(lambda v: v.bn_stats(bn6[:, 0:6], src_fn()[:, 0:512]),
              waits=list(extra_waits) + [prev.get("aggr")])
    e1 = P.v(lambda v: v.bn_stats(bn6[:, 6:12], src_fn()[:, 512:1024]),
             waits=[e1a])
    e2 = P.v(lambda v: v.bn_aggr(st[:], bn6[:]),
             waits=[e1, prev.get("st_read")])
    e3 = P.v(lambda v: v.tensor_scalar(rr[:, 0:1], st[:, 1:2], EPS, None,
                                       ALU.add),
             waits=[e2, prev.get("rr_read")])
    e4 = P.s(lambda s: s.activation(rr[:, 0:1], rr[:, 0:1], AF.Ln), waits=[e3])
    e5 = P.s(lambda s: s.activation(rr[:, 0:1], rr[:, 0:1], AF.Exp, scale=-0.5),
             waits=[e4])
    e6 = P.v(lambda v: v.tensor_scalar(rr[:, 1:2], st[:, 0:1], rr[:, 0:1],
                                       -1.0, ALU.mult, ALU.mult),
             waits=[e5, e2])
    st_new = {"aggr": e2, "st_read": e6}
    state[key] = st_new

    def set_last_reader(ev):
        st_new["rr_read"] = ev
    return e6, set_last_reader


def build_program(T=4096, t_total=None, debug_dump=False, phases="ABC", bstop=None):
    if t_total is None:
        t_total = 2 * T
    nS = T // 128
    nM = T // 512
    SUB = 4

    nc = bass.Bass("TRN2", target_bir_lowering=False)

    x_in = nc.dram_tensor("x", [T, D], F32, kind="ExternalInput").ap()
    wqkv = nc.dram_tensor("wqkv", [D, 3 * D], F32, kind="ExternalInput").ap()
    wout = nc.dram_tensor("wout", [D, D], F32, kind="ExternalInput").ap()
    bout = nc.dram_tensor("bout", [D], F32, kind="ExternalInput").ap()
    basew = nc.dram_tensor("basew", [D, D], F32, kind="ExternalInput").ap()
    baseb = nc.dram_tensor("baseb", [D], F32, kind="ExternalInput").ap()
    mu_in = nc.dram_tensor("mu", [E, D], F32, kind="ExternalInput").ap()
    sig_in = nc.dram_tensor("sigma", [E, D], F32, kind="ExternalInput").ap()
    wd_in = nc.dram_tensor("wd", [E, D, R], F32, kind="ExternalInput").ap()
    bd_in = nc.dram_tensor("bd", [E, R], F32, kind="ExternalInput").ap()
    wu_in = nc.dram_tensor("wu", [E, R, D], F32, kind="ExternalInput").ap()
    bu_in = nc.dram_tensor("bu", [E, D], F32, kind="ExternalInput").ap()
    wc_in = nc.dram_tensor("wc", [E, D, D], F32, kind="ExternalInput").ap()
    bc_in = nc.dram_tensor("bc", [E, D], F32, kind="ExternalInput").ap()
    y_out = nc.dram_tensor("y", [T, D], F32, kind="ExternalOutput").ap()

    qTd = nc.dram_tensor("qTd", [nM, 128, 4096], BF16).ap()
    x1d = nc.dram_tensor("x1d", [T, D], F32).ap()
    wscb = nc.dram_tensor("wscb", [D, D], BF16).ap()
    wccb = nc.dram_tensor("wccb", [D, D], BF16).ap()
    cc1i = nc.dram_tensor("cc1i", [129, D], F32).ap()
    cc1o = nc.dram_tensor("cc1o", [129, D], F32).ap()
    cc2i = nc.dram_tensor("cc2i", [4, 2048], F32).ap()
    cc2o = nc.dram_tensor("cc2o", [4, 2048], F32).ap()
    idxd = nc.dram_tensor("idxd", [1, 1], I32).ap()
    lossd = nc.dram_tensor("lossd", [1, 8], F32).ap()
    if debug_dump:
        kdbg = nc.dram_tensor("kdbg", [T, D], BF16).ap()
        vdbg = nc.dram_tensor("vdbg", [T, D], BF16).ap()
        adbg = nc.dram_tensor("adbg", [T, D], BF16).ap()

    P = Plan(nc)
    RG = [[0, 1], [2, 3], [4, 5], [6, 7]]

    with ExitStack() as st:
        def sb(name, shape, dt):
            return st.enter_context(nc.sbuf_tensor(name, shape, dt))

        for s in ("T", "V", "S", "G", "BAR", "CC", "WH", "WG", "LB",
                  "XO0", "XO1", "GD0", "GD1", "GU", "GC0", "GC1", "WT0", "WT1",
                  "XA0", "XA1", "QO0", "QO1", "QI0", "QI1", "WB", "GC2", "GC3",
                  "X2O0", "X2O1", "XC0", "XC1", "DBG"):
            P.sem(st, s)

        ident = sb("ident", [128, 128], BF16)
        identf = sb("identf", [8, 8], F32)
        ones_b = sb("ones_b", [128, 1], BF16)
        ones_f = sb("ones_f", [128, 1], F32)
        iota_p = sb("iota_p", [128, 1], I32)
        iota8x = sb("iota8x", [128, 8], I32)
        iota8r = sb("iota8r", [1, 8], I32)
        bwT = sb("bwT", [128, NCH * D], BF16)
        bcombT = sb("bcombT", [128, NCH], F32)
        bbT = sb("bbT", [128, NCH], F32)
        bcT = sb("bcT", [128, NCH], F32)
        buT = sb("buT", [128, NCH], F32)
        bdT = sb("bdT", [64, 1], F32)
        loss8 = sb("loss8", [8, 1], F32)
        lrow = sb("lrow", [1, 8], F32)
        lmin = sb("lmin", [1, 1], F32)
        lone = sb("lone", [1, 8], F32)
        liot = sb("liot", [1, 8], F32)
        idxf1 = sb("idxf1", [1, 1], F32)
        idxi1 = sb("idxi1", [1, 1], I32)
        idxb = sb("idxb", [128, 1], I32)
        idxbf = sb("idxbf", [128, 1], F32)
        idc = sb("idc", [128, 1], F32)
        idu = sb("idu", [64, 1], F32)
        gcf = sb("gcf", [128, 8], F32)
        gci = sb("gci", [128, 8], I32)
        guf = sb("guf", [64, 1], F32)
        gui = sb("gui", [64, 1], I32)
        kvz_r = sb("kvz_r", [128, 8 * 130], BF16)
        zrow = sb("zrow", [1, 1024], BF16)
        r2c = sb("r2c", [128, nS], F32)
        q2c = sb("q2c", [128, nS], F32)
        qc2 = sb("qc2", [128, nS], F32)

        evI1 = P.g(lambda g: g.memset(ident[:], 0.0))
        evI2 = P.g(lambda g: g.affine_select(
            out=ident[:], in_=ident[:], compare_op=ALU.not_equal, fill=1.0,
            base=0, pattern=[[-1, 128]], channel_multiplier=1), waits=[evI1])
        evI3 = P.g(lambda g: g.memset(identf[:], 0.0))
        evI4 = P.g(lambda g: g.affine_select(
            out=identf[:], in_=identf[:], compare_op=ALU.not_equal, fill=1.0,
            base=0, pattern=[[-1, 8]], channel_multiplier=1), waits=[evI3])
        evOB = P.g(lambda g: g.memset(ones_b[:], 1.0))
        evOF = P.g(lambda g: g.memset(ones_f[:], 1.0))
        evIp = P.g(lambda g: g.iota(iota_p[:], pattern=[[0, 1]], base=0,
                                    channel_multiplier=1))
        evI8 = P.g(lambda g: g.iota(iota8x[:], pattern=[[128, 8]], base=0,
                                    channel_multiplier=1))
        evIr = P.g(lambda g: g.iota(iota8r[:], pattern=[[1, 8]], base=0,
                                    channel_multiplier=0))
        evWb = P.dma("gpsimd", lambda g: g.dma_start(out=wscb, in_=basew))
        evBW = []
        for c in range(NCH):
            evBW.append(P.dma("scalar", lambda s, c=c: s.dma_start(
                out=bwT[:, c * D:(c + 1) * D],
                in_=wscb[:, c * 128:(c + 1) * 128], transpose=True),
                waits=[evWb], sem="WT%d" % (c % 2)))
        evBB = []
        for c in range(NCH):
            evBB.append(P.dma("sync", lambda e, c=c: e.dma_start(
                out=bbT[:, c:c + 1],
                in_=baseb[c * 128:(c + 1) * 128].unsqueeze(1)), sem="WH"))
        WHs = ("WH", P.cnt["WH"])
        WTs = [("WT0", P.cnt["WT0"]), ("WT1", P.cnt["WT1"])]

        with ExitStack() as stA:
            def sba(n, sh, dt):
                return stA.enter_context(nc.sbuf_tensor(n, sh, dt))
            def psa(n, sh, dt):
                return stA.enter_context(nc.psum_tensor(n, sh, dt))

            wq_sb = sba("wq_sb", [128, NCH * 3 * D], BF16)
            kvsb = sba("kvsb", [128, 1024], F32)
            zsb = sba("zsb", [1, 1024], F32)
            xbuf = [sba("xA0", [128, 4096], F32), sba("xA1", [128, 4096], F32)]
            xn1 = [sba("xn1_%d" % i, [128, D], BF16) for i in range(4)]
            st1 = [sba("st1_0", [128, 2], F32), sba("st1_1", [128, 2], F32)]
            bn6 = [sba("bn6_0", [128, 12], F32), sba("bn6_1", [128, 12], F32)]
            rr1 = [sba("rr1_0", [128, 2], F32), sba("rr1_1", [128, 2], F32)]
            xn1T = sba("xn1T", [128, NCH * 512], BF16)
            kbf = [sba("kbf0", [128, D], BF16), sba("kbf1", [128, D], BF16)]
            vbf = [sba("vbf0", [128, D], BF16), sba("vbf1", [128, D], BF16)]
            texp = [sba("texp0", [128, 512], F32), sba("texp1", [128, 512], F32)]
            tqxp = [sba("tqxp0", [128, 512], F32), sba("tqxp1", [128, 512], F32)]
            ktmp = [sba("ktmp0", [128, 512], F32), sba("ktmp1", [128, 512], F32)]
            qtmp = [sba("qtmp0", [128, 512], F32), sba("qtmp1", [128, 512], F32)]
            kt2 = [sba("kt2_0", [128, 512], F32), sba("kt2_1", [128, 512], F32)]
            qt2 = [sba("qt2_0", [128, 512], F32), sba("qt2_1", [128, 512], F32)]
            qTs = [sba("qTs0", [128, 4096], BF16), sba("qTs1", [128, 4096], BF16)]

            pkv = [psa("pkv0", [128, 512], F32), psa("pkv1", [128, 512], F32)]
            pz = [psa("pz0", [1, 512], F32), psa("pz1", [1, 512], F32)]
            pmm = [psa("pmm0", [128, 512], F32), psa("pmm1", [128, 512], F32)]
            ptrA = psa("ptrA", [128, 1024], BF16)
            pq = psa("pq", [128, 512], F32)

            def x1nT(c, lo, hi):
                return xn1T[:, c * 512 + lo:c * 512 + hi]

            P.dma("gpsimd", lambda g: g.dma_start(
                out=wq_sb.ap().rearrange("p (c j) -> p c j", c=NCH), in_=wqkv.rearrange("(c p) j -> p c j", p=128)))
            WGs = ("WG", P.cnt["WG"])

            ln_state = {}
            ev_xfree = [None, None]
            ev_kread = [None, None]
            ev_texp_read = [None, None]
            ev_tq_read = [None, None]
            ev_ptr_free = None
            ev_pq_free = None
            ev_pmm_free = [None, None]
            ev_qT_free = [None, None]
            ev_xn1_free = [None, None, None, None]
            ev_kv_done = []
            ev_z_done = []
            ev_qT_out = []
            ev_xn1T_read = None

            pool_banks = [pmm[0], pmm[1], pq]
            pool_free = [None, None, None]
            pool_i = [0]

            def take_bank():
                i = pool_i[0] % 3
                pool_i[0] += 1
                return i, pool_banks[i], pool_free[i] or []

            def prepA(m):
                pb = m % 2
                evX = P.dma("sync", lambda e, m=m, pb=pb: e.dma_start(
                    out=xbuf[pb].ap().rearrange("p (s d) -> p s d", s=4),
                    in_=x_in[m * 512:(m + 1) * 512, :].rearrange(
                        "(s p) d -> p s d", p=128)),
                    waits=[ev_xfree[pb]], sem="XA%d" % pb)
                applies = []
                for s4 in range(SUB):
                    sp = s4 % 2
                    ev_rr, rr_done = _ln(
                        P, bn6[sp], st1[sp], rr1[sp],
                        (lambda pb=pb, s4=s4: xbuf[pb][:, s4 * D:(s4 + 1) * D]),
                        ln_state, ("A", sp), extra_waits=[evX])
                    e7 = P.s(lambda s, pb=pb, s4=s4, sp=sp: s.activation(
                        xn1[s4][:], xbuf[pb][:, s4 * D:(s4 + 1) * D], AF.Identity,
                        scale=rr1[sp][:, 0:1], bias=rr1[sp][:, 1:2]),
                        waits=[ev_rr, ev_xn1_free[s4]])
                    rr_done(e7)
                    applies.append(e7)
                return evX, applies

            prepdata = [None] * nM
            prepdata[0] = prepA(0)

            for m in range(nM):
                pb = m % 2
                evX, applies = prepdata[m]
                ev_macro_evac = []
                for s4 in range(SUB):
                    evT = []
                    for c in range(NCH):
                        evT.append(P.t(lambda t, c=c, s4=s4: t.transpose(
                            ptrA[:, c * 128:(c + 1) * 128],
                            xn1[s4][:, c * 128:(c + 1) * 128], ident[:]),
                            waits=[applies[s4], evI2, ev_ptr_free]))
                    ev_xn1_free[s4] = evT[-1]
                    evE = P.s(lambda s, s4=s4: s.activation(
                        xn1T.ap().rearrange("p (c n) -> p c n", c=NCH)[:, :, s4 * 128:(s4 + 1) * 128],
                        ptrA.ap().rearrange("p (c n) -> p c n", c=NCH), AF.Copy),
                        waits=[evT[-1], ev_xn1T_read if s4 == 0 and m > 0 else None])
                    ev_ptr_free = evE
                    ev_macro_evac.append(evE)
                ev_xfree[pb] = applies[-1]
                if m + 1 < nM:
                    prepdata[m + 1] = prepA(m + 1)

                for s4 in range(SUB):
                    stile = m * SUB + s4
                    sp = s4 % 2
                    evE = ev_macro_evac[s4]
                    ev_k_ready = None
                    ev_v_ready = None
                    for ci in range(4):
                        bi, bank_t, bank_w = take_bank()
                        evMM = None
                        for c in range(NCH):
                            evMM = P.t(lambda t, c=c, ci=ci, bank_t=bank_t, s4=s4: t.matmul(
                                bank_t[:],
                                lhsT=x1nT(c, s4 * 128, (s4 + 1) * 128),
                                rhs=wq_sb[:, c * 3 * D + D + ci * 512:
                                          c * 3 * D + D + (ci + 1) * 512],
                                start=(c == 0), stop=(c == NCH - 1)),
                                waits=[evE, WGs] + bank_w)
                        if ci < 2:
                            eU = P.s(lambda s, bank_t=bank_t, ci=ci: s.activation(
                                texp[ci][:], bank_t[:], AF.Relu, scale=-1.0),
                                waits=[evMM, ev_texp_read[ci],
                                       ev_kread[sp] if ci == 0 else None])
                            eW = P.s(lambda s, ci=ci: s.activation(
                                ktmp[ci][:], texp[ci][:], AF.Exp, scale=-1.0),
                                waits=[eU])
                            eT = P.v(lambda v, bank_t=bank_t, ci=ci: v.tensor_add(
                                kt2[ci][:], bank_t[:], texp[ci][:]),
                                waits=[eU])
                            eE2 = P.v(lambda v, ci=ci, sp=sp: v.tensor_add(
                                kbf[sp][:, ci * 512:(ci + 1) * 512], kt2[ci][:],
                                ktmp[ci][:]),
                                waits=[eW, eT])
                            ev_texp_read[ci] = eE2
                            pool_free[bi] = [eU, eT]
                            if ci == 1:
                                ev_k_ready = eE2
                        else:
                            eE2 = P.s(lambda s, bank_t=bank_t, ci=ci, sp=sp: s.copy(
                                vbf[sp][:, (ci - 2) * 512:(ci - 1) * 512],
                                bank_t[:]),
                                waits=[evMM, ev_kread[sp] if ci == 2 else None])
                            pool_free[bi] = [eE2]
                            if ci == 3:
                                ev_v_ready = eE2
                    evKV = []
                    for p8 in range(8):
                        half = p8 // 4
                        evKV.append(P.t(lambda t, p8=p8, half=half, sp=sp, stile=stile: t.matmul(
                            pkv[half][:, (p8 % 4) * 128:(p8 % 4 + 1) * 128],
                            lhsT=kbf[sp][:, p8 * 128:(p8 + 1) * 128],
                            rhs=vbf[sp][:, p8 * 128:(p8 + 1) * 128],
                            start=(stile == 0 and p8 % 4 == 0),
                            stop=(stile == nS - 1 and p8 % 4 == 3)),
                            waits=[ev_k_ready, ev_v_ready]))
                    evZ = []
                    for half in range(2):
                        evZ.append(P.t(lambda t, half=half, sp=sp, stile=stile: t.matmul(
                            pz[half][:],
                            lhsT=ones_b[:],
                            rhs=kbf[sp][:, half * 512:(half + 1) * 512],
                            start=(stile == 0), stop=(stile == nS - 1)),
                            waits=[ev_k_ready, evOB]))
                    if debug_dump:
                        P.dma("gpsimd", lambda g, sp=sp, stile=stile: g.dma_start(
                            out=kdbg[stile * 128:(stile + 1) * 128, :],
                            in_=kbf[sp][:]), waits=[ev_k_ready], sem="DBG")
                        P.dma("gpsimd", lambda g, sp=sp, stile=stile: g.dma_start(
                            out=vdbg[stile * 128:(stile + 1) * 128, :],
                            in_=vbf[sp][:]), waits=[ev_v_ready], sem="DBG")
                    ev_kread[sp] = evZ[-1] if not debug_dump else ("DBG", P.cnt["DBG"])
                    if stile == nS - 1:
                        ev_kv_done = evKV
                        ev_z_done = evZ

                ev_q_evac = []
                for j in range(NCH):
                    bi, bank_t, bank_w = take_bank()
                    evQ = None
                    for c in range(NCH):
                        evQ = P.t(lambda t, c=c, j=j, bank_t=bank_t: t.matmul(
                            bank_t[:],
                            lhsT=wq_sb[:, c * 3 * D + j * 128:
                                       c * 3 * D + (j + 1) * 128],
                            rhs=x1nT(c, 0, 512),
                            start=(c == 0), stop=(c == NCH - 1)),
                            waits=[ev_macro_evac[-1], WGs] + bank_w)
                    sp = j % 2
                    eU = P.s(lambda s, sp=sp, bank_t=bank_t: s.activation(
                        tqxp[sp][:], bank_t[:], AF.Relu, scale=-1.0),
                        waits=[evQ, ev_tq_read[sp]])
                    eW = P.s(lambda s, sp=sp: s.activation(
                        qtmp[sp][:], tqxp[sp][:], AF.Exp, scale=-1.0),
                        waits=[eU])
                    eT = P.v(lambda v, sp=sp, bank_t=bank_t: v.tensor_add(
                        qt2[sp][:], bank_t[:], tqxp[sp][:]),
                        waits=[eU])
                    eQ2 = P.v(lambda v, j=j, sp=sp, pb=pb: v.tensor_add(
                        qTs[pb][:, j * 512:(j + 1) * 512], qt2[sp][:],
                        qtmp[sp][:]),
                        waits=[eW, eT, ev_qT_free[pb] if j == 0 else None])
                    ev_tq_read[sp] = eQ2
                    ev_q_evac.append(eQ2)
                    pool_free[bi] = [eU, eT]
                ev_xn1T_read = evQ
                evQO = P.dma("gpsimd", lambda g, m=m, pb=pb: g.dma_start(
                    out=qTd[m], in_=qTs[pb][:]),
                    waits=ev_q_evac, sem="QO%d" % pb)
                ev_qT_free[pb] = evQO
                ev_qT_out.append(evQO)

            eK0 = P.v(lambda v: v.tensor_copy(kvsb[:, 0:512], pkv[0][:]),
                      waits=ev_kv_done)
            eK1 = P.v(lambda v: v.tensor_copy(kvsb[:, 512:1024], pkv[1][:]),
                      waits=[eK0])
            eZ0 = P.v(lambda v: v.tensor_copy(zsb[:, 0:512], pz[0][:]),
                      waits=ev_z_done + [eK1])
            eZ1 = P.v(lambda v: v.tensor_copy(zsb[:, 512:1024], pz[1][:]),
                      waits=[eZ0])
            evD1 = P.dma("gpsimd", lambda g: g.dma_start(
                out=cc1i[0:128, :], in_=kvsb[:]), waits=[eK1])
            evD2 = P.dma("gpsimd", lambda g: g.dma_start(
                out=cc1i[128:129, :], in_=zsb[:]), waits=[eZ1])
            evCC1 = P.op("gpsimd", lambda g: g.collective_compute(
                "AllReduce", ALU.add, replica_groups=RG,
                ins=[cc1i.opt()], outs=[cc1o.opt()]),
                waits=[("WG", P.cnt["WG"])], sem="CC", inc=1)

            evKz = P.g(lambda g: g.memset(kvz_r[:], 0.0))
            kvv = lambda ap: ap.rearrange("p (g c) -> p g c", c=130)
            cco = lambda ap: ap.rearrange("p (g c) -> p g c", c=128)
            P.dma("gpsimd", lambda g: g.dma_start(
                out=kvv(kvz_r[0:64, :])[:, :, 0:64],
                in_=cco(cc1o[0:64, :])[:, :, 0:64]),
                waits=[evCC1, evKz])
            P.dma("gpsimd", lambda g: g.dma_start(
                out=kvv(kvz_r[64:128, :])[:, :, 65:129],
                in_=cco(cc1o[64:128, :])[:, :, 64:128]),
                waits=[evCC1, evKz])
            evZr = P.dma("gpsimd", lambda g: g.dma_start(
                out=zrow[:], in_=cc1o[128:129, :]), waits=[evCC1])
            evZt = []
            for c in range(NCH):
                evZt.append(P.t(lambda t, c=c: t.transpose(
                    ptrA[:, 2 * c:2 * c + 1], zrow[0:1, c * 128:(c + 1) * 128],
                    ident[0:1, 0:1]), waits=[evZr, evI2, ev_ptr_free]))
            zc1 = P.s(lambda s: s.activation(
                kvv(kvz_r[0:64, :])[:, :, 64:65],
                ptrA[0:64, 0:16:2].unsqueeze(2),
                AF.Copy), waits=[evZt[-1], evKz])
            zc2 = P.s(lambda s: s.activation(
                kvv(kvz_r[64:128, :])[:, :, 129:130],
                ptrA[64:128, 0:16:2].unsqueeze(2),
                AF.Copy), waits=[evZt[-1], evKz])
            ev_rhs = ("WG", P.cnt["WG"])
            ev_rhs2 = [zc1, zc2]

            P.barrier()

        if phases == "A":
            P.op("sync", None, waits=[("CC", 1)])
        if "B" in phases and True:
         with ExitStack() as stB:
            def sbb(n, sh, dt):
                return stB.enter_context(nc.sbuf_tensor(n, sh, dt))
            def psb(n, sh, dt):
                return stB.enter_context(nc.psum_tensor(n, sh, dt))

            wo_sb = sbb("wo_sb", [128, NCH * D], BF16)
            ones_r = sbb("ones_r", [1, 128], BF16)
            bo_row = sbb("bo_row", [1, D], BF16)
            ms_sb = sbb("ms_sb", [3, 2048], F32)
            qs_sb = sbb("qs_sb", [2, 1], F32)
            zr2 = sbb("zr2", [1, 2048], F32)
            qrp = sbb("qrp", [128, 2], F32)
            xbufB = [sbb("xB0", [128, 4096], F32), sbb("xB1", [128, 4096], F32)]
            qTb = [sbb("qTb0", [128, 4096], BF16), sbb("qTb1", [128, 4096], BF16)]
            attn = [sbb("attn0", [128, D], BF16), sbb("attn1", [128, D], BF16)]
            attnT = [sbb("attnT0", [128, D], BF16), sbb("attnT1", [128, D], BF16)]
            rden = [sbb("rden0", [128, 16], F32), sbb("rden1", [128, 16], F32)]
            aout = [sbb("aout%d" % i, [128, D], F32) for i in range(8)]
            aln = [sbb("aln0", [128, D], F32), sbb("aln1", [128, D], F32)]
            x1b_ = [sbb("x1b0", [128, D], F32), sbb("x1b1", [128, D], F32)]
            st2 = [sbb("st2_0", [128, 2], F32), sbb("st2_1", [128, 2], F32)]
            bn6b = [sbb("bn6b_0", [128, 12], F32), sbb("bn6b_1", [128, 12], F32)]
            rr2 = [sbb("rr2_0", [128, 2], F32), sbb("rr2_1", [128, 2], F32)]
            xcat = [sbb("xcat0", [128, 2048], BF16), sbb("xcat1", [128, 2048], BF16)]
            rv3f = [sbb("rv3f_0", [128, 3], F32), sbb("rv3f_1", [128, 3], F32)]
            rv3 = [sbb("rv3_0", [128, 3], BF16), sbb("rv3_1", [128, 3], BF16)]

            png = [psb("png0", [128, 390], F32), psb("png1", [128, 390], F32)]
            ptrB = psb("ptrB", [128, 1024], BF16)
            pwo = psb("pwo", [128, 512], F32)
            pm = [psb("pm0", [3, 512], F32), psb("pm1", [3, 512], F32),
                  psb("pm2", [3, 512], F32), psb("pm3", [3, 512], F32)]
            pqs_view = None

            P.dma("gpsimd", lambda g: g.dma_start(
                out=wo_sb.ap().rearrange("p (c j) -> p c j", c=NCH), in_=wout.rearrange("(c p) j -> p c j", p=128)))
            WGb = ("WG", P.cnt["WG"])
            evOR = P.g(lambda g: g.memset(ones_r[:], 1.0))
            evBo = P.dma("gpsimd", lambda g: g.dma_start(
                out=bo_row[:], in_=bout.unsqueeze(0)), sem="WB")
            WHb = ("WB", P.cnt["WB"])
            ptrBf32 = ptrB.ap().bitcast(F32)
            GROUPS = [(0, 3, 0), (3, 6, 6), (6, 8, 12)]
            ln_state = {}
            ev_png_free = [None, None]
            ev_x1_free = [None, None]
            ev_attn_free = [None, None]
            ev_rden_free = [None, None]
            ev_aout_free = [None] * 8
            ev_aln_free = [None, None]
            ev_xfreeB = [None, None]
            ev_qTb_free = [None, None]
            ev_ptrB_free = [zc2]
            ev_attnT_free = [None, None]
            ev_pwo_free = [None]
            ev_xc_free = [None, None]
            ev_rv_free = [None, None]
            ev_pm_last = [None] * 4
            ev_x1_stores = []
            ev_cache_w = []
            xo_marks = {}

            evXBm = [None] * nM
            evQBm = [None] * nM

            def load_x(m):
                pb = m % 2
                evXBm[m] = P.dma("sync", lambda e, m=m, pb=pb: e.dma_start(
                    out=xbufB[pb].ap().rearrange("p (s d) -> p s d", s=4),
                    in_=x_in[m * 512:(m + 1) * 512, :].rearrange(
                        "(s p) d -> p s d", p=128)),
                    waits=[ev_xfreeB[pb]], sem="XA%d" % pb)

            def load_q(m):
                pb = m % 2
                evQBm[m] = P.dma("scalar", lambda e, m=m, pb=pb: e.dma_start(
                    out=qTb[pb][:], in_=qTd[m]),
                    waits=[ev_qTb_free[pb],
                           ("QO0", P.cnt["QO0"]), ("QO1", P.cnt["QO1"])],
                    sem="QI%d" % pb)

            def numdiv(stile):
                m = stile // SUB
                s4 = stile % SUB
                pb = m % 2
                sp = stile % 2
                evQ = evQBm[m]
                ev_div = []
                last_q = None
                for (p0, p1, rc) in GROUPS:
                    gi = rc // 6
                    bank = gi % 2
                    npair = p1 - p0
                    evN = []
                    for p8 in range(p0, p1):
                        off = (p8 - p0) * 130
                        evN.append(P.t(lambda t, p8=p8, off=off, bank=bank, pb=pb, s4=s4, p0=p0, p1=p1: t.matmul(
                            png[bank][:, off:off + 130],
                            lhsT=qTb[pb][:, p8 * 512 + s4 * 128:
                                         p8 * 512 + (s4 + 1) * 128],
                            rhs=kvz_r[:, p8 * 130:(p8 + 1) * 130],
                            start=(p8 == p0), stop=(p8 == p1 - 1)),
                            waits=[evQ, ev_rhs] + ev_rhs2 + (ev_png_free[bank] or [])))
                    last_q = evN[-1]
                    eD1 = P.v(lambda v, rc=rc, bank=bank, sp=sp, npair=npair: v.tensor_scalar(
                        rden[sp][:, rc:rc + 2 * npair],
                        png[bank][:, 64:npair * 130:65],
                        1e-6, None, ALU.add),
                        waits=evN + (ev_rden_free[sp] or []))
                    eD2 = P.v(lambda v, rc=rc, sp=sp, npair=npair: v.reciprocal(
                        rden[sp][:, rc:rc + 2 * npair],
                        rden[sp][:, rc:rc + 2 * npair]), waits=[eD1])

                    def dvv(e, bank=bank, p0=p0, rc=rc, npair=npair, sp=sp):
                        return e.tensor_tensor(
                            attn[sp].ap().rearrange("p (j c) -> p j c", c=64)[:, 2 * p0:2 * p0 + 2 * npair, :],
                            png[bank].ap().rearrange("p (j c) -> p j c", c=65)[:, 0:2 * npair, 0:64],
                            rden[sp][:, rc:rc + 2 * npair].unsqueeze(2).broadcast_to([128, 2 * npair, 64]),
                            ALU.mult)
                    evDv = P.v(dvv, waits=[eD2, ev_attn_free[sp]])
                    ev_div.append(evDv)
                    ev_png_free[bank] = [evDv]
                ev_rden_free[sp] = [ev_div[-1]]
                return ev_div, last_q

            load_q(0)
            load_x(0)
            div_evs = [None] * (nS + 1)
            div_evs[0] = numdiv(0)

            def headB(stile):
                m = stile // SUB
                s4 = stile % SUB
                pb = m % 2
                sp = stile % 2
                ev_div, last_q = div_evs[stile]
                if debug_dump:
                    P.dma("gpsimd", lambda g, sp=sp, stile=stile: g.dma_start(
                        out=adbg[stile * 128:(stile + 1) * 128, :],
                        in_=attn[sp][:]), waits=ev_div, sem="DBG")
                    ev_div = ev_div + [("DBG", P.cnt["DBG"])]
                evT = []
                for c in range(NCH):
                    evT.append(P.t(lambda t, c=c, sp=sp: t.transpose(
                        ptrB[:, c * 128:(c + 1) * 128],
                        attn[sp][:, c * 128:(c + 1) * 128], ident[:]),
                        waits=ev_div + [ev_ptrB_free[0], evI2]))
                ev_attn_free[sp] = evT[-1]
                eTe = P.v(lambda v, sp=sp: v.tensor_copy(
                    attnT[sp][:], ptrB[:]),
                    waits=[evT[-1], ev_attnT_free[sp]])
                ev_ptrB_free[0] = eTe
                if stile + 1 < nS:
                    div_evs[stile + 1] = numdiv(stile + 1)
                    if (stile + 1) % SUB == SUB - 1:
                        ev_qTb_free[((stile + 1) // SUB) % 2] = div_evs[stile + 1][1]
                ao = stile % 8
                ev_wout = None
                for half in range(2):
                    free_w = ev_pwo_free if half == 0 else ev_ptrB_free
                    dst = pwo[:] if half == 0 else ptrBf32[:, 0:512]
                    evWc = None
                    for c in range(NCH):
                        evWc = P.t(lambda t, c=c, half=half, sp=sp, dst=dst:
                                   t.matmul(
                            dst,
                            lhsT=attnT[sp][:, c * 128:(c + 1) * 128],
                            rhs=wo_sb[:, c * D + half * 512:
                                      c * D + (half + 1) * 512],
                            start=(c == 0), stop=False),
                            waits=[eTe, WGb, free_w[0]])
                    evW = P.t(lambda t, half=half, dst=dst: t.matmul(
                        dst,
                        lhsT=ones_r[0:1, :],
                        rhs=bo_row[0:1, half * 512:(half + 1) * 512],
                        start=False, stop=True),
                        waits=[WHb, evOR])
                    eB = P.s(lambda s, half=half, ao=ao, dst=dst: s.copy(
                        aout[ao][:, half * 512:(half + 1) * 512], dst),
                        waits=[evW,
                               ev_aout_free[ao] if half == 0 else None])
                    free_w[0] = eB
                    ev_wout = eB
                    if half == 1:
                        ev_attnT_free[sp] = evWc
                wout_evs[stile] = ev_wout

            def tailB(stile):
                m = stile // SUB
                s4 = stile % SUB
                pb = m % 2
                sp = stile % 2
                ao = stile % 8
                ev_wout = wout_evs[stile]
                ev_rr2, rr2_done = _ln(
                    P, bn6b[sp], st2[sp], rr2[sp],
                    (lambda ao=ao: aout[ao][:]),
                    ln_state, ("B", sp), extra_waits=[ev_wout])
                b7 = P.s(lambda s, sp=sp, ao=ao: s.activation(
                    aln[sp][:], aout[ao][:], AF.Identity,
                    scale=rr2[sp][:, 0:1], bias=rr2[sp][:, 1:2]),
                    waits=[ev_rr2, ev_aln_free[sp]])
                rr2_done(b7)
                b8 = P.g(lambda g, sp=sp, s4=s4, pb=pb, m=m: g.tensor_tensor(
                    x1b_[sp][:], aln[sp][:],
                    xbufB[pb][:, s4 * D:(s4 + 1) * D], ALU.add),
                    waits=[b7, evXBm[m], ev_x1_free[sp]])
                ev_aout_free[ao] = b7
                ev_aln_free[sp] = b8
                if s4 == SUB - 1:
                    ev_xfreeB[pb] = b8
                evXO = P.dma("gpsimd", lambda g, sp=sp, stile=stile: g.dma_start(
                    out=x1d[stile * 128:(stile + 1) * 128, :],
                    in_=x1b_[sp][:]),
                    waits=[b8], sem="XO%d" % sp)
                ev_x1_free[sp] = evXO
                ev_x1_stores.append(evXO)
                xo_marks[stile] = [("XO0", P.cnt["XO0"]),
                                   ("XO1", P.cnt["XO1"])]

            wout_evs = [None] * nS
            for m in range(nM):
                if m + 1 < nM:
                    load_q(m + 1)
                for s4 in range(SUB):
                    headB(m * SUB + s4)
                    if m >= 1:
                        tailB((m - 1) * SUB + s4)
                if m + 1 < nM:
                    load_x(m + 1)
            for s4 in range(SUB):
                tailB((nM - 1) * SUB + s4)

            pm_gate = ([ev_pwo_free[0]] + (ev_png_free[0] or [])
                       + (ev_png_free[1] or []))
            for sp in range(2):
                ln_state[("B2", sp)] = dict(ln_state.get(("B", sp), {}))
            evX2m = [None] * nM
            xfree2 = [list(ev_xfreeB[0:1]), list(ev_xfreeB[1:2])]

            def load_x2(m):
                pb = m % 2
                evX2m[m] = P.dma("sync", lambda e, m=m, pb=pb: e.dma_start(
                    out=xbufB[pb].ap().rearrange("p (s d) -> p s d", s=4),
                    in_=x1d[m * 512:(m + 1) * 512, :].rearrange(
                        "(s p) d -> p s d", p=128)),
                    waits=xfree2[pb] + xo_marks[m * SUB + SUB - 1],
                    sem="XA%d" % pb)

            load_x2(0)
            if nM > 1:
                load_x2(1)
            for m in range(nM):
                pb = m % 2
                newfree = []
                for s4 in range(SUB):
                    stile = m * SUB + s4
                    sp = stile % 2
                    src = (lambda pb=pb, s4=s4:
                           xbufB[pb][:, s4 * D:(s4 + 1) * D])
                    ev_rr2b, rr2b_done = _ln(
                        P, bn6b[sp], st2[sp], rr2[sp], src,
                        ln_state, ("B2", sp), extra_waits=[evX2m[m]])
                    c7 = P.s(lambda s, sp=sp, stile=stile: s.copy(
                        r2c[:, stile:stile + 1], rr2[sp][:, 0:1]),
                        waits=[ev_rr2b])
                    c8 = P.s(lambda s, sp=sp, stile=stile: s.copy(
                        q2c[:, stile:stile + 1], rr2[sp][:, 1:2]),
                        waits=[ev_rr2b])
                    c10 = P.s(lambda s, sp=sp, stile=stile: s.activation(
                        qc2[:, stile:stile + 1], rr2[sp][:, 1:2], AF.Square),
                        waits=[ev_rr2b])
                    ev_cache_w.extend([c7, c8, c10])
                    d1 = P.v(lambda v, sp=sp: v.tensor_copy(
                        rv3f[sp][:, 0:1], rr2[sp][:, 0:1]),
                        waits=[ev_rr2b, ev_rv_free[sp]])
                    d2 = P.v(lambda v, sp=sp: v.tensor_mul(
                        rv3f[sp][:, 1:2], rr2[sp][:, 0:1], rr2[sp][:, 1:2]),
                        waits=[ev_rr2b, ev_rv_free[sp]])
                    d3 = P.v(lambda v, sp=sp: v.tensor_mul(
                        rv3f[sp][:, 2:3], rr2[sp][:, 0:1], rr2[sp][:, 0:1]),
                        waits=[ev_rr2b, ev_rv_free[sp]])
                    rr2b_done(d3)
                    d4 = P.v(lambda v, sp=sp: v.tensor_copy(
                        rv3[sp][:], rv3f[sp][:]),
                        waits=[d1, d2, d3, ev_rv_free[sp]])
                    d5 = P.v(lambda v, sp=sp, src=src: v.tensor_copy(
                        xcat[sp][:, 0:1024], src()),
                        waits=[evX2m[m], ev_xc_free[sp]])
                    d6 = P.s(lambda s, sp=sp, src=src: s.activation(
                        xcat[sp][:, 1024:2048], src(), AF.Square),
                        waits=[evX2m[m], ev_xc_free[sp]])
                    evMs = []
                    for ci in range(4):
                        evMs.append(P.t(lambda t, ci=ci, sp=sp, stile=stile:
                                        t.matmul(
                            pm[ci][:],
                            lhsT=rv3[sp][:],
                            rhs=xcat[sp][:, ci * 512:(ci + 1) * 512],
                            start=(stile == 0), stop=(stile == nS - 1)),
                            waits=[d4, d5, d6]
                            + (pm_gate if stile == 0 else [])))
                    ev_xc_free[sp] = evMs[-1]
                    ev_rv_free[sp] = evMs[-1]
                    if stile == nS - 1:
                        ev_pm_last = evMs
                    newfree += [ev_rr2b, d5, d6]
                xfree2[pb] = newfree
                if m + 2 < nM:
                    load_x2(m + 2)

            f1 = P.v(lambda v: v.tensor_reduce(qrp[:, 0:1], q2c[:], AX.X,
                                               ALU.add), waits=ev_cache_w)
            f2 = P.v(lambda v: v.tensor_reduce(qrp[:, 1:2], qc2[:], AX.X,
                                               ALU.add), waits=ev_cache_w)
            f3 = P.t(lambda t: t.matmul(ptrB.ap().bitcast(F32)[0:2, 0:1],
                                        lhsT=qrp[:], rhs=ones_f[:],
                                        start=True, stop=True),
                     waits=[f1, f2, evOF, ev_ptrB_free[0]] + ev_pm_last)
            f4 = P.v(lambda v: v.tensor_copy(
                qs_sb[:], ptrB.ap().bitcast(F32)[0:2, 0:1]), waits=[f3])
            evMe = []
            for ci in range(4):
                evMe.append(P.v(lambda v, ci=ci: v.tensor_copy(
                    ms_sb.ap().rearrange("p (c n) -> p c n", c=4)[:, ci, :],
                    pm[ci][:]),
                    waits=ev_pm_last + [f4]))
            gz0 = P.g(lambda g: g.memset(zr2[:], 0.0))
            P.dma("gpsimd", lambda g: g.dma_start(
                out=cc2i[3:4, :], in_=zr2[:]), waits=[gz0])
            g1 = P.dma("gpsimd", lambda g: g.dma_start(
                out=cc2i[0:3, :], in_=ms_sb[:]), waits=evMe)
            g2 = P.dma("gpsimd", lambda g: g.dma_start(
                out=cc2i[3:4, 0:2].rearrange("a b -> b a"), in_=qs_sb[:],
                allow_slow_non_contiguous=True), waits=[f4])
            if bstop is not None:
                evCC2 = None
            evCC2 = evCC2 if bstop is not None else P.op("gpsimd", lambda g: g.collective_compute(
                "AllReduce", ALU.add, replica_groups=RG,
                ins=[cc2i.opt()], outs=[cc2o.opt()]),
                waits=[("WG", P.cnt["WG"])], sem="CC", inc=1)

            P.barrier()

        if phases == "AB":
            P.op("sync", None, waits=[("CC", 1 if bstop else 2)])
        if "C" in phases and True:
         with ExitStack() as stC:
            def sbc(n, sh, dt):
                return stC.enter_context(nc.sbuf_tensor(n, sh, dt))
            def psc(n, sh, dt):
                return stC.enter_context(nc.psum_tensor(n, sh, dt))

            wcT = sbc("wcT", [128, NCH * D], BF16)
            wd_sb = sbc("wd_sb", [128, NCH * R], BF16)
            wu_sb = sbc("wu_sb", [64, D], BF16)
            mu_sb = sbc("mu_sb", [8, D], F32)
            sg_sb = sbc("sg_sb", [8, D], F32)
            iv_sb = sbc("iv_sb", [8, D], F32)
            muiv = sbc("muiv", [8, D], F32)
            c3_sb = sbc("c3_sb", [8, D], F32)
            lw1 = sbc("lw1", [8, D], F32)
            lw2 = sbc("lw2", [8, D], F32)
            m2ab = sbc("m2ab", [8, D], F32)
            q1b = sbc("q1b", [8, 1], F32)
            q2b = sbc("q2b", [8, 1], F32)
            wstg = sbc("wstg", [128, D], F32)
            wstg2 = sbc("wstg2", [128, D], F32)
            wstg3 = sbc("wstg3", [128, D], F32)
            wustg = sbc("wustg", [64, D], F32)
            wdstg = sbc("wdstg", [128, NCH * R], F32)
            bu_b = sbc("bu_b", [8, D], BF16)
            bc_b = sbc("bc_b", [8, D], BF16)
            bd_b = sbc("bd_b", [8, R], BF16)
            ohc = sbc("ohc", [8, 1], BF16)

            P.dma("gpsimd", lambda g: g.dma_start(out=bu_b[:], in_=bu_in))
            P.dma("gpsimd", lambda g: g.dma_start(out=bc_b[:], in_=bc_in))
            P.dma("gpsimd", lambda g: g.dma_start(out=bd_b[:], in_=bd_in))
            WGc = ("WG", P.cnt["WG"])
            P.dma("sync", lambda e: e.dma_start(out=mu_sb[:], in_=mu_in),
                  sem="LB")
            P.dma("sync", lambda e: e.dma_start(out=sg_sb[:], in_=sig_in),
                  sem="LB")
            h1 = P.v(lambda v: v.tensor_mul(iv_sb[:], sg_sb[:], sg_sb[:]),
                         waits=[("LB", P.cnt["LB"])])
            h2 = P.v(lambda v: v.reciprocal(iv_sb[:], iv_sb[:]), waits=[h1])
            h3 = P.v(lambda v: v.tensor_mul(muiv[:], mu_sb[:], iv_sb[:]),
                     waits=[h2, WHs])
            h4 = P.v(lambda v: v.tensor_mul(c3_sb[:], mu_sb[:], muiv[:]),
                     waits=[h3])
            for fn in (lambda e: e.dma_start(out=lw1[:], in_=cc2o[0:1, 0:1024].partition_broadcast(8)),
                       lambda e: e.dma_start(out=lw2[:], in_=cc2o[1:2, 0:1024].partition_broadcast(8)),
                       lambda e: e.dma_start(out=m2ab[:], in_=cc2o[2:3, 1024:2048].partition_broadcast(8)),
                       lambda e: e.dma_start(out=q1b[:], in_=cc2o[3:4, 0:1].partition_broadcast(8)),
                       lambda e: e.dma_start(out=q2b[:], in_=cc2o[3:4, 1:2].partition_broadcast(8))):
                P.dma("sync", fn, waits=[evCC2], sem="LB")
            evLB = ("LB", P.cnt["LB"])
            inv_t = 1.0 / float(t_total)
            k1 = P.v(lambda v: v.tensor_scalar(
                lw1[:], lw1[:], q1b[:], inv_t, ALU.add, ALU.mult), waits=[evLB])
            k2a = P.v(lambda v: v.tensor_scalar(
                lw2[:], lw2[:], 2.0, None, ALU.mult), waits=[evLB])
            k2 = P.v(lambda v: v.tensor_add(lw2[:], lw2[:], m2ab[:]),
                     waits=[k2a])
            k3 = P.v(lambda v: v.tensor_scalar(
                lw2[:], lw2[:], q2b[:], inv_t, ALU.add, ALU.mult), waits=[k2])
            k4 = P.v(lambda v: v.tensor_mul(lw2[:], lw2[:], iv_sb[:]),
                     waits=[k3, h2])
            k5 = P.v(lambda v: v.tensor_mul(lw1[:], lw1[:], muiv[:]),
                     waits=[k1, h3])
            k6a = P.v(lambda v: v.tensor_scalar(
                lw1[:], lw1[:], -2.0, None, ALU.mult), waits=[k5])
            k6 = P.v(lambda v: v.tensor_add(lw1[:], lw1[:], lw2[:]),
                     waits=[k4, k6a])
            k7 = P.v(lambda v: v.tensor_add(lw1[:], lw1[:], c3_sb[:]),
                     waits=[k6, h4])
            k8 = P.v(lambda v: v.tensor_reduce(loss8[:], lw1[:], AX.X, ALU.add),
                     waits=[k7])
            k9 = P.dma("gpsimd", lambda g: g.dma_start(
                out=lossd.rearrange("a b -> b a"), in_=loss8[:],
                allow_slow_non_contiguous=True), waits=[k8])
            k10 = P.dma("gpsimd", lambda g: g.dma_start(out=lrow[:], in_=lossd),
                        waits=[k9])
            k11 = P.v(lambda v: v.tensor_reduce(lmin[:], lrow[:], AX.X, ALU.min),
                      waits=[k10])
            k12 = P.v(lambda v: v.tensor_single_scalar(
                lone[:], lrow[:], lmin[:], ALU.is_equal), waits=[k11])
            k13 = P.v(lambda v: v.tensor_copy(liot[:], iota8r[:]), waits=[evIr])
            k14 = P.v(lambda v: v.tensor_mul(liot[:], liot[:], lone[:]),
                      waits=[k13, k12])
            k15 = P.v(lambda v: v.tensor_reduce(idxf1[:], liot[:], AX.X, ALU.add),
                      waits=[k14])
            k16 = P.v(lambda v: v.tensor_copy(idxi1[:], idxf1[:]), waits=[k15])
            k17 = P.dma("gpsimd", lambda g: g.dma_start(out=idxd, in_=idxi1[:]),
                        waits=[k16])
            k18 = P.dma("gpsimd", lambda g: g.dma_start(
                out=idxb[:], in_=idxd.partition_broadcast(128)), waits=[k17])
            k19 = P.v(lambda v: v.tensor_copy(idxbf[:], idxb[:]), waits=[k18])
            k19b = P.v(lambda v: v.tensor_scalar(
                idc[:], idxbf[:], 1024.0, None, ALU.mult), waits=[k19])
            k19c = P.v(lambda v: v.tensor_scalar(
                idu[:], idxbf[0:64, :], 64.0, None, ALU.mult), waits=[k19])
            k20 = P.v(lambda v: v.tensor_copy(gcf[:], iota8x[:]), waits=[evI8])
            k21 = P.v(lambda v: v.tensor_scalar(
                gcf[:], gcf[:], idc[:], None, ALU.add), waits=[k19b, k20])
            k22 = P.v(lambda v: v.tensor_copy(gci[:], gcf[:]), waits=[k21])
            k23 = P.v(lambda v: v.tensor_copy(guf[:], iota_p[0:64, :]),
                      waits=[evIp])
            k24 = P.v(lambda v: v.tensor_scalar(
                guf[:], guf[:], idu[:], None, ALU.add), waits=[k19c, k23])
            k25 = P.v(lambda v: v.tensor_copy(gui[:], guf[:]), waits=[k24])

            x1buf = [sbc("x1C%d" % i, [128, 4096], F32) for i in range(3)]
            xn2 = [sbc("xn2_%d" % i, [128, D], BF16) for i in range(4)]
            xn2T = sbc("xn2T", [128, NCH * 512], BF16)
            hbf = sbc("hbf", [64, 512], BF16)
            aT = sbc("aT", [128, NCH * 512], BF16)
            ybf = [sbc("ybf0", [128, NCH * 512], BF16),
                   sbc("ybf1", [128, NCH * 512], BF16)]
            ybt = [sbc("ybt0", [128, D], BF16), sbc("ybt1", [128, D], BF16)]
            st3 = [sbc("st3_0", [128, 2], F32), sbc("st3_1", [128, 2], F32)]
            bn6c = [sbc("bn6c_0", [128, 12], F32), sbc("bn6c_1", [128, 12], F32)]
            rr3 = [sbc("rr3_0", [128, 2], F32), sbc("rr3_1", [128, 2], F32)]
            tl = [sbc("tl0", [128, D], F32), sbc("tl1", [128, D], F32)]
            x2b = [sbc("x2b0", [128, D], F32), sbc("x2b1", [128, D], F32)]

            ptrC2 = [psc("ptrC", [128, 1024], BF16), psc("pxtr", [128, 1024], BF16)]
            ptrC = ptrC2[0]
            pbias = ptrC2[1].ap().bitcast(F32)[:, 0:24]
            pa = [psc("pa0", [128, 512], F32), psc("pa1", [128, 512], F32)]
            py = [psc("py0", [128, 512], F32), psc("py1", [128, 512], F32)]
            ptry2 = [psc("ptry", [128, 1024], BF16), psc("ptry2", [128, 1024], BF16)]
            ph = pa[0].ap()[0:64, :]

            def x2nT(c, lo, hi):
                return xn2T[:, c * 512 + lo:c * 512 + hi]

            ln_state = {}
            ev_x1free = [None, None, None]
            ev_ptrC_free = [None, None]
            ev_xn2_free = [None, None, None, None]
            ev_hb_free = None
            ev_ph_free = None
            ev_pa_free = [None, None]
            ev_aT_free = None
            ev_py_free = [None, None]
            ev_ybf_free = [None, None]
            ev_ptry_free = [None, None]
            ev_ybt_free = [None, None]
            ev_tl_free = [None, None]
            ev_x2_free = [None, None]
            evXO_all = [("XO0", P.cnt["XO0"]), ("XO1", P.cnt["XO1"])]

            e1_evs = {}

            def load_c(m):
                pb = m % 3
                return P.dma("sync", lambda e, m=m, pb=pb: e.dma_start(
                    out=x1buf[pb].ap().rearrange("p (s d) -> p s d", s=4),
                    in_=x1d[m * 512:(m + 1) * 512, :].rearrange(
                        "(s p) d -> p s d", p=128)),
                    waits=[ev_x1free[pb]] + (evXO_all if m < 3 else []),
                    sem="XC%d" % (m % 2))

            def prep(m, evX1):
                pb = m % 3
                for s4 in range(SUB):
                    stile = m * SUB + s4
                    e1 = P.s(lambda s, pb=pb, s4=s4, stile=stile: s.activation(
                        xn2[s4][:], x1buf[pb][:, s4 * D:(s4 + 1) * D], AF.Identity,
                        scale=r2c[:, stile:stile + 1], bias=q2c[:, stile:stile + 1]),
                        waits=[evX1, ev_xn2_free[s4]])
                    e1_evs[(m, s4)] = e1

            prep_x1 = [None] * nM
            prep_x1[0] = load_c(0)
            if nM > 1:
                prep_x1[1] = load_c(1)
            prep(0, prep_x1[0])

            gD = []
            for c in range(NCH):
                gD.append(P.dma("gpsimd", lambda g, c=c: g.indirect_dma_start(
                    out=wdstg[:, c * R:(c + 1) * R], out_offset=None,
                    in_=wd_in.rearrange("e d r -> (e d) r"),
                    in_offset=IndirectOffsetOnAxis(gci[:, c:c + 1], 0)),
                    waits=[k22], sem="GD%d" % (c % 2)))
            gD2 = P.dma("gpsimd", lambda g: g.dma_start(out=wd_sb[:], in_=wdstg[:]),
                        waits=[("GD0", P.cnt["GD0"]), ("GD1", P.cnt["GD1"])],
                        sem="GD0")
            gU = P.dma("gpsimd", lambda g: g.indirect_dma_start(
                out=wustg[:], out_offset=None,
                in_=wu_in.rearrange("e r d -> (e r) d"),
                in_offset=IndirectOffsetOnAxis(gui[:, 0:1], 0)), waits=[k25],
                sem="GU")
            gU2 = P.dma("gpsimd", lambda g: g.dma_start(out=wu_sb[:], in_=wustg[:]),
                        waits=[gU], sem="GU")
            ev_stg_free = [None, None, None]
            stgs = [wstg, wstg2, wstg3]
            for c in range(NCH):
                stg = stgs[c % 3]
                gsem = "GC%d" % (c % 3)
                gA = P.dma("gpsimd", lambda g, c=c, stg=stg: g.indirect_dma_start(
                    out=stg[:], out_offset=None,
                    in_=wc_in.rearrange("e o d -> (e o) d"),
                    in_offset=IndirectOffsetOnAxis(gci[:, c:c + 1], 0)),
                    waits=[k22, ev_stg_free[c % 3]], sem=gsem)
                gB = P.dma("gpsimd", lambda g, c=c, stg=stg: g.dma_start(
                    out=wccb[c * 128:(c + 1) * 128, :], in_=stg[:]), waits=[gA],
                    sem=gsem)
                ev_stg_free[c % 3] = gB
            bias_evs = {}

            def emit_bias_sel():
                e_oh = P.t(lambda t: t.transpose(
                    ptrC2[1].ap().bitcast(F32)[0:8, 32:33], lone[:],
                    identf[0:1, 0:1]),
                    waits=[k12, evI4, ev_ptrC_free[1]])
                e_ohc = P.s(lambda s: s.activation(
                    ohc[:], ptrC2[1].ap().bitcast(F32)[0:8, 32:33], AF.Copy),
                    waits=[e_oh])
                evm = []
                evm.append(P.t(lambda t: t.matmul(
                    pbias[:, 0:1], lhsT=bu_b[:, 0:128], rhs=ohc[:],
                    start=True, stop=False), waits=[e_ohc, WGc]))
                evm.append(P.t(lambda t: t.matmul(
                    pbias[0:64, 16:17], lhsT=bd_b[:], rhs=ohc[:],
                    start=False, stop=False), waits=[e_ohc, WGc]))
                for c in range(1, NCH):
                    evm.append(P.t(lambda t, c=c: t.matmul(
                        pbias[:, c:c + 1],
                        lhsT=bu_b[:, c * 128:(c + 1) * 128], rhs=ohc[:],
                        start=False, stop=False), waits=[e_ohc, WGc]))
                for c in range(NCH):
                    evm.append(P.t(lambda t, c=c: t.matmul(
                        pbias[:, 8 + c:9 + c],
                        lhsT=bc_b[:, c * 128:(c + 1) * 128], rhs=ohc[:],
                        start=False, stop=(c == NCH - 1)), waits=[e_ohc, WGc]))
                eb1 = P.s(lambda s: s.activation(buT[:], pbias[:, 0:8], AF.Copy),
                          waits=[evm[-1]])
                eb2 = P.s(lambda s: s.activation(bcT[:], pbias[:, 8:16], AF.Copy),
                          waits=[evm[-1]])
                eb3 = P.s(lambda s: s.activation(bdT[:], pbias[0:64, 16:17],
                                                 AF.Copy), waits=[evm[-1]])
                eb4 = P.v(lambda v: v.tensor_add(bcombT[:], bcT[:], bbT[:]),
                          waits=[eb2, WHs])
                bias_evs["bu"] = eb1
                bias_evs["bd"] = eb3
                bias_evs["bcomb"] = eb4
                ev_ptrC_free[1] = eb3
            ev_wc_tr = []
            ev_sb2sb = [("GC0", P.cnt["GC0"]), ("GC1", P.cnt["GC1"]),
                        ("GC2", P.cnt["GC2"]), ("GC3", P.cnt["GC3"]),
                        ("GD0", P.cnt["GD0"]), ("GD1", P.cnt["GD1"]),
                        ("GU", P.cnt["GU"])]
            for c in range(NCH):
                ev_wc_tr.append(P.dma("sync", lambda s, c=c: s.dma_start(
                    out=wcT[:, c * D:(c + 1) * D],
                    in_=wccb[:, c * 128:(c + 1) * 128], transpose=True),
                    waits=ev_sb2sb, sem="WT%d" % (c % 2)))
            WHwc = [("WT0", P.cnt["WT0"]), ("WT1", P.cnt["WT1"])]
            GDt = ("GD0", P.cnt["GD0"])
            GUt = ("GU", P.cnt["GU"])




            yev_evs = {}

            def Ctail(mm):
                xpb2 = mm % 3
                pb2 = mm % 2
                ev_yev = yev_evs.pop(mm)
                evX1t = prep_x1[mm]
                nonlocal ev_ybf_free, ev_ptry_free, ev_ybt_free
                nonlocal ev_tl_free, ev_x2_free, ev_x1free
                for s4 in range(SUB):
                    sp = s4 % 2
                    yb = ptry2[sp]
                    evT = []
                    for c in range(NCH):
                        evT.append(P.t(lambda t, c=c, s4=s4, yb=yb, pb2=pb2:
                                       t.transpose(
                            yb[:, c * 128:(c + 1) * 128],
                            ybf[pb2][:, c * 512 + s4 * 128:c * 512 + (s4 + 1) * 128],
                            ident[:]),
                            waits=ev_yev + [ev_ptry_free[sp], evI2]))
                    if s4 == SUB - 1:
                        ev_ybf_free[pb2] = evT[-1]
                    eE = P.v(lambda v, sp=sp, yb=yb: v.tensor_copy(
                        ybt[sp][:], yb[:]),
                             waits=[evT[-1], ev_ybt_free[sp]])
                    ev_ptry_free[sp] = eE
                    ev_rr3, rr3_done = _ln(
                        P, bn6c[sp], st3[sp], rr3[sp],
                        (lambda sp=sp: ybt[sp][:]),
                        ln_state, ("C", sp), extra_waits=[eE])
                    b7 = P.v(lambda v, sp=sp: v.tensor_scalar(
                        tl[sp][:], ybt[sp][:], rr3[sp][:, 0:1],
                        rr3[sp][:, 1:2], ALU.mult, ALU.add),
                        waits=[ev_rr3, ev_tl_free[sp]])
                    rr3_done(b7)
                    ev_ybt_free[sp] = b7
                    b8 = P.g(lambda g, sp=sp, s4=s4, xpb2=xpb2: g.tensor_tensor(
                        x2b[sp][:], tl[sp][:],
                        x1buf[xpb2][:, s4 * D:(s4 + 1) * D], ALU.add),
                        waits=[b7, evX1t, ev_x2_free[sp]])
                    ev_tl_free[sp] = b8
                    if s4 == SUB - 1:
                        ev_x1free[xpb2] = b8
                    stile = mm * SUB + s4
                    evX2 = P.dma("sync", lambda e, stile=stile, sp=sp: e.dma_start(
                        out=y_out[stile * 128:(stile + 1) * 128, :],
                        in_=x2b[sp][:]),
                        waits=[b8], sem="X2O%d" % sp)
                    ev_x2_free[sp] = evX2

            for m in range(nM):
                pb = m % 2
                xpb = m % 3
                evX1 = prep_x1[m]
                ev_evE = []
                for s4 in range(SUB):
                    trb = ptrC2[s4 % 2]
                    evT = []
                    for c in range(NCH):
                        evT.append(P.t(lambda t, c=c, s4=s4, trb=trb: t.transpose(
                            trb[:, c * 128:(c + 1) * 128],
                            xn2[s4][:, c * 128:(c + 1) * 128], ident[:]),
                            waits=[e1_evs[(m, s4)], evI2, ev_ptrC_free[s4 % 2]]))
                    ev_xn2_free[s4] = evT[-1]
                    eE = P.v(lambda v, s4=s4, trb=trb: v.tensor_copy(
                        xn2T.ap().rearrange("p (c n) -> p c n", c=NCH)[:, :, s4 * 128:(s4 + 1) * 128],
                        trb.ap().rearrange("p (c n) -> p c n", c=NCH)),
                        waits=[evT[-1]])
                    ev_ptrC_free[s4 % 2] = eE
                    ev_evE.append(eE)
                if m + 1 < nM:
                    prep(m + 1, prep_x1[m + 1])
                if m == 0:
                    emit_bias_sel()
                evDn = None
                for c in range(NCH):
                    evDn = P.t(lambda t, c=c: t.matmul(
                        ph,
                        lhsT=wd_sb[:, c * R:(c + 1) * R],
                        rhs=x2nT(c, 0, 512),
                        start=(c == 0), stop=(c == NCH - 1)),
                        waits=ev_evE + [GDt, ev_ph_free] + (ev_pa_free[0] or []))
                evRl = P.s(lambda s: s.activation(hbf[:], ph, AF.Relu,
                                                  bias=bdT[:]),
                           waits=[evDn, bias_evs["bd"], ev_hb_free])
                ev_ph_free = evRl
                ev_up = []
                for c in range(NCH):
                    bank = (c + 1) % 2
                    evU = P.t(lambda t, c=c, bank=bank: t.matmul(
                        pa[bank][:],
                        lhsT=wu_sb[:, c * 128:(c + 1) * 128],
                        rhs=hbf[:],
                        start=True, stop=True),
                        waits=[evRl, GUt] + (ev_pa_free[bank] or []))
                    evUe = P.s(lambda s, c=c, bank=bank: s.activation(
                        aT[:, c * 512:(c + 1) * 512],
                        pa[bank][:], AF.Identity, bias=buT[:, c:c + 1]),
                        waits=[evU, bias_evs["bu"]] + ([ev_aT_free] if c == 0 else []))
                    ev_pa_free[bank] = [evUe]
                    ev_up.append(evUe)
                ev_hb_free = ev_up[-1]
                if m >= 1:
                    Ctail(m - 1)
                if m + 2 < nM:
                    prep_x1[m + 2] = load_c(m + 2)
                ev_yev = []
                for co in range(NCH):
                    bank = co % 2
                    evY = None
                    for c in range(NCH):
                        evY = P.t(lambda t, c=c, co=co, bank=bank: t.matmul(
                            py[bank][:],
                            lhsT=bwT[:, c * D + co * 128:c * D + (co + 1) * 128],
                            rhs=x2nT(c, 0, 512),
                            start=(c == 0), stop=False),
                            waits=ev_evE + WTs + [ev_py_free[bank]])
                    for c in range(NCH):
                        evY = P.t(lambda t, c=c, co=co, bank=bank: t.matmul(
                            py[bank][:],
                            lhsT=wcT[:, c * D + co * 128:c * D + (co + 1) * 128],
                            rhs=aT[:, c * 512:(c + 1) * 512],
                            start=False, stop=(c == NCH - 1)),
                            waits=ev_up + WHwc)
                    eYe = P.s(lambda s, co=co, bank=bank, pb=pb: s.activation(
                        ybf[pb][:, co * 512:(co + 1) * 512],
                        py[bank][:], AF.Identity, bias=bcombT[:, co:co + 1]),
                        waits=[evY, bias_evs["bcomb"]] + ([ev_ybf_free[pb]] if co == 0 else []))
                    ev_py_free[bank] = eYe
                    ev_yev.append(eYe)
                ev_aT_free = ev_yev[-1]
                yev_evs[m] = ev_yev

            Ctail(nM - 1)

            P.op("sync", None, waits=[("X2O0", P.cnt["X2O0"]),
                                      ("X2O1", P.cnt["X2O1"])])

        with nc.Block() as blk:
            P.emit(blk)
    return nc


def kernel(L0_attn_x, L0_attn_W_qkv, L0_attn_W_out, L0_attn_b_out,
           L0_ffn_base_W, L0_ffn_base_b, L0_ffn_disc_mu, L0_ffn_disc_sigma,
           L0_ffn_down_W, L0_ffn_down_b, L0_ffn_up_W, L0_ffn_up_b,
           L0_ffn_conv_W, L0_ffn_conv_b):
    x = np.ascontiguousarray(np.asarray(L0_attn_x, dtype=np.float32))
    B, N, _ = x.shape
    T = N // 2
    shared = {
        "wqkv": np.ascontiguousarray(L0_attn_W_qkv, dtype=np.float32),
        "wout": np.ascontiguousarray(L0_attn_W_out, dtype=np.float32),
        "bout": np.ascontiguousarray(L0_attn_b_out, dtype=np.float32),
        "basew": np.ascontiguousarray(L0_ffn_base_W, dtype=np.float32),
        "baseb": np.ascontiguousarray(L0_ffn_base_b, dtype=np.float32),
        "mu": np.ascontiguousarray(L0_ffn_disc_mu, dtype=np.float32),
        "sigma": np.ascontiguousarray(L0_ffn_disc_sigma, dtype=np.float32),
        "wd": np.ascontiguousarray(L0_ffn_down_W, dtype=np.float32),
        "bd": np.ascontiguousarray(L0_ffn_down_b, dtype=np.float32),
        "wu": np.ascontiguousarray(L0_ffn_up_W, dtype=np.float32),
        "bu": np.ascontiguousarray(L0_ffn_up_b, dtype=np.float32),
        "wc": np.ascontiguousarray(L0_ffn_conv_W, dtype=np.float32),
        "bc": np.ascontiguousarray(L0_ffn_conv_b, dtype=np.float32),
    }
    nc = build_program(T=T, t_total=N)
    in_maps = []
    for c in range(8):
        b, h = c // 2, c % 2
        mp = dict(shared)
        mp["x"] = np.ascontiguousarray(x[b, h * T:(h + 1) * T, :])
        in_maps.append(mp)
    res = run_bass_kernel_spmd(nc, in_maps, core_ids=list(range(8)))
    out = np.empty((B, N, D), dtype=np.float32)
    for c in range(8):
        b, h = c // 2, c % 2
        out[b, h * T:(h + 1) * T, :] = res.results[c]["y"]
    return out
